# version 05
# baseline (speedup 1.0000x reference)
import numpy as np
import ml_dtypes
import concourse.bass as bass
import concourse.mybir as mybir
from concourse.bass_utils import run_bass_kernel_spmd

F32 = mybir.dt.float32
BF16 = mybir.dt.bfloat16
AF = mybir.ActivationFunctionType
ALU = mybir.AluOpType
BF = ml_dtypes.bfloat16

B, L, DM = 2, 4096, 2048
DI, NH, HP, DS, DCONV = 4096, 64, 64, 128, 4
CONV = DI + 2 * DS
OPROJ = 2 * DI + 2 * DS + NH
OPAD = 8576
Q = 1024
EXT = Q + 3
CH = 128
NCH = Q // CH
EPS = 1e-5
FWHT_SCALE = 1.0 / np.sqrt(8192.0)

ENGS = ("sync", "gpsimd", "tensor", "vector", "scalar")


class Prog:

    def __init__(self, nc):
        self.nc = nc
        self.ops = {e: [] for e in ENGS}
        self.semh = {}
        self.cnt = {}
        self.waited = {e: {} for e in ENGS}

    def sem(self, name):
        self.semh[name] = self.nc.alloc_semaphore(name)
        self.cnt[name] = 0
        return name

    def emit(self, eng, fn, waits=(), sig=None, sig_n=1):
        ws = []
        for s, v in waits:
            if v is None or v <= 0:
                continue
            if v > self.waited[eng].get(s, 0):
                self.waited[eng][s] = v
                ws.append((self.semh[s], v))
        tgt = None
        ret = None
        if sig is not None:
            self.cnt[sig] += sig_n
            ret = self.cnt[sig]
            tgt = (self.semh[sig], sig_n)
        self.ops[eng].append((tuple(ws), fn, tgt))
        return ret

    def vspace(self, nc):
        self.emit("vector", lambda: nc.vector.tensor_scalar_mul(
            self._sp_ap, self._sp_ap, 1.0))

    def wait(self, eng, waits):
        for s, v in waits:
            if v is None or v <= 0:
                continue
            if v > self.waited[eng].get(s, 0):
                self.waited[eng][s] = v
                h = self.semh[s]
                self.ops[eng].append((((h, v),), None, None))

    def dma(self, q, out, in_, waits=(), sig=None, transpose=False):
        eng = {"sync": self.nc.sync, "gpsimd": self.nc.gpsimd,
               "scalar": self.nc.scalar}[q]
        if transpose:
            fn = lambda: eng.dma_start_transpose(out=out, in_=in_)
        else:
            fn = lambda: eng.dma_start(out=out, in_=in_)
        waits = list(waits)
        if sig is not None and self.cnt[sig] > 0:
            waits.append((sig, self.cnt[sig]))
        return self.emit(q, fn, waits=waits, sig=sig, sig_n=16)

    def replay(self, eng_name, eng_obj):
        for ws, fn, tgt in self.ops[eng_name]:
            for h, v in ws:
                eng_obj.wait_ge(h, v)
            if fn is not None:
                inst = fn()
                if tgt is not None:
                    inst.then_inc(tgt[0], tgt[1])


def ceil_div(a, b):
    return (a + b - 1) // b


DEBUG_LN1 = False
DEBUG_PH3 = False
PH3_CHUNKS = 8
PH_LEVEL = 5

def build_program(debug_taps=()):
    nc = bass.Bass("TRN2", target_bir_lowering=False, debug=False)

    def ein(name, shape, dt=F32):
        return nc.dram_tensor(name, shape, dt, kind="ExternalInput").ap()

    u_ext = ein("u_ext", [EXT, DM])
    win_sl = ein("win_sl", [256, OPAD])
    wout_sl = ein("wout_sl", [512, DM])
    w1_sl = ein("w1_sl", [256, 8192])
    w3_sl = ein("w3_sl", [256, 8192])
    w2_sl = ein("w2_sl", [256, 8192])
    nw_t = ein("nw_t", [128, 4])
    convw_t = ein("convw_t", [128, 34, 4])
    convb_t = ein("convb_t", [128, 34])
    dtb_row = ein("dtb_row", [1, 128])
    alog_col = ein("alog_col", [64, 1])
    dp_rep = ein("dp_rep", [1, DI])
    nw_row = ein("nw_row", [1, DI])
    selm = ein("selm", [4, 3])
    selv = ein("selv", [3, 1])

    out_h = nc.dram_tensor("out", [Q, DM], F32, kind="ExternalOutput").ap()

    def internal(name, shape, dt):
        return nc.dram_tensor(name, shape, dt, kind="Internal").ap()

    win_bf = internal("win_bf", [DM, OPAD], BF16)
    wout_bf = internal("wout_bf", [DI, DM], BF16)
    w1_bf = internal("w1_bf", [DM, 8192], BF16)
    w3_bf = internal("w3_bf", [DM, 8192], BF16)
    w2pp_bf = internal("w2pp_bf", [8 * 8192, 256], BF16)
    ccin_win = internal("ccin_win", [256, OPAD], BF16)
    ccin_wout = internal("ccin_wout", [512, DM], BF16)
    ccin_w1 = internal("ccin_w1", [256, 8192], BF16)
    ccin_w3 = internal("ccin_w3", [256, 8192], BF16)
    ccin_w2 = internal("ccin_w2", [8192, 256], BF16)

    z_sp = internal("z_sp", [DI, Q], BF16)
    xfm_sp = internal("xfm_sp", [DI, Q], BF16)
    st_sp = internal("st_sp", [NCH, 128, NH * HP], BF16)
    resid_sp = internal("resid_sp", [Q, DM], F32)
    fl_negrp = internal("fl_negrp", [1, NCH * NH * CH], F32)
    fl_r = internal("fl_r", [1, NCH * NH * CH], F32)
    wrow_b = internal("wrow_b", [1, 4 * NH], F32)
    cc_st_in = internal("cc_st_in", [128, NH * HP], BF16)
    cc_st_out = internal("cc_st_out", [512, NH * HP], BF16)
    cc_ld_in = internal("cc_ld_in", [NH, 1], F32)
    cc_ld_out = internal("cc_ld_out", [4 * NH, 1], F32)

    idb = nc.inline_tensor(np.eye(128, dtype=BF), name="id_bf").ap()
    idf = nc.inline_tensor(np.eye(128, dtype=np.float32), name="id_f32").ap()
    ones_row = nc.inline_tensor(np.ones((1, 512), np.float32), name="ones_row").ap()
    ones_big = nc.inline_tensor(np.ones((1, NH * CH), np.float32), name="ones_big").ap()
    h128c = nc.inline_tensor(
        np.array([[1.0 - 2.0 * (bin(i & j).count("1") & 1) for j in range(128)]
                  for i in range(128)], dtype=BF), name="h128").ap()
    mT = np.triu(np.ones((128, 128), np.float32)).astype(BF)
    maskT_c = nc.inline_tensor(mT, name="maskT").ap()

    SB = {}
    SB_BASE = 16384 + 2048
    SB_LIMIT = 229376
    sb_off = [SB_BASE]

    def sbuf(name, shape, dt, at=None):
        nbytes = int(np.prod(shape[1:])) * mybir.dt.size(dt)
        if at is None:
            at = sb_off[0]
            sb_off[0] = at + ((nbytes + 31) // 32) * 32
        h = nc.alloc_sbuf_tensor_at(name, shape, dt, offset=at)
        SB[name] = h
        return h

    c_fm = sbuf("c_fm", [128, Q], BF16)
    b_fm = sbuf("b_fm", [128, Q], BF16)
    r_tm = sbuf("r_tm", [128, NCH, NH], F32)
    expf_tm = sbuf("expf_tm", [128, NCH, NH], F32)
    dt_tm = sbuf("dt_tm", [128, NCH, NH], F32)
    dt_fm = sbuf("dt_fm", [64, Q], F32)
    lndt_fm = sbuf("lndt_fm", [64, Q], F32)
    r_fm = sbuf("r_fm", [64, Q], F32)
    hT = sbuf("hT", [128, NH * HP], BF16)
    dcol = sbuf("dcol", [128, NCH * NH], F32)
    dpb = sbuf("dpb", [128, DI], BF16)
    lndt_tm = sbuf("lndt_tm", [128, NCH, NH], F32)
    nrp_tm = sbuf("nrp_tm", [128, NCH, NH], F32)
    rl_cols = sbuf("rl_cols", [128, NCH * NH], F32)
    smalls = sbuf("smalls", [128, 256], F32)
    p_spacer_pending = True
    wq = sbuf("wq", [4, 64], F32)
    wq2 = sbuf("wq2", [3, 64], F32)
    wcols = sbuf("wcols", [128, 4 * NH], F32)
    cw_sb = sbuf("cw_sb", [128, 34, 4], F32)
    cb_sb = sbuf("cb_sb", [128, 34], F32)
    maskT_s = sbuf("maskT_s", [128, 128], BF16)
    h128_s = sbuf("h128_s", [128, 128], BF16)
    dtb_s = sbuf("dtb_s", [1, 128], F32)
    idb_s = sbuf("idb_s", [128, 128], BF16)
    idf_s = sbuf("idf_s", [128, 128], F32)
    onesr_s = sbuf("onesr_s", [1, 512], F32)
    rowb_s = sbuf("rowb_s", [1, 512], F32)

    R0 = sb_off[0]
    RSZ = SB_LIMIT - R0
    assert RSZ >= 150 * 1024, (R0, RSZ)

    def reg(name, shape, dt, at):
        nbytes = int(np.prod(shape[1:])) * mybir.dt.size(dt)
        assert at + nbytes <= RSZ, (name, at, nbytes, RSZ)
        return nc.alloc_sbuf_tensor_at(name, shape, dt, offset=R0 + at)

    w2a = reg("w2a", [128, 8192], F32, 0)
    w2b = reg("w2b", [128, 8192], F32, 32768)
    cvt = reg("cvt", [128, DI], F32, 65536)
    cvb = reg("cvb", [128, 2, 2048], BF16, 81920)
    nwb = reg("nwb", [128, DI], F32, 90112)
    srow = reg("srow", [1, 4096], F32, 106496)
    w2stg = reg("w2stg", [128, 128], BF16, 122880)

    PB = []
    for i in range(6):
        PB.append(nc.alloc_psum_tensor(f"pb{i}", [128, 512], F32))
    PT = []
    for i in range(2):
        PT.append(nc.alloc_psum_tensor(f"pt{i}", [128, 1024], BF16))

    p = Prog(nc)
    p._sp_ap = smalls.ap()[:, 254:255]
    SQ = [p.sem(f"sq{i}") for i in range(16)]
    GQ = [p.sem(f"gq{i}") for i in range(6)]
    T = p.sem("T")
    V = p.sem("V")
    A = p.sem("A")
    CC = p.sem("CC")

    bank_free = {i: None for i in range(8)}

    def bank_wait(i):
        return [bank_free[i]] if bank_free[i] is not None else []

    def cast_weight(src, dst, rows, qsem_ld, qsem_st):
        done = None
        r0 = 0
        i = 0
        while r0 < rows:
            n = min(128, rows - r0)
            buf = i % 2
            w = [cvb_free[buf]] if cvb_free[buf] else []
            c1 = p.dma("gpsimd", cvb.ap()[0:n, buf, :], src[r0:r0 + n, :],
                       waits=w, sig=qsem_ld)
            done = p.dma("gpsimd", dst[r0:r0 + n, :], cvb.ap()[0:n, buf, :],
                         waits=[(qsem_ld, c1)], sig=qsem_st)
            cvb_free[buf] = (qsem_st, done)
            r0 += n
            i += 1
        return done

    zv0 = p.emit("vector", lambda: nc.vector.memset(
        cvb.ap()[:, :, :].rearrange("p a c -> p (a c)"), 0.0), sig=V)
    czp = None
    cvb_free = [None, None]

    def cast_weight(src_ap, dst, rows, cols, qsem_ld, qsem_st, extra=None):
        done = None
        i = 0
        for r0 in range(0, rows, 128):
            n = min(128, rows - r0)
            for c0 in range(0, cols, 2048):
                w = min(2048, cols - c0)
                buf = i % 2
                ww = [cvb_free[buf]] if cvb_free[buf] else []
                c1 = p.dma("gpsimd", cvb.ap()[0:n, buf, 0:w],
                           src_ap[r0:r0 + n, c0:c0 + w], waits=ww, sig=qsem_ld)
                done = p.dma("gpsimd", dst[r0:r0 + n, c0:c0 + w],
                             cvb.ap()[0:n, buf, 0:w],
                             waits=[(qsem_ld, c1)], sig=qsem_st)
                cvb_free[buf] = (qsem_st, done)
                i += 1
        return done

    cwin = cast_weight(win_sl, ccin_win, 256, OPAD, GQ[0], GQ[1])
    agwin = p.emit("gpsimd", lambda: nc.gpsimd.collective_compute(
        "AllGather", ALU.bypass, replica_groups=[list(range(8))],
        ins=[ccin_win.opt()], outs=[win_bf[:, :]]),
        waits=[(GQ[1], cwin)], sig=CC)

    lnw = p.dma("sync", smalls.ap()[:, 16:20], nw_t[:, :], sig=SQ[1])
    cwout = None
    cvt_free = None
    for i in range(4):
        w = [cvt_free] if cvt_free else []
        c1 = p.dma("sync", cvt.ap()[:, 0:2048], wout_sl[i * 128:(i + 1) * 128, :],
                   waits=w + [(SQ[1], lnw)], sig=SQ[1])
        buf = i % 2
        ww = [cvb_free[buf]] if cvb_free[buf] else []
        mv = p.emit("vector", (lambda i=i, buf=buf: nc.vector.tensor_single_scalar(
            cvb.ap()[:, buf, :], cvt.ap()[:, 0:2048],
            smalls.ap()[:, 16 + i:17 + i], ALU.mult)),
            waits=[(SQ[1], c1)] + ww, sig=V)
        cvt_free = (V, mv)
        cwout = p.dma("gpsimd", ccin_wout[i * 128:(i + 1) * 128, :],
                      cvb.ap()[:, buf, :], waits=[(V, mv)], sig=GQ[1])
        cvb_free[buf] = (GQ[1], cwout)
    agwout = p.emit("gpsimd", lambda: nc.gpsimd.collective_compute(
        "AllGather", ALU.bypass, replica_groups=[list(range(8))],
        ins=[ccin_wout.opt()], outs=[wout_bf[:, :]]),
        waits=[(GQ[1], cwout)], sig=CC)

    cw1 = cast_weight(w1_sl, ccin_w1, 256, 8192, GQ[0], GQ[1])
    agw1 = p.emit("gpsimd", lambda: nc.gpsimd.collective_compute(
        "AllGather", ALU.bypass, replica_groups=[list(range(8))],
        ins=[ccin_w1.opt()], outs=[w1_bf[:, :]]),
        waits=[(GQ[1], cw1)], sig=CC)
    cw3 = cast_weight(w3_sl, ccin_w3, 256, 8192, GQ[0], GQ[1])
    agw3 = p.emit("gpsimd", lambda: nc.gpsimd.collective_compute(
        "AllGather", ALU.bypass, replica_groups=[list(range(8))],
        ins=[ccin_w3.opt()], outs=[w3_bf[:, :]]),
        waits=[(GQ[1], cw3)], sig=CC)

    idld = p.dma("sync", idb_s.ap()[:, :], idb[:, :], sig=SQ[0])
    vv = None
    cw2 = None
    PT0_F = None
    for d in range(2):
        cw2l = p.dma("sync", w2a.ap()[:, :], w2_sl[d * 128:(d + 1) * 128, :],
                     waits=[(V, vv)] if vv else [], sig=SQ[1])
        pa = w2a.ap()[:, :]
        pb_ = w2b.ap()[:, :]
        for b_ in range(6):
            s = 1 << b_
            blk = 2 * s * 128
            g = 8192 // blk
            cur_ap = pa if (b_ % 2 == 0) else pb_
            oth_ap = pb_ if (b_ % 2 == 0) else pa
            lo_in = cur_ap.rearrange("p (g r) -> p g r", g=g)[:, :, 0:s * 128]
            hi_in = cur_ap.rearrange("p (g r) -> p g r", g=g)[:, :, s * 128:blk]
            lo_out = oth_ap.rearrange("p (g r) -> p g r", g=g)[:, :, 0:s * 128]
            hi_out = oth_ap.rearrange("p (g r) -> p g r", g=g)[:, :, s * 128:blk]
            w = [(SQ[1], cw2l)] if b_ == 0 else []
            vv = p.emit("vector", (lambda a=lo_out, x=lo_in, y=hi_in:
                                   nc.vector.tensor_add(a, x, y)), waits=w, sig=V)
            vv = p.emit("vector", (lambda a=hi_out, x=lo_in, y=hi_in:
                                   nc.vector.tensor_sub(a, x, y)), sig=V)
        for hhalf in range(2):
            vv2 = p.emit("vector", (lambda h=hhalf: nc.vector.tensor_scalar_mul(
                cvb.ap()[:, :, :].rearrange("p a c -> p (a c)")[:, 0:4096],
                w2a.ap()[:, h * 4096:(h + 1) * 4096], FWHT_SCALE)),
                waits=([(V, vv)] if vv else []) + [(T, p.cnt[T])] +
                      [x for x in cvb_free if x], sig=V)
            vv = vv2
            for ft in range(32):
                f = hhalf * 32 + ft
                wpt0 = [PT0_F] if PT0_F else []
                tm = p.emit("tensor", (lambda ft=ft: nc.tensor.transpose(
                    PT[0].ap()[:, 0:128],
                    cvb.ap()[:, :, :].rearrange("p a c -> p (a c)")[:, ft * 128:(ft + 1) * 128],
                    idb_s.ap()[:, :])),
                    waits=[(V, vv2), (SQ[0], idld)] + wpt0, sig=T)
                ev = p.emit("vector", (lambda: nc.vector.tensor_copy(
                    w2stg.ap()[:, :], PT[0].ap()[:, 0:128])),
                    waits=[(T, tm)], sig=V)
                PT0_F = (V, ev)
                cw2 = p.dma("sync", ccin_w2[f * 128:(f + 1) * 128, d * 128:(d + 1) * 128],
                            w2stg.ap()[:, :], waits=[(V, ev)], sig=SQ[3])
                PT0_F = (SQ[3], cw2)
            cvb_free = [(SQ[3], cw2), (SQ[3], cw2)]
    agw2 = p.emit("gpsimd", lambda: nc.gpsimd.collective_compute(
        "AllGather", ALU.bypass, replica_groups=[list(range(8))],
        ins=[ccin_w2.opt()], outs=[w2pp_bf[:, :]]),
        waits=[(SQ[3], cw2)], sig=CC)

    AG_WIN, AG_WOUT, AG_W1, AG_W3, AG_W2 = agwin, agwout, agw1, agw3, agw2


    rowhbm = internal("rowhbm", [1, 512], F32)
    ln1_dbg = internal("ln1_dbg", [2, 128, 16, 515], BF16)
    u_dbg = internal("u_dbg", [128, 2048], F32)
    sm_dbg = internal("sm_dbg", [128, 256], F32)
    lt_dbg = internal("lt_dbg", [128, 2048], BF16)

    wcache = reg("wc", [128, 2, 16, 512], BF16, 0)
    ln1fm = reg("ln1fm", [128, 16, 1027], BF16, 32768)
    xbc = reg("xbc", [128, 4, 1027], BF16, 65632)
    ustage = reg("ustage", [128, 2, 2048], F32, 73856)
    ln1tm = reg("ln1tm", [128, 2, 2048], BF16, 90240)
    lnscr = reg("lnscr", [128, 2048], BF16, 98432)
    zstage = reg("zstage", [128, 2, 512], BF16, 102528)
    convacc = reg("convacc", [128, 2, 1024], F32, 104576)
    convstage = reg("convstage", [128, 2, 1024], BF16, 112768)
    dtscr = reg("dtscr", [128, 64], F32, 116864)
    xstage = reg("xstage", [128, 2, 4096], BF16, 117120)
    xw_sb = reg("xw_sb", [128, 512], BF16, 133504)
    stst = reg("stst", [128, 4096], BF16, 134528)
    wtm = reg("wtm", [128, 64], F32, 142720)
    wtm2 = reg("wtm2", [128, 64], F32, 142976)
    dascr = reg("dascr", [64, 1024], F32, 143232)
    nrpscr = reg("nrpscr", [64, 1024], F32, 147328)
    btm = reg("btm", [128, 128], BF16, 151424)
    stripsb = reg("stripsb", [3, 512], BF16, 151680)

    PH0_V = p.cnt[V]
    PH0_A = p.cnt[A]
    PH0_T = p.cnt[T]
    PH0_GQ1 = p.cnt[GQ[1]]
    PH0_SQ1 = p.cnt[SQ[1]]
    PH0_BAR = [(V, PH0_V), (A, PH0_A), (T, PH0_T), (GQ[1], PH0_GQ1), (SQ[1], PH0_SQ1)]
    for _e in ("sync", "gpsimd", "tensor", "vector", "scalar"):
        p.wait(_e, PH0_BAR)

    lsm = p.dma("sync", cw_sb.ap()[:, :, :], convw_t[:, :, :], sig=SQ[0])
    lsm = p.dma("sync", cb_sb.ap()[:, :], convb_t[:, :], sig=SQ[0])
    lsm = p.dma("sync", maskT_s.ap()[:, :], maskT_c[:, :], sig=SQ[0])
    lsm = p.dma("sync", h128_s.ap()[:, :], h128c[:, :], sig=SQ[0])
    lsm = p.dma("sync", dtb_s.ap()[:, :], dtb_row[:, :], sig=SQ[0])
    lsm = p.dma("sync", onesr_s.ap()[:, :], ones_row[:, :], sig=SQ[0])
    lsm = p.dma("sync", idb_s.ap()[:, :], idb[:, :], sig=SQ[0])
    lsm = p.dma("sync", idf_s.ap()[:, :], idf[:, :], sig=SQ[0])
    lsm = p.dma("sync", smalls.ap()[0:64, 48:49], alog_col[:, :], sig=SQ[0])
    SMALL_LD = lsm
    ldp = p.dma("gpsimd", dpb.ap()[0:1, 0:4096], dp_rep[:, :], sig=GQ[2])
    aa = p.emit("scalar", lambda: nc.scalar.activation(
        smalls.ap()[0:64, 49:50], smalls.ap()[0:64, 48:49], AF.Exp),
        waits=[(SQ[0], SMALL_LD)], sig=A)
    vna = p.emit("vector", lambda: nc.vector.tensor_scalar_mul(
        smalls.ap()[0:64, 50:51], smalls.ap()[0:64, 49:50], -1.0),
        waits=[(A, aa)], sig=V)
    NEGA = vna
    dpw = None
    for j in range(8):
        mm = p.emit("tensor", (lambda j=j: nc.tensor.matmul(
            PB[4].ap()[:, :], h128_s.ap()[0:1, 0:128],
            dpb.ap()[0:1, j * 512:(j + 1) * 512], start=True, stop=True)),
            waits=[(SQ[0], SMALL_LD), (GQ[2], ldp)] + bank_wait(4), sig=T)
        dpw = p.emit("vector", (lambda j=j: nc.vector.tensor_copy(
            dpb.ap()[:, j * 512:(j + 1) * 512], PB[4].ap()[:, :])),
            waits=[(T, mm)], sig=V)
        bank_free[4] = (V, dpw)
    DPB_OK = dpw

    hz = p.emit("vector", lambda: nc.vector.memset(hT.ap()[:, :], 0.0), sig=V)
    ve1 = p.emit("vector", lambda: nc.vector.memset(smalls.ap()[:, 54:55], EPS), sig=V)
    ve2 = p.emit("vector", lambda: nc.vector.memset(smalls.ap()[:, 55:56], 1.0 / DI), sig=V)
    EPSC = ve2

    UST_FREE = [None, None]
    LN1TM_FREE = [None, None]
    PT_FREE = [None, None]
    SSQ_FREE = [None, None]
    tiles = [(0, 128, 0), (128, 128, 128), (256, 128, 256), (384, 128, 384),
             (512, 128, 512), (640, 128, 640), (768, 128, 768), (896, 128, 896),
             (1024, 3, 1024)]
    for ti, (e0, n, co) in enumerate(tiles):
        ub = ti % 2
        w = list(UST_FREE[ub]) if UST_FREE[ub] else []
        usem = SQ[1 + 5 * ub]
        cu = p.dma("sync", ustage.ap()[0:n, ub, :], u_ext[e0:e0 + n, :],
                   waits=w, sig=usem)
        base = 16 * (ti % 2)
        vsum = p.emit("vector", (lambda n=n, ub=ub, b=base: nc.vector.reduce_sum(
            smalls.ap()[0:n, b + 0:b + 1], ustage.ap()[0:n, ub, :],
            axis=mybir.AxisListType.X)), waits=[(usem, cu)], sig=V)
        p.emit("scalar", (lambda n=n, ub=ub, b=base: nc.scalar.activation(
            lnscr.ap()[0:n, :], ustage.ap()[0:n, ub, :], AF.Square,
            accum_out=smalls.ap()[0:n, b + 1:b + 2])),
            waits=[(usem, cu)] + ([SSQ_FREE[ti % 2]] if SSQ_FREE[ti % 2] else []))
        asq = p.emit("scalar", (lambda n=n, b=base: nc.scalar.copy(
            smalls.ap()[0:n, b + 7:b + 8], smalls.ap()[0:n, b + 1:b + 2])),
            sig=A)
        p.vspace(nc)
        v1 = p.emit("vector", (lambda n=n, b=base: nc.vector.tensor_scalar_mul(
            smalls.ap()[0:n, b + 2:b + 3], smalls.ap()[0:n, b + 0:b + 1],
            1.0 / 2048.0)), waits=[(V, vsum)], sig=V)
        v2 = p.emit("vector", (lambda n=n, b=base: nc.vector.tensor_scalar_mul(
            smalls.ap()[0:n, b + 3:b + 4], smalls.ap()[0:n, b + 1:b + 2],
            1.0 / 2048.0)), waits=[(A, asq)], sig=V)
        SSQ_FREE[ti % 2] = (V, v2)
        v3 = p.emit("vector", (lambda n=n, b=base: nc.vector.tensor_single_scalar(
            smalls.ap()[0:n, b + 4:b + 5], smalls.ap()[0:n, b + 2:b + 3],
            smalls.ap()[0:n, b + 2:b + 3], ALU.mult)), sig=V)
        p.vspace(nc)
        v4 = p.emit("vector", (lambda n=n, b=base: nc.vector.tensor_single_scalar(
            smalls.ap()[0:n, b + 3:b + 4], smalls.ap()[0:n, b + 3:b + 4],
            smalls.ap()[0:n, b + 4:b + 5], ALU.subtract)), sig=V)
        a2 = p.emit("scalar", (lambda n=n, b=base: nc.scalar.activation(
            smalls.ap()[0:n, b + 5:b + 6], smalls.ap()[0:n, b + 3:b + 4],
            AF.Sqrt, bias=smalls.ap()[0:n, 54:55])),
            waits=[(V, v4), (V, EPSC)], sig=A)
        v5 = p.emit("vector", (lambda n=n, b=base: nc.vector.reciprocal(
            smalls.ap()[0:n, b + 6:b + 7], smalls.ap()[0:n, b + 5:b + 6])),
            waits=[(A, a2)], sig=V)
        wlt = [LN1TM_FREE[ub]] if LN1TM_FREE[ub] else []
        p.vspace(nc)
        v6 = p.emit("vector", (lambda n=n, ub=ub, b=base: nc.vector.tensor_scalar(
            ln1tm.ap()[0:n, ub, :], ustage.ap()[0:n, ub, :],
            smalls.ap()[0:n, b + 2:b + 3], smalls.ap()[0:n, b + 6:b + 7],
            ALU.subtract, ALU.mult)), waits=[(V, v5)] + wlt, sig=V)
        UST_FREE[ub] = [(V, v6), (A, asq)]
        for k in range(16):
            ptb = k % 2
            wpt = [PT_FREE[ptb]] if PT_FREE[ptb] else []
            tm = p.emit("tensor", (lambda n=n, ub=ub, k=k, ptb=ptb:
                        nc.tensor.transpose(
                            PT[ptb].ap()[:, 0:n],
                            ln1tm.ap()[0:n, ub, k * 128:(k + 1) * 128],
                            idb_s.ap()[0:n, 0:n])),
                        waits=[(V, v6), (SQ[0], SMALL_LD)] + wpt, sig=T)
            if k % 2 == 0:
                ev = p.emit("vector", (lambda n=n, k=k, co=co, ptb=ptb:
                            nc.vector.tensor_copy(
                                ln1fm.ap()[:, k, co:co + n], PT[ptb].ap()[:, 0:n])),
                            waits=[(T, tm)], sig=V)
                PT_FREE[ptb] = (V, ev)
            else:
                ev = p.emit("scalar", (lambda n=n, k=k, co=co, ptb=ptb:
                            nc.scalar.copy(
                                ln1fm.ap()[:, k, co:co + n], PT[ptb].ap()[:, 0:n])),
                            waits=[(T, tm)], sig=A)
                PT_FREE[ptb] = (A, ev)
        LN1TM_FREE[ub] = (T, p.cnt[T])
    LN1FM_OK = [(V, p.cnt[V]), (A, p.cnt[A])]

    OCW = [512] * 16 + [384]
    WC_FREE = [None, None]
    ZST_FREE = [None, None]
    CONVST_FREE = [None, None]
    CONVACC_FREE = [None, None]
    STRIP_FREE = None
    slab_free = {}
    slab_ready = {}
    xspill = {}
    dt_mm_done = {}
    z_done = 0

    for oc in range(17):
        wb = oc % 2
        width = OCW[oc]
        wwc = [WC_FREE[wb]] if WC_FREE[wb] else []
        lw = []
        for k in range(16):
            wsem = SQ[[2, 7, 11, 12][k % 4]]
            lwv = p.dma("sync", wcache.ap()[:, wb, k, 0:width],
                        win_bf[k * 128:(k + 1) * 128, oc * 512:oc * 512 + width],
                        waits=wwc + [(CC, AG_WIN)], sig=wsem)
            lw.append((wsem, lwv))
            wwc = []
        lw = [(sm, max(vv for sm2, vv in lw if sm2 == sm)) for sm in set(s0 for s0, _ in lw)]
        nots = 4 if oc < 16 else 2
        if oc >= 8:
            mms = None
            for k in range(16):
                mms = p.emit("tensor", (lambda k=k, wb=wb, width=width:
                            nc.tensor.matmul(
                                PB[5].ap()[0:3, 0:width],
                                ln1fm.ap()[:, k, 0:3],
                                wcache.ap()[:, wb, k, 0:width],
                                start=(k == 0), stop=(k == 15))),
                            waits=(lw + LN1FM_OK + bank_wait(5)) if k == 0 else [],
                            sig=(T if k == 15 else None))
            wstf = [STRIP_FREE] if STRIP_FREE else []
            evs = p.emit("vector", (lambda width=width: nc.vector.tensor_copy(
                stripsb.ap()[0:3, 0:width], PB[5].ap()[0:3, 0:width])),
                waits=[(T, mms)] + wstf, sig=V)
            bank_free[5] = (V, evs)
            STRIP_V = evs
        for oti in range(nots):
            g = oc * 4 + oti
            rot = g % 4
            pp = g % 2
            for h in range(2):
                mm = None
                for k in range(16):
                    mm = p.emit("tensor", (lambda oti=oti, k=k, pp=pp, wb=wb, h=h:
                                nc.tensor.matmul(
                                    PB[pp].ap()[:, :],
                                    wcache.ap()[:, wb, k, oti * 128:(oti + 1) * 128],
                                    ln1fm.ap()[:, k, 3 + h * 512:3 + (h + 1) * 512],
                                    start=(k == 0), stop=(k == 15))),
                                waits=(lw + LN1FM_OK + bank_wait(pp)) if k == 0 else [],
                                sig=(T if k == 15 else None))
                if g < 32:
                    zb = (g * 2 + h) % 2
                    wz = [ZST_FREE[zb]] if ZST_FREE[zb] else []
                    ve = p.emit("vector", (lambda zb=zb, pp=pp: nc.vector.tensor_copy(
                        zstage.ap()[:, zb, :], PB[pp].ap()[:, :])),
                        waits=[(T, mm)] + wz, sig=V)
                    bank_free[pp] = (V, ve)
                    zsem = SQ[3 + 5 * zb]
                    zd = p.dma("sync", z_sp[g * 128:(g + 1) * 128,
                                            h * 512:(h + 1) * 512],
                               zstage.ap()[:, zb, :], waits=[(V, ve)], sig=zsem)
                    ZST_FREE[zb] = (zsem, zd)
                    z_done = zd
                else:
                    wsl = [slab_free[rot]] if rot in slab_free else []
                    ve = p.emit("vector", (lambda rot=rot, h=h, pp=pp:
                                nc.vector.tensor_copy(
                                    xbc.ap()[:, rot, 3 + h * 512:3 + (h + 1) * 512],
                                    PB[pp].ap()[:, :])),
                                waits=[(T, mm)] + (wsl if h == 0 else []), sig=V)
                    bank_free[pp] = (V, ve)
            if g >= 32:
                wpt = [PT_FREE[0]] if PT_FREE[0] else []
                tms = p.emit("tensor", (lambda oti=oti: nc.tensor.transpose(
                    PT[0].ap()[:, 0:3], stripsb.ap()[0:3, oti * 128:(oti + 1) * 128],
                    idb_s.ap()[0:3, 0:3])),
                    waits=[(V, STRIP_V)] + wpt, sig=T)
                evh = p.emit("vector", (lambda rot=rot: nc.vector.tensor_copy(
                    xbc.ap()[:, rot, 0:3], PT[0].ap()[:, 0:3])),
                    waits=[(T, tms)], sig=V)
                PT_FREE[0] = (V, evh)
                slab_ready[g] = (V, p.cnt[V])
                ot = g - 32
                ca = ot % 2
                wc0 = ([(SQ[0], SMALL_LD)] if ot == 0 else [])
                wca = [CONVACC_FREE[ca]] if CONVACC_FREE[ca] else []
                v1 = p.emit("vector", (lambda ot=ot, ca=ca, rot=rot:
                            nc.vector.tensor_single_scalar(
                                convacc.ap()[:, ca, :], xbc.ap()[:, rot, 0:1024],
                                cw_sb.ap()[:, ot, 0:1], ALU.mult)),
                            waits=wc0 + wca + [slab_ready[g]], sig=V)
                for k in range(1, 4):
                    v1 = p.emit("vector", (lambda ot=ot, k=k, ca=ca, rot=rot:
                                nc.vector.scalar_tensor_tensor(
                                    convacc.ap()[:, ca, :], xbc.ap()[:, rot, k:1024 + k],
                                    cw_sb.ap()[:, ot, k:k + 1], convacc.ap()[:, ca, :],
                                    ALU.mult, ALU.add)), sig=V)
                slab_free[rot] = (V, v1)
                if ot < 32:
                    cb2 = ot % 2
                    wcs = [CONVST_FREE[cb2]] if CONVST_FREE[cb2] else []
                    a1 = p.emit("scalar", (lambda ot=ot, cb2=cb2, ca=ca:
                                nc.scalar.activation(
                                    convstage.ap()[:, cb2, :], convacc.ap()[:, ca, :],
                                    AF.Silu, bias=cb_sb.ap()[:, ot:ot + 1])),
                                waits=[(V, v1)] + wcs, sig=A)
                    CONVACC_FREE[ca] = (A, a1)
                    xsem = SQ[13 + cb2]
                    xd = p.dma("sync", xfm_sp[ot * 128:(ot + 1) * 128, :],
                               convstage.ap()[:, cb2, :], waits=[(A, a1)], sig=xsem)
                    CONVST_FREE[cb2] = (xsem, xd)
                    xspill[ot] = (xsem, xd)
                else:
                    dstap = b_fm if ot == 32 else c_fm
                    a1 = p.emit("scalar", (lambda dstap=dstap, ca=ca, ot=ot:
                                nc.scalar.activation(
                                    dstap.ap()[:, :], convacc.ap()[:, ca, :],
                                    AF.Silu, bias=cb_sb.ap()[:, ot:ot + 1])),
                                waits=[(V, v1)], sig=A)
                    CONVACC_FREE[ca] = (A, a1)
        WC_FREE[wb] = (T, p.cnt[T])
        if oc >= 8:
            STRIP_FREE = (T, p.cnt[T])
        if oc == 16:
            for tt in range(8):
                c = tt
                pp = tt % 2
                for k in range(16):
                    p.emit("tensor", (lambda tt=tt, k=k, pp=pp, wb=wb:
                           nc.tensor.matmul(
                               PB[pp].ap()[:, 0:128],
                               ln1fm.ap()[:, k, 3 + tt * 128:3 + (tt + 1) * 128],
                               wcache.ap()[:, wb, k, 256:384],
                               start=(k == 0), stop=False)),
                           waits=(bank_wait(pp) if k == 0 else []), sig=None)
                mm = p.emit("tensor", (lambda pp=pp: nc.tensor.matmul(
                    PB[pp].ap()[:, 0:128], onesr_s.ap()[0:1, 0:128],
                    dtb_s.ap()[0:1, 0:128], start=False, stop=True)),
                    waits=[(SQ[0], SMALL_LD)], sig=T)
                a1 = p.emit("scalar", (lambda pp=pp: nc.scalar.activation(
                    dtscr.ap()[:, :], PB[pp].ap()[:, 0:64], AF.Exp)),
                    waits=[(T, mm)], sig=A)
                a2 = p.emit("scalar", (lambda c=c: nc.scalar.activation(
                    dt_tm.ap()[:, c, :], dtscr.ap()[:, :], AF.Ln, bias=1.0)),
                    sig=A)
                a3 = p.emit("scalar", (lambda c=c: nc.scalar.activation(
                    lndt_tm.ap()[:, c, :], dt_tm.ap()[:, c, :], AF.Ln)),
                    sig=A)
                bank_free[pp] = (A, a1)
                dt_mm_done[c] = a3
            WC_FREE[wb] = (T, p.cnt[T])
    XSPILL_HALF = [(SQ[13], p.cnt[SQ[13]]), (SQ[14], p.cnt[SQ[14]])]
    Z_OK = [(SQ[3], p.cnt[SQ[3]]), (SQ[8], p.cnt[SQ[8]])]
    BC_OK = (A, p.cnt[A])

    for tt in range(8):
        c = tt
        tm = p.emit("tensor", (lambda c=c: nc.tensor.transpose(
            PB[4].ap()[0:64, 0:128], dt_tm.ap()[:, c, :], idf_s.ap()[:, :])),
            waits=[(A, dt_mm_done[c])] + bank_wait(4), sig=T)
        ev = p.emit("vector", (lambda c=c: nc.vector.tensor_copy(
            dt_fm.ap()[0:64, c * 128:(c + 1) * 128], PB[4].ap()[0:64, 0:128])),
            waits=[(T, tm)], sig=V)
        bank_free[4] = (V, ev)
    a1 = p.emit("scalar", (lambda: nc.scalar.activation(
        lndt_fm.ap()[0:64, :], dt_fm.ap()[0:64, :], AF.Ln)),
        waits=[(V, p.cnt[V])], sig=A)
    v1 = p.emit("vector", (lambda: nc.vector.tensor_single_scalar(
        dascr.ap()[0:64, 0:1024], dt_fm.ap()[0:64, :],
        smalls.ap()[0:64, 50:51], ALU.mult)),
        waits=[(V, NEGA)], sig=V)
    p.vspace(nc)
    for c in range(8):
        p.emit("vector", (lambda c=c: nc.vector.tensor_tensor_scan(
            r_fm.ap()[0:64, c * 128:(c + 1) * 128],
            dascr.ap()[0:64, c * 128:(c + 1) * 128],
            dascr.ap()[0:64, c * 128:(c + 1) * 128],
            0.0, ALU.add, ALU.bypass)), sig=V)
    p.vspace(nc)
    v2 = p.emit("vector", (lambda: nc.vector.tensor_sub(
        nrpscr.ap()[0:64, 0:1024], lndt_fm.ap()[0:64, :], r_fm.ap()[0:64, :])),
        waits=[(A, a1)], sig=V)
    fd = p.dma("gpsimd",
               fl_negrp[0:1, :].rearrange("o (c h j) -> (o h) c j", c=8, h=64, j=128),
               nrpscr.ap()[0:64, 0:1024].rearrange("h (c j) -> h c j", c=8),
               waits=[(V, v2), (CC, AG_W2)], sig=GQ[2])
    fd = p.dma("gpsimd",
               fl_r[0:1, :].rearrange("o (c h j) -> (o h) c j", c=8, h=64, j=128),
               r_fm.ap()[0:64, :].rearrange("h (c j) -> h c j", c=8),
               waits=[(V, v2)], sig=GQ[2])
    FLATS_OK = fd
    for c in range(8):
        tm = p.emit("tensor", (lambda c=c: nc.tensor.transpose(
            PB[4].ap()[:, 0:64], r_fm.ap()[0:64, c * 128:(c + 1) * 128],
            idf_s.ap()[0:64, 0:64])),
            waits=[(V, p.cnt[V])] + bank_wait(4), sig=T)
        ev = p.emit("vector", (lambda c=c: nc.vector.tensor_copy(
            r_tm.ap()[:, c, :], PB[4].ap()[:, 0:64])),
            waits=[(T, tm)], sig=V)
        bank_free[4] = (V, ev)
    aex = p.emit("scalar", (lambda: nc.scalar.activation(
        expf_tm.ap()[:, :, :], r_tm.ap()[:, :, :], AF.Exp)),
        waits=[(V, p.cnt[V])], sig=A)
    rb1 = p.dma("gpsimd", rowhbm[0:1, 0:512].rearrange("o (c h) -> o c h", c=8),
                r_tm.ap()[127:128, :, :], waits=[(V, p.cnt[V])], sig=GQ[2])
    rb2 = p.dma("gpsimd", rowb_s.ap()[0:1, 0:512], rowhbm[0:1, 0:512],
                waits=[(GQ[2], rb1)], sig=GQ[2])
    mm = p.emit("tensor", (lambda: nc.tensor.matmul(
        PB[4].ap()[:, 0:512], onesr_s.ap()[0:1, 0:128],
        rowb_s.ap()[0:1, 0:512], start=True, stop=True)),
        waits=[(GQ[2], rb2), (SQ[0], SMALL_LD)] + bank_wait(4), sig=T)
    ev = p.emit("vector", (lambda: nc.vector.tensor_copy(
        rl_cols.ap()[:, :], PB[4].ap()[:, 0:512])),
        waits=[(T, mm)], sig=V)
    bank_free[4] = (V, ev)
    adc = p.emit("scalar", (lambda: nc.scalar.activation(
        dcol.ap()[:, :], rl_cols.ap()[:, :], AF.Exp)), waits=[(V, ev)], sig=A)
    vnr = p.emit("vector", (lambda: nc.vector.tensor_sub(
        nrp_tm.ap()[:, :, :], lndt_tm.ap()[:, :, :], r_tm.ap()[:, :, :])),
        waits=[(A, p.cnt[A])], sig=V)
    DTM_A = p.cnt[A]
    DTM_V = p.cnt[V]

    XST_FREE = [None, None]
    STST_FREE = None
    hT_tail = hz
    xw_free = None
    stspill = {}
    for c in range(8):
        xb2 = c % 2
        wxs = list(XST_FREE[xb2]) if XST_FREE[xb2] else []
        xrsem = SQ[4 + 5 * xb2]
        cx = p.dma("sync", xstage.ap()[:, xb2, :],
                   xfm_sp[0:4096, c * 128:(c + 1) * 128],
                   waits=wxs + XSPILL_HALF, sig=xrsem, transpose=True)
        tb = p.emit("tensor", (lambda c=c: nc.tensor.transpose(
            PT[0].ap()[:, 0:128], b_fm.ap()[:, c * 128:(c + 1) * 128],
            idb_s.ap()[:, :])),
            waits=[BC_OK] + ([PT_FREE[0]] if PT_FREE[0] else []), sig=T)
        evb = p.emit("vector", lambda: nc.vector.tensor_copy(
            btm.ap()[:, :], PT[0].ap()[:, 0:128]), waits=[(T, tb)], sig=V)
        PT_FREE[0] = (V, evb)
        p.vspace(nc)
        v1 = p.emit("vector", (lambda c=c: nc.vector.tensor_add(
            wtm2.ap()[:, :], nrp_tm.ap()[:, c, :],
            rl_cols.ap()[:, c * 64:(c + 1) * 64])),
            waits=[(V, DTM_V), (A, DTM_A)], sig=V)
        a1 = p.emit("scalar", lambda: nc.scalar.activation(
            wtm.ap()[:, :], wtm2.ap()[:, :], AF.Exp), waits=[(V, v1)], sig=A)
        for g in range(8):
            vxw = None
            for i in range(8):
                h = g * 8 + i
                vxw = p.emit("vector", (lambda h=h, i=i, xb2=xb2:
                            nc.vector.tensor_single_scalar(
                                xw_sb.ap()[:, i * 64:(i + 1) * 64],
                                xstage.ap()[:, xb2, h * 64:(h + 1) * 64],
                                wtm.ap()[:, h:h + 1], ALU.mult)),
                            waits=(([(xrsem, cx), (A, a1)] +
                                    ([xw_free] if xw_free else [])) if i == 0 else []),
                            sig=V)
            pp = 2 + g % 2
            mm = p.emit("tensor", (lambda pp=pp: nc.tensor.matmul(
                PB[pp].ap()[:, :], btm.ap()[:, :], xw_sb.ap()[:, :],
                start=True, stop=True)),
                waits=[(V, vxw), (V, evb)] + bank_wait(pp), sig=T)
            xw_free = (T, p.cnt[T])
            for i in range(8):
                h = g * 8 + i
                p.emit("vector", (lambda h=h, i=i, c=c, pp=pp:
                       nc.vector.scalar_tensor_tensor(
                           hT.ap()[:, h * 64:(h + 1) * 64],
                           hT.ap()[:, h * 64:(h + 1) * 64],
                           dcol.ap()[:, c * 64 + h:c * 64 + h + 1],
                           PB[pp].ap()[:, i * 64:(i + 1) * 64],
                           ALU.mult, ALU.add)),
                       waits=([(T, mm), (A, adc)] if i == 0 else []), sig=V)
            wst = [STST_FREE] if STST_FREE else []
            vcp = p.emit("vector", (lambda g=g, pp=pp: nc.vector.tensor_copy(
                stst.ap()[:, g * 512:(g + 1) * 512], PB[pp].ap()[:, :])),
                waits=wst, sig=V)
            bank_free[pp] = (V, vcp)
        sd = p.dma("sync", st_sp[c, :, :], stst.ap()[:, :],
                   waits=[(V, p.cnt[V])], sig=SQ[10])
        STST_FREE = (SQ[10], sd)
        stspill[c] = sd
        XST_FREE[xb2] = [(V, p.cnt[V])]
        hT_tail = p.cnt[V]
    STSP_OK = (SQ[10], p.cnt[SQ[10]])

    cs1 = p.dma("gpsimd", cc_st_in[:, :], hT.ap()[:, :],
                waits=[(V, hT_tail)], sig=GQ[2])
    vld = p.emit("vector", lambda: nc.vector.reduce_sum(
        smalls.ap()[0:64, 52:53],
        r_fm.ap()[0:64, :].rearrange("h (c j) -> h c j", j=128)[:, :, 127],
        axis=mybir.AxisListType.X), waits=[(V, p.cnt[V])], sig=V)
    cs2 = p.dma("gpsimd", cc_ld_in[:, :], smalls.ap()[0:64, 52:53],
                waits=[(V, vld)], sig=GQ[2])
    agst = p.emit("gpsimd", lambda: nc.gpsimd.collective_compute(
        "AllGather", ALU.bypass, replica_groups=[[0, 1, 2, 3], [4, 5, 6, 7]],
        ins=[cc_st_in.opt()], outs=[cc_st_out.opt()]),
        waits=[(GQ[2], cs1)], sig=CC)
    agld = p.emit("gpsimd", lambda: nc.gpsimd.collective_compute(
        "AllGather", ALU.bypass, replica_groups=[[0, 1, 2, 3], [4, 5, 6, 7]],
        ins=[cc_ld_in.opt()], outs=[cc_ld_out.opt()]),
        waits=[(GQ[2], cs2)], sig=CC)
    lsel = p.dma("sync", wq.ap()[:, :],
                 cc_ld_out[:, :].rearrange("(p h) o -> p (h o)", p=4),
                 waits=[(CC, agld)], sig=SQ[0])
    lsel = p.dma("sync", smalls.ap()[0:4, 32:35], selm[:, :], sig=SQ[0])
    lsel = p.dma("sync", smalls.ap()[0:3, 35:36], selv[:, :], sig=SQ[0])
    mm = p.emit("tensor", lambda: nc.tensor.matmul(
        PB[4].ap()[0:3, 0:64], smalls.ap()[0:4, 32:35], wq.ap()[:, :],
        start=True, stop=True),
        waits=[(SQ[0], lsel)] + bank_wait(4), sig=T)
    ae = p.emit("scalar", lambda: nc.scalar.activation(
        wq2.ap()[:, :], PB[4].ap()[0:3, 0:64], AF.Exp), waits=[(T, mm)], sig=A)
    bank_free[4] = (A, ae)
    vmu = p.emit("vector", lambda: nc.vector.tensor_single_scalar(
        wq2.ap()[:, :], wq2.ap()[:, :], smalls.ap()[0:3, 35:36], ALU.mult),
        waits=[(A, ae)], sig=V)
    bw1 = p.dma("gpsimd", wrow_b[0:1, 0:192].rearrange("o (q h) -> (o q) h", q=3),
                wq2.ap()[:, :], waits=[(V, vmu)], sig=GQ[2])
    bw2 = p.dma("gpsimd", rowb_s.ap()[0:1, 0:192], wrow_b[0:1, 0:192],
                waits=[(GQ[2], bw1)], sig=GQ[2])
    mm = p.emit("tensor", lambda: nc.tensor.matmul(
        PB[4].ap()[:, 0:192], onesr_s.ap()[0:1, 0:128], rowb_s.ap()[0:1, 0:192],
        start=True, stop=True), waits=[(GQ[2], bw2)] + bank_wait(4), sig=T)
    vwc = p.emit("vector", lambda: nc.vector.tensor_copy(
        wcols.ap()[:, 0:192], PB[4].ap()[:, 0:192]), waits=[(T, mm)], sig=V)
    bank_free[4] = (V, vwc)
    hz2 = p.emit("vector", lambda: nc.vector.memset(hT.ap()[:, :], 0.0),
                 waits=[(GQ[2], cs1)], sig=V)
    vh = hz2
    for pidx in range(3):
        gsem = SQ[4 + 5 * (pidx % 2)]
        cg = p.dma("sync", xstage.ap()[:, pidx % 2, :],
                   cc_st_out[pidx * 128:(pidx + 1) * 128, :],
                   waits=[(CC, agst), (V, vh)], sig=gsem)
        for h in range(NH):
            vh = p.emit("vector", (lambda pidx=pidx, h=h: nc.vector.scalar_tensor_tensor(
                hT.ap()[:, h * 64:(h + 1) * 64],
                xstage.ap()[:, pidx % 2, h * 64:(h + 1) * 64],
                wcols.ap()[:, pidx * 64 + h:pidx * 64 + h + 1],
                hT.ap()[:, h * 64:(h + 1) * 64],
                ALU.mult, ALU.add)),
                waits=([(gsem, cg), (V, vwc)] if h == 0 else []), sig=V)
    HIN_OK = vh
    hin_sp = internal("hin_sp", [128, NH * HP], BF16)
    p.dma("gpsimd", hin_sp[:, :], hT.ap()[:, :], waits=[(V, HIN_OK)], sig=GQ[3])

    attn_sp = internal("attn_sp", [Q, DM], BF16)
    y_dbg = internal("y_dbg", [128, DI], BF16)
    sc_dbg = internal("sc_dbg", [128, 4096], BF16)
    cb_dbg = internal("cb_dbg", [128, 128], BF16)

    ygfm = reg("ygfm", [128, 32, Q], BF16, 0)
    segL = reg("segL", [2, 2, 1024], F32, 65536)
    segR = reg("segR", [2, 2, 1024], F32, 73728)
    xst3 = reg("xst3", [128, 2, 4096], BF16, 81920)
    zst3 = reg("zst3", [128, 4096], BF16, 98304)
    stst3 = reg("stst3", [128, 4096], BF16, 106496)
    ysb = reg("ysb", [128, 4096], BF16, 114688)
    scor = reg("scor", [128, 8, 128], BF16, 122880)
    segexp = reg("segexp", [128, 2, 512], BF16, 124928)
    cbm = reg("cbm", [128, 128], BF16, 126976)
    ytmp = reg("ytmp", [128, 512], BF16, 127232)
    ytmp2 = reg("ytmp2", [128, 512], BF16, 128256)

    BAR12 = [(V, p.cnt[V]), (A, p.cnt[A]), (T, p.cnt[T]), (CC, p.cnt[CC])]
    for _e in ENGS:
        p.wait(_e, BAR12)

    oi = p.dma("gpsimd", segL.ap()[1:2, 0, :], ones_big[0:1, 0:1024], sig=GQ[0])
    oi = p.dma("gpsimd", segL.ap()[1:2, 1, :], ones_big[0:1, 0:1024], sig=GQ[0])
    oi = p.dma("gpsimd", segR.ap()[0:1, 0, :], ones_big[0:1, 0:1024], sig=GQ[0])
    oi = p.dma("gpsimd", segR.ap()[0:1, 1, :], ones_big[0:1, 0:1024], sig=GQ[0])
    SEG_ONES = oi

    XST3_FREE = [None, None]
    SEG_FREE = [None, None]
    SEGEXP_FREE = [None, None]
    ZST3_FREE = None
    STST3_FREE = None
    YSB_FREE = None
    hT_read = None
    hT_upd = (V, HIN_OK)
    SCOR_FREE = None
    YTMP_FREE = None
    YTMP2_FREE = None
    ygfm_marks = []
    yMM_SEM_Q = [SQ[9], SQ[15]]

    for c in range(NCH if PH_LEVEL >= 3 else 0):
        xb = c % 2
        if c >= PH3_CHUNKS:
            break
        wx = list(XST3_FREE[xb]) if XST3_FREE[xb] else []
        cx3 = p.dma("sync", xst3.ap()[:, xb, :], xfm_sp[0:4096, c * 128:(c + 1) * 128],
                    waits=wx, sig=SQ[4 + 5 * xb], transpose=True)
        xsem = SQ[4 + 5 * xb]
        wz = [ZST3_FREE] if ZST3_FREE else []
        cz = p.dma("sync", zst3.ap()[:, :], z_sp[0:4096, c * 128:(c + 1) * 128],
                   waits=wz + (Z_OK if c == 0 else []), sig=SQ[1],
                   transpose=True)
        wst = [STST3_FREE] if STST3_FREE else []
        cst = p.dma("sync", stst3.ap()[:, :], st_sp[c, :, :],
                    waits=wst + ([STSP_OK] if c == 0 else []), sig=SQ[6])
        mmcb = p.emit("tensor", (lambda c=c: nc.tensor.matmul(
            PB[4].ap()[:, 0:128], b_fm.ap()[:, c * 128:(c + 1) * 128],
            c_fm.ap()[:, c * 128:(c + 1) * 128], start=True, stop=True)),
            waits=bank_wait(4), sig=T)
        vcb = p.emit("vector", lambda: nc.vector.tensor_mul(
            cbm.ap()[:, :], PB[4].ap()[:, 0:128], maskT_s.ap()[:, :]),
            waits=[(T, mmcb)], sig=V)
        bank_free[4] = (V, vcb)
        seg_ld = {}
        for hg in range(8):
            e = hg
            sb = e % 2
            wsg = list(SEG_FREE[sb]) if SEG_FREE[sb] else []
            if c == 0 and e < 2:
                wsg.append((GQ[0], SEG_ONES))
            g1 = p.dma("gpsimd", segL.ap()[0:1, sb, :],
                       fl_negrp[0:1, (c * 64 + e * 8) * 128:(c * 64 + e * 8 + 8) * 128],
                       waits=wsg + [(GQ[2], FLATS_OK)], sig=GQ[0])
            g2 = p.dma("gpsimd", segR.ap()[1:2, sb, :],
                       fl_r[0:1, (c * 64 + e * 8) * 128:(c * 64 + e * 8 + 8) * 128],
                       waits=list(SEG_FREE[sb]) if SEG_FREE[sb] else [], sig=GQ[1])
            seg_ld[e] = [(GQ[0], g1), (GQ[1], g2)]
            segmm = {}
            for q4 in range(2):
                pp = (hg * 2 + q4) % 2
                mm = None
                for i in range(4):
                    hh = q4 * 4 + i
                    mm = p.emit("tensor", (lambda sb=sb, hh=hh, i=i, pp=pp:
                                nc.tensor.matmul(
                                    PB[pp].ap()[:, i * 128:(i + 1) * 128],
                                    segL.ap()[0:2, sb, hh * 128:(hh + 1) * 128],
                                    segR.ap()[0:2, sb, hh * 128:(hh + 1) * 128],
                                    start=(i == 0), stop=(i == 3))),
                                waits=(seg_ld[e] + bank_wait(pp)) if i == 0 else [],
                                sig=(T if i == 3 else None))
                eb = (hg * 2 + q4) % 2
                wse = [SEGEXP_FREE[eb]] if SEGEXP_FREE[eb] else []
                vmn = p.emit("vector", (lambda eb=eb, pp=pp: nc.vector.tensor_scalar_min(
                    segexp.ap()[:, eb, :], PB[pp].ap()[:, :], 30.0)),
                    waits=[(T, mm)] + wse, sig=V)
                bank_free[pp] = (V, vmn)
                aex = p.emit("scalar", (lambda eb=eb: nc.scalar.activation(
                    segexp.ap()[:, eb, :], segexp.ap()[:, eb, :], AF.Exp)),
                    waits=[(V, vmn)], sig=A)
                vsc = None
                for i in range(4):
                    hh = q4 * 4 + i
                    slot = q4 * 4 + i
                    vsc = p.emit("vector", (lambda eb=eb, i=i, slot=slot: nc.vector.tensor_mul(
                        scor.ap()[:, slot, :], segexp.ap()[:, eb, i * 128:(i + 1) * 128],
                        cbm.ap()[:, :])),
                        waits=([(A, aex), (V, vcb)] +
                               ([SCOR_FREE] if SCOR_FREE else [])) if i == 0 else [],
                        sig=V)
                SEGEXP_FREE[eb] = (V, vsc)
                segmm[q4] = vsc
            mmd = None
            for i in range(8):
                h = hg * 8 + i
                slot = i
                mmd = p.emit("tensor", (lambda h=h, i=i, slot=slot, xb=xb:
                            nc.tensor.matmul(
                                PB[2].ap()[:, i * 64:(i + 1) * 64],
                                scor.ap()[:, slot, :],
                                xst3.ap()[:, xb, h * 64:(h + 1) * 64],
                                start=(i == 0), stop=(i == 7))),
                            waits=([(V, segmm[1]), (xsem, cx3)] + bank_wait(2))
                                  if i == 0 else [],
                            sig=(T if i == 7 else None))
            mmo = None
            for i in range(8):
                h = hg * 8 + i
                mmo = p.emit("tensor", (lambda h=h, i=i, c=c: nc.tensor.matmul(
                            PB[3].ap()[:, i * 64:(i + 1) * 64],
                            c_fm.ap()[:, c * 128:(c + 1) * 128],
                            hT.ap()[:, h * 64:(h + 1) * 64],
                            start=(i == 0), stop=(i == 7))),
                            waits=(list(hT_upd if isinstance(hT_upd, list) else [hT_upd])
                                   + bank_wait(3)) if i == 0 else [],
                            sig=(T if i == 7 else None))
            hT_read = (T, mmo)
            aev = None
            for i in range(8):
                h = hg * 8 + i
                aev = p.emit("scalar", (lambda h=h, i=i, c=c: nc.scalar.activation(
                    ytmp.ap()[:, i * 64:(i + 1) * 64], PB[3].ap()[:, i * 64:(i + 1) * 64],
                    AF.Copy, scale=expf_tm.ap()[:, c, h:h + 1])),
                    waits=([(T, mmo)] + ([YTMP_FREE] if YTMP_FREE else []))
                          if i == 0 else [], sig=A)
            wyb = [YSB_FREE] if (YSB_FREE and hg == 0) else []
            v2 = p.emit("vector", (lambda hg=hg, xb=xb: nc.vector.tensor_mul(
                ytmp2.ap()[:, :], xst3.ap()[:, xb, hg * 512:(hg + 1) * 512],
                dpb.ap()[:, hg * 512:(hg + 1) * 512])),
                waits=[(V, DPB_OK)] + ([YTMP2_FREE] if YTMP2_FREE else []), sig=V)
            v1 = p.emit("vector", (lambda hg=hg: nc.vector.tensor_add(
                ysb.ap()[:, hg * 512:(hg + 1) * 512], PB[2].ap()[:, :], ytmp.ap()[:, :])),
                waits=[(A, aev), (T, mmd)] + wyb, sig=V)
            bank_free[2] = (V, v1)
            bank_free[3] = (A, aev)
            YTMP_FREE = (V, v1)
            v3 = p.emit("vector", (lambda hg=hg: nc.vector.tensor_add(
                ysb.ap()[:, hg * 512:(hg + 1) * 512],
                ysb.ap()[:, hg * 512:(hg + 1) * 512], ytmp2.ap()[:, :])), sig=V)
            YTMP2_FREE = (V, v3)
            SEG_FREE[sb] = [(T, mmd)]
            SCOR_FREE = (T, mmd)
        if DEBUG_PH3 and c == 7:
            tb9 = p.dma("gpsimd", y_dbg[:, :], hT.ap()[:, :],
                        waits=list(hT_upd if isinstance(hT_upd, list) else [hT_upd]),
                        sig=GQ[3])
            for _e in ENGS:
                p.wait(_e, [(GQ[3], tb9)])
        vrec = None
        for h in range(NH):
            vrec = p.emit("vector", (lambda h=h, c=c: nc.vector.scalar_tensor_tensor(
                hT.ap()[:, h * 64:(h + 1) * 64], hT.ap()[:, h * 64:(h + 1) * 64],
                dcol.ap()[:, c * 64 + h:c * 64 + h + 1],
                stst3.ap()[:, h * 64:(h + 1) * 64], ALU.mult, ALU.add)),
                waits=[hT_read, (SQ[6], cst)] if h == 0 else [], sig=V)
        hT_upd = (V, vrec)
        STST3_FREE = (V, vrec)
        asz = p.emit("scalar", (lambda: nc.scalar.activation(
            zst3.ap()[:, :], zst3.ap()[:, :], AF.Silu)),
            waits=[(SQ[1], cz)], sig=A)
        if DEBUG_PH3 and c == 0:
            tbz = p.dma("gpsimd", cb_dbg[0:128, 0:128], zst3.ap()[:, 0:128],
                        waits=[(A, asz)], sig=GQ[3])
            for _e in ENGS:
                p.wait(_e, [(GQ[3], tbz)])
        vg = p.emit("vector", lambda: nc.vector.tensor_mul(
            ysb.ap()[:, :], ysb.ap()[:, :], zst3.ap()[:, :]),
            waits=[(A, asz), (V, p.cnt[V])], sig=V)
        p.emit("scalar", (lambda: nc.scalar.activation(
            zst3.ap()[:, :], ysb.ap()[:, :], AF.Square,
            accum_out=smalls.ap()[:, 60:61])), waits=[(V, vg)])
        asq2 = p.emit("scalar", lambda: nc.scalar.copy(
            smalls.ap()[:, 61:62], smalls.ap()[:, 60:61]), sig=A)
        ZST3_FREE = (A, asq2)
        ast2 = p.emit("scalar", lambda: nc.scalar.activation(
            smalls.ap()[:, 62:63], smalls.ap()[:, 60:61], AF.Sqrt,
            bias=smalls.ap()[:, 54:55], scale=smalls.ap()[:, 55:56]),
            waits=[(A, asq2)], sig=A)
        vrc = p.emit("vector", lambda: nc.vector.reciprocal(
            smalls.ap()[:, 63:64], smalls.ap()[:, 62:63]), waits=[(A, ast2)], sig=V)
        if DEBUG_PH3 and c == 0:
            tbm = p.dma("gpsimd", sm_dbg[:, :], smalls.ap()[:, :],
                        waits=[(V, vrc)], sig=GQ[3])
            tbm = p.dma("gpsimd", u_dbg[0:128, 0:2048],
                        ysb.ap()[:, 0:2048], waits=[(V, vg)], sig=GQ[3])
            for _e in ENGS:
                p.wait(_e, [(GQ[3], tbm)])
        p.vspace(nc)
        vsc2 = p.emit("vector", lambda: nc.vector.tensor_single_scalar(
            ysb.ap()[:, :], ysb.ap()[:, :], smalls.ap()[:, 63:64], ALU.mult),
            waits=[(V, vrc)], sig=V)
        if DEBUG_PH3 and c == 0:
            tb8 = p.dma("gpsimd", sc_dbg[0:128, 0:4096], ysb.ap()[:, :],
                        waits=[(V, vsc2)], sig=GQ[3])
            for _e in ENGS:
                p.wait(_e, [(GQ[3], tb8)])
        for kt in range(32):
            ptb = kt % 2
            wpt = [PT_FREE[ptb]] if PT_FREE[ptb] else []
            tm = p.emit("tensor", (lambda kt=kt, ptb=ptb: nc.tensor.transpose(
                PT[ptb].ap()[:, 0:128], ysb.ap()[:, kt * 128:(kt + 1) * 128],
                idb_s.ap()[:, :])), waits=[(V, vsc2)] + wpt, sig=T)
            if kt % 2 == 0:
                ev = p.emit("vector", (lambda kt=kt, c=c, ptb=ptb: nc.vector.tensor_copy(
                    ygfm.ap()[:, kt, c * 128:(c + 1) * 128], PT[ptb].ap()[:, 0:128])),
                    waits=[(T, tm)], sig=V)
                PT_FREE[ptb] = (V, ev)
            else:
                ev = p.emit("scalar", (lambda kt=kt, c=c, ptb=ptb: nc.scalar.copy(
                    ygfm.ap()[:, kt, c * 128:(c + 1) * 128], PT[ptb].ap()[:, 0:128])),
                    waits=[(T, tm)], sig=A)
                PT_FREE[ptb] = (A, ev)
        YSB_FREE = (T, p.cnt[T])
        XST3_FREE[xb] = [(T, p.cnt[T]), (V, p.cnt[V])]
    YGFM_OK = [(V, p.cnt[V]), (A, p.cnt[A])]
    ygfm_dbg = internal("ygfm_dbg", [128, 32 * Q], BF16)
    woc_dbg = internal("woc_dbg", [128, 32 * 512], BF16)
    if DEBUG_PH3:
        tbg = p.dma("gpsimd", ygfm_dbg[:, :],
                    ygfm.ap()[:, :, :].rearrange("p a b -> p (a b)"),
                    waits=YGFM_OK, sig=GQ[3])
        for _e in ENGS:
            p.wait(_e, [(GQ[3], tbg)])

    PH3B_ON = PH_LEVEL >= 4 and PH3_CHUNKS >= 8
    woc = reg("woc", [128, 32, 512], BF16, 98304)
    atst = reg("atst", [128, 2, 512], BF16, 131072)
    BAR3B = [(V, p.cnt[V]), (A, p.cnt[A]), (T, p.cnt[T])]
    for _e in ENGS:
        p.wait(_e, BAR3B)
    ATST_FREE = [None, None]
    WOSEMS = [SQ[2], SQ[7], SQ[11], SQ[12]]
    at_done = None
    for dc in range(4 if PH3B_ON else 0):
        lw = []
        for k in range(32):
            wsem = WOSEMS[k % 4]
            wwk = [(T, p.cnt[T])] if (dc > 0 and k < 4) else []
            lwv = p.dma("sync", woc.ap()[:, k, :],
                        wout_bf[k * 128:(k + 1) * 128, dc * 512:(dc + 1) * 512],
                        waits=wwk + [(CC, AG_WOUT)], sig=wsem)
            lw.append((wsem, lwv))
        lw = [(sm, max(vv for sm2, vv in lw if sm2 == sm)) for sm in set(s0 for s0, _ in lw)]
        if DEBUG_PH3 and dc == 0:
            tbw = p.dma("gpsimd", woc_dbg[:, :],
                        woc.ap()[:, :, :].rearrange("p a b -> p (a b)"),
                        waits=lw, sig=GQ[3])
            for _e in ENGS:
                p.wait(_e, [(GQ[3], tbw)])
        for tt in range(8):
            pp = tt % 2
            mm = None
            for k in range(32):
                mm = p.emit("tensor", (lambda tt=tt, k=k, pp=pp: nc.tensor.matmul(
                    PB[pp].ap()[:, :], ygfm.ap()[:, k, tt * 128:(tt + 1) * 128],
                    woc.ap()[:, k, :], start=(k == 0), stop=(k == 31))),
                    waits=(lw + YGFM_OK + bank_wait(pp)) if k == 0 else [],
                    sig=(T if k == 31 else None))
            ab = tt % 2
            wat = [ATST_FREE[ab]] if ATST_FREE[ab] else []
            ve = p.emit("vector", (lambda ab=ab, pp=pp: nc.vector.tensor_copy(
                atst.ap()[:, ab, :], PB[pp].ap()[:, :])),
                waits=[(T, mm)] + wat, sig=V)
            bank_free[pp] = (V, ve)
            ad = p.dma("sync", attn_sp[tt * 128:(tt + 1) * 128, dc * 512:(dc + 1) * 512],
                       atst.ap()[:, ab, :], waits=[(V, ve)], sig=SQ[3 + 5 * ab])
            ATST_FREE[ab] = (SQ[3 + 5 * ab], ad)
            at_done = (SQ[3 + 5 * ab], ad)
    AT_ALL = [(SQ[3], p.cnt[SQ[3]]), (SQ[8], p.cnt[SQ[8]])]

    ast3 = reg("ast3", [128, 2048], BF16, 98304)
    ust2 = reg("ust2", [128, 2048], F32, 102400)
    rsb = reg("rsb", [128, 2048], F32, 110592)
    x3tm = reg("x3tm", [128, 2048], BF16, 118784)
    x3fm = reg("x3fm", [128, 16, Q], BF16, 65536)
    BAR3C = [(V, p.cnt[V]), (A, p.cnt[A]), (T, p.cnt[T]),
             (SQ[3], p.cnt[SQ[3]]), (SQ[8], p.cnt[SQ[8]])]
    for _e in ENGS:
        p.wait(_e, BAR3C)
    RS_FREE = None
    for tt in range(8 if PH3B_ON else 0):
        ca = p.dma("sync", ast3.ap()[:, :], attn_sp[tt * 128:(tt + 1) * 128, :],
                   waits=AT_ALL, sig=SQ[1])
        cu2 = p.dma("sync", ust2.ap()[:, :], u_ext[3 + tt * 128:3 + (tt + 1) * 128, :],
                    sig=SQ[6])

        def lnstats(src_ap, wsrc, bb, width=2048):
            vs = p.emit("vector", (lambda: nc.vector.reduce_sum(
                smalls.ap()[:, bb + 0:bb + 1], src_ap, axis=mybir.AxisListType.X)),
                waits=wsrc, sig=V)
            p.emit("scalar", (lambda: nc.scalar.activation(
                x3tm.ap()[:, :], src_ap, AF.Square,
                accum_out=smalls.ap()[:, bb + 1:bb + 2])), waits=wsrc)
            af = p.emit("scalar", (lambda: nc.scalar.copy(
                smalls.ap()[:, bb + 7:bb + 8], smalls.ap()[:, bb + 1:bb + 2])), sig=A)
            p.vspace(nc)
            v1 = p.emit("vector", (lambda: nc.vector.tensor_scalar_mul(
                smalls.ap()[:, bb + 2:bb + 3], smalls.ap()[:, bb + 0:bb + 1],
                1.0 / width)), waits=[(V, vs)], sig=V)
            v2 = p.emit("vector", (lambda: nc.vector.tensor_scalar_mul(
                smalls.ap()[:, bb + 3:bb + 4], smalls.ap()[:, bb + 1:bb + 2],
                1.0 / width)), waits=[(A, af)], sig=V)
            v3 = p.emit("vector", (lambda: nc.vector.tensor_single_scalar(
                smalls.ap()[:, bb + 4:bb + 5], smalls.ap()[:, bb + 2:bb + 3],
                smalls.ap()[:, bb + 2:bb + 3], ALU.mult)), sig=V)
            p.vspace(nc)
            v4 = p.emit("vector", (lambda: nc.vector.tensor_single_scalar(
                smalls.ap()[:, bb + 3:bb + 4], smalls.ap()[:, bb + 3:bb + 4],
                smalls.ap()[:, bb + 4:bb + 5], ALU.subtract)), sig=V)
            a2 = p.emit("scalar", (lambda: nc.scalar.activation(
                smalls.ap()[:, bb + 5:bb + 6], smalls.ap()[:, bb + 3:bb + 4],
                AF.Sqrt, bias=smalls.ap()[:, 54:55])), waits=[(V, v4)], sig=A)
            v5 = p.emit("vector", (lambda: nc.vector.reciprocal(
                smalls.ap()[:, bb + 6:bb + 7], smalls.ap()[:, bb + 5:bb + 6])),
                waits=[(A, a2)], sig=V)
            p.vspace(nc)
            return v5

        v5 = lnstats(ast3.ap()[:, :], [(SQ[1], ca)], 0)
        vno = p.emit("vector", (lambda: nc.vector.tensor_scalar(
            rsb.ap()[:, :], ast3.ap()[:, :], smalls.ap()[:, 2:3],
            smalls.ap()[:, 6:7], ALU.subtract, ALU.mult)),
            waits=[(V, v5)] + ([RS_FREE] if RS_FREE else []), sig=V)
        vra = p.emit("vector", (lambda: nc.vector.tensor_add(
            rsb.ap()[:, :], rsb.ap()[:, :], ust2.ap()[:, :])),
            waits=[(SQ[6], cu2)], sig=V)
        rd = p.dma("sync", resid_sp[tt * 128:(tt + 1) * 128, :], rsb.ap()[:, :],
                   waits=[(V, vra)], sig=SQ[10])
        v5b = lnstats(rsb.ap()[:, :], [(V, vra)], 16)
        vn3 = p.emit("vector", (lambda: nc.vector.tensor_scalar(
            x3tm.ap()[:, :], rsb.ap()[:, :], smalls.ap()[:, 18:19],
            smalls.ap()[:, 22:23], ALU.subtract, ALU.mult)),
            waits=[(V, v5b)], sig=V)
        RS_FREE = (SQ[10], rd)
        for k in range(16):
            ptb = k % 2
            wpt = [PT_FREE[ptb]] if PT_FREE[ptb] else []
            tm = p.emit("tensor", (lambda k=k, ptb=ptb: nc.tensor.transpose(
                PT[ptb].ap()[:, 0:128], x3tm.ap()[:, k * 128:(k + 1) * 128],
                idb_s.ap()[:, :])), waits=[(V, vn3)] + wpt, sig=T)
            if k % 2 == 0:
                ev = p.emit("vector", (lambda k=k, tt=tt, ptb=ptb: nc.vector.tensor_copy(
                    x3fm.ap()[:, k, tt * 128:(tt + 1) * 128], PT[ptb].ap()[:, 0:128])),
                    waits=[(T, tm)], sig=V)
                PT_FREE[ptb] = (V, ev)
            else:
                ev = p.emit("scalar", (lambda k=k, tt=tt, ptb=ptb: nc.scalar.copy(
                    x3fm.ap()[:, k, tt * 128:(tt + 1) * 128], PT[ptb].ap()[:, 0:128])),
                    waits=[(T, tm)], sig=A)
                PT_FREE[ptb] = (A, ev)
    X3FM_OK = [(V, p.cnt[V]), (A, p.cnt[A])]
    RESID_OK = [(SQ[10], p.cnt[SQ[10]])]

    h1 = reg("h1", [128, 64, 512], BF16, 0)
    w1c = reg("w1c", [128, 16, 512], BF16, 98304)
    w3c = reg("w3c", [128, 16, 512], BF16, 114688)
    sg4 = reg("sg4", [128, 512], BF16, 131072)
    hid4 = reg("hid4", [128, 512], BF16, 132096)
    w2c = reg("w2c", [128, 2, 512], BF16, 133120)
    ffnst = reg("ffnst", [128, 4, 4, 512], BF16, 135168)
    rsb2 = reg("rsb2", [128, 2048], F32, 98304)
    ostage = reg("ostage", [128, 2, 2048], F32, 106496)
    BAR4 = [(V, p.cnt[V]), (A, p.cnt[A]), (T, p.cnt[T])]
    for _e in ENGS:
        p.wait(_e, BAR4)
    W1SEMS = [SQ[2], SQ[7]]
    W3SEMS = [SQ[11], SQ[12]]
    W2SEMS = [SQ[4], SQ[9]]
    OUT_SEMS = [SQ[3], SQ[8]]
    H1_FREE = None
    FF_FREE = None
    out_last = []
    for th in range(2 if PH_LEVEL >= 5 and PH3B_ON else 0):
        if th > 0:
            p.wait("sync", [(OUT_SEMS[0], p.cnt[OUT_SEMS[0]]),
                            (OUT_SEMS[1], p.cnt[OUT_SEMS[1]])])
        for ug in range(16):
            lw1 = []
            lw3 = []
            for k in range(16):
                wwk = [(T, p.cnt[T])] if (k < 2 and (ug > 0 or th > 0)) else []
                l1 = p.dma("sync", w1c.ap()[:, k, :],
                           w1_bf[k * 128:(k + 1) * 128, ug * 512:(ug + 1) * 512],
                           waits=wwk + [(CC, AG_W1)], sig=W1SEMS[k % 2])
                lw1.append((W1SEMS[k % 2], l1))
                l3 = p.dma("sync", w3c.ap()[:, k, :],
                           w3_bf[k * 128:(k + 1) * 128, ug * 512:(ug + 1) * 512],
                           waits=wwk + [(CC, AG_W3)], sig=W3SEMS[k % 2])
                lw3.append((W3SEMS[k % 2], l3))
            lw1 = [(sm, max(v for s2, v in lw1 if s2 == sm)) for sm in set(s for s, _ in lw1)]
            lw3 = [(sm, max(v for s2, v in lw3 if s2 == sm)) for sm in set(s for s, _ in lw3)]
            for ut in range(4):
                u = ug * 4 + ut
                mm1 = None
                for k in range(16):
                    mm1 = p.emit("tensor", (lambda ut=ut, k=k, th=th: nc.tensor.matmul(
                        PB[0].ap()[:, :], w1c.ap()[:, k, ut * 128:(ut + 1) * 128],
                        x3fm.ap()[:, k, th * 512:(th + 1) * 512],
                        start=(k == 0), stop=(k == 15))),
                        waits=(lw1 + X3FM_OK + bank_wait(0)) if k == 0 else [],
                        sig=(T if k == 15 else None))
                mm3 = None
                for k in range(16):
                    mm3 = p.emit("tensor", (lambda ut=ut, k=k, th=th: nc.tensor.matmul(
                        PB[1].ap()[:, :], w3c.ap()[:, k, ut * 128:(ut + 1) * 128],
                        x3fm.ap()[:, k, th * 512:(th + 1) * 512],
                        start=(k == 0), stop=(k == 15))),
                        waits=(lw3 + bank_wait(1)) if k == 0 else [],
                        sig=(T if k == 15 else None))
                ags = p.emit("scalar", (lambda: nc.scalar.activation(
                    sg4.ap()[:, :], PB[0].ap()[:, :], AF.Silu)),
                    waits=[(T, mm1), (A, p.cnt[A])], sig=A)
                bank_free[0] = (A, ags)
                vhd = p.emit("vector", (lambda: nc.vector.tensor_mul(
                    hid4.ap()[:, :], sg4.ap()[:, :], PB[1].ap()[:, :])),
                    waits=[(A, ags), (T, mm3), (V, p.cnt[V])], sig=V)
                bank_free[1] = (V, vhd)
                mmh = p.emit("tensor", (lambda: nc.tensor.matmul(
                    PB[2].ap()[:, :], h128_s.ap()[:, :], hid4.ap()[:, :],
                    start=True, stop=True)),
                    waits=[(V, vhd)] + bank_wait(2), sig=T)
                wh1 = [H1_FREE] if (H1_FREE and th == 1 and ug == 0 and ut == 0) else []
                ev = p.emit("vector", (lambda u=u: nc.vector.tensor_copy(
                    h1.ap()[:, u, :], PB[2].ap()[:, :])),
                    waits=[(T, mmh)] + wh1, sig=V)
                bank_free[2] = (V, ev)
        H1_OK = (V, p.cnt[V])
        w2free = [None, None]
        for dc in range(4):
            for u in range(64):
                wb2 = u % 2
                ww2 = ([w2free[wb2]] if w2free[wb2] else []) + [(CC, AG_W2)]
                p.dma("sync", w2c.ap()[:, wb2, 0:256],
                      w2pp_bf[(2 * dc) * 8192 + u * 128:(2 * dc) * 8192 + (u + 1) * 128, :],
                      waits=ww2, sig=W2SEMS[wb2])
                l2 = p.dma("sync", w2c.ap()[:, wb2, 256:512],
                           w2pp_bf[(2 * dc + 1) * 8192 + u * 128:(2 * dc + 1) * 8192 + (u + 1) * 128, :],
                           sig=W2SEMS[wb2])
                mmlast = None
                for tt4 in range(4):
                    mmlast = p.emit("tensor", (lambda u=u, tt4=tt4, wb2=wb2:
                                nc.tensor.matmul(
                                    PB[tt4].ap()[:, :],
                                    h1.ap()[:, u, tt4 * 128:(tt4 + 1) * 128],
                                    w2c.ap()[:, wb2, :],
                                    start=(u == 0), stop=(u == 63))),
                                waits=[(W2SEMS[wb2], l2), H1_OK] +
                                      (bank_wait(tt4) if u == 0 else []),
                                sig=(T if (tt4 == 3) else None))
                w2free[wb2] = (T, mmlast)
            for tt4 in range(4):
                wff = [FF_FREE] if (FF_FREE and dc == 0 and tt4 == 0) else []
                ev = p.emit("vector", (lambda tt4=tt4, dc=dc: nc.vector.tensor_copy(
                    ffnst.ap()[:, tt4, dc, :], PB[tt4].ap()[:, :])),
                    waits=[(T, p.cnt[T])] + wff, sig=V)
                bank_free[tt4] = (V, ev)
        FF_OK = (V, p.cnt[V])
        for tt4 in range(4):
            tt = th * 4 + tt4
            crr = p.dma("sync", rsb2.ap()[:, :], resid_sp[tt * 128:(tt + 1) * 128, :],
                        waits=RESID_OK + [(V, p.cnt[V])], sig=SQ[1])
            src_ap = ffnst.ap()[:, tt4, :, :].rearrange("p a b -> p (a b)")
            vs = p.emit("vector", (lambda src_ap=src_ap: nc.vector.reduce_sum(
                smalls.ap()[:, 0:1], src_ap, axis=mybir.AxisListType.X)),
                waits=[FF_OK], sig=V)
            p.emit("scalar", (lambda src_ap=src_ap: nc.scalar.activation(
                sg4.ap()[:, :], src_ap[:, 0:512], AF.Square,
                accum_out=smalls.ap()[:, 70:71])), waits=[FF_OK])
            for qq in range(1, 4):
                p.emit("scalar", (lambda src_ap=src_ap, qq=qq: nc.scalar.activation(
                    sg4.ap()[:, :], src_ap[:, qq * 512:(qq + 1) * 512], AF.Square,
                    accum_out=smalls.ap()[:, 70 + qq:71 + qq])))
            af = p.emit("scalar", (lambda: nc.scalar.copy(
                smalls.ap()[:, 74:75], smalls.ap()[:, 70:71])), sig=A)
            vq = p.emit("vector", (lambda: nc.vector.tensor_single_scalar(
                smalls.ap()[:, 1:2], smalls.ap()[:, 70:71], smalls.ap()[:, 71:72],
                ALU.add)), waits=[(A, af)], sig=V)
            p.vspace(nc)
            vq = p.emit("vector", (lambda: nc.vector.tensor_single_scalar(
                smalls.ap()[:, 1:2], smalls.ap()[:, 1:2], smalls.ap()[:, 72:73],
                ALU.add)), sig=V)
            p.vspace(nc)
            vq = p.emit("vector", (lambda: nc.vector.tensor_single_scalar(
                smalls.ap()[:, 1:2], smalls.ap()[:, 1:2], smalls.ap()[:, 73:74],
                ALU.add)), sig=V)
            v1 = p.emit("vector", (lambda: nc.vector.tensor_scalar_mul(
                smalls.ap()[:, 2:3], smalls.ap()[:, 0:1], 1.0 / 2048.0)), sig=V)
            v2 = p.emit("vector", (lambda: nc.vector.tensor_scalar_mul(
                smalls.ap()[:, 3:4], smalls.ap()[:, 1:2], 1.0 / 2048.0)), sig=V)
            v3 = p.emit("vector", (lambda: nc.vector.tensor_single_scalar(
                smalls.ap()[:, 4:5], smalls.ap()[:, 2:3], smalls.ap()[:, 2:3],
                ALU.mult)), sig=V)
            p.vspace(nc)
            v4 = p.emit("vector", (lambda: nc.vector.tensor_single_scalar(
                smalls.ap()[:, 3:4], smalls.ap()[:, 3:4], smalls.ap()[:, 4:5],
                ALU.subtract)), sig=V)
            a2 = p.emit("scalar", (lambda: nc.scalar.activation(
                smalls.ap()[:, 5:6], smalls.ap()[:, 3:4], AF.Sqrt,
                bias=smalls.ap()[:, 54:55])), waits=[(V, v4)], sig=A)
            v5 = p.emit("vector", (lambda: nc.vector.reciprocal(
                smalls.ap()[:, 6:7], smalls.ap()[:, 5:6])), waits=[(A, a2)], sig=V)
            p.vspace(nc)
            ob = tt4 % 2
            vno = p.emit("vector", (lambda src_ap=src_ap, ob=ob: nc.vector.tensor_scalar(
                ostage.ap()[:, ob, :], src_ap, smalls.ap()[:, 2:3],
                smalls.ap()[:, 6:7], ALU.subtract, ALU.mult)),
                waits=[(V, v5)] + ([(OUT_SEMS[ob], out_last[-2][1])]
                                   if len(out_last) >= 2 else []), sig=V)
            voa = p.emit("vector", (lambda ob=ob: nc.vector.tensor_add(
                ostage.ap()[:, ob, :], ostage.ap()[:, ob, :], rsb2.ap()[:, :])),
                waits=[(SQ[1], crr)], sig=V)
            od = p.dma("sync", out_h[tt * 128:(tt + 1) * 128, :], ostage.ap()[:, ob, :],
                       waits=[(V, voa)], sig=OUT_SEMS[ob])
            out_last.append((OUT_SEMS[ob], od))
        FF_FREE = (V, p.cnt[V])
        H1_FREE = (T, p.cnt[T])

    tapn = []
    internal_map = dict(win_bf=win_bf, wout_bf=wout_bf, w1_bf=w1_bf, w3_bf=w3_bf,
                        w2pp_bf=w2pp_bf, z_sp=z_sp, xfm_sp=xfm_sp, st_sp=st_sp,
                        resid_sp=resid_sp, fl_negrp=fl_negrp, fl_r=fl_r,
                        cc_st_out=cc_st_out, cc_ld_out=cc_ld_out, hin_sp=hin_sp,
                        ln1_dbg=ln1_dbg, u_dbg=u_dbg, sm_dbg=sm_dbg, lt_dbg=lt_dbg,
                        attn_sp=attn_sp, y_dbg=y_dbg, sc_dbg=sc_dbg, cb_dbg=cb_dbg,
                        ygfm_dbg=ygfm_dbg, woc_dbg=woc_dbg)
    p.wait("sync", [(nm, p.cnt[nm]) for nm in list(p.cnt)])
    final = None
    for name, shape, dt in debug_taps:
        src = internal_map[name]
        dst = nc.dram_tensor("dbg_" + name, list(shape), dt, kind="ExternalOutput").ap()
        final = p.dma("sync", dst[:, :] if len(shape) == 2 else dst[:, :, :],
                      src[:, :] if len(shape) == 2 else src[:, :, :], sig=SQ[5])
        tapn.append("dbg_" + name)


    with nc.Block() as block:
        @block.sync
        def _(e):
            p.replay("sync", e)

        @block.gpsimd
        def _(e):
            p.replay("gpsimd", e)

        @block.tensor
        def _(e):
            p.replay("tensor", e)

        @block.vector
        def _(e):
            p.replay("vector", e)

        @block.scalar
        def _(e):
            p.replay("scalar", e)

    return nc, tapn



def make_in_maps(inp):
    u = np.asarray(inp["L0_attn_u"], np.float32)
    W_in = np.ascontiguousarray(np.asarray(inp["L0_attn_W_in"], np.float32))
    conv_w = np.asarray(inp["L0_attn_conv_w"], np.float32)
    conv_b = np.asarray(inp["L0_attn_conv_b"], np.float32)
    dt_bias = np.asarray(inp["L0_attn_dt_bias"], np.float32)
    A_log = np.asarray(inp["L0_attn_A_log"], np.float32)
    Dp = np.asarray(inp["L0_attn_Dp"], np.float32)
    norm_w = np.asarray(inp["L0_attn_norm_w"], np.float32)
    W_out = np.ascontiguousarray(np.asarray(inp["L0_attn_W_out"], np.float32))
    W1 = np.ascontiguousarray(np.asarray(inp["L0_ffn_W1"], np.float32))
    W2 = np.ascontiguousarray(np.asarray(inp["L0_ffn_W2"], np.float32))
    W3 = np.ascontiguousarray(np.asarray(inp["L0_ffn_W3"], np.float32))

    convw_t = np.ascontiguousarray(conv_w.reshape(34, 128, 4).transpose(1, 0, 2))
    convb_t = np.ascontiguousarray(conv_b.reshape(34, 128).T)
    dtb_row = np.zeros((1, 128), np.float32)
    dtb_row[0, :64] = dt_bias
    alog_col = np.ascontiguousarray(A_log.reshape(64, 1))
    dp_rep = np.ascontiguousarray(np.repeat(Dp, HP).reshape(1, DI))
    nw_row = np.ascontiguousarray(norm_w.reshape(1, DI))

    WinT = np.zeros((DM, OPAD), np.float32)
    WinT[:, :OPROJ] = W_in.T
    WoutT = np.ascontiguousarray(W_out.T)
    W1T = np.ascontiguousarray(W1.T)
    W3T = np.ascontiguousarray(W3.T)
    maps = []
    for c in range(8):
        b, q = divmod(c, 4)
        t0 = q * Q
        ue = np.zeros((EXT, DM), np.float32)
        lo = max(0, t0 - 3)
        ue[3 - (t0 - lo):] = u[b, lo:t0 + Q]
        selm = np.zeros((4, 3), np.float32)
        selv = np.zeros((3, 1), np.float32)
        for s in range(3):
            if s < q:
                selv[s, 0] = 1.0
                for k in range(s + 1, q):
                    selm[k, s] = 1.0
        nwt = np.ascontiguousarray(
            norm_w[512 * c:512 * (c + 1)].reshape(4, 128).T)
        maps.append(dict(
            u_ext=np.ascontiguousarray(ue),
            win_sl=np.ascontiguousarray(WinT[256 * c:256 * (c + 1)]),
            wout_sl=np.ascontiguousarray(WoutT[512 * c:512 * (c + 1)]),
            w1_sl=np.ascontiguousarray(W1T[256 * c:256 * (c + 1)]),
            w3_sl=np.ascontiguousarray(W3T[256 * c:256 * (c + 1)]),
            w2_sl=np.ascontiguousarray(W2[256 * c:256 * (c + 1)]),
            convw_t=convw_t, convb_t=convb_t, dtb_row=dtb_row,
            alog_col=alog_col, dp_rep=dp_rep, nw_row=nw_row,
            nw_t=nwt, selm=selm, selv=selv,
        ))
    return maps


def kernel(**inp):
    nc, _ = build_program()
    maps = make_in_maps(inp)
    res = run_bass_kernel_spmd(nc, maps, core_ids=list(range(8)))
    out = np.stack([res.results[c]["out"] for c in range(8)], axis=0)
    return np.ascontiguousarray(out.reshape(2, 4, Q, DM).reshape(2, L, DM))


# version 06
# speedup vs baseline: 1.0843x; 1.0843x over previous
import numpy as np
import ml_dtypes
import concourse.bass as bass
import concourse.mybir as mybir
from concourse.bass_utils import run_bass_kernel_spmd

F32 = mybir.dt.float32
BF16 = mybir.dt.bfloat16
AF = mybir.ActivationFunctionType
ALU = mybir.AluOpType
BF = ml_dtypes.bfloat16

B, L, DM = 2, 4096, 2048
DI, NH, HP, DS, DCONV = 4096, 64, 64, 128, 4
CONV = DI + 2 * DS
OPROJ = 2 * DI + 2 * DS + NH
OPAD = 8576
Q = 1024
EXT = Q + 3
CH = 128
NCH = Q // CH
EPS = 1e-5
FWHT_SCALE = 1.0 / np.sqrt(8192.0)

ENGS = ("sync", "gpsimd", "tensor", "vector", "scalar")


class Prog:

    def __init__(self, nc):
        self.nc = nc
        self.ops = {e: [] for e in ENGS}
        self.semh = {}
        self.cnt = {}
        self.waited = {e: {} for e in ENGS}

    def sem(self, name):
        self.semh[name] = self.nc.alloc_semaphore(name)
        self.cnt[name] = 0
        return name

    def emit(self, eng, fn, waits=(), sig=None, sig_n=1):
        ws = []
        for s, v in waits:
            if v is None or v <= 0:
                continue
            if v > self.waited[eng].get(s, 0):
                self.waited[eng][s] = v
                ws.append((self.semh[s], v))
        tgt = None
        ret = None
        if sig is not None:
            self.cnt[sig] += sig_n
            ret = self.cnt[sig]
            tgt = (self.semh[sig], sig_n)
        self.ops[eng].append((tuple(ws), fn, tgt))
        return ret

    def vspace(self, nc):
        self.emit("vector", lambda: nc.vector.tensor_scalar_mul(
            self._sp_ap, self._sp_ap, 1.0))

    def wait(self, eng, waits):
        for s, v in waits:
            if v is None or v <= 0:
                continue
            if v > self.waited[eng].get(s, 0):
                self.waited[eng][s] = v
                h = self.semh[s]
                self.ops[eng].append((((h, v),), None, None))

    def dma(self, q, out, in_, waits=(), sig=None, transpose=False):
        eng = {"sync": self.nc.sync, "gpsimd": self.nc.gpsimd,
               "scalar": self.nc.scalar}[q]
        if transpose:
            fn = lambda: eng.dma_start_transpose(out=out, in_=in_)
        else:
            fn = lambda: eng.dma_start(out=out, in_=in_)
        waits = list(waits)
        if sig is not None and self.cnt[sig] > 0:
            waits.append((sig, self.cnt[sig]))
        return self.emit(q, fn, waits=waits, sig=sig, sig_n=16)

    def replay(self, eng_name, eng_obj):
        for ws, fn, tgt in self.ops[eng_name]:
            for h, v in ws:
                eng_obj.wait_ge(h, v)
            if fn is not None:
                inst = fn()
                if tgt is not None:
                    inst.then_inc(tgt[0], tgt[1])


def ceil_div(a, b):
    return (a + b - 1) // b


DEBUG_LN1 = False
DEBUG_PH3 = False
PH3_CHUNKS = 8
PH_LEVEL = 5

def build_program(debug_taps=()):
    nc = bass.Bass("TRN2", target_bir_lowering=False, debug=False)

    def ein(name, shape, dt=F32):
        return nc.dram_tensor(name, shape, dt, kind="ExternalInput").ap()

    u_ext = ein("u_ext", [EXT, DM])
    win_sl = ein("win_sl", [256, OPAD])
    wout_sl = ein("wout_sl", [512, DM])
    w1_sl = ein("w1_sl", [256, 8192])
    w3_sl = ein("w3_sl", [256, 8192])
    w2_sl = ein("w2_sl", [256, 8192])
    nw_t = ein("nw_t", [128, 4])
    convw_t = ein("convw_t", [128, 34, 4])
    convb_t = ein("convb_t", [128, 34])
    dtb_row = ein("dtb_row", [1, 128])
    alog_col = ein("alog_col", [64, 1])
    dp_rep = ein("dp_rep", [1, DI])
    nw_row = ein("nw_row", [1, DI])
    selm = ein("selm", [4, 3])
    selv = ein("selv", [3, 1])

    out_h = nc.dram_tensor("out", [Q, DM], F32, kind="ExternalOutput").ap()

    def internal(name, shape, dt):
        return nc.dram_tensor(name, shape, dt, kind="Internal").ap()

    def shared(name, shape, dt):
        return nc.dram_tensor(name, shape, dt, kind="Internal",
                              addr_space="Shared").ap()

    win_bf = shared("win_bf", [DM, OPAD], BF16)
    wout_bf = shared("wout_bf", [DI, DM], BF16)
    w1_bf = shared("w1_bf", [DM, 8192], BF16)
    w3_bf = shared("w3_bf", [DM, 8192], BF16)
    w2pp_bf = shared("w2pp_bf", [8 * 8192, 256], BF16)
    ccin_win = internal("ccin_win", [256, OPAD], BF16)
    ccin_wout = internal("ccin_wout", [512, DM], BF16)
    ccin_w1 = internal("ccin_w1", [256, 8192], BF16)
    ccin_w3 = internal("ccin_w3", [256, 8192], BF16)
    ccin_w2 = internal("ccin_w2", [8192, 256], BF16)

    z_sp = internal("z_sp", [DI, Q], BF16)
    xfm_sp = internal("xfm_sp", [DI, Q], BF16)
    st_sp = internal("st_sp", [NCH, 128, NH * HP], BF16)
    resid_sp = internal("resid_sp", [Q, DM], F32)
    fl_negrp = internal("fl_negrp", [1, NCH * NH * CH], F32)
    fl_r = internal("fl_r", [1, NCH * NH * CH], F32)
    wrow_b = internal("wrow_b", [1, 4 * NH], F32)
    cc_st_in = internal("cc_st_in", [128, NH * HP], BF16)
    cc_st_out = internal("cc_st_out", [512, NH * HP], BF16)
    cc_ld_in = internal("cc_ld_in", [NH, 1], F32)
    cc_ld_out = internal("cc_ld_out", [4 * NH, 1], F32)

    idb = nc.inline_tensor(np.eye(128, dtype=BF), name="id_bf").ap()
    idf = nc.inline_tensor(np.eye(128, dtype=np.float32), name="id_f32").ap()
    ones_row = nc.inline_tensor(np.ones((1, 512), np.float32), name="ones_row").ap()
    ones_big = nc.inline_tensor(np.ones((1, NH * CH), np.float32), name="ones_big").ap()
    h128c = nc.inline_tensor(
        np.array([[1.0 - 2.0 * (bin(i & j).count("1") & 1) for j in range(128)]
                  for i in range(128)], dtype=BF), name="h128").ap()
    mT = np.triu(np.ones((128, 128), np.float32)).astype(BF)
    maskT_c = nc.inline_tensor(mT, name="maskT").ap()

    SB = {}
    SB_BASE = 16384 + 2048
    SB_LIMIT = 229376
    sb_off = [SB_BASE]

    def sbuf(name, shape, dt, at=None):
        nbytes = int(np.prod(shape[1:])) * mybir.dt.size(dt)
        if at is None:
            at = sb_off[0]
            sb_off[0] = at + ((nbytes + 31) // 32) * 32
        h = nc.alloc_sbuf_tensor_at(name, shape, dt, offset=at)
        SB[name] = h
        return h

    c_fm = sbuf("c_fm", [128, Q], BF16)
    b_fm = sbuf("b_fm", [128, Q], BF16)
    r_tm = sbuf("r_tm", [128, NCH, NH], F32)
    expf_tm = sbuf("expf_tm", [128, NCH, NH], F32)
    dt_tm = sbuf("dt_tm", [128, NCH, NH], F32)
    dt_fm = sbuf("dt_fm", [64, Q], F32)
    lndt_fm = sbuf("lndt_fm", [64, Q], F32)
    r_fm = sbuf("r_fm", [64, Q], F32)
    hT = sbuf("hT", [128, NH * HP], BF16)
    dcol = sbuf("dcol", [128, NCH * NH], F32)
    dpb = sbuf("dpb", [128, DI], BF16)
    lndt_tm = sbuf("lndt_tm", [128, NCH, NH], F32)
    nrp_tm = sbuf("nrp_tm", [128, NCH, NH], F32)
    rl_cols = sbuf("rl_cols", [128, NCH * NH], F32)
    smalls = sbuf("smalls", [128, 256], F32)
    p_spacer_pending = True
    wq = sbuf("wq", [4, 64], F32)
    wq2 = sbuf("wq2", [3, 64], F32)
    wcols = sbuf("wcols", [128, 4 * NH], F32)
    cw_sb = sbuf("cw_sb", [128, 34, 4], F32)
    cb_sb = sbuf("cb_sb", [128, 34], F32)
    maskT_s = sbuf("maskT_s", [128, 128], BF16)
    h128_s = sbuf("h128_s", [128, 128], BF16)
    dtb_s = sbuf("dtb_s", [1, 128], F32)
    idb_s = sbuf("idb_s", [128, 128], BF16)
    idf_s = sbuf("idf_s", [128, 128], F32)
    onesr_s = sbuf("onesr_s", [1, 512], F32)
    rowb_s = sbuf("rowb_s", [1, 512], F32)

    R0 = sb_off[0]
    RSZ = SB_LIMIT - R0
    assert RSZ >= 150 * 1024, (R0, RSZ)

    def reg(name, shape, dt, at):
        nbytes = int(np.prod(shape[1:])) * mybir.dt.size(dt)
        assert at + nbytes <= RSZ, (name, at, nbytes, RSZ)
        return nc.alloc_sbuf_tensor_at(name, shape, dt, offset=R0 + at)

    w2a = reg("w2a", [128, 8192], F32, 0)
    w2b = reg("w2b", [128, 8192], F32, 32768)
    cvt = reg("cvt", [128, DI], F32, 65536)
    cvb = reg("cvb", [128, 2, 2048], BF16, 81920)
    nwb = reg("nwb", [128, DI], F32, 90112)
    srow = reg("srow", [1, 4096], F32, 106496)
    w2stg = reg("w2stg", [128, 4, 128], BF16, 122880)

    PB = []
    for i in range(6):
        PB.append(nc.alloc_psum_tensor(f"pb{i}", [128, 512], F32))
    PT = []
    for i in range(2):
        PT.append(nc.alloc_psum_tensor(f"pt{i}", [128, 1024], BF16))

    p = Prog(nc)
    p._sp_ap = smalls.ap()[:, 254:255]
    SQ = [p.sem(f"sq{i}") for i in range(16)]
    GQ = [p.sem(f"gq{i}") for i in range(6)]
    T = p.sem("T")
    V = p.sem("V")
    A = p.sem("A")
    CC = p.sem("CC")

    bank_free = {i: None for i in range(8)}

    def bank_wait(i):
        return [bank_free[i]] if bank_free[i] is not None else []

    def cast_weight(src, dst, rows, qsem_ld, qsem_st):
        done = None
        r0 = 0
        i = 0
        while r0 < rows:
            n = min(128, rows - r0)
            buf = i % 2
            w = [cvb_free[buf]] if cvb_free[buf] else []
            c1 = p.dma("gpsimd", cvb.ap()[0:n, buf, :], src[r0:r0 + n, :],
                       waits=w, sig=qsem_ld)
            done = p.dma("gpsimd", dst[r0:r0 + n, :], cvb.ap()[0:n, buf, :],
                         waits=[(qsem_ld, c1)], sig=qsem_st)
            cvb_free[buf] = (qsem_st, done)
            r0 += n
            i += 1
        return done

    zv0 = p.emit("vector", lambda: nc.vector.memset(
        cvb.ap()[:, :, :].rearrange("p a c -> p (a c)"), 0.0), sig=V)
    czp = None
    cvb_free = [None, None]

    def cast_weight(src_ap, dst, rows, cols, qsem_ld, qsem_st, extra=None):
        done = None
        i = 0
        for r0 in range(0, rows, 128):
            n = min(128, rows - r0)
            for c0 in range(0, cols, 2048):
                w = min(2048, cols - c0)
                buf = i % 2
                ww = [cvb_free[buf]] if cvb_free[buf] else []
                c1 = p.dma("gpsimd", cvb.ap()[0:n, buf, 0:w],
                           src_ap[r0:r0 + n, c0:c0 + w], waits=ww, sig=qsem_ld)
                done = p.dma("gpsimd", dst[r0:r0 + n, c0:c0 + w],
                             cvb.ap()[0:n, buf, 0:w],
                             waits=[(qsem_ld, c1)], sig=qsem_st)
                cvb_free[buf] = (qsem_st, done)
                i += 1
        return done

    cwin = cast_weight(win_sl, ccin_win, 256, OPAD, GQ[0], GQ[1])
    agwin = p.emit("gpsimd", lambda: nc.gpsimd.collective_compute(
        "AllGather", ALU.bypass, replica_groups=[list(range(8))],
        ins=[ccin_win.opt()], outs=[win_bf[:, :]]),
        waits=[(GQ[1], cwin)], sig=CC)

    lnw = p.dma("sync", smalls.ap()[:, 16:20], nw_t[:, :], sig=SQ[1])
    cwout = None
    cvt_free = None
    for i in range(4):
        w = [cvt_free] if cvt_free else []
        c1 = p.dma("sync", cvt.ap()[:, 0:2048], wout_sl[i * 128:(i + 1) * 128, :],
                   waits=w + [(SQ[1], lnw)], sig=SQ[1])
        buf = i % 2
        ww = [cvb_free[buf]] if cvb_free[buf] else []
        mv = p.emit("vector", (lambda i=i, buf=buf: nc.vector.tensor_single_scalar(
            cvb.ap()[:, buf, :], cvt.ap()[:, 0:2048],
            smalls.ap()[:, 16 + i:17 + i], ALU.mult)),
            waits=[(SQ[1], c1)] + ww, sig=V)
        cvt_free = (V, mv)
        cwout = p.dma("gpsimd", ccin_wout[i * 128:(i + 1) * 128, :],
                      cvb.ap()[:, buf, :], waits=[(V, mv)], sig=GQ[1])
        cvb_free[buf] = (GQ[1], cwout)
    agwout = p.emit("gpsimd", lambda: nc.gpsimd.collective_compute(
        "AllGather", ALU.bypass, replica_groups=[list(range(8))],
        ins=[ccin_wout.opt()], outs=[wout_bf[:, :]]),
        waits=[(GQ[1], cwout)], sig=CC)

    cw1 = cast_weight(w1_sl, ccin_w1, 256, 8192, GQ[0], GQ[1])
    agw1 = p.emit("gpsimd", lambda: nc.gpsimd.collective_compute(
        "AllGather", ALU.bypass, replica_groups=[list(range(8))],
        ins=[ccin_w1.opt()], outs=[w1_bf[:, :]]),
        waits=[(GQ[1], cw1)], sig=CC)
    cw3 = cast_weight(w3_sl, ccin_w3, 256, 8192, GQ[0], GQ[1])
    agw3 = p.emit("gpsimd", lambda: nc.gpsimd.collective_compute(
        "AllGather", ALU.bypass, replica_groups=[list(range(8))],
        ins=[ccin_w3.opt()], outs=[w3_bf[:, :]]),
        waits=[(GQ[1], cw3)], sig=CC)

    idld = p.dma("sync", idb_s.ap()[:, :], idb[:, :], sig=SQ[0])
    vv = None
    cw2 = None
    PT0_F = None
    for d in range(2):
        cw2l = p.dma("sync", w2a.ap()[:, :], w2_sl[d * 128:(d + 1) * 128, :],
                     waits=[(V, vv)] if vv else [], sig=SQ[1])
        pa = w2a.ap()[:, :]
        pb_ = w2b.ap()[:, :]
        for b_ in range(6):
            s = 1 << b_
            blk = 2 * s * 128
            g = 8192 // blk
            cur_ap = pa if (b_ % 2 == 0) else pb_
            oth_ap = pb_ if (b_ % 2 == 0) else pa
            lo_in = cur_ap.rearrange("p (g r) -> p g r", g=g)[:, :, 0:s * 128]
            hi_in = cur_ap.rearrange("p (g r) -> p g r", g=g)[:, :, s * 128:blk]
            lo_out = oth_ap.rearrange("p (g r) -> p g r", g=g)[:, :, 0:s * 128]
            hi_out = oth_ap.rearrange("p (g r) -> p g r", g=g)[:, :, s * 128:blk]
            w = [(SQ[1], cw2l)] if b_ == 0 else []
            vv = p.emit("vector", (lambda a=lo_out, x=lo_in, y=hi_in:
                                   nc.vector.tensor_add(a, x, y)), waits=w, sig=V)
            vv = p.emit("vector", (lambda a=hi_out, x=lo_in, y=hi_in:
                                   nc.vector.tensor_sub(a, x, y)), sig=V)
        W2SEM = [SQ[3], SQ[5], SQ[11], SQ[12]]
        w2stg_free = [None] * 4
        pt_free2 = [None, None]
        cw2_marks = []
        for hhalf in range(2):
            vv2 = p.emit("vector", (lambda h=hhalf: nc.vector.tensor_scalar_mul(
                cvb.ap()[:, :, :].rearrange("p a c -> p (a c)")[:, 0:4096],
                w2a.ap()[:, h * 4096:(h + 1) * 4096], FWHT_SCALE)),
                waits=([(V, vv)] if vv else []) + [(T, p.cnt[T])] +
                      [x for x in cvb_free if x], sig=V)
            vv = vv2
            for ft in range(32):
                f = hhalf * 32 + ft
                ptb = ft % 2
                sb4 = ft % 4
                wpt0 = [pt_free2[ptb]] if pt_free2[ptb] else []
                tm = p.emit("tensor", (lambda ft=ft, ptb=ptb: nc.tensor.transpose(
                    PT[ptb].ap()[:, 0:128],
                    cvb.ap()[:, :, :].rearrange("p a c -> p (a c)")[:, ft * 128:(ft + 1) * 128],
                    idb_s.ap()[:, :])),
                    waits=[(V, vv2), (SQ[0], idld)] + wpt0, sig=T)
                wst4 = [w2stg_free[sb4]] if w2stg_free[sb4] else []
                ev = p.emit("vector", (lambda ptb=ptb, sb4=sb4: nc.vector.tensor_copy(
                    w2stg.ap()[:, sb4, :], PT[ptb].ap()[:, 0:128])),
                    waits=[(T, tm)] + wst4, sig=V)
                pt_free2[ptb] = (V, ev)
                cw2 = p.dma("sync", ccin_w2[f * 128:(f + 1) * 128, d * 128:(d + 1) * 128],
                            w2stg.ap()[:, sb4, :], waits=[(V, ev)], sig=W2SEM[sb4])
                w2stg_free[sb4] = (W2SEM[sb4], cw2)
                cw2_marks.append((W2SEM[sb4], cw2))
            cvb_free = [cw2_marks[-1], cw2_marks[-2]]
    w2done = {}
    for sm, vval in cw2_marks:
        w2done[sm] = max(w2done.get(sm, 0), vval)
    agw2 = p.emit("gpsimd", lambda: nc.gpsimd.collective_compute(
        "AllGather", ALU.bypass, replica_groups=[list(range(8))],
        ins=[ccin_w2.opt()], outs=[w2pp_bf[:, :]]),
        waits=list(w2done.items()), sig=CC)

    AG_WIN, AG_WOUT, AG_W1, AG_W3, AG_W2 = agwin, agwout, agw1, agw3, agw2


    rowhbm = internal("rowhbm", [1, 512], F32)
    ln1_dbg = internal("ln1_dbg", [2, 128, 16, 515], BF16)
    u_dbg = internal("u_dbg", [128, 2048], F32)
    sm_dbg = internal("sm_dbg", [128, 256], F32)
    lt_dbg = internal("lt_dbg", [128, 2048], BF16)

    wcache = reg("wc", [128, 2, 16, 512], BF16, 0)
    ln1fm = reg("ln1fm", [128, 16, 1027], BF16, 32768)
    xbc = reg("xbc", [128, 4, 1027], BF16, 65632)
    ustage = reg("ustage", [128, 2, 2048], F32, 73856)
    ln1tm = reg("ln1tm", [128, 2, 2048], BF16, 90240)
    lnscr = reg("lnscr", [128, 2048], BF16, 98432)
    zstage = reg("zstage", [128, 2, 512], BF16, 102528)
    convacc = reg("convacc", [128, 2, 1024], F32, 104576)
    convstage = reg("convstage", [128, 2, 1024], BF16, 112768)
    dtscr = reg("dtscr", [128, 64], F32, 116864)
    xstage = reg("xstage", [128, 2, 4096], BF16, 117120)
    xw_sb = reg("xw_sb", [128, 512], BF16, 133504)
    stst = reg("stst", [128, 4096], BF16, 134528)
    wtm = reg("wtm", [128, 64], F32, 142720)
    wtm2 = reg("wtm2", [128, 64], F32, 142976)
    dascr = reg("dascr", [64, 1024], F32, 143232)
    nrpscr = reg("nrpscr", [64, 1024], F32, 147328)
    btm = reg("btm", [128, 128], BF16, 151424)
    stripsb = reg("stripsb", [3, 512], BF16, 151680)

    PH0_V = p.cnt[V]
    PH0_A = p.cnt[A]
    PH0_T = p.cnt[T]
    PH0_GQ1 = p.cnt[GQ[1]]
    PH0_SQ1 = p.cnt[SQ[1]]
    PH0_BAR = [(V, PH0_V), (A, PH0_A), (T, PH0_T), (GQ[1], PH0_GQ1), (SQ[1], PH0_SQ1)]
    for _e in ("sync", "gpsimd", "tensor", "vector", "scalar"):
        p.wait(_e, PH0_BAR)

    lsm = p.dma("sync", cw_sb.ap()[:, :, :], convw_t[:, :, :], sig=SQ[0])
    lsm = p.dma("sync", cb_sb.ap()[:, :], convb_t[:, :], sig=SQ[0])
    lsm = p.dma("sync", maskT_s.ap()[:, :], maskT_c[:, :], sig=SQ[0])
    lsm = p.dma("sync", h128_s.ap()[:, :], h128c[:, :], sig=SQ[0])
    lsm = p.dma("sync", dtb_s.ap()[:, :], dtb_row[:, :], sig=SQ[0])
    lsm = p.dma("sync", onesr_s.ap()[:, :], ones_row[:, :], sig=SQ[0])
    lsm = p.dma("sync", idb_s.ap()[:, :], idb[:, :], sig=SQ[0])
    lsm = p.dma("sync", idf_s.ap()[:, :], idf[:, :], sig=SQ[0])
    lsm = p.dma("sync", smalls.ap()[0:64, 48:49], alog_col[:, :], sig=SQ[0])
    SMALL_LD = lsm
    ldp = p.dma("gpsimd", dpb.ap()[0:1, 0:4096], dp_rep[:, :], sig=GQ[2])
    aa = p.emit("scalar", lambda: nc.scalar.activation(
        smalls.ap()[0:64, 49:50], smalls.ap()[0:64, 48:49], AF.Exp),
        waits=[(SQ[0], SMALL_LD)], sig=A)
    vna = p.emit("vector", lambda: nc.vector.tensor_scalar_mul(
        smalls.ap()[0:64, 50:51], smalls.ap()[0:64, 49:50], -1.0),
        waits=[(A, aa)], sig=V)
    NEGA = vna
    dpw = None
    for j in range(8):
        mm = p.emit("tensor", (lambda j=j: nc.tensor.matmul(
            PB[4].ap()[:, :], h128_s.ap()[0:1, 0:128],
            dpb.ap()[0:1, j * 512:(j + 1) * 512], start=True, stop=True)),
            waits=[(SQ[0], SMALL_LD), (GQ[2], ldp)] + bank_wait(4), sig=T)
        dpw = p.emit("vector", (lambda j=j: nc.vector.tensor_copy(
            dpb.ap()[:, j * 512:(j + 1) * 512], PB[4].ap()[:, :])),
            waits=[(T, mm)], sig=V)
        bank_free[4] = (V, dpw)
    DPB_OK = dpw

    hz = p.emit("vector", lambda: nc.vector.memset(hT.ap()[:, :], 0.0), sig=V)
    ve1 = p.emit("vector", lambda: nc.vector.memset(smalls.ap()[:, 54:55], EPS), sig=V)
    ve2 = p.emit("vector", lambda: nc.vector.memset(smalls.ap()[:, 55:56], 1.0 / DI), sig=V)
    EPSC = ve2

    UST_FREE = [None, None]
    LN1TM_FREE = [None, None]
    PT_FREE = [None, None]
    SSQ_FREE = [None, None]
    tiles = [(0, 128, 0), (128, 128, 128), (256, 128, 256), (384, 128, 384),
             (512, 128, 512), (640, 128, 640), (768, 128, 768), (896, 128, 896),
             (1024, 3, 1024)]
    for ti, (e0, n, co) in enumerate(tiles):
        ub = ti % 2
        w = list(UST_FREE[ub]) if UST_FREE[ub] else []
        usem = SQ[1 + 5 * ub]
        cu = p.dma("sync", ustage.ap()[0:n, ub, :], u_ext[e0:e0 + n, :],
                   waits=w, sig=usem)
        base = 16 * (ti % 2)
        vsum = p.emit("vector", (lambda n=n, ub=ub, b=base: nc.vector.reduce_sum(
            smalls.ap()[0:n, b + 0:b + 1], ustage.ap()[0:n, ub, :],
            axis=mybir.AxisListType.X)), waits=[(usem, cu)], sig=V)
        p.emit("scalar", (lambda n=n, ub=ub, b=base: nc.scalar.activation(
            lnscr.ap()[0:n, :], ustage.ap()[0:n, ub, :], AF.Square,
            accum_out=smalls.ap()[0:n, b + 1:b + 2])),
            waits=[(usem, cu)] + ([SSQ_FREE[ti % 2]] if SSQ_FREE[ti % 2] else []))
        asq = p.emit("scalar", (lambda n=n, b=base: nc.scalar.copy(
            smalls.ap()[0:n, b + 7:b + 8], smalls.ap()[0:n, b + 1:b + 2])),
            sig=A)
        p.vspace(nc)
        v1 = p.emit("vector", (lambda n=n, b=base: nc.vector.tensor_scalar_mul(
            smalls.ap()[0:n, b + 2:b + 3], smalls.ap()[0:n, b + 0:b + 1],
            1.0 / 2048.0)), waits=[(V, vsum)], sig=V)
        v2 = p.emit("vector", (lambda n=n, b=base: nc.vector.tensor_scalar_mul(
            smalls.ap()[0:n, b + 3:b + 4], smalls.ap()[0:n, b + 1:b + 2],
            1.0 / 2048.0)), waits=[(A, asq)], sig=V)
        SSQ_FREE[ti % 2] = (V, v2)
        v3 = p.emit("vector", (lambda n=n, b=base: nc.vector.tensor_single_scalar(
            smalls.ap()[0:n, b + 4:b + 5], smalls.ap()[0:n, b + 2:b + 3],
            smalls.ap()[0:n, b + 2:b + 3], ALU.mult)), sig=V)
        p.vspace(nc)
        v4 = p.emit("vector", (lambda n=n, b=base: nc.vector.tensor_single_scalar(
            smalls.ap()[0:n, b + 3:b + 4], smalls.ap()[0:n, b + 3:b + 4],
            smalls.ap()[0:n, b + 4:b + 5], ALU.subtract)), sig=V)
        a2 = p.emit("scalar", (lambda n=n, b=base: nc.scalar.activation(
            smalls.ap()[0:n, b + 5:b + 6], smalls.ap()[0:n, b + 3:b + 4],
            AF.Sqrt, bias=smalls.ap()[0:n, 54:55])),
            waits=[(V, v4), (V, EPSC)], sig=A)
        v5 = p.emit("vector", (lambda n=n, b=base: nc.vector.reciprocal(
            smalls.ap()[0:n, b + 6:b + 7], smalls.ap()[0:n, b + 5:b + 6])),
            waits=[(A, a2)], sig=V)
        wlt = [LN1TM_FREE[ub]] if LN1TM_FREE[ub] else []
        p.vspace(nc)
        v6 = p.emit("vector", (lambda n=n, ub=ub, b=base: nc.vector.tensor_scalar(
            ln1tm.ap()[0:n, ub, :], ustage.ap()[0:n, ub, :],
            smalls.ap()[0:n, b + 2:b + 3], smalls.ap()[0:n, b + 6:b + 7],
            ALU.subtract, ALU.mult)), waits=[(V, v5)] + wlt, sig=V)
        UST_FREE[ub] = [(V, v6), (A, asq)]
        for k in range(16):
            ptb = k % 2
            wpt = [PT_FREE[ptb]] if PT_FREE[ptb] else []
            tm = p.emit("tensor", (lambda n=n, ub=ub, k=k, ptb=ptb:
                        nc.tensor.transpose(
                            PT[ptb].ap()[:, 0:n],
                            ln1tm.ap()[0:n, ub, k * 128:(k + 1) * 128],
                            idb_s.ap()[0:n, 0:n])),
                        waits=[(V, v6), (SQ[0], SMALL_LD)] + wpt, sig=T)
            if k % 2 == 0:
                ev = p.emit("vector", (lambda n=n, k=k, co=co, ptb=ptb:
                            nc.vector.tensor_copy(
                                ln1fm.ap()[:, k, co:co + n], PT[ptb].ap()[:, 0:n])),
                            waits=[(T, tm)], sig=V)
                PT_FREE[ptb] = (V, ev)
            else:
                ev = p.emit("scalar", (lambda n=n, k=k, co=co, ptb=ptb:
                            nc.scalar.copy(
                                ln1fm.ap()[:, k, co:co + n], PT[ptb].ap()[:, 0:n])),
                            waits=[(T, tm)], sig=A)
                PT_FREE[ptb] = (A, ev)
        LN1TM_FREE[ub] = (T, p.cnt[T])
    LN1FM_OK = [(V, p.cnt[V]), (A, p.cnt[A])]

    OCW = [512] * 16 + [384]
    WC_FREE = [None, None]
    ZST_FREE = [None, None]
    CONVST_FREE = [None, None]
    CONVACC_FREE = [None, None]
    STRIP_FREE = None
    slab_free = {}
    slab_ready = {}
    xspill = {}
    dt_mm_done = {}
    z_done = 0

    for oc in range(17):
        wb = oc % 2
        width = OCW[oc]
        wwc = [WC_FREE[wb]] if WC_FREE[wb] else []
        lw = []
        for k in range(16):
            wsem = SQ[[2, 7, 11, 12][k % 4]]
            lwv = p.dma("sync", wcache.ap()[:, wb, k, 0:width],
                        win_bf[k * 128:(k + 1) * 128, oc * 512:oc * 512 + width],
                        waits=wwc + [(CC, AG_WIN)], sig=wsem)
            lw.append((wsem, lwv))
            wwc = []
        lw = [(sm, max(vv for sm2, vv in lw if sm2 == sm)) for sm in set(s0 for s0, _ in lw)]
        nots = 4 if oc < 16 else 2
        if oc >= 8:
            mms = None
            for k in range(16):
                mms = p.emit("tensor", (lambda k=k, wb=wb, width=width:
                            nc.tensor.matmul(
                                PB[5].ap()[0:3, 0:width],
                                ln1fm.ap()[:, k, 0:3],
                                wcache.ap()[:, wb, k, 0:width],
                                start=(k == 0), stop=(k == 15))),
                            waits=(lw + LN1FM_OK + bank_wait(5)) if k == 0 else [],
                            sig=(T if k == 15 else None))
            wstf = [STRIP_FREE] if STRIP_FREE else []
            evs = p.emit("vector", (lambda width=width: nc.vector.tensor_copy(
                stripsb.ap()[0:3, 0:width], PB[5].ap()[0:3, 0:width])),
                waits=[(T, mms)] + wstf, sig=V)
            bank_free[5] = (V, evs)
            STRIP_V = evs
        for oti in range(nots):
            g = oc * 4 + oti
            rot = g % 4
            pp = g % 2
            for h in range(2):
                mm = None
                for k in range(16):
                    mm = p.emit("tensor", (lambda oti=oti, k=k, pp=pp, wb=wb, h=h:
                                nc.tensor.matmul(
                                    PB[pp].ap()[:, :],
                                    wcache.ap()[:, wb, k, oti * 128:(oti + 1) * 128],
                                    ln1fm.ap()[:, k, 3 + h * 512:3 + (h + 1) * 512],
                                    start=(k == 0), stop=(k == 15))),
                                waits=(lw + LN1FM_OK + bank_wait(pp)) if k == 0 else [],
                                sig=(T if k == 15 else None))
                if g < 32:
                    zb = (g * 2 + h) % 2
                    wz = [ZST_FREE[zb]] if ZST_FREE[zb] else []
                    ve = p.emit("vector", (lambda zb=zb, pp=pp: nc.vector.tensor_copy(
                        zstage.ap()[:, zb, :], PB[pp].ap()[:, :])),
                        waits=[(T, mm)] + wz, sig=V)
                    bank_free[pp] = (V, ve)
                    zsem = SQ[3 + 5 * zb]
                    zd = p.dma("sync", z_sp[g * 128:(g + 1) * 128,
                                            h * 512:(h + 1) * 512],
                               zstage.ap()[:, zb, :], waits=[(V, ve)], sig=zsem)
                    ZST_FREE[zb] = (zsem, zd)
                    z_done = zd
                else:
                    wsl = [slab_free[rot]] if rot in slab_free else []
                    ve = p.emit("vector", (lambda rot=rot, h=h, pp=pp:
                                nc.vector.tensor_copy(
                                    xbc.ap()[:, rot, 3 + h * 512:3 + (h + 1) * 512],
                                    PB[pp].ap()[:, :])),
                                waits=[(T, mm)] + (wsl if h == 0 else []), sig=V)
                    bank_free[pp] = (V, ve)
            if g >= 32:
                wpt = [PT_FREE[0]] if PT_FREE[0] else []
                tms = p.emit("tensor", (lambda oti=oti: nc.tensor.transpose(
                    PT[0].ap()[:, 0:3], stripsb.ap()[0:3, oti * 128:(oti + 1) * 128],
                    idb_s.ap()[0:3, 0:3])),
                    waits=[(V, STRIP_V)] + wpt, sig=T)
                evh = p.emit("vector", (lambda rot=rot: nc.vector.tensor_copy(
                    xbc.ap()[:, rot, 0:3], PT[0].ap()[:, 0:3])),
                    waits=[(T, tms)], sig=V)
                PT_FREE[0] = (V, evh)
                slab_ready[g] = (V, p.cnt[V])
                ot = g - 32
                ca = ot % 2
                wc0 = ([(SQ[0], SMALL_LD)] if ot == 0 else [])
                wca = [CONVACC_FREE[ca]] if CONVACC_FREE[ca] else []
                v1 = p.emit("vector", (lambda ot=ot, ca=ca, rot=rot:
                            nc.vector.tensor_single_scalar(
                                convacc.ap()[:, ca, :], xbc.ap()[:, rot, 0:1024],
                                cw_sb.ap()[:, ot, 0:1], ALU.mult)),
                            waits=wc0 + wca + [slab_ready[g]], sig=V)
                for k in range(1, 4):
                    v1 = p.emit("vector", (lambda ot=ot, k=k, ca=ca, rot=rot:
                                nc.vector.scalar_tensor_tensor(
                                    convacc.ap()[:, ca, :], xbc.ap()[:, rot, k:1024 + k],
                                    cw_sb.ap()[:, ot, k:k + 1], convacc.ap()[:, ca, :],
                                    ALU.mult, ALU.add)), sig=V)
                slab_free[rot] = (V, v1)
                if ot < 32:
                    cb2 = ot % 2
                    wcs = [CONVST_FREE[cb2]] if CONVST_FREE[cb2] else []
                    a1 = p.emit("scalar", (lambda ot=ot, cb2=cb2, ca=ca:
                                nc.scalar.activation(
                                    convstage.ap()[:, cb2, :], convacc.ap()[:, ca, :],
                                    AF.Silu, bias=cb_sb.ap()[:, ot:ot + 1])),
                                waits=[(V, v1)] + wcs, sig=A)
                    CONVACC_FREE[ca] = (A, a1)
                    xsem = SQ[13 + cb2]
                    xd = p.dma("sync", xfm_sp[ot * 128:(ot + 1) * 128, :],
                               convstage.ap()[:, cb2, :], waits=[(A, a1)], sig=xsem)
                    CONVST_FREE[cb2] = (xsem, xd)
                    xspill[ot] = (xsem, xd)
                else:
                    dstap = b_fm if ot == 32 else c_fm
                    a1 = p.emit("scalar", (lambda dstap=dstap, ca=ca, ot=ot:
                                nc.scalar.activation(
                                    dstap.ap()[:, :], convacc.ap()[:, ca, :],
                                    AF.Silu, bias=cb_sb.ap()[:, ot:ot + 1])),
                                waits=[(V, v1)], sig=A)
                    CONVACC_FREE[ca] = (A, a1)
        WC_FREE[wb] = (T, p.cnt[T])
        if oc >= 8:
            STRIP_FREE = (T, p.cnt[T])
        if oc == 16:
            for tt in range(8):
                c = tt
                pp = tt % 2
                for k in range(16):
                    p.emit("tensor", (lambda tt=tt, k=k, pp=pp, wb=wb:
                           nc.tensor.matmul(
                               PB[pp].ap()[:, 0:128],
                               ln1fm.ap()[:, k, 3 + tt * 128:3 + (tt + 1) * 128],
                               wcache.ap()[:, wb, k, 256:384],
                               start=(k == 0), stop=False)),
                           waits=(bank_wait(pp) if k == 0 else []), sig=None)
                mm = p.emit("tensor", (lambda pp=pp: nc.tensor.matmul(
                    PB[pp].ap()[:, 0:128], onesr_s.ap()[0:1, 0:128],
                    dtb_s.ap()[0:1, 0:128], start=False, stop=True)),
                    waits=[(SQ[0], SMALL_LD)], sig=T)
                a1 = p.emit("scalar", (lambda pp=pp: nc.scalar.activation(
                    dtscr.ap()[:, :], PB[pp].ap()[:, 0:64], AF.Exp)),
                    waits=[(T, mm)], sig=A)
                a2 = p.emit("scalar", (lambda c=c: nc.scalar.activation(
                    dt_tm.ap()[:, c, :], dtscr.ap()[:, :], AF.Ln, bias=1.0)),
                    sig=A)
                a3 = p.emit("scalar", (lambda c=c: nc.scalar.activation(
                    lndt_tm.ap()[:, c, :], dt_tm.ap()[:, c, :], AF.Ln)),
                    sig=A)
                bank_free[pp] = (A, a1)
                dt_mm_done[c] = a3
            WC_FREE[wb] = (T, p.cnt[T])
    XSPILL_HALF = [(SQ[13], p.cnt[SQ[13]]), (SQ[14], p.cnt[SQ[14]])]
    Z_OK = [(SQ[3], p.cnt[SQ[3]]), (SQ[8], p.cnt[SQ[8]])]
    BC_OK = (A, p.cnt[A])

    for tt in range(8):
        c = tt
        tm = p.emit("tensor", (lambda c=c: nc.tensor.transpose(
            PB[4].ap()[0:64, 0:128], dt_tm.ap()[:, c, :], idf_s.ap()[:, :])),
            waits=[(A, dt_mm_done[c])] + bank_wait(4), sig=T)
        ev = p.emit("vector", (lambda c=c: nc.vector.tensor_copy(
            dt_fm.ap()[0:64, c * 128:(c + 1) * 128], PB[4].ap()[0:64, 0:128])),
            waits=[(T, tm)], sig=V)
        bank_free[4] = (V, ev)
    a1 = p.emit("scalar", (lambda: nc.scalar.activation(
        lndt_fm.ap()[0:64, :], dt_fm.ap()[0:64, :], AF.Ln)),
        waits=[(V, p.cnt[V])], sig=A)
    v1 = p.emit("vector", (lambda: nc.vector.tensor_single_scalar(
        dascr.ap()[0:64, 0:1024], dt_fm.ap()[0:64, :],
        smalls.ap()[0:64, 50:51], ALU.mult)),
        waits=[(V, NEGA)], sig=V)
    p.vspace(nc)
    for c in range(8):
        p.emit("vector", (lambda c=c: nc.vector.tensor_tensor_scan(
            r_fm.ap()[0:64, c * 128:(c + 1) * 128],
            dascr.ap()[0:64, c * 128:(c + 1) * 128],
            dascr.ap()[0:64, c * 128:(c + 1) * 128],
            0.0, ALU.add, ALU.bypass)), sig=V)
    p.vspace(nc)
    v2 = p.emit("vector", (lambda: nc.vector.tensor_sub(
        nrpscr.ap()[0:64, 0:1024], lndt_fm.ap()[0:64, :], r_fm.ap()[0:64, :])),
        waits=[(A, a1)], sig=V)
    fd = p.dma("gpsimd",
               fl_negrp[0:1, :].rearrange("o (c h j) -> (o h) c j", c=8, h=64, j=128),
               nrpscr.ap()[0:64, 0:1024].rearrange("h (c j) -> h c j", c=8),
               waits=[(V, v2), (CC, AG_W2)], sig=GQ[2])
    fd = p.dma("gpsimd",
               fl_r[0:1, :].rearrange("o (c h j) -> (o h) c j", c=8, h=64, j=128),
               r_fm.ap()[0:64, :].rearrange("h (c j) -> h c j", c=8),
               waits=[(V, v2)], sig=GQ[2])
    FLATS_OK = fd
    for c in range(8):
        tm = p.emit("tensor", (lambda c=c: nc.tensor.transpose(
            PB[4].ap()[:, 0:64], r_fm.ap()[0:64, c * 128:(c + 1) * 128],
            idf_s.ap()[0:64, 0:64])),
            waits=[(V, p.cnt[V])] + bank_wait(4), sig=T)
        ev = p.emit("vector", (lambda c=c: nc.vector.tensor_copy(
            r_tm.ap()[:, c, :], PB[4].ap()[:, 0:64])),
            waits=[(T, tm)], sig=V)
        bank_free[4] = (V, ev)
    aex = p.emit("scalar", (lambda: nc.scalar.activation(
        expf_tm.ap()[:, :, :], r_tm.ap()[:, :, :], AF.Exp)),
        waits=[(V, p.cnt[V])], sig=A)
    rb1 = p.dma("gpsimd", rowhbm[0:1, 0:512].rearrange("o (c h) -> o c h", c=8),
                r_tm.ap()[127:128, :, :], waits=[(V, p.cnt[V])], sig=GQ[2])
    rb2 = p.dma("gpsimd", rowb_s.ap()[0:1, 0:512], rowhbm[0:1, 0:512],
                waits=[(GQ[2], rb1)], sig=GQ[2])
    mm = p.emit("tensor", (lambda: nc.tensor.matmul(
        PB[4].ap()[:, 0:512], onesr_s.ap()[0:1, 0:128],
        rowb_s.ap()[0:1, 0:512], start=True, stop=True)),
        waits=[(GQ[2], rb2), (SQ[0], SMALL_LD)] + bank_wait(4), sig=T)
    ev = p.emit("vector", (lambda: nc.vector.tensor_copy(
        rl_cols.ap()[:, :], PB[4].ap()[:, 0:512])),
        waits=[(T, mm)], sig=V)
    bank_free[4] = (V, ev)
    adc = p.emit("scalar", (lambda: nc.scalar.activation(
        dcol.ap()[:, :], rl_cols.ap()[:, :], AF.Exp)), waits=[(V, ev)], sig=A)
    vnr = p.emit("vector", (lambda: nc.vector.tensor_sub(
        nrp_tm.ap()[:, :, :], lndt_tm.ap()[:, :, :], r_tm.ap()[:, :, :])),
        waits=[(A, p.cnt[A])], sig=V)
    DTM_A = p.cnt[A]
    DTM_V = p.cnt[V]

    XST_FREE = [None, None]
    STST_FREE = None
    hT_tail = hz
    xw_free = None
    stspill = {}
    for c in range(8):
        xb2 = c % 2
        wxs = list(XST_FREE[xb2]) if XST_FREE[xb2] else []
        xrsem = SQ[4 + 5 * xb2]
        cx = p.dma("sync", xstage.ap()[:, xb2, :],
                   xfm_sp[0:4096, c * 128:(c + 1) * 128],
                   waits=wxs + XSPILL_HALF, sig=xrsem, transpose=True)
        tb = p.emit("tensor", (lambda c=c: nc.tensor.transpose(
            PT[0].ap()[:, 0:128], b_fm.ap()[:, c * 128:(c + 1) * 128],
            idb_s.ap()[:, :])),
            waits=[BC_OK] + ([PT_FREE[0]] if PT_FREE[0] else []), sig=T)
        evb = p.emit("vector", lambda: nc.vector.tensor_copy(
            btm.ap()[:, :], PT[0].ap()[:, 0:128]), waits=[(T, tb)], sig=V)
        PT_FREE[0] = (V, evb)
        p.vspace(nc)
        v1 = p.emit("vector", (lambda c=c: nc.vector.tensor_add(
            wtm2.ap()[:, :], nrp_tm.ap()[:, c, :],
            rl_cols.ap()[:, c * 64:(c + 1) * 64])),
            waits=[(V, DTM_V), (A, DTM_A)], sig=V)
        a1 = p.emit("scalar", lambda: nc.scalar.activation(
            wtm.ap()[:, :], wtm2.ap()[:, :], AF.Exp), waits=[(V, v1)], sig=A)
        for g in range(8):
            vxw = None
            for i in range(8):
                h = g * 8 + i
                vxw = p.emit("vector", (lambda h=h, i=i, xb2=xb2:
                            nc.vector.tensor_single_scalar(
                                xw_sb.ap()[:, i * 64:(i + 1) * 64],
                                xstage.ap()[:, xb2, h * 64:(h + 1) * 64],
                                wtm.ap()[:, h:h + 1], ALU.mult)),
                            waits=(([(xrsem, cx), (A, a1)] +
                                    ([xw_free] if xw_free else [])) if i == 0 else []),
                            sig=V)
            pp = 2 + g % 2
            mm = p.emit("tensor", (lambda pp=pp: nc.tensor.matmul(
                PB[pp].ap()[:, :], btm.ap()[:, :], xw_sb.ap()[:, :],
                start=True, stop=True)),
                waits=[(V, vxw), (V, evb)] + bank_wait(pp), sig=T)
            xw_free = (T, p.cnt[T])
            for i in range(8):
                h = g * 8 + i
                p.emit("vector", (lambda h=h, i=i, c=c, pp=pp:
                       nc.vector.scalar_tensor_tensor(
                           hT.ap()[:, h * 64:(h + 1) * 64],
                           hT.ap()[:, h * 64:(h + 1) * 64],
                           dcol.ap()[:, c * 64 + h:c * 64 + h + 1],
                           PB[pp].ap()[:, i * 64:(i + 1) * 64],
                           ALU.mult, ALU.add)),
                       waits=([(T, mm), (A, adc)] if i == 0 else []), sig=V)
            wst = [STST_FREE] if STST_FREE else []
            vcp = p.emit("vector", (lambda g=g, pp=pp: nc.vector.tensor_copy(
                stst.ap()[:, g * 512:(g + 1) * 512], PB[pp].ap()[:, :])),
                waits=wst, sig=V)
            bank_free[pp] = (V, vcp)
        sd = p.dma("sync", st_sp[c, :, :], stst.ap()[:, :],
                   waits=[(V, p.cnt[V])], sig=SQ[10])
        STST_FREE = (SQ[10], sd)
        stspill[c] = sd
        XST_FREE[xb2] = [(V, p.cnt[V])]
        hT_tail = p.cnt[V]
    STSP_OK = (SQ[10], p.cnt[SQ[10]])

    cs1 = p.dma("gpsimd", cc_st_in[:, :], hT.ap()[:, :],
                waits=[(V, hT_tail)], sig=GQ[2])
    vld = p.emit("vector", lambda: nc.vector.reduce_sum(
        smalls.ap()[0:64, 52:53],
        r_fm.ap()[0:64, :].rearrange("h (c j) -> h c j", j=128)[:, :, 127],
        axis=mybir.AxisListType.X), waits=[(V, p.cnt[V])], sig=V)
    cs2 = p.dma("gpsimd", cc_ld_in[:, :], smalls.ap()[0:64, 52:53],
                waits=[(V, vld)], sig=GQ[2])
    agst = p.emit("gpsimd", lambda: nc.gpsimd.collective_compute(
        "AllGather", ALU.bypass, replica_groups=[[0, 1, 2, 3], [4, 5, 6, 7]],
        ins=[cc_st_in.opt()], outs=[cc_st_out.opt()]),
        waits=[(GQ[2], cs1)], sig=CC)
    agld = p.emit("gpsimd", lambda: nc.gpsimd.collective_compute(
        "AllGather", ALU.bypass, replica_groups=[[0, 1, 2, 3], [4, 5, 6, 7]],
        ins=[cc_ld_in.opt()], outs=[cc_ld_out.opt()]),
        waits=[(GQ[2], cs2)], sig=CC)
    lsel = p.dma("sync", wq.ap()[:, :],
                 cc_ld_out[:, :].rearrange("(p h) o -> p (h o)", p=4),
                 waits=[(CC, agld)], sig=SQ[0])
    lsel = p.dma("sync", smalls.ap()[0:4, 32:35], selm[:, :], sig=SQ[0])
    lsel = p.dma("sync", smalls.ap()[0:3, 35:36], selv[:, :], sig=SQ[0])
    mm = p.emit("tensor", lambda: nc.tensor.matmul(
        PB[4].ap()[0:3, 0:64], smalls.ap()[0:4, 32:35], wq.ap()[:, :],
        start=True, stop=True),
        waits=[(SQ[0], lsel)] + bank_wait(4), sig=T)
    ae = p.emit("scalar", lambda: nc.scalar.activation(
        wq2.ap()[:, :], PB[4].ap()[0:3, 0:64], AF.Exp), waits=[(T, mm)], sig=A)
    bank_free[4] = (A, ae)
    vmu = p.emit("vector", lambda: nc.vector.tensor_single_scalar(
        wq2.ap()[:, :], wq2.ap()[:, :], smalls.ap()[0:3, 35:36], ALU.mult),
        waits=[(A, ae)], sig=V)
    bw1 = p.dma("gpsimd", wrow_b[0:1, 0:192].rearrange("o (q h) -> (o q) h", q=3),
                wq2.ap()[:, :], waits=[(V, vmu)], sig=GQ[2])
    bw2 = p.dma("gpsimd", rowb_s.ap()[0:1, 0:192], wrow_b[0:1, 0:192],
                waits=[(GQ[2], bw1)], sig=GQ[2])
    mm = p.emit("tensor", lambda: nc.tensor.matmul(
        PB[4].ap()[:, 0:192], onesr_s.ap()[0:1, 0:128], rowb_s.ap()[0:1, 0:192],
        start=True, stop=True), waits=[(GQ[2], bw2)] + bank_wait(4), sig=T)
    vwc = p.emit("vector", lambda: nc.vector.tensor_copy(
        wcols.ap()[:, 0:192], PB[4].ap()[:, 0:192]), waits=[(T, mm)], sig=V)
    bank_free[4] = (V, vwc)
    hz2 = p.emit("vector", lambda: nc.vector.memset(hT.ap()[:, :], 0.0),
                 waits=[(GQ[2], cs1)], sig=V)
    vh = hz2
    for pidx in range(3):
        gsem = SQ[4 + 5 * (pidx % 2)]
        cg = p.dma("sync", xstage.ap()[:, pidx % 2, :],
                   cc_st_out[pidx * 128:(pidx + 1) * 128, :],
                   waits=[(CC, agst), (V, vh)], sig=gsem)
        for h in range(NH):
            vh = p.emit("vector", (lambda pidx=pidx, h=h: nc.vector.scalar_tensor_tensor(
                hT.ap()[:, h * 64:(h + 1) * 64],
                xstage.ap()[:, pidx % 2, h * 64:(h + 1) * 64],
                wcols.ap()[:, pidx * 64 + h:pidx * 64 + h + 1],
                hT.ap()[:, h * 64:(h + 1) * 64],
                ALU.mult, ALU.add)),
                waits=([(gsem, cg), (V, vwc)] if h == 0 else []), sig=V)
    HIN_OK = vh
    hin_sp = internal("hin_sp", [128, NH * HP], BF16)
    p.dma("gpsimd", hin_sp[:, :], hT.ap()[:, :], waits=[(V, HIN_OK)], sig=GQ[3])

    attn_sp = internal("attn_sp", [Q, DM], BF16)
    y_dbg = internal("y_dbg", [128, DI], BF16)
    sc_dbg = internal("sc_dbg", [128, 4096], BF16)
    cb_dbg = internal("cb_dbg", [128, 128], BF16)

    ygfm = reg("ygfm", [128, 32, Q], BF16, 0)
    segL = reg("segL", [2, 2, 1024], F32, 65536)
    segR = reg("segR", [2, 2, 1024], F32, 73728)
    xst3 = reg("xst3", [128, 2, 4096], BF16, 81920)
    zst3 = reg("zst3", [128, 4096], BF16, 98304)
    stst3 = reg("stst3", [128, 4096], BF16, 106496)
    ysb = reg("ysb", [128, 4096], BF16, 114688)
    scor = reg("scor", [128, 8, 128], BF16, 122880)
    segexp = reg("segexp", [128, 2, 512], BF16, 124928)
    cbm = reg("cbm", [128, 128], BF16, 126976)
    ytmp = reg("ytmp", [128, 512], BF16, 127232)
    ytmp2 = reg("ytmp2", [128, 512], BF16, 128256)

    BAR12 = [(V, p.cnt[V]), (A, p.cnt[A]), (T, p.cnt[T]), (CC, p.cnt[CC])]
    for _e in ENGS:
        p.wait(_e, BAR12)

    oi = p.dma("gpsimd", segL.ap()[1:2, 0, :], ones_big[0:1, 0:1024], sig=GQ[0])
    oi = p.dma("gpsimd", segL.ap()[1:2, 1, :], ones_big[0:1, 0:1024], sig=GQ[0])
    oi = p.dma("gpsimd", segR.ap()[0:1, 0, :], ones_big[0:1, 0:1024], sig=GQ[0])
    oi = p.dma("gpsimd", segR.ap()[0:1, 1, :], ones_big[0:1, 0:1024], sig=GQ[0])
    SEG_ONES = oi

    XST3_FREE = [None, None]
    SEG_FREE = [None, None]
    SEGEXP_FREE = [None, None]
    ZST3_FREE = None
    STST3_FREE = None
    YSB_FREE = None
    hT_read = None
    hT_upd = (V, HIN_OK)
    SCOR_FREE = None
    YTMP_FREE = None
    YTMP2_FREE = None
    ygfm_marks = []
    yMM_SEM_Q = [SQ[9], SQ[15]]

    for c in range(NCH if PH_LEVEL >= 3 else 0):
        xb = c % 2
        if c >= PH3_CHUNKS:
            break
        wx = list(XST3_FREE[xb]) if XST3_FREE[xb] else []
        cx3 = p.dma("sync", xst3.ap()[:, xb, :], xfm_sp[0:4096, c * 128:(c + 1) * 128],
                    waits=wx, sig=SQ[4 + 5 * xb], transpose=True)
        xsem = SQ[4 + 5 * xb]
        wz = [ZST3_FREE] if ZST3_FREE else []
        cz = p.dma("sync", zst3.ap()[:, :], z_sp[0:4096, c * 128:(c + 1) * 128],
                   waits=wz + (Z_OK if c == 0 else []), sig=SQ[1],
                   transpose=True)
        wst = [STST3_FREE] if STST3_FREE else []
        cst = p.dma("sync", stst3.ap()[:, :], st_sp[c, :, :],
                    waits=wst + ([STSP_OK] if c == 0 else []), sig=SQ[6])
        mmcb = p.emit("tensor", (lambda c=c: nc.tensor.matmul(
            PB[4].ap()[:, 0:128], b_fm.ap()[:, c * 128:(c + 1) * 128],
            c_fm.ap()[:, c * 128:(c + 1) * 128], start=True, stop=True)),
            waits=bank_wait(4), sig=T)
        vcb = p.emit("vector", lambda: nc.vector.tensor_mul(
            cbm.ap()[:, :], PB[4].ap()[:, 0:128], maskT_s.ap()[:, :]),
            waits=[(T, mmcb)], sig=V)
        bank_free[4] = (V, vcb)
        seg_ld = {}
        for hg in range(8):
            e = hg
            sb = e % 2
            wsg = list(SEG_FREE[sb]) if SEG_FREE[sb] else []
            if c == 0 and e < 2:
                wsg.append((GQ[0], SEG_ONES))
            g1 = p.dma("gpsimd", segL.ap()[0:1, sb, :],
                       fl_negrp[0:1, (c * 64 + e * 8) * 128:(c * 64 + e * 8 + 8) * 128],
                       waits=wsg + [(GQ[2], FLATS_OK)], sig=GQ[0])
            g2 = p.dma("gpsimd", segR.ap()[1:2, sb, :],
                       fl_r[0:1, (c * 64 + e * 8) * 128:(c * 64 + e * 8 + 8) * 128],
                       waits=list(SEG_FREE[sb]) if SEG_FREE[sb] else [], sig=GQ[1])
            seg_ld[e] = [(GQ[0], g1), (GQ[1], g2)]
            segmm = {}
            for q4 in range(2):
                pp = (hg * 2 + q4) % 2
                mm = None
                for i in range(4):
                    hh = q4 * 4 + i
                    mm = p.emit("tensor", (lambda sb=sb, hh=hh, i=i, pp=pp:
                                nc.tensor.matmul(
                                    PB[pp].ap()[:, i * 128:(i + 1) * 128],
                                    segL.ap()[0:2, sb, hh * 128:(hh + 1) * 128],
                                    segR.ap()[0:2, sb, hh * 128:(hh + 1) * 128],
                                    start=(i == 0), stop=(i == 3))),
                                waits=(seg_ld[e] + bank_wait(pp)) if i == 0 else [],
                                sig=(T if i == 3 else None))
                eb = (hg * 2 + q4) % 2
                wse = [SEGEXP_FREE[eb]] if SEGEXP_FREE[eb] else []
                vmn = p.emit("vector", (lambda eb=eb, pp=pp: nc.vector.tensor_scalar_min(
                    segexp.ap()[:, eb, :], PB[pp].ap()[:, :], 30.0)),
                    waits=[(T, mm)] + wse, sig=V)
                bank_free[pp] = (V, vmn)
                aex = p.emit("scalar", (lambda eb=eb: nc.scalar.activation(
                    segexp.ap()[:, eb, :], segexp.ap()[:, eb, :], AF.Exp)),
                    waits=[(V, vmn)], sig=A)
                vsc = None
                for i in range(4):
                    hh = q4 * 4 + i
                    slot = q4 * 4 + i
                    vsc = p.emit("vector", (lambda eb=eb, i=i, slot=slot: nc.vector.tensor_mul(
                        scor.ap()[:, slot, :], segexp.ap()[:, eb, i * 128:(i + 1) * 128],
                        cbm.ap()[:, :])),
                        waits=([(A, aex), (V, vcb)] +
                               ([SCOR_FREE] if SCOR_FREE else [])) if i == 0 else [],
                        sig=V)
                SEGEXP_FREE[eb] = (V, vsc)
                segmm[q4] = vsc
            mmd = None
            for i in range(8):
                h = hg * 8 + i
                slot = i
                mmd = p.emit("tensor", (lambda h=h, i=i, slot=slot, xb=xb:
                            nc.tensor.matmul(
                                PB[2].ap()[:, i * 64:(i + 1) * 64],
                                scor.ap()[:, slot, :],
                                xst3.ap()[:, xb, h * 64:(h + 1) * 64],
                                start=(i == 0), stop=(i == 7))),
                            waits=([(V, segmm[1]), (xsem, cx3)] + bank_wait(2))
                                  if i == 0 else [],
                            sig=(T if i == 7 else None))
            mmo = None
            for i in range(8):
                h = hg * 8 + i
                mmo = p.emit("tensor", (lambda h=h, i=i, c=c: nc.tensor.matmul(
                            PB[3].ap()[:, i * 64:(i + 1) * 64],
                            c_fm.ap()[:, c * 128:(c + 1) * 128],
                            hT.ap()[:, h * 64:(h + 1) * 64],
                            start=(i == 0), stop=(i == 7))),
                            waits=(list(hT_upd if isinstance(hT_upd, list) else [hT_upd])
                                   + bank_wait(3)) if i == 0 else [],
                            sig=(T if i == 7 else None))
            hT_read = (T, mmo)
            aev = None
            for i in range(8):
                h = hg * 8 + i
                aev = p.emit("scalar", (lambda h=h, i=i, c=c: nc.scalar.activation(
                    ytmp.ap()[:, i * 64:(i + 1) * 64], PB[3].ap()[:, i * 64:(i + 1) * 64],
                    AF.Copy, scale=expf_tm.ap()[:, c, h:h + 1])),
                    waits=([(T, mmo)] + ([YTMP_FREE] if YTMP_FREE else []))
                          if i == 0 else [], sig=A)
            wyb = [YSB_FREE] if (YSB_FREE and hg == 0) else []
            v2 = p.emit("vector", (lambda hg=hg, xb=xb: nc.vector.tensor_mul(
                ytmp2.ap()[:, :], xst3.ap()[:, xb, hg * 512:(hg + 1) * 512],
                dpb.ap()[:, hg * 512:(hg + 1) * 512])),
                waits=[(V, DPB_OK)] + ([YTMP2_FREE] if YTMP2_FREE else []), sig=V)
            v1 = p.emit("vector", (lambda hg=hg: nc.vector.tensor_add(
                ysb.ap()[:, hg * 512:(hg + 1) * 512], PB[2].ap()[:, :], ytmp.ap()[:, :])),
                waits=[(A, aev), (T, mmd)] + wyb, sig=V)
            bank_free[2] = (V, v1)
            bank_free[3] = (A, aev)
            YTMP_FREE = (V, v1)
            v3 = p.emit("vector", (lambda hg=hg: nc.vector.tensor_add(
                ysb.ap()[:, hg * 512:(hg + 1) * 512],
                ysb.ap()[:, hg * 512:(hg + 1) * 512], ytmp2.ap()[:, :])), sig=V)
            YTMP2_FREE = (V, v3)
            SEG_FREE[sb] = [(T, mmd)]
            SCOR_FREE = (T, mmd)
        if DEBUG_PH3 and c == 7:
            tb9 = p.dma("gpsimd", y_dbg[:, :], hT.ap()[:, :],
                        waits=list(hT_upd if isinstance(hT_upd, list) else [hT_upd]),
                        sig=GQ[3])
            for _e in ENGS:
                p.wait(_e, [(GQ[3], tb9)])
        vrec = None
        for h in range(NH):
            vrec = p.emit("vector", (lambda h=h, c=c: nc.vector.scalar_tensor_tensor(
                hT.ap()[:, h * 64:(h + 1) * 64], hT.ap()[:, h * 64:(h + 1) * 64],
                dcol.ap()[:, c * 64 + h:c * 64 + h + 1],
                stst3.ap()[:, h * 64:(h + 1) * 64], ALU.mult, ALU.add)),
                waits=[hT_read, (SQ[6], cst)] if h == 0 else [], sig=V)
        hT_upd = (V, vrec)
        STST3_FREE = (V, vrec)
        asz = p.emit("scalar", (lambda: nc.scalar.activation(
            zst3.ap()[:, :], zst3.ap()[:, :], AF.Silu)),
            waits=[(SQ[1], cz)], sig=A)
        if DEBUG_PH3 and c == 0:
            tbz = p.dma("gpsimd", cb_dbg[0:128, 0:128], zst3.ap()[:, 0:128],
                        waits=[(A, asz)], sig=GQ[3])
            for _e in ENGS:
                p.wait(_e, [(GQ[3], tbz)])
        vg = p.emit("vector", lambda: nc.vector.tensor_mul(
            ysb.ap()[:, :], ysb.ap()[:, :], zst3.ap()[:, :]),
            waits=[(A, asz), (V, p.cnt[V])], sig=V)
        p.emit("scalar", (lambda: nc.scalar.activation(
            zst3.ap()[:, :], ysb.ap()[:, :], AF.Square,
            accum_out=smalls.ap()[:, 60:61])), waits=[(V, vg)])
        asq2 = p.emit("scalar", lambda: nc.scalar.copy(
            smalls.ap()[:, 61:62], smalls.ap()[:, 60:61]), sig=A)
        ZST3_FREE = (A, asq2)
        ast2 = p.emit("scalar", lambda: nc.scalar.activation(
            smalls.ap()[:, 62:63], smalls.ap()[:, 60:61], AF.Sqrt,
            bias=smalls.ap()[:, 54:55], scale=smalls.ap()[:, 55:56]),
            waits=[(A, asq2)], sig=A)
        vrc = p.emit("vector", lambda: nc.vector.reciprocal(
            smalls.ap()[:, 63:64], smalls.ap()[:, 62:63]), waits=[(A, ast2)], sig=V)
        if DEBUG_PH3 and c == 0:
            tbm = p.dma("gpsimd", sm_dbg[:, :], smalls.ap()[:, :],
                        waits=[(V, vrc)], sig=GQ[3])
            tbm = p.dma("gpsimd", u_dbg[0:128, 0:2048],
                        ysb.ap()[:, 0:2048], waits=[(V, vg)], sig=GQ[3])
            for _e in ENGS:
                p.wait(_e, [(GQ[3], tbm)])
        p.vspace(nc)
        vsc2 = p.emit("vector", lambda: nc.vector.tensor_single_scalar(
            ysb.ap()[:, :], ysb.ap()[:, :], smalls.ap()[:, 63:64], ALU.mult),
            waits=[(V, vrc)], sig=V)
        if DEBUG_PH3 and c == 0:
            tb8 = p.dma("gpsimd", sc_dbg[0:128, 0:4096], ysb.ap()[:, :],
                        waits=[(V, vsc2)], sig=GQ[3])
            for _e in ENGS:
                p.wait(_e, [(GQ[3], tb8)])
        for kt in range(32):
            ptb = kt % 2
            wpt = [PT_FREE[ptb]] if PT_FREE[ptb] else []
            tm = p.emit("tensor", (lambda kt=kt, ptb=ptb: nc.tensor.transpose(
                PT[ptb].ap()[:, 0:128], ysb.ap()[:, kt * 128:(kt + 1) * 128],
                idb_s.ap()[:, :])), waits=[(V, vsc2)] + wpt, sig=T)
            if kt % 2 == 0:
                ev = p.emit("vector", (lambda kt=kt, c=c, ptb=ptb: nc.vector.tensor_copy(
                    ygfm.ap()[:, kt, c * 128:(c + 1) * 128], PT[ptb].ap()[:, 0:128])),
                    waits=[(T, tm)], sig=V)
                PT_FREE[ptb] = (V, ev)
            else:
                ev = p.emit("scalar", (lambda kt=kt, c=c, ptb=ptb: nc.scalar.copy(
                    ygfm.ap()[:, kt, c * 128:(c + 1) * 128], PT[ptb].ap()[:, 0:128])),
                    waits=[(T, tm)], sig=A)
                PT_FREE[ptb] = (A, ev)
        YSB_FREE = (T, p.cnt[T])
        XST3_FREE[xb] = [(T, p.cnt[T]), (V, p.cnt[V])]
    YGFM_OK = [(V, p.cnt[V]), (A, p.cnt[A])]
    ygfm_dbg = internal("ygfm_dbg", [128, 32 * Q], BF16)
    woc_dbg = internal("woc_dbg", [128, 32 * 512], BF16)
    if DEBUG_PH3:
        tbg = p.dma("gpsimd", ygfm_dbg[:, :],
                    ygfm.ap()[:, :, :].rearrange("p a b -> p (a b)"),
                    waits=YGFM_OK, sig=GQ[3])
        for _e in ENGS:
            p.wait(_e, [(GQ[3], tbg)])

    PH3B_ON = PH_LEVEL >= 4 and PH3_CHUNKS >= 8
    woc = reg("woc", [128, 32, 512], BF16, 98304)
    atst = reg("atst", [128, 2, 512], BF16, 131072)
    BAR3B = [(V, p.cnt[V]), (A, p.cnt[A]), (T, p.cnt[T])]
    for _e in ENGS:
        p.wait(_e, BAR3B)
    ATST_FREE = [None, None]
    WOSEMS = [SQ[2], SQ[7], SQ[11], SQ[12]]
    at_done = None
    for dc in range(4 if PH3B_ON else 0):
        lw = []
        for k in range(32):
            wsem = WOSEMS[k % 4]
            wwk = [(T, p.cnt[T])] if (dc > 0 and k < 4) else []
            lwv = p.dma("sync", woc.ap()[:, k, :],
                        wout_bf[k * 128:(k + 1) * 128, dc * 512:(dc + 1) * 512],
                        waits=wwk + [(CC, AG_WOUT)], sig=wsem)
            lw.append((wsem, lwv))
        lw = [(sm, max(vv for sm2, vv in lw if sm2 == sm)) for sm in set(s0 for s0, _ in lw)]
        if DEBUG_PH3 and dc == 0:
            tbw = p.dma("gpsimd", woc_dbg[:, :],
                        woc.ap()[:, :, :].rearrange("p a b -> p (a b)"),
                        waits=lw, sig=GQ[3])
            for _e in ENGS:
                p.wait(_e, [(GQ[3], tbw)])
        for tt in range(8):
            pp = tt % 2
            mm = None
            for k in range(32):
                mm = p.emit("tensor", (lambda tt=tt, k=k, pp=pp: nc.tensor.matmul(
                    PB[pp].ap()[:, :], ygfm.ap()[:, k, tt * 128:(tt + 1) * 128],
                    woc.ap()[:, k, :], start=(k == 0), stop=(k == 31))),
                    waits=(lw + YGFM_OK + bank_wait(pp)) if k == 0 else [],
                    sig=(T if k == 31 else None))
            ab = tt % 2
            wat = [ATST_FREE[ab]] if ATST_FREE[ab] else []
            ve = p.emit("vector", (lambda ab=ab, pp=pp: nc.vector.tensor_copy(
                atst.ap()[:, ab, :], PB[pp].ap()[:, :])),
                waits=[(T, mm)] + wat, sig=V)
            bank_free[pp] = (V, ve)
            ad = p.dma("sync", attn_sp[tt * 128:(tt + 1) * 128, dc * 512:(dc + 1) * 512],
                       atst.ap()[:, ab, :], waits=[(V, ve)], sig=SQ[3 + 5 * ab])
            ATST_FREE[ab] = (SQ[3 + 5 * ab], ad)
            at_done = (SQ[3 + 5 * ab], ad)
    AT_ALL = [(SQ[3], p.cnt[SQ[3]]), (SQ[8], p.cnt[SQ[8]])]

    ast3 = reg("ast3", [128, 2048], BF16, 98304)
    ust2 = reg("ust2", [128, 2048], F32, 102400)
    rsb = reg("rsb", [128, 2048], F32, 110592)
    x3tm = reg("x3tm", [128, 2048], BF16, 118784)
    x3fm = reg("x3fm", [128, 16, Q], BF16, 65536)
    BAR3C = [(V, p.cnt[V]), (A, p.cnt[A]), (T, p.cnt[T]),
             (SQ[3], p.cnt[SQ[3]]), (SQ[8], p.cnt[SQ[8]])]
    for _e in ENGS:
        p.wait(_e, BAR3C)
    RS_FREE = None
    for tt in range(8 if PH3B_ON else 0):
        ca = p.dma("sync", ast3.ap()[:, :], attn_sp[tt * 128:(tt + 1) * 128, :],
                   waits=AT_ALL, sig=SQ[1])
        cu2 = p.dma("sync", ust2.ap()[:, :], u_ext[3 + tt * 128:3 + (tt + 1) * 128, :],
                    sig=SQ[6])

        def lnstats(src_ap, wsrc, bb, width=2048):
            vs = p.emit("vector", (lambda: nc.vector.reduce_sum(
                smalls.ap()[:, bb + 0:bb + 1], src_ap, axis=mybir.AxisListType.X)),
                waits=wsrc, sig=V)
            p.emit("scalar", (lambda: nc.scalar.activation(
                x3tm.ap()[:, :], src_ap, AF.Square,
                accum_out=smalls.ap()[:, bb + 1:bb + 2])), waits=wsrc)
            af = p.emit("scalar", (lambda: nc.scalar.copy(
                smalls.ap()[:, bb + 7:bb + 8], smalls.ap()[:, bb + 1:bb + 2])), sig=A)
            p.vspace(nc)
            v1 = p.emit("vector", (lambda: nc.vector.tensor_scalar_mul(
                smalls.ap()[:, bb + 2:bb + 3], smalls.ap()[:, bb + 0:bb + 1],
                1.0 / width)), waits=[(V, vs)], sig=V)
            v2 = p.emit("vector", (lambda: nc.vector.tensor_scalar_mul(
                smalls.ap()[:, bb + 3:bb + 4], smalls.ap()[:, bb + 1:bb + 2],
                1.0 / width)), waits=[(A, af)], sig=V)
            v3 = p.emit("vector", (lambda: nc.vector.tensor_single_scalar(
                smalls.ap()[:, bb + 4:bb + 5], smalls.ap()[:, bb + 2:bb + 3],
                smalls.ap()[:, bb + 2:bb + 3], ALU.mult)), sig=V)
            p.vspace(nc)
            v4 = p.emit("vector", (lambda: nc.vector.tensor_single_scalar(
                smalls.ap()[:, bb + 3:bb + 4], smalls.ap()[:, bb + 3:bb + 4],
                smalls.ap()[:, bb + 4:bb + 5], ALU.subtract)), sig=V)
            a2 = p.emit("scalar", (lambda: nc.scalar.activation(
                smalls.ap()[:, bb + 5:bb + 6], smalls.ap()[:, bb + 3:bb + 4],
                AF.Sqrt, bias=smalls.ap()[:, 54:55])), waits=[(V, v4)], sig=A)
            v5 = p.emit("vector", (lambda: nc.vector.reciprocal(
                smalls.ap()[:, bb + 6:bb + 7], smalls.ap()[:, bb + 5:bb + 6])),
                waits=[(A, a2)], sig=V)
            p.vspace(nc)
            return v5

        v5 = lnstats(ast3.ap()[:, :], [(SQ[1], ca)], 0)
        vno = p.emit("vector", (lambda: nc.vector.tensor_scalar(
            rsb.ap()[:, :], ast3.ap()[:, :], smalls.ap()[:, 2:3],
            smalls.ap()[:, 6:7], ALU.subtract, ALU.mult)),
            waits=[(V, v5)] + ([RS_FREE] if RS_FREE else []), sig=V)
        vra = p.emit("vector", (lambda: nc.vector.tensor_add(
            rsb.ap()[:, :], rsb.ap()[:, :], ust2.ap()[:, :])),
            waits=[(SQ[6], cu2)], sig=V)
        rd = p.dma("sync", resid_sp[tt * 128:(tt + 1) * 128, :], rsb.ap()[:, :],
                   waits=[(V, vra)], sig=SQ[10])
        v5b = lnstats(rsb.ap()[:, :], [(V, vra)], 16)
        vn3 = p.emit("vector", (lambda: nc.vector.tensor_scalar(
            x3tm.ap()[:, :], rsb.ap()[:, :], smalls.ap()[:, 18:19],
            smalls.ap()[:, 22:23], ALU.subtract, ALU.mult)),
            waits=[(V, v5b)], sig=V)
        RS_FREE = (SQ[10], rd)
        for k in range(16):
            ptb = k % 2
            wpt = [PT_FREE[ptb]] if PT_FREE[ptb] else []
            tm = p.emit("tensor", (lambda k=k, ptb=ptb: nc.tensor.transpose(
                PT[ptb].ap()[:, 0:128], x3tm.ap()[:, k * 128:(k + 1) * 128],
                idb_s.ap()[:, :])), waits=[(V, vn3)] + wpt, sig=T)
            if k % 2 == 0:
                ev = p.emit("vector", (lambda k=k, tt=tt, ptb=ptb: nc.vector.tensor_copy(
                    x3fm.ap()[:, k, tt * 128:(tt + 1) * 128], PT[ptb].ap()[:, 0:128])),
                    waits=[(T, tm)], sig=V)
                PT_FREE[ptb] = (V, ev)
            else:
                ev = p.emit("scalar", (lambda k=k, tt=tt, ptb=ptb: nc.scalar.copy(
                    x3fm.ap()[:, k, tt * 128:(tt + 1) * 128], PT[ptb].ap()[:, 0:128])),
                    waits=[(T, tm)], sig=A)
                PT_FREE[ptb] = (A, ev)
    X3FM_OK = [(V, p.cnt[V]), (A, p.cnt[A])]
    RESID_OK = [(SQ[10], p.cnt[SQ[10]])]

    h1 = reg("h1", [128, 64, 512], BF16, 0)
    w1c = reg("w1c", [128, 16, 512], BF16, 98304)
    w3c = reg("w3c", [128, 16, 512], BF16, 114688)
    sg4 = reg("sg4", [128, 512], BF16, 131072)
    hid4 = reg("hid4", [128, 512], BF16, 132096)
    w2c = reg("w2c", [128, 2, 512], BF16, 133120)
    ffnst = reg("ffnst", [128, 4, 4, 512], BF16, 135168)
    rsb2 = reg("rsb2", [128, 2048], F32, 98304)
    ostage = reg("ostage", [128, 2, 2048], F32, 106496)
    BAR4 = [(V, p.cnt[V]), (A, p.cnt[A]), (T, p.cnt[T])]
    for _e in ENGS:
        p.wait(_e, BAR4)
    W1SEMS = [SQ[2], SQ[7]]
    W3SEMS = [SQ[11], SQ[12]]
    W2SEMS = [SQ[4], SQ[9]]
    OUT_SEMS = [SQ[3], SQ[8]]
    H1_FREE = None
    FF_FREE = None
    out_last = []
    for th in range(2 if PH_LEVEL >= 5 and PH3B_ON else 0):
        if th > 0:
            p.wait("sync", [(OUT_SEMS[0], p.cnt[OUT_SEMS[0]]),
                            (OUT_SEMS[1], p.cnt[OUT_SEMS[1]])])
        for ug in range(16):
            lw1 = []
            lw3 = []
            for k in range(16):
                wwk = [(T, p.cnt[T])] if (k < 2 and (ug > 0 or th > 0)) else []
                l1 = p.dma("sync", w1c.ap()[:, k, :],
                           w1_bf[k * 128:(k + 1) * 128, ug * 512:(ug + 1) * 512],
                           waits=wwk + [(CC, AG_W1)], sig=W1SEMS[k % 2])
                lw1.append((W1SEMS[k % 2], l1))
                l3 = p.dma("sync", w3c.ap()[:, k, :],
                           w3_bf[k * 128:(k + 1) * 128, ug * 512:(ug + 1) * 512],
                           waits=wwk + [(CC, AG_W3)], sig=W3SEMS[k % 2])
                lw3.append((W3SEMS[k % 2], l3))
            lw1 = [(sm, max(v for s2, v in lw1 if s2 == sm)) for sm in set(s for s, _ in lw1)]
            lw3 = [(sm, max(v for s2, v in lw3 if s2 == sm)) for sm in set(s for s, _ in lw3)]
            for ut in range(4):
                u = ug * 4 + ut
                mm1 = None
                for k in range(16):
                    mm1 = p.emit("tensor", (lambda ut=ut, k=k, th=th: nc.tensor.matmul(
                        PB[0].ap()[:, :], w1c.ap()[:, k, ut * 128:(ut + 1) * 128],
                        x3fm.ap()[:, k, th * 512:(th + 1) * 512],
                        start=(k == 0), stop=(k == 15))),
                        waits=(lw1 + X3FM_OK + bank_wait(0)) if k == 0 else [],
                        sig=(T if k == 15 else None))
                mm3 = None
                for k in range(16):
                    mm3 = p.emit("tensor", (lambda ut=ut, k=k, th=th: nc.tensor.matmul(
                        PB[1].ap()[:, :], w3c.ap()[:, k, ut * 128:(ut + 1) * 128],
                        x3fm.ap()[:, k, th * 512:(th + 1) * 512],
                        start=(k == 0), stop=(k == 15))),
                        waits=(lw3 + bank_wait(1)) if k == 0 else [],
                        sig=(T if k == 15 else None))
                ags = p.emit("scalar", (lambda: nc.scalar.activation(
                    sg4.ap()[:, :], PB[0].ap()[:, :], AF.Silu)),
                    waits=[(T, mm1), (A, p.cnt[A])], sig=A)
                bank_free[0] = (A, ags)
                vhd = p.emit("vector", (lambda: nc.vector.tensor_mul(
                    hid4.ap()[:, :], sg4.ap()[:, :], PB[1].ap()[:, :])),
                    waits=[(A, ags), (T, mm3), (V, p.cnt[V])], sig=V)
                bank_free[1] = (V, vhd)
                mmh = p.emit("tensor", (lambda: nc.tensor.matmul(
                    PB[2].ap()[:, :], h128_s.ap()[:, :], hid4.ap()[:, :],
                    start=True, stop=True)),
                    waits=[(V, vhd)] + bank_wait(2), sig=T)
                wh1 = [H1_FREE] if (H1_FREE and th == 1 and ug == 0 and ut == 0) else []
                ev = p.emit("vector", (lambda u=u: nc.vector.tensor_copy(
                    h1.ap()[:, u, :], PB[2].ap()[:, :])),
                    waits=[(T, mmh)] + wh1, sig=V)
                bank_free[2] = (V, ev)
        H1_OK = (V, p.cnt[V])
        w2free = [None, None]
        for dc in range(4):
            for u in range(64):
                wb2 = u % 2
                ww2 = ([w2free[wb2]] if w2free[wb2] else []) + [(CC, AG_W2)]
                p.dma("sync", w2c.ap()[:, wb2, 0:256],
                      w2pp_bf[(2 * dc) * 8192 + u * 128:(2 * dc) * 8192 + (u + 1) * 128, :],
                      waits=ww2, sig=W2SEMS[wb2])
                l2 = p.dma("sync", w2c.ap()[:, wb2, 256:512],
                           w2pp_bf[(2 * dc + 1) * 8192 + u * 128:(2 * dc + 1) * 8192 + (u + 1) * 128, :],
                           sig=W2SEMS[wb2])
                mmlast = None
                for tt4 in range(4):
                    mmlast = p.emit("tensor", (lambda u=u, tt4=tt4, wb2=wb2:
                                nc.tensor.matmul(
                                    PB[tt4].ap()[:, :],
                                    h1.ap()[:, u, tt4 * 128:(tt4 + 1) * 128],
                                    w2c.ap()[:, wb2, :],
                                    start=(u == 0), stop=(u == 63))),
                                waits=[(W2SEMS[wb2], l2), H1_OK] +
                                      (bank_wait(tt4) if u == 0 else []),
                                sig=(T if (tt4 == 3) else None))
                w2free[wb2] = (T, mmlast)
            for tt4 in range(4):
                wff = [FF_FREE] if (FF_FREE and dc == 0 and tt4 == 0) else []
                ev = p.emit("vector", (lambda tt4=tt4, dc=dc: nc.vector.tensor_copy(
                    ffnst.ap()[:, tt4, dc, :], PB[tt4].ap()[:, :])),
                    waits=[(T, p.cnt[T])] + wff, sig=V)
                bank_free[tt4] = (V, ev)
        FF_OK = (V, p.cnt[V])
        for tt4 in range(4):
            tt = th * 4 + tt4
            crr = p.dma("sync", rsb2.ap()[:, :], resid_sp[tt * 128:(tt + 1) * 128, :],
                        waits=RESID_OK + [(V, p.cnt[V])], sig=SQ[1])
            src_ap = ffnst.ap()[:, tt4, :, :].rearrange("p a b -> p (a b)")
            vs = p.emit("vector", (lambda src_ap=src_ap: nc.vector.reduce_sum(
                smalls.ap()[:, 0:1], src_ap, axis=mybir.AxisListType.X)),
                waits=[FF_OK], sig=V)
            p.emit("scalar", (lambda src_ap=src_ap: nc.scalar.activation(
                sg4.ap()[:, :], src_ap[:, 0:512], AF.Square,
                accum_out=smalls.ap()[:, 70:71])), waits=[FF_OK])
            for qq in range(1, 4):
                p.emit("scalar", (lambda src_ap=src_ap, qq=qq: nc.scalar.activation(
                    sg4.ap()[:, :], src_ap[:, qq * 512:(qq + 1) * 512], AF.Square,
                    accum_out=smalls.ap()[:, 70 + qq:71 + qq])))
            af = p.emit("scalar", (lambda: nc.scalar.copy(
                smalls.ap()[:, 74:75], smalls.ap()[:, 70:71])), sig=A)
            vq = p.emit("vector", (lambda: nc.vector.tensor_single_scalar(
                smalls.ap()[:, 1:2], smalls.ap()[:, 70:71], smalls.ap()[:, 71:72],
                ALU.add)), waits=[(A, af)], sig=V)
            p.vspace(nc)
            vq = p.emit("vector", (lambda: nc.vector.tensor_single_scalar(
                smalls.ap()[:, 1:2], smalls.ap()[:, 1:2], smalls.ap()[:, 72:73],
                ALU.add)), sig=V)
            p.vspace(nc)
            vq = p.emit("vector", (lambda: nc.vector.tensor_single_scalar(
                smalls.ap()[:, 1:2], smalls.ap()[:, 1:2], smalls.ap()[:, 73:74],
                ALU.add)), sig=V)
            v1 = p.emit("vector", (lambda: nc.vector.tensor_scalar_mul(
                smalls.ap()[:, 2:3], smalls.ap()[:, 0:1], 1.0 / 2048.0)), sig=V)
            v2 = p.emit("vector", (lambda: nc.vector.tensor_scalar_mul(
                smalls.ap()[:, 3:4], smalls.ap()[:, 1:2], 1.0 / 2048.0)), sig=V)
            v3 = p.emit("vector", (lambda: nc.vector.tensor_single_scalar(
                smalls.ap()[:, 4:5], smalls.ap()[:, 2:3], smalls.ap()[:, 2:3],
                ALU.mult)), sig=V)
            p.vspace(nc)
            v4 = p.emit("vector", (lambda: nc.vector.tensor_single_scalar(
                smalls.ap()[:, 3:4], smalls.ap()[:, 3:4], smalls.ap()[:, 4:5],
                ALU.subtract)), sig=V)
            a2 = p.emit("scalar", (lambda: nc.scalar.activation(
                smalls.ap()[:, 5:6], smalls.ap()[:, 3:4], AF.Sqrt,
                bias=smalls.ap()[:, 54:55])), waits=[(V, v4)], sig=A)
            v5 = p.emit("vector", (lambda: nc.vector.reciprocal(
                smalls.ap()[:, 6:7], smalls.ap()[:, 5:6])), waits=[(A, a2)], sig=V)
            p.vspace(nc)
            ob = tt4 % 2
            vno = p.emit("vector", (lambda src_ap=src_ap, ob=ob: nc.vector.tensor_scalar(
                ostage.ap()[:, ob, :], src_ap, smalls.ap()[:, 2:3],
                smalls.ap()[:, 6:7], ALU.subtract, ALU.mult)),
                waits=[(V, v5)] + ([(OUT_SEMS[ob], out_last[-2][1])]
                                   if len(out_last) >= 2 else []), sig=V)
            voa = p.emit("vector", (lambda ob=ob: nc.vector.tensor_add(
                ostage.ap()[:, ob, :], ostage.ap()[:, ob, :], rsb2.ap()[:, :])),
                waits=[(SQ[1], crr)], sig=V)
            od = p.dma("sync", out_h[tt * 128:(tt + 1) * 128, :], ostage.ap()[:, ob, :],
                       waits=[(V, voa)], sig=OUT_SEMS[ob])
            out_last.append((OUT_SEMS[ob], od))
        FF_FREE = (V, p.cnt[V])
        H1_FREE = (T, p.cnt[T])

    tapn = []
    internal_map = dict(win_bf=win_bf, wout_bf=wout_bf, w1_bf=w1_bf, w3_bf=w3_bf,
                        w2pp_bf=w2pp_bf, z_sp=z_sp, xfm_sp=xfm_sp, st_sp=st_sp,
                        resid_sp=resid_sp, fl_negrp=fl_negrp, fl_r=fl_r,
                        cc_st_out=cc_st_out, cc_ld_out=cc_ld_out, hin_sp=hin_sp,
                        ln1_dbg=ln1_dbg, u_dbg=u_dbg, sm_dbg=sm_dbg, lt_dbg=lt_dbg,
                        attn_sp=attn_sp, y_dbg=y_dbg, sc_dbg=sc_dbg, cb_dbg=cb_dbg,
                        ygfm_dbg=ygfm_dbg, woc_dbg=woc_dbg)
    p.wait("sync", [(nm, p.cnt[nm]) for nm in list(p.cnt)])
    final = None
    for name, shape, dt in debug_taps:
        src = internal_map[name]
        dst = nc.dram_tensor("dbg_" + name, list(shape), dt, kind="ExternalOutput").ap()
        final = p.dma("sync", dst[:, :] if len(shape) == 2 else dst[:, :, :],
                      src[:, :] if len(shape) == 2 else src[:, :, :], sig=SQ[5])
        tapn.append("dbg_" + name)


    with nc.Block() as block:
        @block.sync
        def _(e):
            p.replay("sync", e)

        @block.gpsimd
        def _(e):
            p.replay("gpsimd", e)

        @block.tensor
        def _(e):
            p.replay("tensor", e)

        @block.vector
        def _(e):
            p.replay("vector", e)

        @block.scalar
        def _(e):
            p.replay("scalar", e)

    return nc, tapn



def make_in_maps(inp):
    u = np.asarray(inp["L0_attn_u"], np.float32)
    W_in = np.ascontiguousarray(np.asarray(inp["L0_attn_W_in"], np.float32))
    conv_w = np.asarray(inp["L0_attn_conv_w"], np.float32)
    conv_b = np.asarray(inp["L0_attn_conv_b"], np.float32)
    dt_bias = np.asarray(inp["L0_attn_dt_bias"], np.float32)
    A_log = np.asarray(inp["L0_attn_A_log"], np.float32)
    Dp = np.asarray(inp["L0_attn_Dp"], np.float32)
    norm_w = np.asarray(inp["L0_attn_norm_w"], np.float32)
    W_out = np.ascontiguousarray(np.asarray(inp["L0_attn_W_out"], np.float32))
    W1 = np.ascontiguousarray(np.asarray(inp["L0_ffn_W1"], np.float32))
    W2 = np.ascontiguousarray(np.asarray(inp["L0_ffn_W2"], np.float32))
    W3 = np.ascontiguousarray(np.asarray(inp["L0_ffn_W3"], np.float32))

    convw_t = np.ascontiguousarray(conv_w.reshape(34, 128, 4).transpose(1, 0, 2))
    convb_t = np.ascontiguousarray(conv_b.reshape(34, 128).T)
    dtb_row = np.zeros((1, 128), np.float32)
    dtb_row[0, :64] = dt_bias
    alog_col = np.ascontiguousarray(A_log.reshape(64, 1))
    dp_rep = np.ascontiguousarray(np.repeat(Dp, HP).reshape(1, DI))
    nw_row = np.ascontiguousarray(norm_w.reshape(1, DI))

    WinT = np.zeros((DM, OPAD), np.float32)
    WinT[:, :OPROJ] = W_in.T
    WoutT = np.ascontiguousarray(W_out.T)
    W1T = np.ascontiguousarray(W1.T)
    W3T = np.ascontiguousarray(W3.T)
    maps = []
    for c in range(8):
        b, q = divmod(c, 4)
        t0 = q * Q
        ue = np.zeros((EXT, DM), np.float32)
        lo = max(0, t0 - 3)
        ue[3 - (t0 - lo):] = u[b, lo:t0 + Q]
        selm = np.zeros((4, 3), np.float32)
        selv = np.zeros((3, 1), np.float32)
        for s in range(3):
            if s < q:
                selv[s, 0] = 1.0
                for k in range(s + 1, q):
                    selm[k, s] = 1.0
        nwt = np.ascontiguousarray(
            norm_w[512 * c:512 * (c + 1)].reshape(4, 128).T)
        maps.append(dict(
            u_ext=np.ascontiguousarray(ue),
            win_sl=np.ascontiguousarray(WinT[256 * c:256 * (c + 1)]),
            wout_sl=np.ascontiguousarray(WoutT[512 * c:512 * (c + 1)]),
            w1_sl=np.ascontiguousarray(W1T[256 * c:256 * (c + 1)]),
            w3_sl=np.ascontiguousarray(W3T[256 * c:256 * (c + 1)]),
            w2_sl=np.ascontiguousarray(W2[256 * c:256 * (c + 1)]),
            convw_t=convw_t, convb_t=convb_t, dtb_row=dtb_row,
            alog_col=alog_col, dp_rep=dp_rep, nw_row=nw_row,
            nw_t=nwt, selm=selm, selv=selv,
        ))
    return maps


def kernel(**inp):
    nc, _ = build_program()
    maps = make_in_maps(inp)
    res = run_bass_kernel_spmd(nc, maps, core_ids=list(range(8)))
    out = np.stack([res.results[c]["out"] for c in range(8)], axis=0)
    return np.ascontiguousarray(out.reshape(2, 4, Q, DM).reshape(2, L, DM))


# version 08
# speedup vs baseline: 1.0874x; 1.0028x over previous
import numpy as np
import ml_dtypes
import concourse.bass as bass
import concourse.mybir as mybir
from concourse.bass_utils import run_bass_kernel_spmd

F32 = mybir.dt.float32
BF16 = mybir.dt.bfloat16
AF = mybir.ActivationFunctionType
ALU = mybir.AluOpType
BF = ml_dtypes.bfloat16

B, L, DM = 2, 4096, 2048
DI, NH, HP, DS, DCONV = 4096, 64, 64, 128, 4
CONV = DI + 2 * DS
OPROJ = 2 * DI + 2 * DS + NH
OPAD = 8576
Q = 1024
EXT = Q + 3
CH = 128
NCH = Q // CH
EPS = 1e-5
FWHT_SCALE = 1.0 / np.sqrt(8192.0)

ENGS = ("sync", "gpsimd", "tensor", "vector", "scalar")


class Prog:

    def __init__(self, nc):
        self.nc = nc
        self.ops = {e: [] for e in ENGS}
        self.semh = {}
        self.cnt = {}
        self.waited = {e: {} for e in ENGS}

    def sem(self, name):
        self.semh[name] = self.nc.alloc_semaphore(name)
        self.cnt[name] = 0
        return name

    def emit(self, eng, fn, waits=(), sig=None, sig_n=1):
        ws = []
        for s, v in waits:
            if v is None or v <= 0:
                continue
            if v > self.waited[eng].get(s, 0):
                self.waited[eng][s] = v
                ws.append((self.semh[s], v))
        tgt = None
        ret = None
        if sig is not None:
            self.cnt[sig] += sig_n
            ret = self.cnt[sig]
            tgt = (self.semh[sig], sig_n)
        self.ops[eng].append((tuple(ws), fn, tgt))
        return ret

    def vspace(self, nc):
        self.emit("vector", lambda: nc.vector.tensor_scalar_mul(
            self._sp_ap, self._sp_ap, 1.0))

    def wait(self, eng, waits):
        for s, v in waits:
            if v is None or v <= 0:
                continue
            if v > self.waited[eng].get(s, 0):
                self.waited[eng][s] = v
                h = self.semh[s]
                self.ops[eng].append((((h, v),), None, None))

    def dma(self, q, out, in_, waits=(), sig=None, transpose=False):
        eng = {"sync": self.nc.sync, "gpsimd": self.nc.gpsimd,
               "scalar": self.nc.scalar}[q]
        if transpose:
            fn = lambda: eng.dma_start_transpose(out=out, in_=in_)
        else:
            fn = lambda: eng.dma_start(out=out, in_=in_)
        waits = list(waits)
        if sig is not None and self.cnt[sig] > 0:
            waits.append((sig, self.cnt[sig]))
        return self.emit(q, fn, waits=waits, sig=sig, sig_n=16)

    def replay(self, eng_name, eng_obj):
        for ws, fn, tgt in self.ops[eng_name]:
            for h, v in ws:
                eng_obj.wait_ge(h, v)
            if fn is not None:
                inst = fn()
                if tgt is not None:
                    inst.then_inc(tgt[0], tgt[1])


def ceil_div(a, b):
    return (a + b - 1) // b


DEBUG_LN1 = False
DEBUG_PH3 = False
PH3_CHUNKS = 8
PH_LEVEL = 5

def build_program(debug_taps=()):
    nc = bass.Bass("TRN2", target_bir_lowering=False, debug=False)

    def ein(name, shape, dt=F32):
        return nc.dram_tensor(name, shape, dt, kind="ExternalInput").ap()

    u_ext = ein("u_ext", [EXT, DM])
    win_sl = ein("win_sl", [256, OPAD])
    wout_sl = ein("wout_sl", [512, DM])
    w1_sl = ein("w1_sl", [256, 8192])
    w3_sl = ein("w3_sl", [256, 8192])
    w2_sl = ein("w2_sl", [256, 8192])
    nw_t = ein("nw_t", [128, 4])
    convw_t = ein("convw_t", [128, 34, 4])
    convb_t = ein("convb_t", [128, 34])
    dtb_row = ein("dtb_row", [1, 128])
    alog_col = ein("alog_col", [64, 1])
    dp_rep = ein("dp_rep", [1, DI])
    nw_row = ein("nw_row", [1, DI])
    selm = ein("selm", [4, 3])
    selv = ein("selv", [3, 1])

    out_h = nc.dram_tensor("out", [Q, DM], F32, kind="ExternalOutput").ap()

    def internal(name, shape, dt):
        return nc.dram_tensor(name, shape, dt, kind="Internal").ap()

    def shared(name, shape, dt):
        return nc.dram_tensor(name, shape, dt, kind="Internal",
                              addr_space="Shared").ap()

    win_bf = shared("win_bf", [DM, OPAD], BF16)
    wout_bf = shared("wout_bf", [DI, DM], BF16)
    w1_bf = shared("w1_bf", [DM, 8192], BF16)
    w3_bf = shared("w3_bf", [DM, 8192], BF16)
    w2pp_bf = shared("w2pp_bf", [8 * 8192, 256], BF16)
    ccin_win = internal("ccin_win", [256, OPAD], BF16)
    ccin_wout = internal("ccin_wout", [512, DM], BF16)
    ccin_w1 = internal("ccin_w1", [256, 8192], BF16)
    ccin_w3 = internal("ccin_w3", [256, 8192], BF16)
    ccin_w2 = internal("ccin_w2", [8192, 256], BF16)

    z_sp = internal("z_sp", [DI, Q], BF16)
    xfm_sp = internal("xfm_sp", [DI, Q], BF16)
    st_sp = internal("st_sp", [NCH, 128, NH * HP], BF16)
    resid_sp = internal("resid_sp", [Q, DM], F32)
    fl_negrp = internal("fl_negrp", [1, NCH * NH * CH], F32)
    fl_r = internal("fl_r", [1, NCH * NH * CH], F32)
    wrow_b = internal("wrow_b", [1, 4 * NH], F32)
    cc_st_in = internal("cc_st_in", [128, NH * HP], BF16)
    cc_st_out = internal("cc_st_out", [512, NH * HP], BF16)
    cc_ld_in = internal("cc_ld_in", [NH, 1], F32)
    cc_ld_out = internal("cc_ld_out", [4 * NH, 1], F32)

    idb = nc.inline_tensor(np.eye(128, dtype=BF), name="id_bf").ap()
    idf = nc.inline_tensor(np.eye(128, dtype=np.float32), name="id_f32").ap()
    ones_row = nc.inline_tensor(np.ones((1, 512), np.float32), name="ones_row").ap()
    ones_big = nc.inline_tensor(np.ones((1, NH * CH), np.float32), name="ones_big").ap()
    h128c = nc.inline_tensor(
        np.array([[1.0 - 2.0 * (bin(i & j).count("1") & 1) for j in range(128)]
                  for i in range(128)], dtype=BF), name="h128").ap()
    mT = np.triu(np.ones((128, 128), np.float32)).astype(BF)
    maskT_c = nc.inline_tensor(mT, name="maskT").ap()

    SB = {}
    SB_BASE = 16384 + 2048
    SB_LIMIT = 229376
    sb_off = [SB_BASE]

    def sbuf(name, shape, dt, at=None):
        nbytes = int(np.prod(shape[1:])) * mybir.dt.size(dt)
        if at is None:
            at = sb_off[0]
            sb_off[0] = at + ((nbytes + 31) // 32) * 32
        h = nc.alloc_sbuf_tensor_at(name, shape, dt, offset=at)
        SB[name] = h
        return h

    c_fm = sbuf("c_fm", [128, Q], BF16)
    b_fm = sbuf("b_fm", [128, Q], BF16)
    r_tm = sbuf("r_tm", [128, NCH, NH], F32)
    expf_tm = sbuf("expf_tm", [128, NCH, NH], F32)
    dt_tm = sbuf("dt_tm", [128, NCH, NH], F32)
    dt_fm = sbuf("dt_fm", [64, Q], F32)
    lndt_fm = sbuf("lndt_fm", [64, Q], F32)
    r_fm = sbuf("r_fm", [64, Q], F32)
    hT = sbuf("hT", [128, NH * HP], BF16)
    dcol = sbuf("dcol", [128, NCH * NH], F32)
    dpb = sbuf("dpb", [128, DI], BF16)
    lndt_tm = sbuf("lndt_tm", [128, NCH, NH], F32)
    nrp_tm = sbuf("nrp_tm", [128, NCH, NH], F32)
    rl_cols = sbuf("rl_cols", [128, NCH * NH], F32)
    smalls = sbuf("smalls", [128, 256], F32)
    p_spacer_pending = True
    wq = sbuf("wq", [4, 64], F32)
    wq2 = sbuf("wq2", [3, 64], F32)
    wcols = sbuf("wcols", [128, 4 * NH], F32)
    cw_sb = sbuf("cw_sb", [128, 34, 4], F32)
    cb_sb = sbuf("cb_sb", [128, 34], F32)
    maskT_s = sbuf("maskT_s", [128, 128], BF16)
    h128_s = sbuf("h128_s", [128, 128], BF16)
    dtb_s = sbuf("dtb_s", [1, 128], F32)
    idb_s = sbuf("idb_s", [128, 128], BF16)
    idf_s = sbuf("idf_s", [128, 128], F32)
    onesr_s = sbuf("onesr_s", [1, 512], F32)
    rowb_s = sbuf("rowb_s", [1, 512], F32)

    R0 = sb_off[0]
    RSZ = SB_LIMIT - R0
    assert RSZ >= 150 * 1024, (R0, RSZ)

    def reg(name, shape, dt, at):
        nbytes = int(np.prod(shape[1:])) * mybir.dt.size(dt)
        assert at + nbytes <= RSZ, (name, at, nbytes, RSZ)
        return nc.alloc_sbuf_tensor_at(name, shape, dt, offset=R0 + at)

    w2a = reg("w2a", [128, 8192], F32, 0)
    w2b = reg("w2b", [128, 8192], F32, 32768)
    cvt = reg("cvt", [128, DI], F32, 65536)
    cvb = reg("cvb", [128, 2, 2048], BF16, 81920)
    nwb = reg("nwb", [128, DI], F32, 90112)
    w2cv = reg("w2cv", [128, 4096], BF16, 106496)
    w2stg = reg("w2stg", [128, 4, 128], BF16, 122880)

    PB = []
    for i in range(6):
        PB.append(nc.alloc_psum_tensor(f"pb{i}", [128, 512], F32))
    PT = []
    for i in range(2):
        PT.append(nc.alloc_psum_tensor(f"pt{i}", [128, 1024], BF16))

    p = Prog(nc)
    p._sp_ap = smalls.ap()[:, 254:255]
    SQ = [p.sem(f"sq{i}") for i in range(16)]
    GQ = [p.sem(f"gq{i}") for i in range(6)]
    T = p.sem("T")
    V = p.sem("V")
    A = p.sem("A")
    CC = p.sem("CC")

    bank_free = {i: None for i in range(8)}

    def bank_wait(i):
        return [bank_free[i]] if bank_free[i] is not None else []

    def cast_weight(src, dst, rows, qsem_ld, qsem_st):
        done = None
        r0 = 0
        i = 0
        while r0 < rows:
            n = min(128, rows - r0)
            buf = i % 2
            w = [cvb_free[buf]] if cvb_free[buf] else []
            c1 = p.dma("gpsimd", cvb.ap()[0:n, buf, :], src[r0:r0 + n, :],
                       waits=w, sig=qsem_ld)
            done = p.dma("gpsimd", dst[r0:r0 + n, :], cvb.ap()[0:n, buf, :],
                         waits=[(qsem_ld, c1)], sig=qsem_st)
            cvb_free[buf] = (qsem_st, done)
            r0 += n
            i += 1
        return done

    zv0 = p.emit("vector", lambda: nc.vector.memset(
        cvb.ap()[:, :, :].rearrange("p a c -> p (a c)"), 0.0), sig=V)
    czp = None
    cvb_free = [None, None]

    def cast_weight(src_ap, dst, rows, cols, qsem_ld, qsem_st, extra=None):
        done = None
        i = 0
        for r0 in range(0, rows, 128):
            n = min(128, rows - r0)
            for c0 in range(0, cols, 2048):
                w = min(2048, cols - c0)
                buf = i % 2
                ww = [cvb_free[buf]] if cvb_free[buf] else []
                c1 = p.dma("gpsimd", cvb.ap()[0:n, buf, 0:w],
                           src_ap[r0:r0 + n, c0:c0 + w], waits=ww, sig=qsem_ld)
                done = p.dma("gpsimd", dst[r0:r0 + n, c0:c0 + w],
                             cvb.ap()[0:n, buf, 0:w],
                             waits=[(qsem_ld, c1)], sig=qsem_st)
                cvb_free[buf] = (qsem_st, done)
                i += 1
        return done

    cwin = cast_weight(win_sl, ccin_win, 256, OPAD, GQ[0], GQ[1])
    agwin = p.emit("gpsimd", lambda: nc.gpsimd.collective_compute(
        "AllGather", ALU.bypass, replica_groups=[list(range(8))],
        ins=[ccin_win.opt()], outs=[win_bf[:, :]]),
        waits=[(GQ[1], cwin)], sig=CC)

    idld = p.dma("sync", idb_s.ap()[:, :], idb[:, :], sig=SQ[0])
    vv = None
    cw2 = None
    PT0_F = None
    for d in range(2):
        cw2l = p.dma("sync", w2a.ap()[:, :], w2_sl[d * 128:(d + 1) * 128, :],
                     waits=[(V, vv)] if vv else [], sig=SQ[1])
        pa = w2a.ap()[:, :]
        pb_ = w2b.ap()[:, :]
        for b_ in range(6):
            s = 1 << b_
            blk = 2 * s * 128
            g = 8192 // blk
            cur_ap = pa if (b_ % 2 == 0) else pb_
            oth_ap = pb_ if (b_ % 2 == 0) else pa
            lo_in = cur_ap.rearrange("p (g r) -> p g r", g=g)[:, :, 0:s * 128]
            hi_in = cur_ap.rearrange("p (g r) -> p g r", g=g)[:, :, s * 128:blk]
            lo_out = oth_ap.rearrange("p (g r) -> p g r", g=g)[:, :, 0:s * 128]
            hi_out = oth_ap.rearrange("p (g r) -> p g r", g=g)[:, :, s * 128:blk]
            w = [(SQ[1], cw2l)] if b_ == 0 else []
            vv = p.emit("vector", (lambda a=lo_out, x=lo_in, y=hi_in:
                                   nc.vector.tensor_add(a, x, y)), waits=w, sig=V)
            vv = p.emit("vector", (lambda a=hi_out, x=lo_in, y=hi_in:
                                   nc.vector.tensor_sub(a, x, y)), sig=V)
        W2SEM = [SQ[3], SQ[5], SQ[11], SQ[12]]
        w2stg_free = [None] * 4
        pt_free2 = [None, None]
        cw2_marks = []
        for hhalf in range(2):
            vv2 = p.emit("vector", (lambda h=hhalf: nc.vector.tensor_scalar_mul(
                w2cv.ap()[:, :], w2a.ap()[:, h * 4096:(h + 1) * 4096],
                FWHT_SCALE)),
                waits=([(V, vv)] if vv else []) + [(T, p.cnt[T])], sig=V)
            vv = vv2
            for ft in range(32):
                f = hhalf * 32 + ft
                ptb = ft % 2
                sb4 = ft % 4
                wpt0 = [pt_free2[ptb]] if pt_free2[ptb] else []
                tm = p.emit("tensor", (lambda ft=ft, ptb=ptb: nc.tensor.transpose(
                    PT[ptb].ap()[:, 0:128],
                    w2cv.ap()[:, ft * 128:(ft + 1) * 128],
                    idb_s.ap()[:, :])),
                    waits=[(V, vv2), (SQ[0], idld)] + wpt0, sig=T)
                wst4 = [w2stg_free[sb4]] if w2stg_free[sb4] else []
                ev = p.emit("vector", (lambda ptb=ptb, sb4=sb4: nc.vector.tensor_copy(
                    w2stg.ap()[:, sb4, :], PT[ptb].ap()[:, 0:128])),
                    waits=[(T, tm)] + wst4, sig=V)
                pt_free2[ptb] = (V, ev)
                cw2 = p.dma("sync", ccin_w2[f * 128:(f + 1) * 128, d * 128:(d + 1) * 128],
                            w2stg.ap()[:, sb4, :], waits=[(V, ev)], sig=W2SEM[sb4])
                w2stg_free[sb4] = (W2SEM[sb4], cw2)
                cw2_marks.append((W2SEM[sb4], cw2))
    w2done = {}
    for sm, vval in cw2_marks:
        w2done[sm] = max(w2done.get(sm, 0), vval)
    agw2 = p.emit("gpsimd", lambda: nc.gpsimd.collective_compute(
        "AllGather", ALU.bypass, replica_groups=[list(range(8))],
        ins=[ccin_w2.opt()], outs=[w2pp_bf[:, :]]),
        waits=list(w2done.items()), sig=CC)


    lnw = p.dma("sync", smalls.ap()[:, 16:20], nw_t[:, :], sig=SQ[1])
    cwout = None
    cvt_free = None
    for i in range(4):
        w = [cvt_free] if cvt_free else []
        c1 = p.dma("sync", cvt.ap()[:, 0:2048], wout_sl[i * 128:(i + 1) * 128, :],
                   waits=w + [(SQ[1], lnw)], sig=SQ[1])
        buf = i % 2
        ww = [cvb_free[buf]] if cvb_free[buf] else []
        mv = p.emit("vector", (lambda i=i, buf=buf: nc.vector.tensor_single_scalar(
            cvb.ap()[:, buf, :], cvt.ap()[:, 0:2048],
            smalls.ap()[:, 16 + i:17 + i], ALU.mult)),
            waits=[(SQ[1], c1)] + ww, sig=V)
        cvt_free = (V, mv)
        cwout = p.dma("gpsimd", ccin_wout[i * 128:(i + 1) * 128, :],
                      cvb.ap()[:, buf, :], waits=[(V, mv)], sig=GQ[1])
        cvb_free[buf] = (GQ[1], cwout)
    agwout = p.emit("gpsimd", lambda: nc.gpsimd.collective_compute(
        "AllGather", ALU.bypass, replica_groups=[list(range(8))],
        ins=[ccin_wout.opt()], outs=[wout_bf[:, :]]),
        waits=[(GQ[1], cwout)], sig=CC)

    cw1 = cast_weight(w1_sl, ccin_w1, 256, 8192, GQ[0], GQ[1])
    agw1 = p.emit("gpsimd", lambda: nc.gpsimd.collective_compute(
        "AllGather", ALU.bypass, replica_groups=[list(range(8))],
        ins=[ccin_w1.opt()], outs=[w1_bf[:, :]]),
        waits=[(GQ[1], cw1)], sig=CC)
    cw3 = cast_weight(w3_sl, ccin_w3, 256, 8192, GQ[0], GQ[1])
    agw3 = p.emit("gpsimd", lambda: nc.gpsimd.collective_compute(
        "AllGather", ALU.bypass, replica_groups=[list(range(8))],
        ins=[ccin_w3.opt()], outs=[w3_bf[:, :]]),
        waits=[(GQ[1], cw3)], sig=CC)

    AG_WIN, AG_WOUT, AG_W1, AG_W3, AG_W2 = agwin, agwout, agw1, agw3, agw2


    rowhbm = internal("rowhbm", [1, 512], F32)
    ln1_dbg = internal("ln1_dbg", [2, 128, 16, 515], BF16)
    u_dbg = internal("u_dbg", [128, 2048], F32)
    sm_dbg = internal("sm_dbg", [128, 256], F32)
    lt_dbg = internal("lt_dbg", [128, 2048], BF16)

    wcache = reg("wc", [128, 2, 16, 512], BF16, 0)
    ln1fm = reg("ln1fm", [128, 16, 1027], BF16, 32768)
    xbc = reg("xbc", [128, 4, 1027], BF16, 65632)
    ustage = reg("ustage", [128, 2, 2048], F32, 73856)
    ln1tm = reg("ln1tm", [128, 2, 2048], BF16, 90240)
    lnscr = reg("lnscr", [128, 2048], BF16, 98432)
    zstage = reg("zstage", [128, 2, 512], BF16, 102528)
    convacc = reg("convacc", [128, 2, 1024], F32, 104576)
    convstage = reg("convstage", [128, 2, 1024], BF16, 112768)
    dtscr = reg("dtscr", [128, 64], F32, 116864)
    xstage = reg("xstage", [128, 2, 4096], BF16, 117120)
    xw_sb = reg("xw_sb", [128, 512], BF16, 133504)
    stst = reg("stst", [128, 4096], BF16, 134528)
    wtm = reg("wtm", [128, 64], F32, 142720)
    wtm2 = reg("wtm2", [128, 64], F32, 142976)
    dascr = reg("dascr", [64, 1024], F32, 143232)
    nrpscr = reg("nrpscr", [64, 1024], F32, 147328)
    btm = reg("btm", [128, 128], BF16, 151424)
    stripsb = reg("stripsb", [3, 512], BF16, 151680)

    PH0_V = p.cnt[V]
    PH0_A = p.cnt[A]
    PH0_T = p.cnt[T]
    PH0_GQ1 = p.cnt[GQ[1]]
    PH0_SQ1 = p.cnt[SQ[1]]
    PH0_BAR = [(V, PH0_V), (A, PH0_A), (T, PH0_T), (GQ[1], PH0_GQ1), (SQ[1], PH0_SQ1)]
    for _e in ("sync", "gpsimd", "tensor", "vector", "scalar"):
        p.wait(_e, PH0_BAR)

    lsm = p.dma("sync", cw_sb.ap()[:, :, :], convw_t[:, :, :], sig=SQ[0])
    lsm = p.dma("sync", cb_sb.ap()[:, :], convb_t[:, :], sig=SQ[0])
    lsm = p.dma("sync", maskT_s.ap()[:, :], maskT_c[:, :], sig=SQ[0])
    lsm = p.dma("sync", h128_s.ap()[:, :], h128c[:, :], sig=SQ[0])
    lsm = p.dma("sync", dtb_s.ap()[:, :], dtb_row[:, :], sig=SQ[0])
    lsm = p.dma("sync", onesr_s.ap()[:, :], ones_row[:, :], sig=SQ[0])
    lsm = p.dma("sync", idb_s.ap()[:, :], idb[:, :], sig=SQ[0])
    lsm = p.dma("sync", idf_s.ap()[:, :], idf[:, :], sig=SQ[0])
    lsm = p.dma("sync", smalls.ap()[0:64, 48:49], alog_col[:, :], sig=SQ[0])
    SMALL_LD = lsm
    ldp = p.dma("gpsimd", dpb.ap()[0:1, 0:4096], dp_rep[:, :], sig=GQ[2])
    aa = p.emit("scalar", lambda: nc.scalar.activation(
        smalls.ap()[0:64, 49:50], smalls.ap()[0:64, 48:49], AF.Exp),
        waits=[(SQ[0], SMALL_LD)], sig=A)
    vna = p.emit("vector", lambda: nc.vector.tensor_scalar_mul(
        smalls.ap()[0:64, 50:51], smalls.ap()[0:64, 49:50], -1.0),
        waits=[(A, aa)], sig=V)
    NEGA = vna
    dpw = None
    for j in range(8):
        mm = p.emit("tensor", (lambda j=j: nc.tensor.matmul(
            PB[4].ap()[:, :], h128_s.ap()[0:1, 0:128],
            dpb.ap()[0:1, j * 512:(j + 1) * 512], start=True, stop=True)),
            waits=[(SQ[0], SMALL_LD), (GQ[2], ldp)] + bank_wait(4), sig=T)
        dpw = p.emit("vector", (lambda j=j: nc.vector.tensor_copy(
            dpb.ap()[:, j * 512:(j + 1) * 512], PB[4].ap()[:, :])),
            waits=[(T, mm)], sig=V)
        bank_free[4] = (V, dpw)
    DPB_OK = dpw

    hz = p.emit("vector", lambda: nc.vector.memset(hT.ap()[:, :], 0.0), sig=V)
    ve1 = p.emit("vector", lambda: nc.vector.memset(smalls.ap()[:, 54:55], EPS), sig=V)
    ve2 = p.emit("vector", lambda: nc.vector.memset(smalls.ap()[:, 55:56], 1.0 / DI), sig=V)
    EPSC = ve2

    UST_FREE = [None, None]
    LN1TM_FREE = [None, None]
    PT_FREE = list(pt_free2)
    SSQ_FREE = [None, None]
    tiles = [(0, 128, 0), (128, 128, 128), (256, 128, 256), (384, 128, 384),
             (512, 128, 512), (640, 128, 640), (768, 128, 768), (896, 128, 896),
             (1024, 3, 1024)]
    for ti, (e0, n, co) in enumerate(tiles):
        ub = ti % 2
        w = list(UST_FREE[ub]) if UST_FREE[ub] else []
        usem = SQ[1 + 5 * ub]
        cu = p.dma("sync", ustage.ap()[0:n, ub, :], u_ext[e0:e0 + n, :],
                   waits=w, sig=usem)
        base = 16 * (ti % 2)
        vsum = p.emit("vector", (lambda n=n, ub=ub, b=base: nc.vector.reduce_sum(
            smalls.ap()[0:n, b + 0:b + 1], ustage.ap()[0:n, ub, :],
            axis=mybir.AxisListType.X)), waits=[(usem, cu)], sig=V)
        p.emit("scalar", (lambda n=n, ub=ub, b=base: nc.scalar.activation(
            lnscr.ap()[0:n, :], ustage.ap()[0:n, ub, :], AF.Square,
            accum_out=smalls.ap()[0:n, b + 1:b + 2])),
            waits=[(usem, cu)] + ([SSQ_FREE[ti % 2]] if SSQ_FREE[ti % 2] else []))
        asq = p.emit("scalar", (lambda n=n, b=base: nc.scalar.copy(
            smalls.ap()[0:n, b + 7:b + 8], smalls.ap()[0:n, b + 1:b + 2])),
            sig=A)
        p.vspace(nc)
        v1 = p.emit("vector", (lambda n=n, b=base: nc.vector.tensor_scalar_mul(
            smalls.ap()[0:n, b + 2:b + 3], smalls.ap()[0:n, b + 0:b + 1],
            1.0 / 2048.0)), waits=[(V, vsum)], sig=V)
        v2 = p.emit("vector", (lambda n=n, b=base: nc.vector.tensor_scalar_mul(
            smalls.ap()[0:n, b + 3:b + 4], smalls.ap()[0:n, b + 1:b + 2],
            1.0 / 2048.0)), waits=[(A, asq)], sig=V)
        SSQ_FREE[ti % 2] = (V, v2)
        v3 = p.emit("vector", (lambda n=n, b=base: nc.vector.tensor_single_scalar(
            smalls.ap()[0:n, b + 4:b + 5], smalls.ap()[0:n, b + 2:b + 3],
            smalls.ap()[0:n, b + 2:b + 3], ALU.mult)), sig=V)
        p.vspace(nc)
        v4 = p.emit("vector", (lambda n=n, b=base: nc.vector.tensor_single_scalar(
            smalls.ap()[0:n, b + 3:b + 4], smalls.ap()[0:n, b + 3:b + 4],
            smalls.ap()[0:n, b + 4:b + 5], ALU.subtract)), sig=V)
        a2 = p.emit("scalar", (lambda n=n, b=base: nc.scalar.activation(
            smalls.ap()[0:n, b + 5:b + 6], smalls.ap()[0:n, b + 3:b + 4],
            AF.Sqrt, bias=smalls.ap()[0:n, 54:55])),
            waits=[(V, v4), (V, EPSC)], sig=A)
        v5 = p.emit("vector", (lambda n=n, b=base: nc.vector.reciprocal(
            smalls.ap()[0:n, b + 6:b + 7], smalls.ap()[0:n, b + 5:b + 6])),
            waits=[(A, a2)], sig=V)
        wlt = [LN1TM_FREE[ub]] if LN1TM_FREE[ub] else []
        p.vspace(nc)
        v6 = p.emit("vector", (lambda n=n, ub=ub, b=base: nc.vector.tensor_scalar(
            ln1tm.ap()[0:n, ub, :], ustage.ap()[0:n, ub, :],
            smalls.ap()[0:n, b + 2:b + 3], smalls.ap()[0:n, b + 6:b + 7],
            ALU.subtract, ALU.mult)), waits=[(V, v5)] + wlt, sig=V)
        UST_FREE[ub] = [(V, v6), (A, asq)]
        for k in range(16):
            ptb = k % 2
            wpt = [PT_FREE[ptb]] if PT_FREE[ptb] else []
            tm = p.emit("tensor", (lambda n=n, ub=ub, k=k, ptb=ptb:
                        nc.tensor.transpose(
                            PT[ptb].ap()[:, 0:n],
                            ln1tm.ap()[0:n, ub, k * 128:(k + 1) * 128],
                            idb_s.ap()[0:n, 0:n])),
                        waits=[(V, v6), (SQ[0], SMALL_LD)] + wpt, sig=T)
            if k % 2 == 0:
                ev = p.emit("vector", (lambda n=n, k=k, co=co, ptb=ptb:
                            nc.vector.tensor_copy(
                                ln1fm.ap()[:, k, co:co + n], PT[ptb].ap()[:, 0:n])),
                            waits=[(T, tm)], sig=V)
                PT_FREE[ptb] = (V, ev)
            else:
                ev = p.emit("scalar", (lambda n=n, k=k, co=co, ptb=ptb:
                            nc.scalar.copy(
                                ln1fm.ap()[:, k, co:co + n], PT[ptb].ap()[:, 0:n])),
                            waits=[(T, tm)], sig=A)
                PT_FREE[ptb] = (A, ev)
        LN1TM_FREE[ub] = (T, p.cnt[T])
    LN1FM_OK = [(V, p.cnt[V]), (A, p.cnt[A])]

    OCW = [512] * 16 + [384]
    WC_FREE = [None, None]
    ZST_FREE = [None, None]
    CONVST_FREE = [None, None]
    CONVACC_FREE = [None, None]
    STRIP_FREE = None
    slab_free = {}
    slab_ready = {}
    xspill = {}
    dt_mm_done = {}
    z_done = 0

    for oc in range(17):
        wb = oc % 2
        width = OCW[oc]
        wwc = [WC_FREE[wb]] if WC_FREE[wb] else []
        lw = []
        for k in range(16):
            wsem = SQ[[2, 7, 11, 12][k % 4]]
            lwv = p.dma("sync", wcache.ap()[:, wb, k, 0:width],
                        win_bf[k * 128:(k + 1) * 128, oc * 512:oc * 512 + width],
                        waits=wwc + [(CC, AG_WIN)], sig=wsem)
            lw.append((wsem, lwv))
            wwc = []
        lw = [(sm, max(vv for sm2, vv in lw if sm2 == sm)) for sm in set(s0 for s0, _ in lw)]
        nots = 4 if oc < 16 else 2
        if oc >= 8:
            mms = None
            for k in range(16):
                mms = p.emit("tensor", (lambda k=k, wb=wb, width=width:
                            nc.tensor.matmul(
                                PB[5].ap()[0:3, 0:width],
                                ln1fm.ap()[:, k, 0:3],
                                wcache.ap()[:, wb, k, 0:width],
                                start=(k == 0), stop=(k == 15))),
                            waits=(lw + LN1FM_OK + bank_wait(5)) if k == 0 else [],
                            sig=(T if k == 15 else None))
            wstf = [STRIP_FREE] if STRIP_FREE else []
            evs = p.emit("vector", (lambda width=width: nc.vector.tensor_copy(
                stripsb.ap()[0:3, 0:width], PB[5].ap()[0:3, 0:width])),
                waits=[(T, mms)] + wstf, sig=V)
            bank_free[5] = (V, evs)
            STRIP_V = evs
        for oti in range(nots):
            g = oc * 4 + oti
            rot = g % 4
            pp = g % 2
            for h in range(2):
                mm = None
                for k in range(16):
                    mm = p.emit("tensor", (lambda oti=oti, k=k, pp=pp, wb=wb, h=h:
                                nc.tensor.matmul(
                                    PB[pp].ap()[:, :],
                                    wcache.ap()[:, wb, k, oti * 128:(oti + 1) * 128],
                                    ln1fm.ap()[:, k, 3 + h * 512:3 + (h + 1) * 512],
                                    start=(k == 0), stop=(k == 15))),
                                waits=(lw + LN1FM_OK + bank_wait(pp)) if k == 0 else [],
                                sig=(T if k == 15 else None))
                if g < 32:
                    zb = (g * 2 + h) % 2
                    wz = [ZST_FREE[zb]] if ZST_FREE[zb] else []
                    ve = p.emit("vector", (lambda zb=zb, pp=pp: nc.vector.tensor_copy(
                        zstage.ap()[:, zb, :], PB[pp].ap()[:, :])),
                        waits=[(T, mm)] + wz, sig=V)
                    bank_free[pp] = (V, ve)
                    zsem = SQ[3 + 5 * zb]
                    zd = p.dma("sync", z_sp[g * 128:(g + 1) * 128,
                                            h * 512:(h + 1) * 512],
                               zstage.ap()[:, zb, :], waits=[(V, ve)], sig=zsem)
                    ZST_FREE[zb] = (zsem, zd)
                    z_done = zd
                else:
                    wsl = [slab_free[rot]] if rot in slab_free else []
                    ve = p.emit("vector", (lambda rot=rot, h=h, pp=pp:
                                nc.vector.tensor_copy(
                                    xbc.ap()[:, rot, 3 + h * 512:3 + (h + 1) * 512],
                                    PB[pp].ap()[:, :])),
                                waits=[(T, mm)] + (wsl if h == 0 else []), sig=V)
                    bank_free[pp] = (V, ve)
            if g >= 32:
                wpt = [PT_FREE[0]] if PT_FREE[0] else []
                tms = p.emit("tensor", (lambda oti=oti: nc.tensor.transpose(
                    PT[0].ap()[:, 0:3], stripsb.ap()[0:3, oti * 128:(oti + 1) * 128],
                    idb_s.ap()[0:3, 0:3])),
                    waits=[(V, STRIP_V)] + wpt, sig=T)
                evh = p.emit("vector", (lambda rot=rot: nc.vector.tensor_copy(
                    xbc.ap()[:, rot, 0:3], PT[0].ap()[:, 0:3])),
                    waits=[(T, tms)], sig=V)
                PT_FREE[0] = (V, evh)
                slab_ready[g] = (V, p.cnt[V])
                ot = g - 32
                ca = ot % 2
                wc0 = ([(SQ[0], SMALL_LD)] if ot == 0 else [])
                wca = [CONVACC_FREE[ca]] if CONVACC_FREE[ca] else []
                v1 = p.emit("vector", (lambda ot=ot, ca=ca, rot=rot:
                            nc.vector.tensor_single_scalar(
                                convacc.ap()[:, ca, :], xbc.ap()[:, rot, 0:1024],
                                cw_sb.ap()[:, ot, 0:1], ALU.mult)),
                            waits=wc0 + wca + [slab_ready[g]], sig=V)
                for k in range(1, 4):
                    v1 = p.emit("vector", (lambda ot=ot, k=k, ca=ca, rot=rot:
                                nc.vector.scalar_tensor_tensor(
                                    convacc.ap()[:, ca, :], xbc.ap()[:, rot, k:1024 + k],
                                    cw_sb.ap()[:, ot, k:k + 1], convacc.ap()[:, ca, :],
                                    ALU.mult, ALU.add)), sig=V)
                slab_free[rot] = (V, v1)
                if ot < 32:
                    cb2 = ot % 2
                    wcs = [CONVST_FREE[cb2]] if CONVST_FREE[cb2] else []
                    a1 = p.emit("scalar", (lambda ot=ot, cb2=cb2, ca=ca:
                                nc.scalar.activation(
                                    convstage.ap()[:, cb2, :], convacc.ap()[:, ca, :],
                                    AF.Silu, bias=cb_sb.ap()[:, ot:ot + 1])),
                                waits=[(V, v1)] + wcs, sig=A)
                    CONVACC_FREE[ca] = (A, a1)
                    xsem = SQ[13 + cb2]
                    xd = p.dma("sync", xfm_sp[ot * 128:(ot + 1) * 128, :],
                               convstage.ap()[:, cb2, :], waits=[(A, a1)], sig=xsem)
                    CONVST_FREE[cb2] = (xsem, xd)
                    xspill[ot] = (xsem, xd)
                else:
                    dstap = b_fm if ot == 32 else c_fm
                    a1 = p.emit("scalar", (lambda dstap=dstap, ca=ca, ot=ot:
                                nc.scalar.activation(
                                    dstap.ap()[:, :], convacc.ap()[:, ca, :],
                                    AF.Silu, bias=cb_sb.ap()[:, ot:ot + 1])),
                                waits=[(V, v1)], sig=A)
                    CONVACC_FREE[ca] = (A, a1)
        WC_FREE[wb] = (T, p.cnt[T])
        if oc >= 8:
            STRIP_FREE = (T, p.cnt[T])
        if oc == 16:
            for tt in range(8):
                c = tt
                pp = tt % 2
                for k in range(16):
                    p.emit("tensor", (lambda tt=tt, k=k, pp=pp, wb=wb:
                           nc.tensor.matmul(
                               PB[pp].ap()[:, 0:128],
                               ln1fm.ap()[:, k, 3 + tt * 128:3 + (tt + 1) * 128],
                               wcache.ap()[:, wb, k, 256:384],
                               start=(k == 0), stop=False)),
                           waits=(bank_wait(pp) if k == 0 else []), sig=None)
                mm = p.emit("tensor", (lambda pp=pp: nc.tensor.matmul(
                    PB[pp].ap()[:, 0:128], onesr_s.ap()[0:1, 0:128],
                    dtb_s.ap()[0:1, 0:128], start=False, stop=True)),
                    waits=[(SQ[0], SMALL_LD)], sig=T)
                a1 = p.emit("scalar", (lambda pp=pp: nc.scalar.activation(
                    dtscr.ap()[:, :], PB[pp].ap()[:, 0:64], AF.Exp)),
                    waits=[(T, mm)], sig=A)
                a2 = p.emit("scalar", (lambda c=c: nc.scalar.activation(
                    dt_tm.ap()[:, c, :], dtscr.ap()[:, :], AF.Ln, bias=1.0)),
                    sig=A)
                a3 = p.emit("scalar", (lambda c=c: nc.scalar.activation(
                    lndt_tm.ap()[:, c, :], dt_tm.ap()[:, c, :], AF.Ln)),
                    sig=A)
                bank_free[pp] = (A, a1)
                dt_mm_done[c] = a3
            WC_FREE[wb] = (T, p.cnt[T])
    XSPILL_HALF = [(SQ[13], p.cnt[SQ[13]]), (SQ[14], p.cnt[SQ[14]])]
    Z_OK = [(SQ[3], p.cnt[SQ[3]]), (SQ[8], p.cnt[SQ[8]])]
    BC_OK = (A, p.cnt[A])

    for tt in range(8):
        c = tt
        tm = p.emit("tensor", (lambda c=c: nc.tensor.transpose(
            PB[4].ap()[0:64, 0:128], dt_tm.ap()[:, c, :], idf_s.ap()[:, :])),
            waits=[(A, dt_mm_done[c])] + bank_wait(4), sig=T)
        ev = p.emit("vector", (lambda c=c: nc.vector.tensor_copy(
            dt_fm.ap()[0:64, c * 128:(c + 1) * 128], PB[4].ap()[0:64, 0:128])),
            waits=[(T, tm)], sig=V)
        bank_free[4] = (V, ev)
    a1 = p.emit("scalar", (lambda: nc.scalar.activation(
        lndt_fm.ap()[0:64, :], dt_fm.ap()[0:64, :], AF.Ln)),
        waits=[(V, p.cnt[V])], sig=A)
    v1 = p.emit("vector", (lambda: nc.vector.tensor_single_scalar(
        dascr.ap()[0:64, 0:1024], dt_fm.ap()[0:64, :],
        smalls.ap()[0:64, 50:51], ALU.mult)),
        waits=[(V, NEGA)], sig=V)
    p.vspace(nc)
    for c in range(8):
        p.emit("vector", (lambda c=c: nc.vector.tensor_tensor_scan(
            r_fm.ap()[0:64, c * 128:(c + 1) * 128],
            dascr.ap()[0:64, c * 128:(c + 1) * 128],
            dascr.ap()[0:64, c * 128:(c + 1) * 128],
            0.0, ALU.add, ALU.bypass)), sig=V)
    p.vspace(nc)
    v2 = p.emit("vector", (lambda: nc.vector.tensor_sub(
        nrpscr.ap()[0:64, 0:1024], lndt_fm.ap()[0:64, :], r_fm.ap()[0:64, :])),
        waits=[(A, a1)], sig=V)
    fd = p.dma("gpsimd",
               fl_negrp[0:1, :].rearrange("o (c h j) -> (o h) c j", c=8, h=64, j=128),
               nrpscr.ap()[0:64, 0:1024].rearrange("h (c j) -> h c j", c=8),
               waits=[(V, v2), (CC, AG_W2)], sig=GQ[2])
    fd = p.dma("gpsimd",
               fl_r[0:1, :].rearrange("o (c h j) -> (o h) c j", c=8, h=64, j=128),
               r_fm.ap()[0:64, :].rearrange("h (c j) -> h c j", c=8),
               waits=[(V, v2)], sig=GQ[2])
    FLATS_OK = fd
    for c in range(8):
        tm = p.emit("tensor", (lambda c=c: nc.tensor.transpose(
            PB[4].ap()[:, 0:64], r_fm.ap()[0:64, c * 128:(c + 1) * 128],
            idf_s.ap()[0:64, 0:64])),
            waits=[(V, p.cnt[V])] + bank_wait(4), sig=T)
        ev = p.emit("vector", (lambda c=c: nc.vector.tensor_copy(
            r_tm.ap()[:, c, :], PB[4].ap()[:, 0:64])),
            waits=[(T, tm)], sig=V)
        bank_free[4] = (V, ev)
    aex = p.emit("scalar", (lambda: nc.scalar.activation(
        expf_tm.ap()[:, :, :], r_tm.ap()[:, :, :], AF.Exp)),
        waits=[(V, p.cnt[V])], sig=A)
    rb1 = p.dma("gpsimd", rowhbm[0:1, 0:512].rearrange("o (c h) -> o c h", c=8),
                r_tm.ap()[127:128, :, :], waits=[(V, p.cnt[V])], sig=GQ[2])
    rb2 = p.dma("gpsimd", rowb_s.ap()[0:1, 0:512], rowhbm[0:1, 0:512],
                waits=[(GQ[2], rb1)], sig=GQ[2])
    mm = p.emit("tensor", (lambda: nc.tensor.matmul(
        PB[4].ap()[:, 0:512], onesr_s.ap()[0:1, 0:128],
        rowb_s.ap()[0:1, 0:512], start=True, stop=True)),
        waits=[(GQ[2], rb2), (SQ[0], SMALL_LD)] + bank_wait(4), sig=T)
    ev = p.emit("vector", (lambda: nc.vector.tensor_copy(
        rl_cols.ap()[:, :], PB[4].ap()[:, 0:512])),
        waits=[(T, mm)], sig=V)
    bank_free[4] = (V, ev)
    adc = p.emit("scalar", (lambda: nc.scalar.activation(
        dcol.ap()[:, :], rl_cols.ap()[:, :], AF.Exp)), waits=[(V, ev)], sig=A)
    vnr = p.emit("vector", (lambda: nc.vector.tensor_sub(
        nrp_tm.ap()[:, :, :], lndt_tm.ap()[:, :, :], r_tm.ap()[:, :, :])),
        waits=[(A, p.cnt[A])], sig=V)
    DTM_A = p.cnt[A]
    DTM_V = p.cnt[V]

    XST_FREE = [None, None]
    STST_FREE = None
    hT_tail = hz
    xw_free = None
    stspill = {}
    for c in range(8):
        xb2 = c % 2
        wxs = list(XST_FREE[xb2]) if XST_FREE[xb2] else []
        xrsem = SQ[4 + 5 * xb2]
        cx = p.dma("sync", xstage.ap()[:, xb2, :],
                   xfm_sp[0:4096, c * 128:(c + 1) * 128],
                   waits=wxs + XSPILL_HALF, sig=xrsem, transpose=True)
        tb = p.emit("tensor", (lambda c=c: nc.tensor.transpose(
            PT[0].ap()[:, 0:128], b_fm.ap()[:, c * 128:(c + 1) * 128],
            idb_s.ap()[:, :])),
            waits=[BC_OK] + ([PT_FREE[0]] if PT_FREE[0] else []), sig=T)
        evb = p.emit("vector", lambda: nc.vector.tensor_copy(
            btm.ap()[:, :], PT[0].ap()[:, 0:128]), waits=[(T, tb)], sig=V)
        PT_FREE[0] = (V, evb)
        p.vspace(nc)
        v1 = p.emit("vector", (lambda c=c: nc.vector.tensor_add(
            wtm2.ap()[:, :], nrp_tm.ap()[:, c, :],
            rl_cols.ap()[:, c * 64:(c + 1) * 64])),
            waits=[(V, DTM_V), (A, DTM_A)], sig=V)
        a1 = p.emit("scalar", lambda: nc.scalar.activation(
            wtm.ap()[:, :], wtm2.ap()[:, :], AF.Exp), waits=[(V, v1)], sig=A)
        for g in range(8):
            vxw = None
            for i in range(8):
                h = g * 8 + i
                vxw = p.emit("vector", (lambda h=h, i=i, xb2=xb2:
                            nc.vector.tensor_single_scalar(
                                xw_sb.ap()[:, i * 64:(i + 1) * 64],
                                xstage.ap()[:, xb2, h * 64:(h + 1) * 64],
                                wtm.ap()[:, h:h + 1], ALU.mult)),
                            waits=(([(xrsem, cx), (A, a1)] +
                                    ([xw_free] if xw_free else [])) if i == 0 else []),
                            sig=V)
            pp = 2 + g % 2
            mm = p.emit("tensor", (lambda pp=pp: nc.tensor.matmul(
                PB[pp].ap()[:, :], btm.ap()[:, :], xw_sb.ap()[:, :],
                start=True, stop=True)),
                waits=[(V, vxw), (V, evb)] + bank_wait(pp), sig=T)
            xw_free = (T, p.cnt[T])
            for i in range(8):
                h = g * 8 + i
                p.emit("vector", (lambda h=h, i=i, c=c, pp=pp:
                       nc.vector.scalar_tensor_tensor(
                           hT.ap()[:, h * 64:(h + 1) * 64],
                           hT.ap()[:, h * 64:(h + 1) * 64],
                           dcol.ap()[:, c * 64 + h:c * 64 + h + 1],
                           PB[pp].ap()[:, i * 64:(i + 1) * 64],
                           ALU.mult, ALU.add)),
                       waits=([(T, mm), (A, adc)] if i == 0 else []), sig=V)
            wst = [STST_FREE] if STST_FREE else []
            vcp = p.emit("vector", (lambda g=g, pp=pp: nc.vector.tensor_copy(
                stst.ap()[:, g * 512:(g + 1) * 512], PB[pp].ap()[:, :])),
                waits=wst, sig=V)
            bank_free[pp] = (V, vcp)
        sd = p.dma("sync", st_sp[c, :, :], stst.ap()[:, :],
                   waits=[(V, p.cnt[V])], sig=SQ[10])
        STST_FREE = (SQ[10], sd)
        stspill[c] = sd
        XST_FREE[xb2] = [(V, p.cnt[V])]
        hT_tail = p.cnt[V]
    STSP_OK = (SQ[10], p.cnt[SQ[10]])

    cs1 = p.dma("gpsimd", cc_st_in[:, :], hT.ap()[:, :],
                waits=[(V, hT_tail)], sig=GQ[2])
    vld = p.emit("vector", lambda: nc.vector.reduce_sum(
        smalls.ap()[0:64, 52:53],
        r_fm.ap()[0:64, :].rearrange("h (c j) -> h c j", j=128)[:, :, 127],
        axis=mybir.AxisListType.X), waits=[(V, p.cnt[V])], sig=V)
    cs2 = p.dma("gpsimd", cc_ld_in[:, :], smalls.ap()[0:64, 52:53],
                waits=[(V, vld)], sig=GQ[2])
    agst = p.emit("gpsimd", lambda: nc.gpsimd.collective_compute(
        "AllGather", ALU.bypass, replica_groups=[[0, 1, 2, 3], [4, 5, 6, 7]],
        ins=[cc_st_in.opt()], outs=[cc_st_out.opt()]),
        waits=[(GQ[2], cs1)], sig=CC)
    agld = p.emit("gpsimd", lambda: nc.gpsimd.collective_compute(
        "AllGather", ALU.bypass, replica_groups=[[0, 1, 2, 3], [4, 5, 6, 7]],
        ins=[cc_ld_in.opt()], outs=[cc_ld_out.opt()]),
        waits=[(GQ[2], cs2)], sig=CC)
    lsel = p.dma("sync", wq.ap()[:, :],
                 cc_ld_out[:, :].rearrange("(p h) o -> p (h o)", p=4),
                 waits=[(CC, agld)], sig=SQ[0])
    lsel = p.dma("sync", smalls.ap()[0:4, 32:35], selm[:, :], sig=SQ[0])
    lsel = p.dma("sync", smalls.ap()[0:3, 35:36], selv[:, :], sig=SQ[0])
    mm = p.emit("tensor", lambda: nc.tensor.matmul(
        PB[4].ap()[0:3, 0:64], smalls.ap()[0:4, 32:35], wq.ap()[:, :],
        start=True, stop=True),
        waits=[(SQ[0], lsel)] + bank_wait(4), sig=T)
    ae = p.emit("scalar", lambda: nc.scalar.activation(
        wq2.ap()[:, :], PB[4].ap()[0:3, 0:64], AF.Exp), waits=[(T, mm)], sig=A)
    bank_free[4] = (A, ae)
    vmu = p.emit("vector", lambda: nc.vector.tensor_single_scalar(
        wq2.ap()[:, :], wq2.ap()[:, :], smalls.ap()[0:3, 35:36], ALU.mult),
        waits=[(A, ae)], sig=V)
    bw1 = p.dma("gpsimd", wrow_b[0:1, 0:192].rearrange("o (q h) -> (o q) h", q=3),
                wq2.ap()[:, :], waits=[(V, vmu)], sig=GQ[2])
    bw2 = p.dma("gpsimd", rowb_s.ap()[0:1, 0:192], wrow_b[0:1, 0:192],
                waits=[(GQ[2], bw1)], sig=GQ[2])
    mm = p.emit("tensor", lambda: nc.tensor.matmul(
        PB[4].ap()[:, 0:192], onesr_s.ap()[0:1, 0:128], rowb_s.ap()[0:1, 0:192],
        start=True, stop=True), waits=[(GQ[2], bw2)] + bank_wait(4), sig=T)
    vwc = p.emit("vector", lambda: nc.vector.tensor_copy(
        wcols.ap()[:, 0:192], PB[4].ap()[:, 0:192]), waits=[(T, mm)], sig=V)
    bank_free[4] = (V, vwc)
    hz2 = p.emit("vector", lambda: nc.vector.memset(hT.ap()[:, :], 0.0),
                 waits=[(GQ[2], cs1)], sig=V)
    vh = hz2
    for pidx in range(3):
        gsem = SQ[4 + 5 * (pidx % 2)]
        cg = p.dma("sync", xstage.ap()[:, pidx % 2, :],
                   cc_st_out[pidx * 128:(pidx + 1) * 128, :],
                   waits=[(CC, agst), (V, vh)], sig=gsem)
        for h in range(NH):
            vh = p.emit("vector", (lambda pidx=pidx, h=h: nc.vector.scalar_tensor_tensor(
                hT.ap()[:, h * 64:(h + 1) * 64],
                xstage.ap()[:, pidx % 2, h * 64:(h + 1) * 64],
                wcols.ap()[:, pidx * 64 + h:pidx * 64 + h + 1],
                hT.ap()[:, h * 64:(h + 1) * 64],
                ALU.mult, ALU.add)),
                waits=([(gsem, cg), (V, vwc)] if h == 0 else []), sig=V)
    HIN_OK = vh
    hin_sp = internal("hin_sp", [128, NH * HP], BF16)
    p.dma("gpsimd", hin_sp[:, :], hT.ap()[:, :], waits=[(V, HIN_OK)], sig=GQ[3])

    attn_sp = internal("attn_sp", [Q, DM], BF16)
    y_dbg = internal("y_dbg", [128, DI], BF16)
    sc_dbg = internal("sc_dbg", [128, 4096], BF16)
    cb_dbg = internal("cb_dbg", [128, 128], BF16)

    ygfm = reg("ygfm", [128, 32, Q], BF16, 0)
    segL = reg("segL", [2, 2, 1024], F32, 65536)
    segR = reg("segR", [2, 2, 1024], F32, 73728)
    xst3 = reg("xst3", [128, 2, 4096], BF16, 81920)
    zst3 = reg("zst3", [128, 4096], BF16, 98304)
    stst3 = reg("stst3", [128, 4096], BF16, 106496)
    ysb = reg("ysb", [128, 4096], BF16, 114688)
    scor = reg("scor", [128, 8, 128], BF16, 122880)
    segexp = reg("segexp", [128, 2, 512], BF16, 124928)
    cbm = reg("cbm", [128, 128], BF16, 126976)
    ytmp = reg("ytmp", [128, 512], BF16, 127232)
    ytmp2 = reg("ytmp2", [128, 512], BF16, 128256)

    BAR12 = [(V, p.cnt[V]), (A, p.cnt[A]), (T, p.cnt[T]), (CC, p.cnt[CC])]
    for _e in ENGS:
        p.wait(_e, BAR12)

    oi = p.dma("gpsimd", segL.ap()[1:2, 0, :], ones_big[0:1, 0:1024], sig=GQ[0])
    oi = p.dma("gpsimd", segL.ap()[1:2, 1, :], ones_big[0:1, 0:1024], sig=GQ[0])
    oi = p.dma("gpsimd", segR.ap()[0:1, 0, :], ones_big[0:1, 0:1024], sig=GQ[0])
    oi = p.dma("gpsimd", segR.ap()[0:1, 1, :], ones_big[0:1, 0:1024], sig=GQ[0])
    SEG_ONES = oi

    XST3_FREE = [None, None]
    SEG_FREE = [None, None]
    SEGEXP_FREE = [None, None]
    ZST3_FREE = None
    STST3_FREE = None
    YSB_FREE = None
    hT_read = None
    hT_upd = (V, HIN_OK)
    SCOR_FREE = None
    YTMP_FREE = None
    YTMP2_FREE = None
    ygfm_marks = []
    yMM_SEM_Q = [SQ[9], SQ[15]]

    for c in range(NCH if PH_LEVEL >= 3 else 0):
        xb = c % 2
        if c >= PH3_CHUNKS:
            break
        wx = list(XST3_FREE[xb]) if XST3_FREE[xb] else []
        cx3 = p.dma("sync", xst3.ap()[:, xb, :], xfm_sp[0:4096, c * 128:(c + 1) * 128],
                    waits=wx, sig=SQ[4 + 5 * xb], transpose=True)
        xsem = SQ[4 + 5 * xb]
        wz = [ZST3_FREE] if ZST3_FREE else []
        cz = p.dma("sync", zst3.ap()[:, :], z_sp[0:4096, c * 128:(c + 1) * 128],
                   waits=wz + (Z_OK if c == 0 else []), sig=SQ[1],
                   transpose=True)
        wst = [STST3_FREE] if STST3_FREE else []
        cst = p.dma("sync", stst3.ap()[:, :], st_sp[c, :, :],
                    waits=wst + ([STSP_OK] if c == 0 else []), sig=SQ[6])
        mmcb = p.emit("tensor", (lambda c=c: nc.tensor.matmul(
            PB[4].ap()[:, 0:128], b_fm.ap()[:, c * 128:(c + 1) * 128],
            c_fm.ap()[:, c * 128:(c + 1) * 128], start=True, stop=True)),
            waits=bank_wait(4), sig=T)
        vcb = p.emit("vector", lambda: nc.vector.tensor_mul(
            cbm.ap()[:, :], PB[4].ap()[:, 0:128], maskT_s.ap()[:, :]),
            waits=[(T, mmcb)], sig=V)
        bank_free[4] = (V, vcb)
        seg_ld = {}
        for hg in range(8):
            e = hg
            sb = e % 2
            wsg = list(SEG_FREE[sb]) if SEG_FREE[sb] else []
            if c == 0 and e < 2:
                wsg.append((GQ[0], SEG_ONES))
            g1 = p.dma("gpsimd", segL.ap()[0:1, sb, :],
                       fl_negrp[0:1, (c * 64 + e * 8) * 128:(c * 64 + e * 8 + 8) * 128],
                       waits=wsg + [(GQ[2], FLATS_OK)], sig=GQ[0])
            g2 = p.dma("gpsimd", segR.ap()[1:2, sb, :],
                       fl_r[0:1, (c * 64 + e * 8) * 128:(c * 64 + e * 8 + 8) * 128],
                       waits=list(SEG_FREE[sb]) if SEG_FREE[sb] else [], sig=GQ[1])
            seg_ld[e] = [(GQ[0], g1), (GQ[1], g2)]
            segmm = {}
            for q4 in range(2):
                pp = (hg * 2 + q4) % 2
                mm = None
                for i in range(4):
                    hh = q4 * 4 + i
                    mm = p.emit("tensor", (lambda sb=sb, hh=hh, i=i, pp=pp:
                                nc.tensor.matmul(
                                    PB[pp].ap()[:, i * 128:(i + 1) * 128],
                                    segL.ap()[0:2, sb, hh * 128:(hh + 1) * 128],
                                    segR.ap()[0:2, sb, hh * 128:(hh + 1) * 128],
                                    start=(i == 0), stop=(i == 3))),
                                waits=(seg_ld[e] + bank_wait(pp)) if i == 0 else [],
                                sig=(T if i == 3 else None))
                eb = (hg * 2 + q4) % 2
                wse = [SEGEXP_FREE[eb]] if SEGEXP_FREE[eb] else []
                vmn = p.emit("vector", (lambda eb=eb, pp=pp: nc.vector.tensor_scalar_min(
                    segexp.ap()[:, eb, :], PB[pp].ap()[:, :], 30.0)),
                    waits=[(T, mm)] + wse, sig=V)
                bank_free[pp] = (V, vmn)
                aex = p.emit("scalar", (lambda eb=eb: nc.scalar.activation(
                    segexp.ap()[:, eb, :], segexp.ap()[:, eb, :], AF.Exp)),
                    waits=[(V, vmn)], sig=A)
                vsc = None
                for i in range(4):
                    hh = q4 * 4 + i
                    slot = q4 * 4 + i
                    vsc = p.emit("vector", (lambda eb=eb, i=i, slot=slot: nc.vector.tensor_mul(
                        scor.ap()[:, slot, :], segexp.ap()[:, eb, i * 128:(i + 1) * 128],
                        cbm.ap()[:, :])),
                        waits=([(A, aex), (V, vcb)] +
                               ([SCOR_FREE] if SCOR_FREE else [])) if i == 0 else [],
                        sig=V)
                SEGEXP_FREE[eb] = (V, vsc)
                segmm[q4] = vsc
            mmd = None
            for i in range(8):
                h = hg * 8 + i
                slot = i
                mmd = p.emit("tensor", (lambda h=h, i=i, slot=slot, xb=xb:
                            nc.tensor.matmul(
                                PB[2].ap()[:, i * 64:(i + 1) * 64],
                                scor.ap()[:, slot, :],
                                xst3.ap()[:, xb, h * 64:(h + 1) * 64],
                                start=(i == 0), stop=(i == 7))),
                            waits=([(V, segmm[1]), (xsem, cx3)] + bank_wait(2))
                                  if i == 0 else [],
                            sig=(T if i == 7 else None))
            mmo = None
            for i in range(8):
                h = hg * 8 + i
                mmo = p.emit("tensor", (lambda h=h, i=i, c=c: nc.tensor.matmul(
                            PB[3].ap()[:, i * 64:(i + 1) * 64],
                            c_fm.ap()[:, c * 128:(c + 1) * 128],
                            hT.ap()[:, h * 64:(h + 1) * 64],
                            start=(i == 0), stop=(i == 7))),
                            waits=(list(hT_upd if isinstance(hT_upd, list) else [hT_upd])
                                   + bank_wait(3)) if i == 0 else [],
                            sig=(T if i == 7 else None))
            hT_read = (T, mmo)
            aev = None
            for i in range(8):
                h = hg * 8 + i
                aev = p.emit("scalar", (lambda h=h, i=i, c=c: nc.scalar.activation(
                    ytmp.ap()[:, i * 64:(i + 1) * 64], PB[3].ap()[:, i * 64:(i + 1) * 64],
                    AF.Copy, scale=expf_tm.ap()[:, c, h:h + 1])),
                    waits=([(T, mmo)] + ([YTMP_FREE] if YTMP_FREE else []))
                          if i == 0 else [], sig=A)
            wyb = [YSB_FREE] if (YSB_FREE and hg == 0) else []
            v2 = p.emit("vector", (lambda hg=hg, xb=xb: nc.vector.tensor_mul(
                ytmp2.ap()[:, :], xst3.ap()[:, xb, hg * 512:(hg + 1) * 512],
                dpb.ap()[:, hg * 512:(hg + 1) * 512])),
                waits=[(V, DPB_OK)] + ([YTMP2_FREE] if YTMP2_FREE else []), sig=V)
            v1 = p.emit("vector", (lambda hg=hg: nc.vector.tensor_add(
                ysb.ap()[:, hg * 512:(hg + 1) * 512], PB[2].ap()[:, :], ytmp.ap()[:, :])),
                waits=[(A, aev), (T, mmd)] + wyb, sig=V)
            bank_free[2] = (V, v1)
            bank_free[3] = (A, aev)
            YTMP_FREE = (V, v1)
            v3 = p.emit("vector", (lambda hg=hg: nc.vector.tensor_add(
                ysb.ap()[:, hg * 512:(hg + 1) * 512],
                ysb.ap()[:, hg * 512:(hg + 1) * 512], ytmp2.ap()[:, :])), sig=V)
            YTMP2_FREE = (V, v3)
            SEG_FREE[sb] = [(T, mmd)]
            SCOR_FREE = (T, mmd)
        if DEBUG_PH3 and c == 7:
            tb9 = p.dma("gpsimd", y_dbg[:, :], hT.ap()[:, :],
                        waits=list(hT_upd if isinstance(hT_upd, list) else [hT_upd]),
                        sig=GQ[3])
            for _e in ENGS:
                p.wait(_e, [(GQ[3], tb9)])
        vrec = None
        for h in range(NH):
            vrec = p.emit("vector", (lambda h=h, c=c: nc.vector.scalar_tensor_tensor(
                hT.ap()[:, h * 64:(h + 1) * 64], hT.ap()[:, h * 64:(h + 1) * 64],
                dcol.ap()[:, c * 64 + h:c * 64 + h + 1],
                stst3.ap()[:, h * 64:(h + 1) * 64], ALU.mult, ALU.add)),
                waits=[hT_read, (SQ[6], cst)] if h == 0 else [], sig=V)
        hT_upd = (V, vrec)
        STST3_FREE = (V, vrec)
        asz = p.emit("scalar", (lambda: nc.scalar.activation(
            zst3.ap()[:, :], zst3.ap()[:, :], AF.Silu)),
            waits=[(SQ[1], cz)], sig=A)
        if DEBUG_PH3 and c == 0:
            tbz = p.dma("gpsimd", cb_dbg[0:128, 0:128], zst3.ap()[:, 0:128],
                        waits=[(A, asz)], sig=GQ[3])
            for _e in ENGS:
                p.wait(_e, [(GQ[3], tbz)])
        vg = p.emit("vector", lambda: nc.vector.tensor_mul(
            ysb.ap()[:, :], ysb.ap()[:, :], zst3.ap()[:, :]),
            waits=[(A, asz), (V, p.cnt[V])], sig=V)
        p.emit("scalar", (lambda: nc.scalar.activation(
            zst3.ap()[:, :], ysb.ap()[:, :], AF.Square,
            accum_out=smalls.ap()[:, 60:61])), waits=[(V, vg)])
        asq2 = p.emit("scalar", lambda: nc.scalar.copy(
            smalls.ap()[:, 61:62], smalls.ap()[:, 60:61]), sig=A)
        ZST3_FREE = (A, asq2)
        ast2 = p.emit("scalar", lambda: nc.scalar.activation(
            smalls.ap()[:, 62:63], smalls.ap()[:, 60:61], AF.Sqrt,
            bias=smalls.ap()[:, 54:55], scale=smalls.ap()[:, 55:56]),
            waits=[(A, asq2)], sig=A)
        vrc = p.emit("vector", lambda: nc.vector.reciprocal(
            smalls.ap()[:, 63:64], smalls.ap()[:, 62:63]), waits=[(A, ast2)], sig=V)
        if DEBUG_PH3 and c == 0:
            tbm = p.dma("gpsimd", sm_dbg[:, :], smalls.ap()[:, :],
                        waits=[(V, vrc)], sig=GQ[3])
            tbm = p.dma("gpsimd", u_dbg[0:128, 0:2048],
                        ysb.ap()[:, 0:2048], waits=[(V, vg)], sig=GQ[3])
            for _e in ENGS:
                p.wait(_e, [(GQ[3], tbm)])
        p.vspace(nc)
        vsc2 = p.emit("vector", lambda: nc.vector.tensor_single_scalar(
            ysb.ap()[:, :], ysb.ap()[:, :], smalls.ap()[:, 63:64], ALU.mult),
            waits=[(V, vrc)], sig=V)
        if DEBUG_PH3 and c == 0:
            tb8 = p.dma("gpsimd", sc_dbg[0:128, 0:4096], ysb.ap()[:, :],
                        waits=[(V, vsc2)], sig=GQ[3])
            for _e in ENGS:
                p.wait(_e, [(GQ[3], tb8)])
        for kt in range(32):
            ptb = kt % 2
            wpt = [PT_FREE[ptb]] if PT_FREE[ptb] else []
            tm = p.emit("tensor", (lambda kt=kt, ptb=ptb: nc.tensor.transpose(
                PT[ptb].ap()[:, 0:128], ysb.ap()[:, kt * 128:(kt + 1) * 128],
                idb_s.ap()[:, :])), waits=[(V, vsc2)] + wpt, sig=T)
            if kt % 2 == 0:
                ev = p.emit("vector", (lambda kt=kt, c=c, ptb=ptb: nc.vector.tensor_copy(
                    ygfm.ap()[:, kt, c * 128:(c + 1) * 128], PT[ptb].ap()[:, 0:128])),
                    waits=[(T, tm)], sig=V)
                PT_FREE[ptb] = (V, ev)
            else:
                ev = p.emit("scalar", (lambda kt=kt, c=c, ptb=ptb: nc.scalar.copy(
                    ygfm.ap()[:, kt, c * 128:(c + 1) * 128], PT[ptb].ap()[:, 0:128])),
                    waits=[(T, tm)], sig=A)
                PT_FREE[ptb] = (A, ev)
        YSB_FREE = (T, p.cnt[T])
        XST3_FREE[xb] = [(T, p.cnt[T]), (V, p.cnt[V])]
    YGFM_OK = [(V, p.cnt[V]), (A, p.cnt[A])]
    ygfm_dbg = internal("ygfm_dbg", [128, 32 * Q], BF16)
    woc_dbg = internal("woc_dbg", [128, 32 * 512], BF16)
    if DEBUG_PH3:
        tbg = p.dma("gpsimd", ygfm_dbg[:, :],
                    ygfm.ap()[:, :, :].rearrange("p a b -> p (a b)"),
                    waits=YGFM_OK, sig=GQ[3])
        for _e in ENGS:
            p.wait(_e, [(GQ[3], tbg)])

    PH3B_ON = PH_LEVEL >= 4 and PH3_CHUNKS >= 8
    woc = reg("woc", [128, 32, 512], BF16, 98304)
    atst = reg("atst", [128, 2, 512], BF16, 131072)
    BAR3B = [(V, p.cnt[V]), (A, p.cnt[A]), (T, p.cnt[T])]
    for _e in ENGS:
        p.wait(_e, BAR3B)
    ATST_FREE = [None, None]
    WOSEMS = [SQ[2], SQ[7], SQ[11], SQ[12]]
    at_done = None
    for dc in range(4 if PH3B_ON else 0):
        lw = []
        for k in range(32):
            wsem = WOSEMS[k % 4]
            wwk = [(T, p.cnt[T])] if (dc > 0 and k < 4) else []
            lwv = p.dma("sync", woc.ap()[:, k, :],
                        wout_bf[k * 128:(k + 1) * 128, dc * 512:(dc + 1) * 512],
                        waits=wwk + [(CC, AG_WOUT)], sig=wsem)
            lw.append((wsem, lwv))
        lw = [(sm, max(vv for sm2, vv in lw if sm2 == sm)) for sm in set(s0 for s0, _ in lw)]
        if DEBUG_PH3 and dc == 0:
            tbw = p.dma("gpsimd", woc_dbg[:, :],
                        woc.ap()[:, :, :].rearrange("p a b -> p (a b)"),
                        waits=lw, sig=GQ[3])
            for _e in ENGS:
                p.wait(_e, [(GQ[3], tbw)])
        for tt in range(8):
            pp = tt % 2
            mm = None
            for k in range(32):
                mm = p.emit("tensor", (lambda tt=tt, k=k, pp=pp: nc.tensor.matmul(
                    PB[pp].ap()[:, :], ygfm.ap()[:, k, tt * 128:(tt + 1) * 128],
                    woc.ap()[:, k, :], start=(k == 0), stop=(k == 31))),
                    waits=(lw + YGFM_OK + bank_wait(pp)) if k == 0 else [],
                    sig=(T if k == 31 else None))
            ab = tt % 2
            wat = [ATST_FREE[ab]] if ATST_FREE[ab] else []
            ve = p.emit("vector", (lambda ab=ab, pp=pp: nc.vector.tensor_copy(
                atst.ap()[:, ab, :], PB[pp].ap()[:, :])),
                waits=[(T, mm)] + wat, sig=V)
            bank_free[pp] = (V, ve)
            ad = p.dma("sync", attn_sp[tt * 128:(tt + 1) * 128, dc * 512:(dc + 1) * 512],
                       atst.ap()[:, ab, :], waits=[(V, ve)], sig=SQ[3 + 5 * ab])
            ATST_FREE[ab] = (SQ[3 + 5 * ab], ad)
            at_done = (SQ[3 + 5 * ab], ad)
    AT_ALL = [(SQ[3], p.cnt[SQ[3]]), (SQ[8], p.cnt[SQ[8]])]

    ast3 = reg("ast3", [128, 2048], BF16, 98304)
    ust2 = reg("ust2", [128, 2048], F32, 102400)
    rsb = reg("rsb", [128, 2048], F32, 110592)
    x3tm = reg("x3tm", [128, 2048], BF16, 118784)
    x3fm = reg("x3fm", [128, 16, Q], BF16, 65536)
    BAR3C = [(V, p.cnt[V]), (A, p.cnt[A]), (T, p.cnt[T]),
             (SQ[3], p.cnt[SQ[3]]), (SQ[8], p.cnt[SQ[8]])]
    for _e in ENGS:
        p.wait(_e, BAR3C)
    RS_FREE = None
    for tt in range(8 if PH3B_ON else 0):
        ca = p.dma("sync", ast3.ap()[:, :], attn_sp[tt * 128:(tt + 1) * 128, :],
                   waits=AT_ALL, sig=SQ[1])
        cu2 = p.dma("sync", ust2.ap()[:, :], u_ext[3 + tt * 128:3 + (tt + 1) * 128, :],
                    sig=SQ[6])

        def lnstats(src_ap, wsrc, bb, width=2048):
            vs = p.emit("vector", (lambda: nc.vector.reduce_sum(
                smalls.ap()[:, bb + 0:bb + 1], src_ap, axis=mybir.AxisListType.X)),
                waits=wsrc, sig=V)
            p.emit("scalar", (lambda: nc.scalar.activation(
                x3tm.ap()[:, :], src_ap, AF.Square,
                accum_out=smalls.ap()[:, bb + 1:bb + 2])), waits=wsrc)
            af = p.emit("scalar", (lambda: nc.scalar.copy(
                smalls.ap()[:, bb + 7:bb + 8], smalls.ap()[:, bb + 1:bb + 2])), sig=A)
            p.vspace(nc)
            v1 = p.emit("vector", (lambda: nc.vector.tensor_scalar_mul(
                smalls.ap()[:, bb + 2:bb + 3], smalls.ap()[:, bb + 0:bb + 1],
                1.0 / width)), waits=[(V, vs)], sig=V)
            v2 = p.emit("vector", (lambda: nc.vector.tensor_scalar_mul(
                smalls.ap()[:, bb + 3:bb + 4], smalls.ap()[:, bb + 1:bb + 2],
                1.0 / width)), waits=[(A, af)], sig=V)
            v3 = p.emit("vector", (lambda: nc.vector.tensor_single_scalar(
                smalls.ap()[:, bb + 4:bb + 5], smalls.ap()[:, bb + 2:bb + 3],
                smalls.ap()[:, bb + 2:bb + 3], ALU.mult)), sig=V)
            p.vspace(nc)
            v4 = p.emit("vector", (lambda: nc.vector.tensor_single_scalar(
                smalls.ap()[:, bb + 3:bb + 4], smalls.ap()[:, bb + 3:bb + 4],
                smalls.ap()[:, bb + 4:bb + 5], ALU.subtract)), sig=V)
            a2 = p.emit("scalar", (lambda: nc.scalar.activation(
                smalls.ap()[:, bb + 5:bb + 6], smalls.ap()[:, bb + 3:bb + 4],
                AF.Sqrt, bias=smalls.ap()[:, 54:55])), waits=[(V, v4)], sig=A)
            v5 = p.emit("vector", (lambda: nc.vector.reciprocal(
                smalls.ap()[:, bb + 6:bb + 7], smalls.ap()[:, bb + 5:bb + 6])),
                waits=[(A, a2)], sig=V)
            p.vspace(nc)
            return v5

        v5 = lnstats(ast3.ap()[:, :], [(SQ[1], ca)], 0)
        vno = p.emit("vector", (lambda: nc.vector.tensor_scalar(
            rsb.ap()[:, :], ast3.ap()[:, :], smalls.ap()[:, 2:3],
            smalls.ap()[:, 6:7], ALU.subtract, ALU.mult)),
            waits=[(V, v5)] + ([RS_FREE] if RS_FREE else []), sig=V)
        vra = p.emit("vector", (lambda: nc.vector.tensor_add(
            rsb.ap()[:, :], rsb.ap()[:, :], ust2.ap()[:, :])),
            waits=[(SQ[6], cu2)], sig=V)
        rd = p.dma("sync", resid_sp[tt * 128:(tt + 1) * 128, :], rsb.ap()[:, :],
                   waits=[(V, vra)], sig=SQ[10])
        v5b = lnstats(rsb.ap()[:, :], [(V, vra)], 16)
        vn3 = p.emit("vector", (lambda: nc.vector.tensor_scalar(
            x3tm.ap()[:, :], rsb.ap()[:, :], smalls.ap()[:, 18:19],
            smalls.ap()[:, 22:23], ALU.subtract, ALU.mult)),
            waits=[(V, v5b)], sig=V)
        RS_FREE = (SQ[10], rd)
        for k in range(16):
            ptb = k % 2
            wpt = [PT_FREE[ptb]] if PT_FREE[ptb] else []
            tm = p.emit("tensor", (lambda k=k, ptb=ptb: nc.tensor.transpose(
                PT[ptb].ap()[:, 0:128], x3tm.ap()[:, k * 128:(k + 1) * 128],
                idb_s.ap()[:, :])), waits=[(V, vn3)] + wpt, sig=T)
            if k % 2 == 0:
                ev = p.emit("vector", (lambda k=k, tt=tt, ptb=ptb: nc.vector.tensor_copy(
                    x3fm.ap()[:, k, tt * 128:(tt + 1) * 128], PT[ptb].ap()[:, 0:128])),
                    waits=[(T, tm)], sig=V)
                PT_FREE[ptb] = (V, ev)
            else:
                ev = p.emit("scalar", (lambda k=k, tt=tt, ptb=ptb: nc.scalar.copy(
                    x3fm.ap()[:, k, tt * 128:(tt + 1) * 128], PT[ptb].ap()[:, 0:128])),
                    waits=[(T, tm)], sig=A)
                PT_FREE[ptb] = (A, ev)
    X3FM_OK = [(V, p.cnt[V]), (A, p.cnt[A])]
    RESID_OK = [(SQ[10], p.cnt[SQ[10]])]

    h1 = reg("h1", [128, 64, 512], BF16, 0)
    w1c = reg("w1c", [128, 16, 512], BF16, 98304)
    w3c = reg("w3c", [128, 16, 512], BF16, 114688)
    sg4 = reg("sg4", [128, 512], BF16, 131072)
    hid4 = reg("hid4", [128, 512], BF16, 132096)
    w2c = reg("w2c", [128, 2, 512], BF16, 133120)
    ffnst = reg("ffnst", [128, 4, 4, 512], BF16, 135168)
    rsb2 = reg("rsb2", [128, 2048], F32, 98304)
    ostage = reg("ostage", [128, 2, 2048], F32, 106496)
    BAR4 = [(V, p.cnt[V]), (A, p.cnt[A]), (T, p.cnt[T])]
    for _e in ENGS:
        p.wait(_e, BAR4)
    W1SEMS = [SQ[2], SQ[7]]
    W3SEMS = [SQ[11], SQ[12]]
    W2SEMS = [SQ[4], SQ[9]]
    OUT_SEMS = [SQ[3], SQ[8]]
    H1_FREE = None
    FF_FREE = None
    out_last = []
    for th in range(2 if PH_LEVEL >= 5 and PH3B_ON else 0):
        if th > 0:
            p.wait("sync", [(OUT_SEMS[0], p.cnt[OUT_SEMS[0]]),
                            (OUT_SEMS[1], p.cnt[OUT_SEMS[1]])])
        for ug in range(16):
            lw1 = []
            lw3 = []
            for k in range(16):
                wwk = [(T, p.cnt[T])] if (k < 2 and (ug > 0 or th > 0)) else []
                l1 = p.dma("sync", w1c.ap()[:, k, :],
                           w1_bf[k * 128:(k + 1) * 128, ug * 512:(ug + 1) * 512],
                           waits=wwk + [(CC, AG_W1)], sig=W1SEMS[k % 2])
                lw1.append((W1SEMS[k % 2], l1))
                l3 = p.dma("sync", w3c.ap()[:, k, :],
                           w3_bf[k * 128:(k + 1) * 128, ug * 512:(ug + 1) * 512],
                           waits=wwk + [(CC, AG_W3)], sig=W3SEMS[k % 2])
                lw3.append((W3SEMS[k % 2], l3))
            lw1 = [(sm, max(v for s2, v in lw1 if s2 == sm)) for sm in set(s for s, _ in lw1)]
            lw3 = [(sm, max(v for s2, v in lw3 if s2 == sm)) for sm in set(s for s, _ in lw3)]
            for ut in range(4):
                u = ug * 4 + ut
                mm1 = None
                for k in range(16):
                    mm1 = p.emit("tensor", (lambda ut=ut, k=k, th=th: nc.tensor.matmul(
                        PB[0].ap()[:, :], w1c.ap()[:, k, ut * 128:(ut + 1) * 128],
                        x3fm.ap()[:, k, th * 512:(th + 1) * 512],
                        start=(k == 0), stop=(k == 15))),
                        waits=(lw1 + X3FM_OK + bank_wait(0)) if k == 0 else [],
                        sig=(T if k == 15 else None))
                mm3 = None
                for k in range(16):
                    mm3 = p.emit("tensor", (lambda ut=ut, k=k, th=th: nc.tensor.matmul(
                        PB[1].ap()[:, :], w3c.ap()[:, k, ut * 128:(ut + 1) * 128],
                        x3fm.ap()[:, k, th * 512:(th + 1) * 512],
                        start=(k == 0), stop=(k == 15))),
                        waits=(lw3 + bank_wait(1)) if k == 0 else [],
                        sig=(T if k == 15 else None))
                ags = p.emit("scalar", (lambda: nc.scalar.activation(
                    sg4.ap()[:, :], PB[0].ap()[:, :], AF.Silu)),
                    waits=[(T, mm1), (A, p.cnt[A])], sig=A)
                bank_free[0] = (A, ags)
                vhd = p.emit("vector", (lambda: nc.vector.tensor_mul(
                    hid4.ap()[:, :], sg4.ap()[:, :], PB[1].ap()[:, :])),
                    waits=[(A, ags), (T, mm3), (V, p.cnt[V])], sig=V)
                bank_free[1] = (V, vhd)
                mmh = p.emit("tensor", (lambda: nc.tensor.matmul(
                    PB[2].ap()[:, :], h128_s.ap()[:, :], hid4.ap()[:, :],
                    start=True, stop=True)),
                    waits=[(V, vhd)] + bank_wait(2), sig=T)
                wh1 = [H1_FREE] if (H1_FREE and th == 1 and ug == 0 and ut == 0) else []
                ev = p.emit("vector", (lambda u=u: nc.vector.tensor_copy(
                    h1.ap()[:, u, :], PB[2].ap()[:, :])),
                    waits=[(T, mmh)] + wh1, sig=V)
                bank_free[2] = (V, ev)
        H1_OK = (V, p.cnt[V])
        w2free = [None, None]
        for dc in range(4):
            for u in range(64):
                wb2 = u % 2
                ww2 = ([w2free[wb2]] if w2free[wb2] else []) + [(CC, AG_W2)]
                p.dma("sync", w2c.ap()[:, wb2, 0:256],
                      w2pp_bf[(2 * dc) * 8192 + u * 128:(2 * dc) * 8192 + (u + 1) * 128, :],
                      waits=ww2, sig=W2SEMS[wb2])
                l2 = p.dma("sync", w2c.ap()[:, wb2, 256:512],
                           w2pp_bf[(2 * dc + 1) * 8192 + u * 128:(2 * dc + 1) * 8192 + (u + 1) * 128, :],
                           sig=W2SEMS[wb2])
                mmlast = None
                for tt4 in range(4):
                    mmlast = p.emit("tensor", (lambda u=u, tt4=tt4, wb2=wb2:
                                nc.tensor.matmul(
                                    PB[tt4].ap()[:, :],
                                    h1.ap()[:, u, tt4 * 128:(tt4 + 1) * 128],
                                    w2c.ap()[:, wb2, :],
                                    start=(u == 0), stop=(u == 63))),
                                waits=[(W2SEMS[wb2], l2), H1_OK] +
                                      (bank_wait(tt4) if u == 0 else []),
                                sig=(T if (tt4 == 3) else None))
                w2free[wb2] = (T, mmlast)
            for tt4 in range(4):
                wff = [FF_FREE] if (FF_FREE and dc == 0 and tt4 == 0) else []
                ev = p.emit("vector", (lambda tt4=tt4, dc=dc: nc.vector.tensor_copy(
                    ffnst.ap()[:, tt4, dc, :], PB[tt4].ap()[:, :])),
                    waits=[(T, p.cnt[T])] + wff, sig=V)
                bank_free[tt4] = (V, ev)
        FF_OK = (V, p.cnt[V])
        for tt4 in range(4):
            tt = th * 4 + tt4
            crr = p.dma("sync", rsb2.ap()[:, :], resid_sp[tt * 128:(tt + 1) * 128, :],
                        waits=RESID_OK + [(V, p.cnt[V])], sig=SQ[1])
            src_ap = ffnst.ap()[:, tt4, :, :].rearrange("p a b -> p (a b)")
            vs = p.emit("vector", (lambda src_ap=src_ap: nc.vector.reduce_sum(
                smalls.ap()[:, 0:1], src_ap, axis=mybir.AxisListType.X)),
                waits=[FF_OK], sig=V)
            p.emit("scalar", (lambda src_ap=src_ap: nc.scalar.activation(
                sg4.ap()[:, :], src_ap[:, 0:512], AF.Square,
                accum_out=smalls.ap()[:, 70:71])), waits=[FF_OK])
            for qq in range(1, 4):
                p.emit("scalar", (lambda src_ap=src_ap, qq=qq: nc.scalar.activation(
                    sg4.ap()[:, :], src_ap[:, qq * 512:(qq + 1) * 512], AF.Square,
                    accum_out=smalls.ap()[:, 70 + qq:71 + qq])))
            af = p.emit("scalar", (lambda: nc.scalar.copy(
                smalls.ap()[:, 74:75], smalls.ap()[:, 70:71])), sig=A)
            vq = p.emit("vector", (lambda: nc.vector.tensor_single_scalar(
                smalls.ap()[:, 1:2], smalls.ap()[:, 70:71], smalls.ap()[:, 71:72],
                ALU.add)), waits=[(A, af)], sig=V)
            p.vspace(nc)
            vq = p.emit("vector", (lambda: nc.vector.tensor_single_scalar(
                smalls.ap()[:, 1:2], smalls.ap()[:, 1:2], smalls.ap()[:, 72:73],
                ALU.add)), sig=V)
            p.vspace(nc)
            vq = p.emit("vector", (lambda: nc.vector.tensor_single_scalar(
                smalls.ap()[:, 1:2], smalls.ap()[:, 1:2], smalls.ap()[:, 73:74],
                ALU.add)), sig=V)
            v1 = p.emit("vector", (lambda: nc.vector.tensor_scalar_mul(
                smalls.ap()[:, 2:3], smalls.ap()[:, 0:1], 1.0 / 2048.0)), sig=V)
            v2 = p.emit("vector", (lambda: nc.vector.tensor_scalar_mul(
                smalls.ap()[:, 3:4], smalls.ap()[:, 1:2], 1.0 / 2048.0)), sig=V)
            v3 = p.emit("vector", (lambda: nc.vector.tensor_single_scalar(
                smalls.ap()[:, 4:5], smalls.ap()[:, 2:3], smalls.ap()[:, 2:3],
                ALU.mult)), sig=V)
            p.vspace(nc)
            v4 = p.emit("vector", (lambda: nc.vector.tensor_single_scalar(
                smalls.ap()[:, 3:4], smalls.ap()[:, 3:4], smalls.ap()[:, 4:5],
                ALU.subtract)), sig=V)
            a2 = p.emit("scalar", (lambda: nc.scalar.activation(
                smalls.ap()[:, 5:6], smalls.ap()[:, 3:4], AF.Sqrt,
                bias=smalls.ap()[:, 54:55])), waits=[(V, v4)], sig=A)
            v5 = p.emit("vector", (lambda: nc.vector.reciprocal(
                smalls.ap()[:, 6:7], smalls.ap()[:, 5:6])), waits=[(A, a2)], sig=V)
            p.vspace(nc)
            ob = tt4 % 2
            vno = p.emit("vector", (lambda src_ap=src_ap, ob=ob: nc.vector.tensor_scalar(
                ostage.ap()[:, ob, :], src_ap, smalls.ap()[:, 2:3],
                smalls.ap()[:, 6:7], ALU.subtract, ALU.mult)),
                waits=[(V, v5)] + ([(OUT_SEMS[ob], out_last[-2][1])]
                                   if len(out_last) >= 2 else []), sig=V)
            voa = p.emit("vector", (lambda ob=ob: nc.vector.tensor_add(
                ostage.ap()[:, ob, :], ostage.ap()[:, ob, :], rsb2.ap()[:, :])),
                waits=[(SQ[1], crr)], sig=V)
            od = p.dma("sync", out_h[tt * 128:(tt + 1) * 128, :], ostage.ap()[:, ob, :],
                       waits=[(V, voa)], sig=OUT_SEMS[ob])
            out_last.append((OUT_SEMS[ob], od))
        FF_FREE = (V, p.cnt[V])
        H1_FREE = (T, p.cnt[T])

    tapn = []
    internal_map = dict(win_bf=win_bf, wout_bf=wout_bf, w1_bf=w1_bf, w3_bf=w3_bf,
                        w2pp_bf=w2pp_bf, z_sp=z_sp, xfm_sp=xfm_sp, st_sp=st_sp,
                        resid_sp=resid_sp, fl_negrp=fl_negrp, fl_r=fl_r,
                        cc_st_out=cc_st_out, cc_ld_out=cc_ld_out, hin_sp=hin_sp,
                        ln1_dbg=ln1_dbg, u_dbg=u_dbg, sm_dbg=sm_dbg, lt_dbg=lt_dbg,
                        attn_sp=attn_sp, y_dbg=y_dbg, sc_dbg=sc_dbg, cb_dbg=cb_dbg,
                        ygfm_dbg=ygfm_dbg, woc_dbg=woc_dbg)
    p.wait("sync", [(nm, p.cnt[nm]) for nm in list(p.cnt)])
    final = None
    for name, shape, dt in debug_taps:
        src = internal_map[name]
        dst = nc.dram_tensor("dbg_" + name, list(shape), dt, kind="ExternalOutput").ap()
        final = p.dma("sync", dst[:, :] if len(shape) == 2 else dst[:, :, :],
                      src[:, :] if len(shape) == 2 else src[:, :, :], sig=SQ[5])
        tapn.append("dbg_" + name)


    with nc.Block() as block:
        @block.sync
        def _(e):
            p.replay("sync", e)

        @block.gpsimd
        def _(e):
            p.replay("gpsimd", e)

        @block.tensor
        def _(e):
            p.replay("tensor", e)

        @block.vector
        def _(e):
            p.replay("vector", e)

        @block.scalar
        def _(e):
            p.replay("scalar", e)

    return nc, tapn



def make_in_maps(inp):
    u = np.asarray(inp["L0_attn_u"], np.float32)
    W_in = np.ascontiguousarray(np.asarray(inp["L0_attn_W_in"], np.float32))
    conv_w = np.asarray(inp["L0_attn_conv_w"], np.float32)
    conv_b = np.asarray(inp["L0_attn_conv_b"], np.float32)
    dt_bias = np.asarray(inp["L0_attn_dt_bias"], np.float32)
    A_log = np.asarray(inp["L0_attn_A_log"], np.float32)
    Dp = np.asarray(inp["L0_attn_Dp"], np.float32)
    norm_w = np.asarray(inp["L0_attn_norm_w"], np.float32)
    W_out = np.ascontiguousarray(np.asarray(inp["L0_attn_W_out"], np.float32))
    W1 = np.ascontiguousarray(np.asarray(inp["L0_ffn_W1"], np.float32))
    W2 = np.ascontiguousarray(np.asarray(inp["L0_ffn_W2"], np.float32))
    W3 = np.ascontiguousarray(np.asarray(inp["L0_ffn_W3"], np.float32))

    convw_t = np.ascontiguousarray(conv_w.reshape(34, 128, 4).transpose(1, 0, 2))
    convb_t = np.ascontiguousarray(conv_b.reshape(34, 128).T)
    dtb_row = np.zeros((1, 128), np.float32)
    dtb_row[0, :64] = dt_bias
    alog_col = np.ascontiguousarray(A_log.reshape(64, 1))
    dp_rep = np.ascontiguousarray(np.repeat(Dp, HP).reshape(1, DI))
    nw_row = np.ascontiguousarray(norm_w.reshape(1, DI))

    WinT = np.zeros((DM, OPAD), np.float32)
    WinT[:, :OPROJ] = W_in.T
    WoutT = np.ascontiguousarray(W_out.T)
    W1T = np.ascontiguousarray(W1.T)
    W3T = np.ascontiguousarray(W3.T)
    maps = []
    for c in range(8):
        b, q = divmod(c, 4)
        t0 = q * Q
        ue = np.zeros((EXT, DM), np.float32)
        lo = max(0, t0 - 3)
        ue[3 - (t0 - lo):] = u[b, lo:t0 + Q]
        selm = np.zeros((4, 3), np.float32)
        selv = np.zeros((3, 1), np.float32)
        for s in range(3):
            if s < q:
                selv[s, 0] = 1.0
                for k in range(s + 1, q):
                    selm[k, s] = 1.0
        nwt = np.ascontiguousarray(
            norm_w[512 * c:512 * (c + 1)].reshape(4, 128).T)
        maps.append(dict(
            u_ext=np.ascontiguousarray(ue),
            win_sl=np.ascontiguousarray(WinT[256 * c:256 * (c + 1)]),
            wout_sl=np.ascontiguousarray(WoutT[512 * c:512 * (c + 1)]),
            w1_sl=np.ascontiguousarray(W1T[256 * c:256 * (c + 1)]),
            w3_sl=np.ascontiguousarray(W3T[256 * c:256 * (c + 1)]),
            w2_sl=np.ascontiguousarray(W2[256 * c:256 * (c + 1)]),
            convw_t=convw_t, convb_t=convb_t, dtb_row=dtb_row,
            alog_col=alog_col, dp_rep=dp_rep, nw_row=nw_row,
            nw_t=nwt, selm=selm, selv=selv,
        ))
    return maps


def kernel(**inp):
    nc, _ = build_program()
    maps = make_in_maps(inp)
    res = run_bass_kernel_spmd(nc, maps, core_ids=list(range(8)))
    out = np.stack([res.results[c]["out"] for c in range(8)], axis=0)
    return np.ascontiguousarray(out.reshape(2, 4, Q, DM).reshape(2, L, DM))


# version 09
# speedup vs baseline: 1.1247x; 1.0343x over previous
import numpy as np
import ml_dtypes
import concourse.bass as bass
import concourse.mybir as mybir
from concourse.bass_utils import run_bass_kernel_spmd

F32 = mybir.dt.float32
BF16 = mybir.dt.bfloat16
AF = mybir.ActivationFunctionType
ALU = mybir.AluOpType
BF = ml_dtypes.bfloat16

B, L, DM = 2, 4096, 2048
DI, NH, HP, DS, DCONV = 4096, 64, 64, 128, 4
CONV = DI + 2 * DS
OPROJ = 2 * DI + 2 * DS + NH
OPAD = 8576
Q = 1024
EXT = Q + 3
CH = 128
NCH = Q // CH
EPS = 1e-5
FWHT_SCALE = 1.0 / np.sqrt(8192.0)

ENGS = ("sync", "gpsimd", "tensor", "vector", "scalar")


class Prog:

    def __init__(self, nc):
        self.nc = nc
        self.ops = {e: [] for e in ENGS}
        self.semh = {}
        self.cnt = {}
        self.waited = {e: {} for e in ENGS}

    def sem(self, name):
        self.semh[name] = self.nc.alloc_semaphore(name)
        self.cnt[name] = 0
        return name

    def emit(self, eng, fn, waits=(), sig=None, sig_n=1):
        ws = []
        for s, v in waits:
            if v is None or v <= 0:
                continue
            if v > self.waited[eng].get(s, 0):
                self.waited[eng][s] = v
                ws.append((self.semh[s], v))
        tgt = None
        ret = None
        if sig is not None:
            self.cnt[sig] += sig_n
            ret = self.cnt[sig]
            tgt = (self.semh[sig], sig_n)
        self.ops[eng].append((tuple(ws), fn, tgt))
        return ret

    def vspace(self, nc):
        self.emit("vector", lambda: nc.vector.tensor_scalar_mul(
            self._sp_ap, self._sp_ap, 1.0))

    def wait(self, eng, waits):
        for s, v in waits:
            if v is None or v <= 0:
                continue
            if v > self.waited[eng].get(s, 0):
                self.waited[eng][s] = v
                h = self.semh[s]
                self.ops[eng].append((((h, v),), None, None))

    def dma(self, q, out, in_, waits=(), sig=None, transpose=False):
        eng = {"sync": self.nc.sync, "gpsimd": self.nc.gpsimd,
               "scalar": self.nc.scalar}[q]
        if transpose:
            fn = lambda: eng.dma_start_transpose(out=out, in_=in_)
        else:
            fn = lambda: eng.dma_start(out=out, in_=in_)
        waits = list(waits)
        if sig is not None and self.cnt[sig] > 0:
            waits.append((sig, self.cnt[sig]))
        return self.emit(q, fn, waits=waits, sig=sig, sig_n=16)

    def replay(self, eng_name, eng_obj):
        for ws, fn, tgt in self.ops[eng_name]:
            for h, v in ws:
                eng_obj.wait_ge(h, v)
            if fn is not None:
                inst = fn()
                if tgt is not None:
                    inst.then_inc(tgt[0], tgt[1])


def ceil_div(a, b):
    return (a + b - 1) // b


DEBUG_LN1 = False
DEBUG_PH3 = False
PH3_CHUNKS = 8
PH_LEVEL = 5

def build_program(debug_taps=()):
    nc = bass.Bass("TRN2", target_bir_lowering=False, debug=False)

    def ein(name, shape, dt=F32):
        return nc.dram_tensor(name, shape, dt, kind="ExternalInput").ap()

    u_ext = ein("u_ext", [EXT, DM])
    win_sl = ein("win_sl", [256, OPAD])
    wout_sl = ein("wout_sl", [512, DM])
    w1_sl = ein("w1_sl", [256, 8192])
    w3_sl = ein("w3_sl", [256, 8192])
    w2_sl = ein("w2_sl", [256, 8192])
    nw_t = ein("nw_t", [128, 4])
    convw_t = ein("convw_t", [128, 34, 4])
    convb_t = ein("convb_t", [128, 34])
    dtb_row = ein("dtb_row", [1, 128])
    alog_col = ein("alog_col", [64, 1])
    dp_rep = ein("dp_rep", [1, DI])
    nw_row = ein("nw_row", [1, DI])
    selm = ein("selm", [4, 3])
    selv = ein("selv", [3, 1])

    out_h = nc.dram_tensor("out", [Q, DM], F32, kind="ExternalOutput").ap()

    def internal(name, shape, dt):
        return nc.dram_tensor(name, shape, dt, kind="Internal").ap()

    def shared(name, shape, dt):
        return nc.dram_tensor(name, shape, dt, kind="Internal",
                              addr_space="Shared").ap()

    win_bf = shared("win_bf", [DM, OPAD], BF16)
    wout_bf = shared("wout_bf", [DI, DM], BF16)
    w1_bf = shared("w1_bf", [DM, 8192], BF16)
    w3_bf = shared("w3_bf", [DM, 8192], BF16)
    w2pp_bf = shared("w2pp_bf", [8 * 8192, 256], BF16)
    ccin_win = internal("ccin_win", [256, OPAD], BF16)
    ccin_wout = internal("ccin_wout", [512, DM], BF16)
    ccin_w1 = internal("ccin_w1", [256, 8192], BF16)
    ccin_w3 = internal("ccin_w3", [256, 8192], BF16)
    ccin_w2 = internal("ccin_w2", [8192, 256], BF16)

    z_sp = internal("z_sp", [DI, Q], BF16)
    xfm_sp = internal("xfm_sp", [DI, Q], BF16)
    st_sp = internal("st_sp", [NCH, 128, NH * HP], BF16)
    resid_sp = internal("resid_sp", [Q, DM], F32)
    fl_negrp = internal("fl_negrp", [1, NCH * NH * CH], F32)
    fl_r = internal("fl_r", [1, NCH * NH * CH], F32)
    wrow_b = internal("wrow_b", [1, 4 * NH], F32)
    cc_st_in = internal("cc_st_in", [128, NH * HP], BF16)
    cc_st_out = internal("cc_st_out", [512, NH * HP], BF16)
    cc_ld_in = internal("cc_ld_in", [NH, 1], F32)
    cc_ld_out = internal("cc_ld_out", [4 * NH, 1], F32)

    idb = nc.inline_tensor(np.eye(128, dtype=BF), name="id_bf").ap()
    idf = nc.inline_tensor(np.eye(128, dtype=np.float32), name="id_f32").ap()
    ones_row = nc.inline_tensor(np.ones((1, 512), np.float32), name="ones_row").ap()
    ones_big = nc.inline_tensor(np.ones((1, NH * CH), np.float32), name="ones_big").ap()
    h128c = nc.inline_tensor(
        np.array([[1.0 - 2.0 * (bin(i & j).count("1") & 1) for j in range(128)]
                  for i in range(128)], dtype=BF), name="h128").ap()
    mT = np.triu(np.ones((128, 128), np.float32)).astype(BF)
    maskT_c = nc.inline_tensor(mT, name="maskT").ap()

    SB = {}
    SB_BASE = 16384 + 2048
    SB_LIMIT = 229376
    sb_off = [SB_BASE]

    def sbuf(name, shape, dt, at=None):
        nbytes = int(np.prod(shape[1:])) * mybir.dt.size(dt)
        if at is None:
            at = sb_off[0]
            sb_off[0] = at + ((nbytes + 31) // 32) * 32
        h = nc.alloc_sbuf_tensor_at(name, shape, dt, offset=at)
        SB[name] = h
        return h

    c_fm = sbuf("c_fm", [128, Q], BF16)
    b_fm = sbuf("b_fm", [128, Q], BF16)
    r_tm = sbuf("r_tm", [128, NCH, NH], F32)
    expf_tm = sbuf("expf_tm", [128, NCH, NH], F32)
    dt_tm = sbuf("dt_tm", [128, NCH, NH], F32)
    dt_fm = sbuf("dt_fm", [64, Q], F32)
    lndt_fm = sbuf("lndt_fm", [64, Q], F32)
    r_fm = sbuf("r_fm", [64, Q], F32)
    hT = sbuf("hT", [128, NH * HP], BF16)
    dcol = sbuf("dcol", [128, NCH * NH], F32)
    dpb = sbuf("dpb", [128, DI], BF16)
    lndt_tm = sbuf("lndt_tm", [128, NCH, NH], F32)
    nrp_tm = sbuf("nrp_tm", [128, NCH, NH], F32)
    rl_cols = sbuf("rl_cols", [128, NCH * NH], F32)
    smalls = sbuf("smalls", [128, 256], F32)
    p_spacer_pending = True
    wq = sbuf("wq", [4, 64], F32)
    wq2 = sbuf("wq2", [3, 64], F32)
    wcols = sbuf("wcols", [128, 4 * NH], F32)
    cw_sb = sbuf("cw_sb", [128, 34, 4], F32)
    cb_sb = sbuf("cb_sb", [128, 34], F32)
    maskT_s = sbuf("maskT_s", [128, 128], BF16)
    h128_s = sbuf("h128_s", [128, 128], BF16)
    dtb_s = sbuf("dtb_s", [1, 128], F32)
    idb_s = sbuf("idb_s", [128, 128], BF16)
    idf_s = sbuf("idf_s", [128, 128], F32)
    onesr_s = sbuf("onesr_s", [1, 512], F32)
    rowb_s = sbuf("rowb_s", [1, 512], F32)

    R0 = sb_off[0]
    RSZ = SB_LIMIT - R0
    assert RSZ >= 150 * 1024, (R0, RSZ)

    def reg(name, shape, dt, at):
        nbytes = int(np.prod(shape[1:])) * mybir.dt.size(dt)
        assert at + nbytes <= RSZ, (name, at, nbytes, RSZ)
        return nc.alloc_sbuf_tensor_at(name, shape, dt, offset=R0 + at)

    w2a = reg("w2a", [128, 8192], F32, 0)
    w2b = reg("w2b", [128, 8192], F32, 32768)
    cvt = reg("cvt", [128, DI], F32, 65536)
    cvb = reg("cvb", [128, 2, 2048], BF16, 81920)
    nwb = reg("nwb", [128, DI], F32, 90112)
    srow = reg("srow", [1, 4096], F32, 106496)
    w2stg = reg("w2stg", [128, 4, 128], BF16, 122880)

    PB = []
    for i in range(6):
        PB.append(nc.alloc_psum_tensor(f"pb{i}", [128, 512], F32))
    PT = []
    for i in range(2):
        PT.append(nc.alloc_psum_tensor(f"pt{i}", [128, 1024], BF16))

    p = Prog(nc)
    p._sp_ap = smalls.ap()[:, 254:255]
    SQ = [p.sem(f"sq{i}") for i in range(16)]
    GQ = [p.sem(f"gq{i}") for i in range(6)]
    T = p.sem("T")
    V = p.sem("V")
    A = p.sem("A")
    CC = p.sem("CC")

    bank_free = {i: None for i in range(8)}

    def bank_wait(i):
        return [bank_free[i]] if bank_free[i] is not None else []

    def cast_weight(src, dst, rows, qsem_ld, qsem_st):
        done = None
        r0 = 0
        i = 0
        while r0 < rows:
            n = min(128, rows - r0)
            buf = i % 2
            w = [cvb_free[buf]] if cvb_free[buf] else []
            c1 = p.dma("gpsimd", cvb.ap()[0:n, buf, :], src[r0:r0 + n, :],
                       waits=w, sig=qsem_ld)
            done = p.dma("gpsimd", dst[r0:r0 + n, :], cvb.ap()[0:n, buf, :],
                         waits=[(qsem_ld, c1)], sig=qsem_st)
            cvb_free[buf] = (qsem_st, done)
            r0 += n
            i += 1
        return done

    zv0 = p.emit("vector", lambda: nc.vector.memset(
        cvb.ap()[:, :, :].rearrange("p a c -> p (a c)"), 0.0), sig=V)
    czp = None
    cvb_free = [None, None]

    def cast_weight(src_ap, dst, rows, cols, qsem_ld, qsem_st, extra=None):
        done = None
        i = 0
        for r0 in range(0, rows, 128):
            n = min(128, rows - r0)
            for c0 in range(0, cols, 2048):
                w = min(2048, cols - c0)
                buf = i % 2
                ww = [cvb_free[buf]] if cvb_free[buf] else []
                c1 = p.dma("gpsimd", cvb.ap()[0:n, buf, 0:w],
                           src_ap[r0:r0 + n, c0:c0 + w], waits=ww, sig=qsem_ld)
                done = p.dma("gpsimd", dst[r0:r0 + n, c0:c0 + w],
                             cvb.ap()[0:n, buf, 0:w],
                             waits=[(qsem_ld, c1)], sig=qsem_st)
                cvb_free[buf] = (qsem_st, done)
                i += 1
        return done

    cwin = cast_weight(win_sl, ccin_win, 256, OPAD, GQ[0], GQ[1])
    agwin = p.emit("gpsimd", lambda: nc.gpsimd.collective_compute(
        "AllGather", ALU.bypass, replica_groups=[list(range(8))],
        ins=[ccin_win.opt()], outs=[win_bf[:, :]]),
        waits=[(GQ[1], cwin)], sig=CC)

    lnw = p.dma("sync", smalls.ap()[:, 16:20], nw_t[:, :], sig=SQ[1])
    cwout = None
    cvt_free = None
    for i in range(4):
        w = [cvt_free] if cvt_free else []
        c1 = p.dma("sync", cvt.ap()[:, 0:2048], wout_sl[i * 128:(i + 1) * 128, :],
                   waits=w + [(SQ[1], lnw)], sig=SQ[1])
        buf = i % 2
        ww = [cvb_free[buf]] if cvb_free[buf] else []
        mv = p.emit("vector", (lambda i=i, buf=buf: nc.vector.tensor_single_scalar(
            cvb.ap()[:, buf, :], cvt.ap()[:, 0:2048],
            smalls.ap()[:, 16 + i:17 + i], ALU.mult)),
            waits=[(SQ[1], c1)] + ww, sig=V)
        cvt_free = (V, mv)
        cwout = p.dma("gpsimd", ccin_wout[i * 128:(i + 1) * 128, :],
                      cvb.ap()[:, buf, :], waits=[(V, mv)], sig=GQ[1])
        cvb_free[buf] = (GQ[1], cwout)
    agwout = p.emit("gpsimd", lambda: nc.gpsimd.collective_compute(
        "AllGather", ALU.bypass, replica_groups=[list(range(8))],
        ins=[ccin_wout.opt()], outs=[wout_bf[:, :]]),
        waits=[(GQ[1], cwout)], sig=CC)

    cw1 = cast_weight(w1_sl, ccin_w1, 256, 8192, GQ[0], GQ[1])
    agw1 = p.emit("gpsimd", lambda: nc.gpsimd.collective_compute(
        "AllGather", ALU.bypass, replica_groups=[list(range(8))],
        ins=[ccin_w1.opt()], outs=[w1_bf[:, :]]),
        waits=[(GQ[1], cw1)], sig=CC)
    cw3 = cast_weight(w3_sl, ccin_w3, 256, 8192, GQ[0], GQ[1])
    agw3 = p.emit("gpsimd", lambda: nc.gpsimd.collective_compute(
        "AllGather", ALU.bypass, replica_groups=[list(range(8))],
        ins=[ccin_w3.opt()], outs=[w3_bf[:, :]]),
        waits=[(GQ[1], cw3)], sig=CC)

    idld = p.dma("sync", idb_s.ap()[:, :], idb[:, :], sig=SQ[0])
    vv = None
    cw2 = None
    PT0_F = None
    for d in range(2):
        cw2l = p.dma("sync", w2a.ap()[:, :], w2_sl[d * 128:(d + 1) * 128, :],
                     waits=[(V, vv)] if vv else [], sig=SQ[1])
        pa = w2a.ap()[:, :]
        pb_ = w2b.ap()[:, :]
        for b_ in range(6):
            s = 1 << b_
            blk = 2 * s * 128
            g = 8192 // blk
            cur_ap = pa if (b_ % 2 == 0) else pb_
            oth_ap = pb_ if (b_ % 2 == 0) else pa
            lo_in = cur_ap.rearrange("p (g r) -> p g r", g=g)[:, :, 0:s * 128]
            hi_in = cur_ap.rearrange("p (g r) -> p g r", g=g)[:, :, s * 128:blk]
            lo_out = oth_ap.rearrange("p (g r) -> p g r", g=g)[:, :, 0:s * 128]
            hi_out = oth_ap.rearrange("p (g r) -> p g r", g=g)[:, :, s * 128:blk]
            w = [(SQ[1], cw2l)] if b_ == 0 else []
            vv = p.emit("vector", (lambda a=lo_out, x=lo_in, y=hi_in:
                                   nc.vector.tensor_add(a, x, y)), waits=w, sig=V)
            vv = p.emit("vector", (lambda a=hi_out, x=lo_in, y=hi_in:
                                   nc.vector.tensor_sub(a, x, y)), sig=V)
        W2SEM = [SQ[3], SQ[5], SQ[11], SQ[12]]
        w2stg_free = [None] * 4
        pt_free2 = [None, None]
        cw2_marks = []
        for hhalf in range(2):
            vv2 = p.emit("vector", (lambda h=hhalf: nc.vector.tensor_scalar_mul(
                cvb.ap()[:, :, :].rearrange("p a c -> p (a c)")[:, 0:4096],
                w2a.ap()[:, h * 4096:(h + 1) * 4096], FWHT_SCALE)),
                waits=([(V, vv)] if vv else []) + [(T, p.cnt[T])] +
                      [x for x in cvb_free if x], sig=V)
            vv = vv2
            for ft in range(32):
                f = hhalf * 32 + ft
                ptb = ft % 2
                sb4 = ft % 4
                wpt0 = [pt_free2[ptb]] if pt_free2[ptb] else []
                tm = p.emit("tensor", (lambda ft=ft, ptb=ptb: nc.tensor.transpose(
                    PT[ptb].ap()[:, 0:128],
                    cvb.ap()[:, :, :].rearrange("p a c -> p (a c)")[:, ft * 128:(ft + 1) * 128],
                    idb_s.ap()[:, :])),
                    waits=[(V, vv2), (SQ[0], idld)] + wpt0, sig=T)
                wst4 = [w2stg_free[sb4]] if w2stg_free[sb4] else []
                ev = p.emit("vector", (lambda ptb=ptb, sb4=sb4: nc.vector.tensor_copy(
                    w2stg.ap()[:, sb4, :], PT[ptb].ap()[:, 0:128])),
                    waits=[(T, tm)] + wst4, sig=V)
                pt_free2[ptb] = (V, ev)
                cw2 = p.dma("sync", ccin_w2[f * 128:(f + 1) * 128, d * 128:(d + 1) * 128],
                            w2stg.ap()[:, sb4, :], waits=[(V, ev)], sig=W2SEM[sb4])
                w2stg_free[sb4] = (W2SEM[sb4], cw2)
                cw2_marks.append((W2SEM[sb4], cw2))
            cvb_free = [cw2_marks[-1], cw2_marks[-2]]
    w2done = {}
    for sm, vval in cw2_marks:
        w2done[sm] = max(w2done.get(sm, 0), vval)
    agw2 = p.emit("gpsimd", lambda: nc.gpsimd.collective_compute(
        "AllGather", ALU.bypass, replica_groups=[list(range(8))],
        ins=[ccin_w2.opt()], outs=[w2pp_bf[:, :]]),
        waits=list(w2done.items()), sig=CC)

    AG_WIN, AG_WOUT, AG_W1, AG_W3, AG_W2 = agwin, agwout, agw1, agw3, agw2


    rowhbm = internal("rowhbm", [1, 512], F32)
    ln1_dbg = internal("ln1_dbg", [2, 128, 16, 515], BF16)
    u_dbg = internal("u_dbg", [128, 2048], F32)
    sm_dbg = internal("sm_dbg", [128, 256], F32)
    lt_dbg = internal("lt_dbg", [128, 2048], BF16)

    wcache = reg("wc", [128, 2, 16, 512], BF16, 0)
    ln1fm = reg("ln1fm", [128, 16, 1027], BF16, 32768)
    xbc = reg("xbc", [128, 4, 1027], BF16, 65632)
    ustage = reg("ustage", [128, 2, 2048], F32, 73856)
    ln1tm = reg("ln1tm", [128, 2, 2048], BF16, 90240)
    lnscr = reg("lnscr", [128, 2048], BF16, 98432)
    zstage = reg("zstage", [128, 2, 512], BF16, 102528)
    convacc = reg("convacc", [128, 2, 1024], F32, 104576)
    convstage = reg("convstage", [128, 2, 1024], BF16, 112768)
    dtscr = reg("dtscr", [128, 64], F32, 116864)
    xstage = reg("xstage", [128, 2, 4096], BF16, 117120)
    xw_sb = reg("xw_sb", [128, 512], BF16, 133504)
    stst = reg("stst", [128, 4096], BF16, 134528)
    wtm = reg("wtm", [128, 64], F32, 142720)
    wtm2 = reg("wtm2", [128, 64], F32, 142976)
    dascr = reg("dascr", [64, 1024], F32, 143232)
    nrpscr = reg("nrpscr", [64, 1024], F32, 147328)
    btm = reg("btm", [128, 128], BF16, 151424)
    stripsb = reg("stripsb", [3, 512], BF16, 151680)

    PH0_V = p.cnt[V]
    PH0_A = p.cnt[A]
    PH0_T = p.cnt[T]
    PH0_GQ1 = p.cnt[GQ[1]]
    PH0_SQ1 = p.cnt[SQ[1]]
    PH0_BAR = [(V, PH0_V), (A, PH0_A), (T, PH0_T), (GQ[1], PH0_GQ1), (SQ[1], PH0_SQ1)]
    for _e in ("sync", "gpsimd", "tensor", "vector", "scalar"):
        p.wait(_e, PH0_BAR)

    lsm = p.dma("sync", cw_sb.ap()[:, :, :], convw_t[:, :, :], sig=SQ[0])
    lsm = p.dma("sync", cb_sb.ap()[:, :], convb_t[:, :], sig=SQ[0])
    lsm = p.dma("sync", maskT_s.ap()[:, :], maskT_c[:, :], sig=SQ[0])
    lsm = p.dma("sync", h128_s.ap()[:, :], h128c[:, :], sig=SQ[0])
    lsm = p.dma("sync", dtb_s.ap()[:, :], dtb_row[:, :], sig=SQ[0])
    lsm = p.dma("sync", onesr_s.ap()[:, :], ones_row[:, :], sig=SQ[0])
    lsm = p.dma("sync", idb_s.ap()[:, :], idb[:, :], sig=SQ[0])
    lsm = p.dma("sync", idf_s.ap()[:, :], idf[:, :], sig=SQ[0])
    lsm = p.dma("sync", smalls.ap()[0:64, 48:49], alog_col[:, :], sig=SQ[0])
    SMALL_LD = lsm
    ldp = p.dma("gpsimd", dpb.ap()[0:1, 0:4096], dp_rep[:, :], sig=GQ[2])
    aa = p.emit("scalar", lambda: nc.scalar.activation(
        smalls.ap()[0:64, 49:50], smalls.ap()[0:64, 48:49], AF.Exp),
        waits=[(SQ[0], SMALL_LD)], sig=A)
    vna = p.emit("vector", lambda: nc.vector.tensor_scalar_mul(
        smalls.ap()[0:64, 50:51], smalls.ap()[0:64, 49:50], -1.0),
        waits=[(A, aa)], sig=V)
    NEGA = vna
    dpw = None
    for j in range(8):
        mm = p.emit("tensor", (lambda j=j: nc.tensor.matmul(
            PB[4].ap()[:, :], h128_s.ap()[0:1, 0:128],
            dpb.ap()[0:1, j * 512:(j + 1) * 512], start=True, stop=True)),
            waits=[(SQ[0], SMALL_LD), (GQ[2], ldp)] + bank_wait(4), sig=T)
        dpw = p.emit("vector", (lambda j=j: nc.vector.tensor_copy(
            dpb.ap()[:, j * 512:(j + 1) * 512], PB[4].ap()[:, :])),
            waits=[(T, mm)], sig=V)
        bank_free[4] = (V, dpw)
    DPB_OK = dpw

    hz = p.emit("vector", lambda: nc.vector.memset(hT.ap()[:, :], 0.0), sig=V)
    ve1 = p.emit("vector", lambda: nc.vector.memset(smalls.ap()[:, 54:55], EPS), sig=V)
    ve2 = p.emit("vector", lambda: nc.vector.memset(smalls.ap()[:, 55:56], 1.0 / DI), sig=V)
    EPSC = ve2

    UST_FREE = [None, None]
    LN1TM_FREE = [None, None]
    PT_FREE = [None, None]
    SSQ_FREE = [None, None]
    tiles = [(0, 128, 0), (128, 128, 128), (256, 128, 256), (384, 128, 384),
             (512, 128, 512), (640, 128, 640), (768, 128, 768), (896, 128, 896),
             (1024, 3, 1024)]
    for ti, (e0, n, co) in enumerate(tiles):
        ub = ti % 2
        w = list(UST_FREE[ub]) if UST_FREE[ub] else []
        usem = SQ[1 + 5 * ub]
        cu = p.dma("sync", ustage.ap()[0:n, ub, :], u_ext[e0:e0 + n, :],
                   waits=w, sig=usem)
        base = 16 * (ti % 2)
        vsum = p.emit("vector", (lambda n=n, ub=ub, b=base: nc.vector.reduce_sum(
            smalls.ap()[0:n, b + 0:b + 1], ustage.ap()[0:n, ub, :],
            axis=mybir.AxisListType.X)), waits=[(usem, cu)], sig=V)
        p.emit("scalar", (lambda n=n, ub=ub, b=base: nc.scalar.activation(
            lnscr.ap()[0:n, :], ustage.ap()[0:n, ub, :], AF.Square,
            accum_out=smalls.ap()[0:n, b + 1:b + 2])),
            waits=[(usem, cu)] + ([SSQ_FREE[ti % 2]] if SSQ_FREE[ti % 2] else []))
        asq = p.emit("scalar", (lambda n=n, b=base: nc.scalar.copy(
            smalls.ap()[0:n, b + 7:b + 8], smalls.ap()[0:n, b + 1:b + 2])),
            sig=A)
        p.vspace(nc)
        v1 = p.emit("vector", (lambda n=n, b=base: nc.vector.tensor_scalar_mul(
            smalls.ap()[0:n, b + 2:b + 3], smalls.ap()[0:n, b + 0:b + 1],
            1.0 / 2048.0)), waits=[(V, vsum)], sig=V)
        v2 = p.emit("vector", (lambda n=n, b=base: nc.vector.tensor_scalar_mul(
            smalls.ap()[0:n, b + 3:b + 4], smalls.ap()[0:n, b + 1:b + 2],
            1.0 / 2048.0)), waits=[(A, asq)], sig=V)
        SSQ_FREE[ti % 2] = (V, v2)
        v3 = p.emit("vector", (lambda n=n, b=base: nc.vector.tensor_single_scalar(
            smalls.ap()[0:n, b + 4:b + 5], smalls.ap()[0:n, b + 2:b + 3],
            smalls.ap()[0:n, b + 2:b + 3], ALU.mult)), sig=V)
        p.vspace(nc)
        v4 = p.emit("vector", (lambda n=n, b=base: nc.vector.tensor_single_scalar(
            smalls.ap()[0:n, b + 3:b + 4], smalls.ap()[0:n, b + 3:b + 4],
            smalls.ap()[0:n, b + 4:b + 5], ALU.subtract)), sig=V)
        a2 = p.emit("scalar", (lambda n=n, b=base: nc.scalar.activation(
            smalls.ap()[0:n, b + 5:b + 6], smalls.ap()[0:n, b + 3:b + 4],
            AF.Sqrt, bias=smalls.ap()[0:n, 54:55])),
            waits=[(V, v4), (V, EPSC)], sig=A)
        v5 = p.emit("vector", (lambda n=n, b=base: nc.vector.reciprocal(
            smalls.ap()[0:n, b + 6:b + 7], smalls.ap()[0:n, b + 5:b + 6])),
            waits=[(A, a2)], sig=V)
        wlt = [LN1TM_FREE[ub]] if LN1TM_FREE[ub] else []
        p.vspace(nc)
        v6 = p.emit("vector", (lambda n=n, ub=ub, b=base: nc.vector.tensor_scalar(
            ln1tm.ap()[0:n, ub, :], ustage.ap()[0:n, ub, :],
            smalls.ap()[0:n, b + 2:b + 3], smalls.ap()[0:n, b + 6:b + 7],
            ALU.subtract, ALU.mult)), waits=[(V, v5)] + wlt, sig=V)
        UST_FREE[ub] = [(V, v6), (A, asq)]
        for k in range(16):
            ptb = k % 2
            wpt = [PT_FREE[ptb]] if PT_FREE[ptb] else []
            tm = p.emit("tensor", (lambda n=n, ub=ub, k=k, ptb=ptb:
                        nc.tensor.transpose(
                            PT[ptb].ap()[:, 0:n],
                            ln1tm.ap()[0:n, ub, k * 128:(k + 1) * 128],
                            idb_s.ap()[0:n, 0:n])),
                        waits=[(V, v6), (SQ[0], SMALL_LD)] + wpt, sig=T)
            if k % 2 == 0:
                ev = p.emit("vector", (lambda n=n, k=k, co=co, ptb=ptb:
                            nc.vector.tensor_copy(
                                ln1fm.ap()[:, k, co:co + n], PT[ptb].ap()[:, 0:n])),
                            waits=[(T, tm)], sig=V)
                PT_FREE[ptb] = (V, ev)
            else:
                ev = p.emit("scalar", (lambda n=n, k=k, co=co, ptb=ptb:
                            nc.scalar.copy(
                                ln1fm.ap()[:, k, co:co + n], PT[ptb].ap()[:, 0:n])),
                            waits=[(T, tm)], sig=A)
                PT_FREE[ptb] = (A, ev)
        LN1TM_FREE[ub] = (T, p.cnt[T])
    LN1FM_OK = [(V, p.cnt[V]), (A, p.cnt[A])]

    OCW = [512] * 16 + [384]
    WC_FREE = [None, None]
    ZST_FREE = [None, None]
    CONVST_FREE = [None, None]
    CONVACC_FREE = [None, None]
    STRIP_FREE = None
    slab_free = {}
    slab_ready = {}
    xspill = {}
    dt_mm_done = {}
    z_done = 0

    for oc in range(17):
        wb = oc % 2
        width = OCW[oc]
        wwc = [WC_FREE[wb]] if WC_FREE[wb] else []
        lw = []
        for k in range(16):
            wsem = SQ[[2, 7, 11, 12][k % 4]]
            lwv = p.dma("sync", wcache.ap()[:, wb, k, 0:width],
                        win_bf[k * 128:(k + 1) * 128, oc * 512:oc * 512 + width],
                        waits=wwc + [(CC, AG_WIN)], sig=wsem)
            lw.append((wsem, lwv))
            wwc = []
        lw = [(sm, max(vv for sm2, vv in lw if sm2 == sm)) for sm in set(s0 for s0, _ in lw)]
        nots = 4 if oc < 16 else 2
        if oc >= 8:
            mms = None
            for k in range(16):
                mms = p.emit("tensor", (lambda k=k, wb=wb, width=width:
                            nc.tensor.matmul(
                                PB[5].ap()[0:3, 0:width],
                                ln1fm.ap()[:, k, 0:3],
                                wcache.ap()[:, wb, k, 0:width],
                                start=(k == 0), stop=(k == 15))),
                            waits=(lw + LN1FM_OK + bank_wait(5)) if k == 0 else [],
                            sig=(T if k == 15 else None))
            wstf = [STRIP_FREE] if STRIP_FREE else []
            evs = p.emit("vector", (lambda width=width: nc.vector.tensor_copy(
                stripsb.ap()[0:3, 0:width], PB[5].ap()[0:3, 0:width])),
                waits=[(T, mms)] + wstf, sig=V)
            bank_free[5] = (V, evs)
            STRIP_V = evs
        for oti in range(nots):
            g = oc * 4 + oti
            rot = g % 4
            pp = g % 2
            for h in range(2):
                mm = None
                for k in range(16):
                    mm = p.emit("tensor", (lambda oti=oti, k=k, pp=pp, wb=wb, h=h:
                                nc.tensor.matmul(
                                    PB[pp].ap()[:, :],
                                    wcache.ap()[:, wb, k, oti * 128:(oti + 1) * 128],
                                    ln1fm.ap()[:, k, 3 + h * 512:3 + (h + 1) * 512],
                                    start=(k == 0), stop=(k == 15))),
                                waits=(lw + LN1FM_OK + bank_wait(pp)) if k == 0 else [],
                                sig=(T if k == 15 else None))
                if g < 32:
                    zb = (g * 2 + h) % 2
                    wz = [ZST_FREE[zb]] if ZST_FREE[zb] else []
                    ve = p.emit("vector", (lambda zb=zb, pp=pp: nc.vector.tensor_copy(
                        zstage.ap()[:, zb, :], PB[pp].ap()[:, :])),
                        waits=[(T, mm)] + wz, sig=V)
                    bank_free[pp] = (V, ve)
                    zsem = SQ[3 + 5 * zb]
                    zd = p.dma("sync", z_sp[g * 128:(g + 1) * 128,
                                            h * 512:(h + 1) * 512],
                               zstage.ap()[:, zb, :], waits=[(V, ve)], sig=zsem)
                    ZST_FREE[zb] = (zsem, zd)
                    z_done = zd
                else:
                    wsl = [slab_free[rot]] if rot in slab_free else []
                    ve = p.emit("vector", (lambda rot=rot, h=h, pp=pp:
                                nc.vector.tensor_copy(
                                    xbc.ap()[:, rot, 3 + h * 512:3 + (h + 1) * 512],
                                    PB[pp].ap()[:, :])),
                                waits=[(T, mm)] + (wsl if h == 0 else []), sig=V)
                    bank_free[pp] = (V, ve)
            if g >= 32:
                wpt = [PT_FREE[0]] if PT_FREE[0] else []
                tms = p.emit("tensor", (lambda oti=oti: nc.tensor.transpose(
                    PT[0].ap()[:, 0:3], stripsb.ap()[0:3, oti * 128:(oti + 1) * 128],
                    idb_s.ap()[0:3, 0:3])),
                    waits=[(V, STRIP_V)] + wpt, sig=T)
                evh = p.emit("vector", (lambda rot=rot: nc.vector.tensor_copy(
                    xbc.ap()[:, rot, 0:3], PT[0].ap()[:, 0:3])),
                    waits=[(T, tms)], sig=V)
                PT_FREE[0] = (V, evh)
                slab_ready[g] = (V, p.cnt[V])
                ot = g - 32
                ca = ot % 2
                wc0 = ([(SQ[0], SMALL_LD)] if ot == 0 else [])
                wca = [CONVACC_FREE[ca]] if CONVACC_FREE[ca] else []
                v1 = p.emit("vector", (lambda ot=ot, ca=ca, rot=rot:
                            nc.vector.tensor_single_scalar(
                                convacc.ap()[:, ca, :], xbc.ap()[:, rot, 0:1024],
                                cw_sb.ap()[:, ot, 0:1], ALU.mult)),
                            waits=wc0 + wca + [slab_ready[g]], sig=V)
                for k in range(1, 4):
                    v1 = p.emit("vector", (lambda ot=ot, k=k, ca=ca, rot=rot:
                                nc.vector.scalar_tensor_tensor(
                                    convacc.ap()[:, ca, :], xbc.ap()[:, rot, k:1024 + k],
                                    cw_sb.ap()[:, ot, k:k + 1], convacc.ap()[:, ca, :],
                                    ALU.mult, ALU.add)), sig=V)
                slab_free[rot] = (V, v1)
                if ot < 32:
                    cb2 = ot % 2
                    wcs = [CONVST_FREE[cb2]] if CONVST_FREE[cb2] else []
                    a1 = p.emit("scalar", (lambda ot=ot, cb2=cb2, ca=ca:
                                nc.scalar.activation(
                                    convstage.ap()[:, cb2, :], convacc.ap()[:, ca, :],
                                    AF.Silu, bias=cb_sb.ap()[:, ot:ot + 1])),
                                waits=[(V, v1)] + wcs, sig=A)
                    CONVACC_FREE[ca] = (A, a1)
                    xsem = SQ[13 + cb2]
                    xd = p.dma("sync", xfm_sp[ot * 128:(ot + 1) * 128, :],
                               convstage.ap()[:, cb2, :], waits=[(A, a1)], sig=xsem)
                    CONVST_FREE[cb2] = (xsem, xd)
                    xspill[ot] = (xsem, xd)
                else:
                    dstap = b_fm if ot == 32 else c_fm
                    a1 = p.emit("scalar", (lambda dstap=dstap, ca=ca, ot=ot:
                                nc.scalar.activation(
                                    dstap.ap()[:, :], convacc.ap()[:, ca, :],
                                    AF.Silu, bias=cb_sb.ap()[:, ot:ot + 1])),
                                waits=[(V, v1)], sig=A)
                    CONVACC_FREE[ca] = (A, a1)
        WC_FREE[wb] = (T, p.cnt[T])
        if oc >= 8:
            STRIP_FREE = (T, p.cnt[T])
        if oc == 16:
            for tt in range(8):
                c = tt
                pp = tt % 2
                for k in range(16):
                    p.emit("tensor", (lambda tt=tt, k=k, pp=pp, wb=wb:
                           nc.tensor.matmul(
                               PB[pp].ap()[:, 0:128],
                               ln1fm.ap()[:, k, 3 + tt * 128:3 + (tt + 1) * 128],
                               wcache.ap()[:, wb, k, 256:384],
                               start=(k == 0), stop=False)),
                           waits=(bank_wait(pp) if k == 0 else []), sig=None)
                mm = p.emit("tensor", (lambda pp=pp: nc.tensor.matmul(
                    PB[pp].ap()[:, 0:128], onesr_s.ap()[0:1, 0:128],
                    dtb_s.ap()[0:1, 0:128], start=False, stop=True)),
                    waits=[(SQ[0], SMALL_LD)], sig=T)
                a1 = p.emit("scalar", (lambda pp=pp: nc.scalar.activation(
                    dtscr.ap()[:, :], PB[pp].ap()[:, 0:64], AF.Exp)),
                    waits=[(T, mm)], sig=A)
                a2 = p.emit("scalar", (lambda c=c: nc.scalar.activation(
                    dt_tm.ap()[:, c, :], dtscr.ap()[:, :], AF.Ln, bias=1.0)),
                    sig=A)
                a3 = p.emit("scalar", (lambda c=c: nc.scalar.activation(
                    lndt_tm.ap()[:, c, :], dt_tm.ap()[:, c, :], AF.Ln)),
                    sig=A)
                bank_free[pp] = (A, a1)
                dt_mm_done[c] = a3
            WC_FREE[wb] = (T, p.cnt[T])
    XSPILL_HALF = [(SQ[13], p.cnt[SQ[13]]), (SQ[14], p.cnt[SQ[14]])]
    Z_OK = [(SQ[3], p.cnt[SQ[3]]), (SQ[8], p.cnt[SQ[8]])]
    BC_OK = (A, p.cnt[A])

    for tt in range(8):
        c = tt
        tm = p.emit("tensor", (lambda c=c: nc.tensor.transpose(
            PB[4].ap()[0:64, 0:128], dt_tm.ap()[:, c, :], idf_s.ap()[:, :])),
            waits=[(A, dt_mm_done[c])] + bank_wait(4), sig=T)
        ev = p.emit("vector", (lambda c=c: nc.vector.tensor_copy(
            dt_fm.ap()[0:64, c * 128:(c + 1) * 128], PB[4].ap()[0:64, 0:128])),
            waits=[(T, tm)], sig=V)
        bank_free[4] = (V, ev)
    a1 = p.emit("scalar", (lambda: nc.scalar.activation(
        lndt_fm.ap()[0:64, :], dt_fm.ap()[0:64, :], AF.Ln)),
        waits=[(V, p.cnt[V])], sig=A)
    v1 = p.emit("vector", (lambda: nc.vector.tensor_single_scalar(
        dascr.ap()[0:64, 0:1024], dt_fm.ap()[0:64, :],
        smalls.ap()[0:64, 50:51], ALU.mult)),
        waits=[(V, NEGA)], sig=V)
    p.vspace(nc)
    for c in range(8):
        p.emit("vector", (lambda c=c: nc.vector.tensor_tensor_scan(
            r_fm.ap()[0:64, c * 128:(c + 1) * 128],
            dascr.ap()[0:64, c * 128:(c + 1) * 128],
            dascr.ap()[0:64, c * 128:(c + 1) * 128],
            0.0, ALU.add, ALU.bypass)), sig=V)
    p.vspace(nc)
    v2 = p.emit("vector", (lambda: nc.vector.tensor_sub(
        nrpscr.ap()[0:64, 0:1024], lndt_fm.ap()[0:64, :], r_fm.ap()[0:64, :])),
        waits=[(A, a1)], sig=V)
    fd = p.dma("gpsimd",
               fl_negrp[0:1, :].rearrange("o (c h j) -> (o h) c j", c=8, h=64, j=128),
               nrpscr.ap()[0:64, 0:1024].rearrange("h (c j) -> h c j", c=8),
               waits=[(V, v2), (CC, AG_W2)], sig=GQ[2])
    fd = p.dma("gpsimd",
               fl_r[0:1, :].rearrange("o (c h j) -> (o h) c j", c=8, h=64, j=128),
               r_fm.ap()[0:64, :].rearrange("h (c j) -> h c j", c=8),
               waits=[(V, v2)], sig=GQ[2])
    FLATS_OK = fd
    for c in range(8):
        tm = p.emit("tensor", (lambda c=c: nc.tensor.transpose(
            PB[4].ap()[:, 0:64], r_fm.ap()[0:64, c * 128:(c + 1) * 128],
            idf_s.ap()[0:64, 0:64])),
            waits=[(V, p.cnt[V])] + bank_wait(4), sig=T)
        ev = p.emit("vector", (lambda c=c: nc.vector.tensor_copy(
            r_tm.ap()[:, c, :], PB[4].ap()[:, 0:64])),
            waits=[(T, tm)], sig=V)
        bank_free[4] = (V, ev)
    aex = p.emit("scalar", (lambda: nc.scalar.activation(
        expf_tm.ap()[:, :, :], r_tm.ap()[:, :, :], AF.Exp)),
        waits=[(V, p.cnt[V])], sig=A)
    rb1 = p.dma("gpsimd", rowhbm[0:1, 0:512].rearrange("o (c h) -> o c h", c=8),
                r_tm.ap()[127:128, :, :], waits=[(V, p.cnt[V])], sig=GQ[2])
    rb2 = p.dma("gpsimd", rowb_s.ap()[0:1, 0:512], rowhbm[0:1, 0:512],
                waits=[(GQ[2], rb1)], sig=GQ[2])
    mm = p.emit("tensor", (lambda: nc.tensor.matmul(
        PB[4].ap()[:, 0:512], onesr_s.ap()[0:1, 0:128],
        rowb_s.ap()[0:1, 0:512], start=True, stop=True)),
        waits=[(GQ[2], rb2), (SQ[0], SMALL_LD)] + bank_wait(4), sig=T)
    ev = p.emit("vector", (lambda: nc.vector.tensor_copy(
        rl_cols.ap()[:, :], PB[4].ap()[:, 0:512])),
        waits=[(T, mm)], sig=V)
    bank_free[4] = (V, ev)
    adc = p.emit("scalar", (lambda: nc.scalar.activation(
        dcol.ap()[:, :], rl_cols.ap()[:, :], AF.Exp)), waits=[(V, ev)], sig=A)
    vnr = p.emit("vector", (lambda: nc.vector.tensor_sub(
        nrp_tm.ap()[:, :, :], lndt_tm.ap()[:, :, :], r_tm.ap()[:, :, :])),
        waits=[(A, p.cnt[A])], sig=V)
    DTM_A = p.cnt[A]
    DTM_V = p.cnt[V]

    XST_FREE = [None, None]
    STST_FREE = None
    hT_tail = hz
    xw_free = None
    stspill = {}
    for c in range(8):
        xb2 = c % 2
        wxs = list(XST_FREE[xb2]) if XST_FREE[xb2] else []
        xrsem = SQ[4 + 5 * xb2]
        cx = p.dma("sync", xstage.ap()[:, xb2, :],
                   xfm_sp[0:4096, c * 128:(c + 1) * 128],
                   waits=wxs + XSPILL_HALF, sig=xrsem, transpose=True)
        tb = p.emit("tensor", (lambda c=c: nc.tensor.transpose(
            PT[0].ap()[:, 0:128], b_fm.ap()[:, c * 128:(c + 1) * 128],
            idb_s.ap()[:, :])),
            waits=[BC_OK] + ([PT_FREE[0]] if PT_FREE[0] else []), sig=T)
        evb = p.emit("vector", lambda: nc.vector.tensor_copy(
            btm.ap()[:, :], PT[0].ap()[:, 0:128]), waits=[(T, tb)], sig=V)
        PT_FREE[0] = (V, evb)
        p.vspace(nc)
        v1 = p.emit("vector", (lambda c=c: nc.vector.tensor_add(
            wtm2.ap()[:, :], nrp_tm.ap()[:, c, :],
            rl_cols.ap()[:, c * 64:(c + 1) * 64])),
            waits=[(V, DTM_V), (A, DTM_A)], sig=V)
        a1 = p.emit("scalar", lambda: nc.scalar.activation(
            wtm.ap()[:, :], wtm2.ap()[:, :], AF.Exp), waits=[(V, v1)], sig=A)
        for g in range(8):
            vxw = None
            for i in range(8):
                h = g * 8 + i
                vxw = p.emit("vector", (lambda h=h, i=i, xb2=xb2:
                            nc.vector.tensor_single_scalar(
                                xw_sb.ap()[:, i * 64:(i + 1) * 64],
                                xstage.ap()[:, xb2, h * 64:(h + 1) * 64],
                                wtm.ap()[:, h:h + 1], ALU.mult)),
                            waits=(([(xrsem, cx), (A, a1)] +
                                    ([xw_free] if xw_free else [])) if i == 0 else []),
                            sig=V)
            pp = 2 + g % 2
            mm = p.emit("tensor", (lambda pp=pp: nc.tensor.matmul(
                PB[pp].ap()[:, :], btm.ap()[:, :], xw_sb.ap()[:, :],
                start=True, stop=True)),
                waits=[(V, vxw), (V, evb)] + bank_wait(pp), sig=T)
            xw_free = (T, p.cnt[T])
            for i in range(8):
                h = g * 8 + i
                p.emit("vector", (lambda h=h, i=i, c=c, pp=pp:
                       nc.vector.scalar_tensor_tensor(
                           hT.ap()[:, h * 64:(h + 1) * 64],
                           hT.ap()[:, h * 64:(h + 1) * 64],
                           dcol.ap()[:, c * 64 + h:c * 64 + h + 1],
                           PB[pp].ap()[:, i * 64:(i + 1) * 64],
                           ALU.mult, ALU.add)),
                       waits=([(T, mm), (A, adc)] if i == 0 else []), sig=V)
            wst = [STST_FREE] if STST_FREE else []
            vcp = p.emit("vector", (lambda g=g, pp=pp: nc.vector.tensor_copy(
                stst.ap()[:, g * 512:(g + 1) * 512], PB[pp].ap()[:, :])),
                waits=wst, sig=V)
            bank_free[pp] = (V, vcp)
        sd = p.dma("sync", st_sp[c, :, :], stst.ap()[:, :],
                   waits=[(V, p.cnt[V])], sig=SQ[10])
        STST_FREE = (SQ[10], sd)
        stspill[c] = sd
        XST_FREE[xb2] = [(V, p.cnt[V])]
        hT_tail = p.cnt[V]
    STSP_OK = (SQ[10], p.cnt[SQ[10]])

    cs1 = p.dma("gpsimd", cc_st_in[:, :], hT.ap()[:, :],
                waits=[(V, hT_tail)], sig=GQ[2])
    vld = p.emit("vector", lambda: nc.vector.reduce_sum(
        smalls.ap()[0:64, 52:53],
        r_fm.ap()[0:64, :].rearrange("h (c j) -> h c j", j=128)[:, :, 127],
        axis=mybir.AxisListType.X), waits=[(V, p.cnt[V])], sig=V)
    cs2 = p.dma("gpsimd", cc_ld_in[:, :], smalls.ap()[0:64, 52:53],
                waits=[(V, vld)], sig=GQ[2])
    agst = p.emit("gpsimd", lambda: nc.gpsimd.collective_compute(
        "AllGather", ALU.bypass, replica_groups=[[0, 1, 2, 3], [4, 5, 6, 7]],
        ins=[cc_st_in.opt()], outs=[cc_st_out.opt()]),
        waits=[(GQ[2], cs1)], sig=CC)
    agld = p.emit("gpsimd", lambda: nc.gpsimd.collective_compute(
        "AllGather", ALU.bypass, replica_groups=[[0, 1, 2, 3], [4, 5, 6, 7]],
        ins=[cc_ld_in.opt()], outs=[cc_ld_out.opt()]),
        waits=[(GQ[2], cs2)], sig=CC)
    lsel = p.dma("sync", wq.ap()[:, :],
                 cc_ld_out[:, :].rearrange("(p h) o -> p (h o)", p=4),
                 waits=[(CC, agld)], sig=SQ[0])
    lsel = p.dma("sync", smalls.ap()[0:4, 32:35], selm[:, :], sig=SQ[0])
    lsel = p.dma("sync", smalls.ap()[0:3, 35:36], selv[:, :], sig=SQ[0])
    mm = p.emit("tensor", lambda: nc.tensor.matmul(
        PB[4].ap()[0:3, 0:64], smalls.ap()[0:4, 32:35], wq.ap()[:, :],
        start=True, stop=True),
        waits=[(SQ[0], lsel)] + bank_wait(4), sig=T)
    ae = p.emit("scalar", lambda: nc.scalar.activation(
        wq2.ap()[:, :], PB[4].ap()[0:3, 0:64], AF.Exp), waits=[(T, mm)], sig=A)
    bank_free[4] = (A, ae)
    vmu = p.emit("vector", lambda: nc.vector.tensor_single_scalar(
        wq2.ap()[:, :], wq2.ap()[:, :], smalls.ap()[0:3, 35:36], ALU.mult),
        waits=[(A, ae)], sig=V)
    bw1 = p.dma("gpsimd", wrow_b[0:1, 0:192].rearrange("o (q h) -> (o q) h", q=3),
                wq2.ap()[:, :], waits=[(V, vmu)], sig=GQ[2])
    bw2 = p.dma("gpsimd", rowb_s.ap()[0:1, 0:192], wrow_b[0:1, 0:192],
                waits=[(GQ[2], bw1)], sig=GQ[2])
    mm = p.emit("tensor", lambda: nc.tensor.matmul(
        PB[4].ap()[:, 0:192], onesr_s.ap()[0:1, 0:128], rowb_s.ap()[0:1, 0:192],
        start=True, stop=True), waits=[(GQ[2], bw2)] + bank_wait(4), sig=T)
    vwc = p.emit("vector", lambda: nc.vector.tensor_copy(
        wcols.ap()[:, 0:192], PB[4].ap()[:, 0:192]), waits=[(T, mm)], sig=V)
    bank_free[4] = (V, vwc)
    hz2 = p.emit("vector", lambda: nc.vector.memset(hT.ap()[:, :], 0.0),
                 waits=[(GQ[2], cs1)], sig=V)
    vh = hz2
    for pidx in range(3):
        gsem = SQ[4 + 5 * (pidx % 2)]
        cg = p.dma("sync", xstage.ap()[:, pidx % 2, :],
                   cc_st_out[pidx * 128:(pidx + 1) * 128, :],
                   waits=[(CC, agst), (V, vh)], sig=gsem)
        for h in range(NH):
            vh = p.emit("vector", (lambda pidx=pidx, h=h: nc.vector.scalar_tensor_tensor(
                hT.ap()[:, h * 64:(h + 1) * 64],
                xstage.ap()[:, pidx % 2, h * 64:(h + 1) * 64],
                wcols.ap()[:, pidx * 64 + h:pidx * 64 + h + 1],
                hT.ap()[:, h * 64:(h + 1) * 64],
                ALU.mult, ALU.add)),
                waits=([(gsem, cg), (V, vwc)] if h == 0 else []), sig=V)
    HIN_OK = vh
    hin_sp = internal("hin_sp", [128, NH * HP], BF16)
    p.dma("gpsimd", hin_sp[:, :], hT.ap()[:, :], waits=[(V, HIN_OK)], sig=GQ[3])

    attn_sp = internal("attn_sp", [Q, DM], BF16)
    y_dbg = internal("y_dbg", [128, DI], BF16)
    sc_dbg = internal("sc_dbg", [128, 4096], BF16)
    cb_dbg = internal("cb_dbg", [128, 128], BF16)

    ygfm = reg("ygfm", [128, 32, Q], BF16, 0)
    segL = reg("segL", [2, 2, 1024], F32, 65536)
    segR = reg("segR", [2, 2, 1024], F32, 73728)
    xst3 = reg("xst3", [128, 2, 4096], BF16, 81920)
    zst3 = reg("zst3", [128, 4096], BF16, 98304)
    stst3 = reg("stst3", [128, 4096], BF16, 106496)
    ysb = reg("ysb", [128, 4096], BF16, 114688)
    scor = reg("scor", [128, 8, 128], BF16, 122880)
    segexp = reg("segexp", [128, 2, 512], BF16, 124928)
    cbm = reg("cbm", [128, 128], BF16, 126976)
    ytmp = reg("ytmp", [128, 512], BF16, 127232)
    ytmp2 = reg("ytmp2", [128, 512], BF16, 128256)

    BAR12 = [(V, p.cnt[V]), (A, p.cnt[A]), (T, p.cnt[T]), (CC, p.cnt[CC])]
    for _e in ENGS:
        p.wait(_e, BAR12)

    oi = p.dma("gpsimd", segL.ap()[1:2, 0, :], ones_big[0:1, 0:1024], sig=GQ[0])
    oi = p.dma("gpsimd", segL.ap()[1:2, 1, :], ones_big[0:1, 0:1024], sig=GQ[0])
    oi = p.dma("gpsimd", segR.ap()[0:1, 0, :], ones_big[0:1, 0:1024], sig=GQ[0])
    oi = p.dma("gpsimd", segR.ap()[0:1, 1, :], ones_big[0:1, 0:1024], sig=GQ[0])
    SEG_ONES = oi

    XST3_FREE = [None, None]
    SEG_FREE = [None, None]
    SEGEXP_FREE = [None, None]
    ZST3_FREE = None
    STST3_FREE = None
    YSB_FREE = None
    hT_read = None
    hT_upd = (V, HIN_OK)
    SCOR_FREE = None
    YTMP_FREE = None
    YTMP2_FREE = None
    ygfm_marks = []
    yMM_SEM_Q = [SQ[9], SQ[15]]

    for c in range(NCH if PH_LEVEL >= 3 else 0):
        xb = c % 2
        if c >= PH3_CHUNKS:
            break
        wx = list(XST3_FREE[xb]) if XST3_FREE[xb] else []
        cx3 = p.dma("sync", xst3.ap()[:, xb, :], xfm_sp[0:4096, c * 128:(c + 1) * 128],
                    waits=wx, sig=SQ[4 + 5 * xb], transpose=True)
        xsem = SQ[4 + 5 * xb]
        wz = [ZST3_FREE] if ZST3_FREE else []
        cz = p.dma("sync", zst3.ap()[:, :], z_sp[0:4096, c * 128:(c + 1) * 128],
                   waits=wz + (Z_OK if c == 0 else []), sig=SQ[1],
                   transpose=True)
        wst = [STST3_FREE] if STST3_FREE else []
        cst = p.dma("sync", stst3.ap()[:, :], st_sp[c, :, :],
                    waits=wst + ([STSP_OK] if c == 0 else []), sig=SQ[6])
        mmcb = p.emit("tensor", (lambda c=c: nc.tensor.matmul(
            PB[4].ap()[:, 0:128], b_fm.ap()[:, c * 128:(c + 1) * 128],
            c_fm.ap()[:, c * 128:(c + 1) * 128], start=True, stop=True)),
            waits=bank_wait(4), sig=T)
        vcb = p.emit("vector", lambda: nc.vector.tensor_mul(
            cbm.ap()[:, :], PB[4].ap()[:, 0:128], maskT_s.ap()[:, :]),
            waits=[(T, mmcb)], sig=V)
        bank_free[4] = (V, vcb)
        seg_ld = {}
        for hg in range(8):
            e = hg
            sb = e % 2
            wsg = list(SEG_FREE[sb]) if SEG_FREE[sb] else []
            if c == 0 and e < 2:
                wsg.append((GQ[0], SEG_ONES))
            g1 = p.dma("gpsimd", segL.ap()[0:1, sb, :],
                       fl_negrp[0:1, (c * 64 + e * 8) * 128:(c * 64 + e * 8 + 8) * 128],
                       waits=wsg + [(GQ[2], FLATS_OK)], sig=GQ[0])
            g2 = p.dma("gpsimd", segR.ap()[1:2, sb, :],
                       fl_r[0:1, (c * 64 + e * 8) * 128:(c * 64 + e * 8 + 8) * 128],
                       waits=list(SEG_FREE[sb]) if SEG_FREE[sb] else [], sig=GQ[1])
            seg_ld[e] = [(GQ[0], g1), (GQ[1], g2)]
            segmm = {}
            for q4 in range(2):
                pp = (hg * 2 + q4) % 2
                mm = None
                for i in range(4):
                    hh = q4 * 4 + i
                    mm = p.emit("tensor", (lambda sb=sb, hh=hh, i=i, pp=pp:
                                nc.tensor.matmul(
                                    PB[pp].ap()[:, i * 128:(i + 1) * 128],
                                    segL.ap()[0:2, sb, hh * 128:(hh + 1) * 128],
                                    segR.ap()[0:2, sb, hh * 128:(hh + 1) * 128],
                                    start=(i == 0), stop=(i == 3))),
                                waits=(seg_ld[e] + bank_wait(pp)) if i == 0 else [],
                                sig=(T if i == 3 else None))
                eb = (hg * 2 + q4) % 2
                wse = [SEGEXP_FREE[eb]] if SEGEXP_FREE[eb] else []
                vmn = p.emit("vector", (lambda eb=eb, pp=pp: nc.vector.tensor_scalar_min(
                    segexp.ap()[:, eb, :], PB[pp].ap()[:, :], 30.0)),
                    waits=[(T, mm)] + wse, sig=V)
                bank_free[pp] = (V, vmn)
                aex = p.emit("scalar", (lambda eb=eb: nc.scalar.activation(
                    segexp.ap()[:, eb, :], segexp.ap()[:, eb, :], AF.Exp)),
                    waits=[(V, vmn)], sig=A)
                vsc = None
                for i in range(4):
                    hh = q4 * 4 + i
                    slot = q4 * 4 + i
                    vsc = p.emit("vector", (lambda eb=eb, i=i, slot=slot: nc.vector.tensor_mul(
                        scor.ap()[:, slot, :], segexp.ap()[:, eb, i * 128:(i + 1) * 128],
                        cbm.ap()[:, :])),
                        waits=([(A, aex), (V, vcb)] +
                               ([SCOR_FREE] if SCOR_FREE else [])) if i == 0 else [],
                        sig=V)
                SEGEXP_FREE[eb] = (V, vsc)
                segmm[q4] = vsc
            mmd = None
            for i in range(8):
                h = hg * 8 + i
                slot = i
                mmd = p.emit("tensor", (lambda h=h, i=i, slot=slot, xb=xb:
                            nc.tensor.matmul(
                                PB[2].ap()[:, i * 64:(i + 1) * 64],
                                scor.ap()[:, slot, :],
                                xst3.ap()[:, xb, h * 64:(h + 1) * 64],
                                start=(i == 0), stop=(i == 7))),
                            waits=([(V, segmm[1]), (xsem, cx3)] + bank_wait(2))
                                  if i == 0 else [],
                            sig=(T if i == 7 else None))
            mmo = None
            for i in range(8):
                h = hg * 8 + i
                mmo = p.emit("tensor", (lambda h=h, i=i, c=c: nc.tensor.matmul(
                            PB[3].ap()[:, i * 64:(i + 1) * 64],
                            c_fm.ap()[:, c * 128:(c + 1) * 128],
                            hT.ap()[:, h * 64:(h + 1) * 64],
                            start=(i == 0), stop=(i == 7))),
                            waits=(list(hT_upd if isinstance(hT_upd, list) else [hT_upd])
                                   + bank_wait(3)) if i == 0 else [],
                            sig=(T if i == 7 else None))
            hT_read = (T, mmo)
            aev = None
            for i in range(8):
                h = hg * 8 + i
                aev = p.emit("scalar", (lambda h=h, i=i, c=c: nc.scalar.activation(
                    ytmp.ap()[:, i * 64:(i + 1) * 64], PB[3].ap()[:, i * 64:(i + 1) * 64],
                    AF.Copy, scale=expf_tm.ap()[:, c, h:h + 1])),
                    waits=([(T, mmo)] + ([YTMP_FREE] if YTMP_FREE else []))
                          if i == 0 else [], sig=A)
            wyb = [YSB_FREE] if (YSB_FREE and hg == 0) else []
            v2 = p.emit("vector", (lambda hg=hg, xb=xb: nc.vector.tensor_mul(
                ytmp2.ap()[:, :], xst3.ap()[:, xb, hg * 512:(hg + 1) * 512],
                dpb.ap()[:, hg * 512:(hg + 1) * 512])),
                waits=[(V, DPB_OK)] + ([YTMP2_FREE] if YTMP2_FREE else []), sig=V)
            v1 = p.emit("vector", (lambda hg=hg: nc.vector.tensor_add(
                ysb.ap()[:, hg * 512:(hg + 1) * 512], PB[2].ap()[:, :], ytmp.ap()[:, :])),
                waits=[(A, aev), (T, mmd)] + wyb, sig=V)
            bank_free[2] = (V, v1)
            bank_free[3] = (A, aev)
            YTMP_FREE = (V, v1)
            v3 = p.emit("vector", (lambda hg=hg: nc.vector.tensor_add(
                ysb.ap()[:, hg * 512:(hg + 1) * 512],
                ysb.ap()[:, hg * 512:(hg + 1) * 512], ytmp2.ap()[:, :])), sig=V)
            YTMP2_FREE = (V, v3)
            SEG_FREE[sb] = [(T, mmd)]
            SCOR_FREE = (T, mmd)
        if DEBUG_PH3 and c == 7:
            tb9 = p.dma("gpsimd", y_dbg[:, :], hT.ap()[:, :],
                        waits=list(hT_upd if isinstance(hT_upd, list) else [hT_upd]),
                        sig=GQ[3])
            for _e in ENGS:
                p.wait(_e, [(GQ[3], tb9)])
        vrec = None
        for h in range(NH):
            vrec = p.emit("vector", (lambda h=h, c=c: nc.vector.scalar_tensor_tensor(
                hT.ap()[:, h * 64:(h + 1) * 64], hT.ap()[:, h * 64:(h + 1) * 64],
                dcol.ap()[:, c * 64 + h:c * 64 + h + 1],
                stst3.ap()[:, h * 64:(h + 1) * 64], ALU.mult, ALU.add)),
                waits=[hT_read, (SQ[6], cst)] if h == 0 else [], sig=V)
        hT_upd = (V, vrec)
        STST3_FREE = (V, vrec)
        asz = p.emit("scalar", (lambda: nc.scalar.activation(
            zst3.ap()[:, :], zst3.ap()[:, :], AF.Silu)),
            waits=[(SQ[1], cz)], sig=A)
        if DEBUG_PH3 and c == 0:
            tbz = p.dma("gpsimd", cb_dbg[0:128, 0:128], zst3.ap()[:, 0:128],
                        waits=[(A, asz)], sig=GQ[3])
            for _e in ENGS:
                p.wait(_e, [(GQ[3], tbz)])
        vg = p.emit("vector", lambda: nc.vector.tensor_mul(
            ysb.ap()[:, :], ysb.ap()[:, :], zst3.ap()[:, :]),
            waits=[(A, asz), (V, p.cnt[V])], sig=V)
        p.emit("scalar", (lambda: nc.scalar.activation(
            zst3.ap()[:, :], ysb.ap()[:, :], AF.Square,
            accum_out=smalls.ap()[:, 60:61])), waits=[(V, vg)])
        asq2 = p.emit("scalar", lambda: nc.scalar.copy(
            smalls.ap()[:, 61:62], smalls.ap()[:, 60:61]), sig=A)
        ZST3_FREE = (A, asq2)
        ast2 = p.emit("scalar", lambda: nc.scalar.activation(
            smalls.ap()[:, 62:63], smalls.ap()[:, 60:61], AF.Sqrt,
            bias=smalls.ap()[:, 54:55], scale=smalls.ap()[:, 55:56]),
            waits=[(A, asq2)], sig=A)
        vrc = p.emit("vector", lambda: nc.vector.reciprocal(
            smalls.ap()[:, 63:64], smalls.ap()[:, 62:63]), waits=[(A, ast2)], sig=V)
        if DEBUG_PH3 and c == 0:
            tbm = p.dma("gpsimd", sm_dbg[:, :], smalls.ap()[:, :],
                        waits=[(V, vrc)], sig=GQ[3])
            tbm = p.dma("gpsimd", u_dbg[0:128, 0:2048],
                        ysb.ap()[:, 0:2048], waits=[(V, vg)], sig=GQ[3])
            for _e in ENGS:
                p.wait(_e, [(GQ[3], tbm)])
        p.vspace(nc)
        vsc2 = p.emit("vector", lambda: nc.vector.tensor_single_scalar(
            ysb.ap()[:, :], ysb.ap()[:, :], smalls.ap()[:, 63:64], ALU.mult),
            waits=[(V, vrc)], sig=V)
        if DEBUG_PH3 and c == 0:
            tb8 = p.dma("gpsimd", sc_dbg[0:128, 0:4096], ysb.ap()[:, :],
                        waits=[(V, vsc2)], sig=GQ[3])
            for _e in ENGS:
                p.wait(_e, [(GQ[3], tb8)])
        for kt in range(32):
            ptb = kt % 2
            wpt = [PT_FREE[ptb]] if PT_FREE[ptb] else []
            tm = p.emit("tensor", (lambda kt=kt, ptb=ptb: nc.tensor.transpose(
                PT[ptb].ap()[:, 0:128], ysb.ap()[:, kt * 128:(kt + 1) * 128],
                idb_s.ap()[:, :])), waits=[(V, vsc2)] + wpt, sig=T)
            if kt % 2 == 0:
                ev = p.emit("vector", (lambda kt=kt, c=c, ptb=ptb: nc.vector.tensor_copy(
                    ygfm.ap()[:, kt, c * 128:(c + 1) * 128], PT[ptb].ap()[:, 0:128])),
                    waits=[(T, tm)], sig=V)
                PT_FREE[ptb] = (V, ev)
            else:
                ev = p.emit("scalar", (lambda kt=kt, c=c, ptb=ptb: nc.scalar.copy(
                    ygfm.ap()[:, kt, c * 128:(c + 1) * 128], PT[ptb].ap()[:, 0:128])),
                    waits=[(T, tm)], sig=A)
                PT_FREE[ptb] = (A, ev)
        YSB_FREE = (T, p.cnt[T])
        XST3_FREE[xb] = [(T, p.cnt[T]), (V, p.cnt[V])]
    YGFM_OK = [(V, p.cnt[V]), (A, p.cnt[A])]
    ygfm_dbg = internal("ygfm_dbg", [128, 32 * Q], BF16)
    woc_dbg = internal("woc_dbg", [128, 32 * 512], BF16)
    if DEBUG_PH3:
        tbg = p.dma("gpsimd", ygfm_dbg[:, :],
                    ygfm.ap()[:, :, :].rearrange("p a b -> p (a b)"),
                    waits=YGFM_OK, sig=GQ[3])
        for _e in ENGS:
            p.wait(_e, [(GQ[3], tbg)])

    PH3B_ON = PH_LEVEL >= 4 and PH3_CHUNKS >= 8
    woc = reg("woc", [128, 32, 512], BF16, 98304)
    atst = reg("atst", [128, 2, 512], BF16, 131072)
    BAR3B = [(V, p.cnt[V]), (A, p.cnt[A]), (T, p.cnt[T])]
    for _e in ENGS:
        p.wait(_e, BAR3B)
    ATST_FREE = [None, None]
    WOSEMS = [SQ[2], SQ[7], SQ[11], SQ[12]]
    at_done = None
    for dc in range(4 if PH3B_ON else 0):
        lw = []
        for k in range(32):
            wsem = WOSEMS[k % 4]
            wwk = [(T, p.cnt[T])] if (dc > 0 and k < 4) else []
            lwv = p.dma("sync", woc.ap()[:, k, :],
                        wout_bf[k * 128:(k + 1) * 128, dc * 512:(dc + 1) * 512],
                        waits=wwk + [(CC, AG_WOUT)], sig=wsem)
            lw.append((wsem, lwv))
        lw = [(sm, max(vv for sm2, vv in lw if sm2 == sm)) for sm in set(s0 for s0, _ in lw)]
        if DEBUG_PH3 and dc == 0:
            tbw = p.dma("gpsimd", woc_dbg[:, :],
                        woc.ap()[:, :, :].rearrange("p a b -> p (a b)"),
                        waits=lw, sig=GQ[3])
            for _e in ENGS:
                p.wait(_e, [(GQ[3], tbw)])
        for tt in range(8):
            pp = tt % 2
            mm = None
            for k in range(32):
                mm = p.emit("tensor", (lambda tt=tt, k=k, pp=pp: nc.tensor.matmul(
                    PB[pp].ap()[:, :], ygfm.ap()[:, k, tt * 128:(tt + 1) * 128],
                    woc.ap()[:, k, :], start=(k == 0), stop=(k == 31))),
                    waits=(lw + YGFM_OK + bank_wait(pp)) if k == 0 else [],
                    sig=(T if k == 31 else None))
            ab = tt % 2
            wat = [ATST_FREE[ab]] if ATST_FREE[ab] else []
            ve = p.emit("vector", (lambda ab=ab, pp=pp: nc.vector.tensor_copy(
                atst.ap()[:, ab, :], PB[pp].ap()[:, :])),
                waits=[(T, mm)] + wat, sig=V)
            bank_free[pp] = (V, ve)
            ad = p.dma("sync", attn_sp[tt * 128:(tt + 1) * 128, dc * 512:(dc + 1) * 512],
                       atst.ap()[:, ab, :], waits=[(V, ve)], sig=SQ[3 + 5 * ab])
            ATST_FREE[ab] = (SQ[3 + 5 * ab], ad)
            at_done = (SQ[3 + 5 * ab], ad)
    AT_ALL = [(SQ[3], p.cnt[SQ[3]]), (SQ[8], p.cnt[SQ[8]])]

    ast3 = reg("ast3", [128, 2048], BF16, 98304)
    ust2 = reg("ust2", [128, 2048], F32, 102400)
    rsb = reg("rsb", [128, 2048], F32, 110592)
    x3tm = reg("x3tm", [128, 2048], BF16, 118784)
    x3fm = reg("x3fm", [128, 16, Q], BF16, 65536)
    BAR3C = [(V, p.cnt[V]), (A, p.cnt[A]), (T, p.cnt[T]),
             (SQ[3], p.cnt[SQ[3]]), (SQ[8], p.cnt[SQ[8]])]
    for _e in ENGS:
        p.wait(_e, BAR3C)
    RS_FREE = None
    for tt in range(8 if PH3B_ON else 0):
        ca = p.dma("sync", ast3.ap()[:, :], attn_sp[tt * 128:(tt + 1) * 128, :],
                   waits=AT_ALL, sig=SQ[1])
        cu2 = p.dma("sync", ust2.ap()[:, :], u_ext[3 + tt * 128:3 + (tt + 1) * 128, :],
                    sig=SQ[6])

        def lnstats(src_ap, wsrc, bb, width=2048):
            vs = p.emit("vector", (lambda: nc.vector.reduce_sum(
                smalls.ap()[:, bb + 0:bb + 1], src_ap, axis=mybir.AxisListType.X)),
                waits=wsrc, sig=V)
            p.emit("scalar", (lambda: nc.scalar.activation(
                x3tm.ap()[:, :], src_ap, AF.Square,
                accum_out=smalls.ap()[:, bb + 1:bb + 2])), waits=wsrc)
            af = p.emit("scalar", (lambda: nc.scalar.copy(
                smalls.ap()[:, bb + 7:bb + 8], smalls.ap()[:, bb + 1:bb + 2])), sig=A)
            p.vspace(nc)
            v1 = p.emit("vector", (lambda: nc.vector.tensor_scalar_mul(
                smalls.ap()[:, bb + 2:bb + 3], smalls.ap()[:, bb + 0:bb + 1],
                1.0 / width)), waits=[(V, vs)], sig=V)
            v2 = p.emit("vector", (lambda: nc.vector.tensor_scalar_mul(
                smalls.ap()[:, bb + 3:bb + 4], smalls.ap()[:, bb + 1:bb + 2],
                1.0 / width)), waits=[(A, af)], sig=V)
            v3 = p.emit("vector", (lambda: nc.vector.tensor_single_scalar(
                smalls.ap()[:, bb + 4:bb + 5], smalls.ap()[:, bb + 2:bb + 3],
                smalls.ap()[:, bb + 2:bb + 3], ALU.mult)), sig=V)
            p.vspace(nc)
            v4 = p.emit("vector", (lambda: nc.vector.tensor_single_scalar(
                smalls.ap()[:, bb + 3:bb + 4], smalls.ap()[:, bb + 3:bb + 4],
                smalls.ap()[:, bb + 4:bb + 5], ALU.subtract)), sig=V)
            a2 = p.emit("scalar", (lambda: nc.scalar.activation(
                smalls.ap()[:, bb + 5:bb + 6], smalls.ap()[:, bb + 3:bb + 4],
                AF.Sqrt, bias=smalls.ap()[:, 54:55])), waits=[(V, v4)], sig=A)
            v5 = p.emit("vector", (lambda: nc.vector.reciprocal(
                smalls.ap()[:, bb + 6:bb + 7], smalls.ap()[:, bb + 5:bb + 6])),
                waits=[(A, a2)], sig=V)
            p.vspace(nc)
            return v5

        v5 = lnstats(ast3.ap()[:, :], [(SQ[1], ca)], 0)
        vno = p.emit("vector", (lambda: nc.vector.tensor_scalar(
            rsb.ap()[:, :], ast3.ap()[:, :], smalls.ap()[:, 2:3],
            smalls.ap()[:, 6:7], ALU.subtract, ALU.mult)),
            waits=[(V, v5)] + ([RS_FREE] if RS_FREE else []), sig=V)
        vra = p.emit("vector", (lambda: nc.vector.tensor_add(
            rsb.ap()[:, :], rsb.ap()[:, :], ust2.ap()[:, :])),
            waits=[(SQ[6], cu2)], sig=V)
        rd = p.dma("sync", resid_sp[tt * 128:(tt + 1) * 128, :], rsb.ap()[:, :],
                   waits=[(V, vra)], sig=SQ[10])
        v5b = lnstats(rsb.ap()[:, :], [(V, vra)], 16)
        vn3 = p.emit("vector", (lambda: nc.vector.tensor_scalar(
            x3tm.ap()[:, :], rsb.ap()[:, :], smalls.ap()[:, 18:19],
            smalls.ap()[:, 22:23], ALU.subtract, ALU.mult)),
            waits=[(V, v5b)], sig=V)
        RS_FREE = (SQ[10], rd)
        for k in range(16):
            ptb = k % 2
            wpt = [PT_FREE[ptb]] if PT_FREE[ptb] else []
            tm = p.emit("tensor", (lambda k=k, ptb=ptb: nc.tensor.transpose(
                PT[ptb].ap()[:, 0:128], x3tm.ap()[:, k * 128:(k + 1) * 128],
                idb_s.ap()[:, :])), waits=[(V, vn3)] + wpt, sig=T)
            if k % 2 == 0:
                ev = p.emit("vector", (lambda k=k, tt=tt, ptb=ptb: nc.vector.tensor_copy(
                    x3fm.ap()[:, k, tt * 128:(tt + 1) * 128], PT[ptb].ap()[:, 0:128])),
                    waits=[(T, tm)], sig=V)
                PT_FREE[ptb] = (V, ev)
            else:
                ev = p.emit("scalar", (lambda k=k, tt=tt, ptb=ptb: nc.scalar.copy(
                    x3fm.ap()[:, k, tt * 128:(tt + 1) * 128], PT[ptb].ap()[:, 0:128])),
                    waits=[(T, tm)], sig=A)
                PT_FREE[ptb] = (A, ev)
    X3FM_OK = [(V, p.cnt[V]), (A, p.cnt[A])]
    RESID_OK = [(SQ[10], p.cnt[SQ[10]])]

    h1 = reg("h1", [128, 64, 512], BF16, 0)
    w1c = reg("w1c", [128, 16, 512], BF16, 98304)
    w3c = reg("w3c", [128, 16, 512], BF16, 114688)
    sg4 = reg("sg4", [128, 512], BF16, 131072)
    hid4 = reg("hid4", [128, 2, 512], BF16, 132096)
    w2c = reg("w2c", [128, 2, 512], BF16, 134144)
    ffnst = reg("ffnst", [128, 4, 4, 512], BF16, 136192)
    h1stg = reg("h1stg", [128, 2, 512], BF16, 152576)
    rsb2 = reg("rsb2", [128, 2048], F32, 98304)
    ostage = reg("ostage", [128, 2, 2048], F32, 106496)
    BAR4 = [(V, p.cnt[V]), (A, p.cnt[A]), (T, p.cnt[T])]
    for _e in ENGS:
        p.wait(_e, BAR4)
    W1SEMS = [SQ[2], SQ[7]]
    W3SEMS = [SQ[11], SQ[12]]
    W2SEMS = [SQ[4], SQ[9]]
    OUT_SEMS = [SQ[3], SQ[8]]
    H1SP_SEMS = [SQ[13], SQ[14]]
    h1sp = internal("h1sp", [8192, 512], BF16)
    out_last = []
    SG4_FREE = None
    HIDF = [None, None]
    h1stg_free = [None, None]
    for ug in range(16 if PH_LEVEL >= 5 and PH3B_ON else 0):
        lw1 = []
        lw3 = []
        for k in range(16):
            wwk = [(T, p.cnt[T])] if (k < 2 and ug > 0) else []
            l1 = p.dma("sync", w1c.ap()[:, k, :],
                       w1_bf[k * 128:(k + 1) * 128, ug * 512:(ug + 1) * 512],
                       waits=wwk + [(CC, AG_W1)], sig=W1SEMS[k % 2])
            lw1.append((W1SEMS[k % 2], l1))
            l3 = p.dma("sync", w3c.ap()[:, k, :],
                       w3_bf[k * 128:(k + 1) * 128, ug * 512:(ug + 1) * 512],
                       waits=wwk + [(CC, AG_W3)], sig=W3SEMS[k % 2])
            lw3.append((W3SEMS[k % 2], l3))
        lw1 = [(sm, max(v for s2, v in lw1 if s2 == sm)) for sm in set(s for s, _ in lw1)]
        lw3 = [(sm, max(v for s2, v in lw3 if s2 == sm)) for sm in set(s for s, _ in lw3)]
        for ut in range(4):
            u = ug * 4 + ut
            mms = {}
            for grp, (wc, pba, pbb, lww) in enumerate(
                    [(w1c, 0, 1, lw1), (w3c, 2, 3, lw3)]):
                for k in range(16):
                    m0 = p.emit("tensor", (lambda wc=wc, ut=ut, k=k, pba=pba:
                                nc.tensor.matmul(
                                    PB[pba].ap()[:, :],
                                    wc.ap()[:, k, ut * 128:(ut + 1) * 128],
                                    x3fm.ap()[:, k, 0:512],
                                    start=(k == 0), stop=(k == 15))),
                                waits=(lww + X3FM_OK + bank_wait(pba))
                                      if k == 0 else [],
                                sig=(T if k == 15 else None))
                    m1 = p.emit("tensor", (lambda wc=wc, ut=ut, k=k, pbb=pbb:
                                nc.tensor.matmul(
                                    PB[pbb].ap()[:, :],
                                    wc.ap()[:, k, ut * 128:(ut + 1) * 128],
                                    x3fm.ap()[:, k, 512:1024],
                                    start=(k == 0), stop=(k == 15))),
                                waits=bank_wait(pbb) if k == 0 else [],
                                sig=(T if k == 15 else None))
                    mms[pba] = m0
                    mms[pbb] = m1
            for hh in range(2):
                wsg = [SG4_FREE] if SG4_FREE else []
                ags = p.emit("scalar", (lambda hh=hh: nc.scalar.activation(
                    sg4.ap()[:, :], PB[hh].ap()[:, :], AF.Silu)),
                    waits=[(T, mms[hh])] + wsg, sig=A)
                bank_free[hh] = (A, ags)
                whd = [HIDF[hh]] if HIDF[hh] else []
                vhd = p.emit("vector", (lambda hh=hh: nc.vector.tensor_mul(
                    hid4.ap()[:, hh, :], sg4.ap()[:, :], PB[2 + hh].ap()[:, :])),
                    waits=[(A, ags), (T, mms[2 + hh])] + whd, sig=V)
                bank_free[2 + hh] = (V, vhd)
                SG4_FREE = (V, vhd)
                mmh = p.emit("tensor", (lambda hh=hh: nc.tensor.matmul(
                    PB[4 + hh].ap()[:, :], h128_s.ap()[:, :], hid4.ap()[:, hh, :],
                    start=True, stop=True)),
                    waits=[(V, vhd)] + bank_wait(4 + hh), sig=T)
                HIDF[hh] = (T, mmh)
                if hh == 0:
                    ev = p.emit("vector", (lambda u=u: nc.vector.tensor_copy(
                        h1.ap()[:, u, :], PB[4].ap()[:, :])),
                        waits=[(T, mmh)], sig=V)
                    bank_free[4] = (V, ev)
                else:
                    sb = u % 2
                    whs = [h1stg_free[sb]] if h1stg_free[sb] else []
                    ev = p.emit("vector", (lambda sb=sb: nc.vector.tensor_copy(
                        h1stg.ap()[:, sb, :], PB[5].ap()[:, :])),
                        waits=[(T, mmh)] + whs, sig=V)
                    bank_free[5] = (V, ev)
                    d1 = p.dma("sync", h1sp[u * 128:(u + 1) * 128, :],
                               h1stg.ap()[:, sb, :], waits=[(V, ev)],
                               sig=H1SP_SEMS[sb])
                    h1stg_free[sb] = (H1SP_SEMS[sb], d1)
    H1_OK = (V, p.cnt[V])
    H1SP_OK = [(H1SP_SEMS[0], p.cnt[H1SP_SEMS[0]]),
               (H1SP_SEMS[1], p.cnt[H1SP_SEMS[1]])]
    FF_FREE = None
    for th in range(2 if PH_LEVEL >= 5 and PH3B_ON else 0):
        rb_marks = []
        if th == 1:
            W2TH0 = (T, p.cnt[T])
            for ch in range(8):
                rsem = H1SP_SEMS[ch % 2]
                rb = p.dma("sync", h1.ap()[:, ch * 8:(ch + 1) * 8, :],
                           h1sp[ch * 1024:(ch + 1) * 1024, :].rearrange(
                               "(u j) t -> j u t", j=128),
                           waits=([W2TH0] + H1SP_OK) if ch < 2 else [],
                           sig=rsem)
                rb_marks.append((rsem, rb))
        w2free = [None, None]
        for dc in range(4):
            for u in range(64):
                wb2 = u % 2
                ww2 = ([w2free[wb2]] if w2free[wb2] else []) + [(CC, AG_W2)]
                p.dma("sync", w2c.ap()[:, wb2, 0:256],
                      w2pp_bf[(2 * dc) * 8192 + u * 128:(2 * dc) * 8192 + (u + 1) * 128, :],
                      waits=ww2, sig=W2SEMS[wb2])
                l2 = p.dma("sync", w2c.ap()[:, wb2, 256:512],
                           w2pp_bf[(2 * dc + 1) * 8192 + u * 128:(2 * dc + 1) * 8192 + (u + 1) * 128, :],
                           sig=W2SEMS[wb2])
                mmlast = None
                for tt4 in range(4):
                    wrb = ([rb_marks[u // 8]]
                           if (th == 1 and dc == 0 and tt4 == 0) else [])
                    mmlast = p.emit("tensor", (lambda u=u, tt4=tt4, wb2=wb2:
                                nc.tensor.matmul(
                                    PB[tt4].ap()[:, :],
                                    h1.ap()[:, u, tt4 * 128:(tt4 + 1) * 128],
                                    w2c.ap()[:, wb2, :],
                                    start=(u == 0), stop=(u == 63))),
                                waits=[(W2SEMS[wb2], l2), H1_OK] + wrb +
                                      (bank_wait(tt4) if u == 0 else []),
                                sig=(T if (tt4 == 3) else None))
                w2free[wb2] = (T, mmlast)
            for tt4 in range(4):
                wff = [FF_FREE] if (FF_FREE and dc == 0 and tt4 == 0) else []
                ev = p.emit("vector", (lambda tt4=tt4, dc=dc: nc.vector.tensor_copy(
                    ffnst.ap()[:, tt4, dc, :], PB[tt4].ap()[:, :])),
                    waits=[(T, p.cnt[T])] + wff, sig=V)
                bank_free[tt4] = (V, ev)
        FF_OK = (V, p.cnt[V])
        for tt4 in range(4):
            tt = th * 4 + tt4
            crr = p.dma("sync", rsb2.ap()[:, :], resid_sp[tt * 128:(tt + 1) * 128, :],
                        waits=RESID_OK + [(V, p.cnt[V])], sig=SQ[1])
            src_ap = ffnst.ap()[:, tt4, :, :].rearrange("p a b -> p (a b)")
            vs = p.emit("vector", (lambda src_ap=src_ap: nc.vector.reduce_sum(
                smalls.ap()[:, 0:1], src_ap, axis=mybir.AxisListType.X)),
                waits=[FF_OK], sig=V)
            p.emit("scalar", (lambda src_ap=src_ap: nc.scalar.activation(
                sg4.ap()[:, :], src_ap[:, 0:512], AF.Square,
                accum_out=smalls.ap()[:, 70:71])), waits=[FF_OK])
            for qq in range(1, 4):
                p.emit("scalar", (lambda src_ap=src_ap, qq=qq: nc.scalar.activation(
                    sg4.ap()[:, :], src_ap[:, qq * 512:(qq + 1) * 512], AF.Square,
                    accum_out=smalls.ap()[:, 70 + qq:71 + qq])))
            af = p.emit("scalar", (lambda: nc.scalar.copy(
                smalls.ap()[:, 74:75], smalls.ap()[:, 70:71])), sig=A)
            vq = p.emit("vector", (lambda: nc.vector.tensor_single_scalar(
                smalls.ap()[:, 1:2], smalls.ap()[:, 70:71], smalls.ap()[:, 71:72],
                ALU.add)), waits=[(A, af)], sig=V)
            p.vspace(nc)
            vq = p.emit("vector", (lambda: nc.vector.tensor_single_scalar(
                smalls.ap()[:, 1:2], smalls.ap()[:, 1:2], smalls.ap()[:, 72:73],
                ALU.add)), sig=V)
            p.vspace(nc)
            vq = p.emit("vector", (lambda: nc.vector.tensor_single_scalar(
                smalls.ap()[:, 1:2], smalls.ap()[:, 1:2], smalls.ap()[:, 73:74],
                ALU.add)), sig=V)
            v1 = p.emit("vector", (lambda: nc.vector.tensor_scalar_mul(
                smalls.ap()[:, 2:3], smalls.ap()[:, 0:1], 1.0 / 2048.0)), sig=V)
            v2 = p.emit("vector", (lambda: nc.vector.tensor_scalar_mul(
                smalls.ap()[:, 3:4], smalls.ap()[:, 1:2], 1.0 / 2048.0)), sig=V)
            v3 = p.emit("vector", (lambda: nc.vector.tensor_single_scalar(
                smalls.ap()[:, 4:5], smalls.ap()[:, 2:3], smalls.ap()[:, 2:3],
                ALU.mult)), sig=V)
            p.vspace(nc)
            v4 = p.emit("vector", (lambda: nc.vector.tensor_single_scalar(
                smalls.ap()[:, 3:4], smalls.ap()[:, 3:4], smalls.ap()[:, 4:5],
                ALU.subtract)), sig=V)
            a2 = p.emit("scalar", (lambda: nc.scalar.activation(
                smalls.ap()[:, 5:6], smalls.ap()[:, 3:4], AF.Sqrt,
                bias=smalls.ap()[:, 54:55])), waits=[(V, v4)], sig=A)
            v5 = p.emit("vector", (lambda: nc.vector.reciprocal(
                smalls.ap()[:, 6:7], smalls.ap()[:, 5:6])), waits=[(A, a2)], sig=V)
            p.vspace(nc)
            ob = tt4 % 2
            vno = p.emit("vector", (lambda src_ap=src_ap, ob=ob: nc.vector.tensor_scalar(
                ostage.ap()[:, ob, :], src_ap, smalls.ap()[:, 2:3],
                smalls.ap()[:, 6:7], ALU.subtract, ALU.mult)),
                waits=[(V, v5)] + ([(OUT_SEMS[ob], out_last[-2][1])]
                                   if len(out_last) >= 2 else []), sig=V)
            voa = p.emit("vector", (lambda ob=ob: nc.vector.tensor_add(
                ostage.ap()[:, ob, :], ostage.ap()[:, ob, :], rsb2.ap()[:, :])),
                waits=[(SQ[1], crr)], sig=V)
            od = p.dma("sync", out_h[tt * 128:(tt + 1) * 128, :], ostage.ap()[:, ob, :],
                       waits=[(V, voa)], sig=OUT_SEMS[ob])
            out_last.append((OUT_SEMS[ob], od))
        FF_FREE = (V, p.cnt[V])

    tapn = []
    internal_map = dict(win_bf=win_bf, wout_bf=wout_bf, w1_bf=w1_bf, w3_bf=w3_bf,
                        w2pp_bf=w2pp_bf, z_sp=z_sp, xfm_sp=xfm_sp, st_sp=st_sp,
                        resid_sp=resid_sp, fl_negrp=fl_negrp, fl_r=fl_r,
                        cc_st_out=cc_st_out, cc_ld_out=cc_ld_out, hin_sp=hin_sp,
                        ln1_dbg=ln1_dbg, u_dbg=u_dbg, sm_dbg=sm_dbg, lt_dbg=lt_dbg,
                        attn_sp=attn_sp, y_dbg=y_dbg, sc_dbg=sc_dbg, cb_dbg=cb_dbg,
                        ygfm_dbg=ygfm_dbg, woc_dbg=woc_dbg)
    p.wait("sync", [(nm, p.cnt[nm]) for nm in list(p.cnt)])
    final = None
    for name, shape, dt in debug_taps:
        src = internal_map[name]
        dst = nc.dram_tensor("dbg_" + name, list(shape), dt, kind="ExternalOutput").ap()
        final = p.dma("sync", dst[:, :] if len(shape) == 2 else dst[:, :, :],
                      src[:, :] if len(shape) == 2 else src[:, :, :], sig=SQ[5])
        tapn.append("dbg_" + name)


    with nc.Block() as block:
        @block.sync
        def _(e):
            p.replay("sync", e)

        @block.gpsimd
        def _(e):
            p.replay("gpsimd", e)

        @block.tensor
        def _(e):
            p.replay("tensor", e)

        @block.vector
        def _(e):
            p.replay("vector", e)

        @block.scalar
        def _(e):
            p.replay("scalar", e)

    return nc, tapn



def make_in_maps(inp):
    u = np.asarray(inp["L0_attn_u"], np.float32)
    W_in = np.ascontiguousarray(np.asarray(inp["L0_attn_W_in"], np.float32))
    conv_w = np.asarray(inp["L0_attn_conv_w"], np.float32)
    conv_b = np.asarray(inp["L0_attn_conv_b"], np.float32)
    dt_bias = np.asarray(inp["L0_attn_dt_bias"], np.float32)
    A_log = np.asarray(inp["L0_attn_A_log"], np.float32)
    Dp = np.asarray(inp["L0_attn_Dp"], np.float32)
    norm_w = np.asarray(inp["L0_attn_norm_w"], np.float32)
    W_out = np.ascontiguousarray(np.asarray(inp["L0_attn_W_out"], np.float32))
    W1 = np.ascontiguousarray(np.asarray(inp["L0_ffn_W1"], np.float32))
    W2 = np.ascontiguousarray(np.asarray(inp["L0_ffn_W2"], np.float32))
    W3 = np.ascontiguousarray(np.asarray(inp["L0_ffn_W3"], np.float32))

    convw_t = np.ascontiguousarray(conv_w.reshape(34, 128, 4).transpose(1, 0, 2))
    convb_t = np.ascontiguousarray(conv_b.reshape(34, 128).T)
    dtb_row = np.zeros((1, 128), np.float32)
    dtb_row[0, :64] = dt_bias
    alog_col = np.ascontiguousarray(A_log.reshape(64, 1))
    dp_rep = np.ascontiguousarray(np.repeat(Dp, HP).reshape(1, DI))
    nw_row = np.ascontiguousarray(norm_w.reshape(1, DI))

    WinT = np.zeros((DM, OPAD), np.float32)
    WinT[:, :OPROJ] = W_in.T
    WoutT = np.ascontiguousarray(W_out.T)
    W1T = np.ascontiguousarray(W1.T)
    W3T = np.ascontiguousarray(W3.T)
    maps = []
    for c in range(8):
        b, q = divmod(c, 4)
        t0 = q * Q
        ue = np.zeros((EXT, DM), np.float32)
        lo = max(0, t0 - 3)
        ue[3 - (t0 - lo):] = u[b, lo:t0 + Q]
        selm = np.zeros((4, 3), np.float32)
        selv = np.zeros((3, 1), np.float32)
        for s in range(3):
            if s < q:
                selv[s, 0] = 1.0
                for k in range(s + 1, q):
                    selm[k, s] = 1.0
        nwt = np.ascontiguousarray(
            norm_w[512 * c:512 * (c + 1)].reshape(4, 128).T)
        maps.append(dict(
            u_ext=np.ascontiguousarray(ue),
            win_sl=np.ascontiguousarray(WinT[256 * c:256 * (c + 1)]),
            wout_sl=np.ascontiguousarray(WoutT[512 * c:512 * (c + 1)]),
            w1_sl=np.ascontiguousarray(W1T[256 * c:256 * (c + 1)]),
            w3_sl=np.ascontiguousarray(W3T[256 * c:256 * (c + 1)]),
            w2_sl=np.ascontiguousarray(W2[256 * c:256 * (c + 1)]),
            convw_t=convw_t, convb_t=convb_t, dtb_row=dtb_row,
            alog_col=alog_col, dp_rep=dp_rep, nw_row=nw_row,
            nw_t=nwt, selm=selm, selv=selv,
        ))
    return maps


def kernel(**inp):
    nc, _ = build_program()
    maps = make_in_maps(inp)
    res = run_bass_kernel_spmd(nc, maps, core_ids=list(range(8)))
    out = np.stack([res.results[c]["out"] for c in range(8)], axis=0)
    return np.ascontiguousarray(out.reshape(2, 4, Q, DM).reshape(2, L, DM))


# version 10
# speedup vs baseline: 1.1282x; 1.0032x over previous
import numpy as np
import ml_dtypes
import concourse.bass as bass
import concourse.mybir as mybir
from concourse.bass_utils import run_bass_kernel_spmd

F32 = mybir.dt.float32
BF16 = mybir.dt.bfloat16
AF = mybir.ActivationFunctionType
ALU = mybir.AluOpType
BF = ml_dtypes.bfloat16

B, L, DM = 2, 4096, 2048
DI, NH, HP, DS, DCONV = 4096, 64, 64, 128, 4
CONV = DI + 2 * DS
OPROJ = 2 * DI + 2 * DS + NH
OPAD = 8576
Q = 1024
EXT = Q + 3
CH = 128
NCH = Q // CH
EPS = 1e-5
FWHT_SCALE = 1.0 / np.sqrt(8192.0)

ENGS = ("sync", "gpsimd", "tensor", "vector", "scalar")


class Prog:

    def __init__(self, nc):
        self.nc = nc
        self.ops = {e: [] for e in ENGS}
        self.semh = {}
        self.cnt = {}
        self.waited = {e: {} for e in ENGS}

    def sem(self, name):
        self.semh[name] = self.nc.alloc_semaphore(name)
        self.cnt[name] = 0
        return name

    def emit(self, eng, fn, waits=(), sig=None, sig_n=1):
        ws = []
        for s, v in waits:
            if v is None or v <= 0:
                continue
            if v > self.waited[eng].get(s, 0):
                self.waited[eng][s] = v
                ws.append((self.semh[s], v))
        tgt = None
        ret = None
        if sig is not None:
            self.cnt[sig] += sig_n
            ret = self.cnt[sig]
            tgt = (self.semh[sig], sig_n)
        self.ops[eng].append((tuple(ws), fn, tgt))
        return ret

    def vspace(self, nc):
        self.emit("vector", lambda: nc.vector.tensor_scalar_mul(
            self._sp_ap, self._sp_ap, 1.0))

    def wait(self, eng, waits):
        for s, v in waits:
            if v is None or v <= 0:
                continue
            if v > self.waited[eng].get(s, 0):
                self.waited[eng][s] = v
                h = self.semh[s]
                self.ops[eng].append((((h, v),), None, None))

    def dma(self, q, out, in_, waits=(), sig=None, transpose=False):
        eng = {"sync": self.nc.sync, "gpsimd": self.nc.gpsimd,
               "scalar": self.nc.scalar}[q]
        if transpose:
            fn = lambda: eng.dma_start_transpose(out=out, in_=in_)
        else:
            fn = lambda: eng.dma_start(out=out, in_=in_)
        waits = list(waits)
        if sig is not None and self.cnt[sig] > 0:
            waits.append((sig, self.cnt[sig]))
        return self.emit(q, fn, waits=waits, sig=sig, sig_n=16)

    def replay(self, eng_name, eng_obj):
        for ws, fn, tgt in self.ops[eng_name]:
            for h, v in ws:
                eng_obj.wait_ge(h, v)
            if fn is not None:
                inst = fn()
                if tgt is not None:
                    inst.then_inc(tgt[0], tgt[1])


def ceil_div(a, b):
    return (a + b - 1) // b


DEBUG_LN1 = False
DEBUG_PH3 = False
PH3_CHUNKS = 8
PH_LEVEL = 5

def build_program(debug_taps=()):
    nc = bass.Bass("TRN2", target_bir_lowering=False, debug=False)

    def ein(name, shape, dt=F32):
        return nc.dram_tensor(name, shape, dt, kind="ExternalInput").ap()

    u_ext = ein("u_ext", [EXT, DM])
    win_sl = ein("win_sl", [256, OPAD])
    wout_sl = ein("wout_sl", [512, DM])
    w1_sl = ein("w1_sl", [256, 8192])
    w3_sl = ein("w3_sl", [256, 8192])
    w2_sl = ein("w2_sl", [256, 8192])
    nw_t = ein("nw_t", [128, 4])
    convw_t = ein("convw_t", [128, 34, 4])
    convb_t = ein("convb_t", [128, 34])
    dtb_row = ein("dtb_row", [1, 128])
    alog_col = ein("alog_col", [64, 1])
    dp_rep = ein("dp_rep", [1, DI])
    nw_row = ein("nw_row", [1, DI])
    selm = ein("selm", [4, 3])
    selv = ein("selv", [3, 1])

    out_h = nc.dram_tensor("out", [Q, DM], F32, kind="ExternalOutput").ap()

    def internal(name, shape, dt):
        return nc.dram_tensor(name, shape, dt, kind="Internal").ap()

    def shared(name, shape, dt):
        return nc.dram_tensor(name, shape, dt, kind="Internal",
                              addr_space="Shared").ap()

    win_bf = shared("win_bf", [DM, OPAD], BF16)
    wout_bf = shared("wout_bf", [DI, DM], BF16)
    w1_bf = shared("w1_bf", [DM, 8192], BF16)
    w3_bf = shared("w3_bf", [DM, 8192], BF16)
    w2pp_bf = shared("w2pp_bf", [8 * 8192, 256], BF16)
    ccin_win = internal("ccin_win", [256, OPAD], BF16)
    ccin_wout = internal("ccin_wout", [512, DM], BF16)
    ccin_w1 = internal("ccin_w1", [256, 8192], BF16)
    ccin_w3 = internal("ccin_w3", [256, 8192], BF16)
    ccin_w2 = internal("ccin_w2", [8192, 256], BF16)

    z_sp = internal("z_sp", [DI, Q], BF16)
    xfm_sp = internal("xfm_sp", [DI, Q], BF16)
    st_sp = internal("st_sp", [NCH, 128, NH * HP], BF16)
    resid_sp = internal("resid_sp", [Q, DM], F32)
    fl_negrp = internal("fl_negrp", [1, NCH * NH * CH], F32)
    fl_r = internal("fl_r", [1, NCH * NH * CH], F32)
    wrow_b = internal("wrow_b", [1, 4 * NH], F32)
    cc_st_in = internal("cc_st_in", [128, NH * HP], BF16)
    cc_st_out = internal("cc_st_out", [512, NH * HP], BF16)
    cc_ld_in = internal("cc_ld_in", [NH, 1], F32)
    cc_ld_out = internal("cc_ld_out", [4 * NH, 1], F32)

    idb = nc.inline_tensor(np.eye(128, dtype=BF), name="id_bf").ap()
    idf = nc.inline_tensor(np.eye(128, dtype=np.float32), name="id_f32").ap()
    ones_row = nc.inline_tensor(np.ones((1, 512), np.float32), name="ones_row").ap()
    ones_big = nc.inline_tensor(np.ones((1, NH * CH), np.float32), name="ones_big").ap()
    h128c = nc.inline_tensor(
        np.array([[1.0 - 2.0 * (bin(i & j).count("1") & 1) for j in range(128)]
                  for i in range(128)], dtype=BF), name="h128").ap()
    mT = np.triu(np.ones((128, 128), np.float32)).astype(BF)
    maskT_c = nc.inline_tensor(mT, name="maskT").ap()

    SB = {}
    SB_BASE = 16384 + 2048
    SB_LIMIT = 229376
    sb_off = [SB_BASE]

    def sbuf(name, shape, dt, at=None):
        nbytes = int(np.prod(shape[1:])) * mybir.dt.size(dt)
        if at is None:
            at = sb_off[0]
            sb_off[0] = at + ((nbytes + 31) // 32) * 32
        h = nc.alloc_sbuf_tensor_at(name, shape, dt, offset=at)
        SB[name] = h
        return h

    c_fm = sbuf("c_fm", [128, Q], BF16)
    b_fm = sbuf("b_fm", [128, Q], BF16)
    r_tm = sbuf("r_tm", [128, NCH, NH], F32)
    expf_tm = sbuf("expf_tm", [128, NCH, NH], F32)
    dt_tm = sbuf("dt_tm", [128, NCH, NH], F32)
    dt_fm = sbuf("dt_fm", [64, Q], F32)
    lndt_fm = sbuf("lndt_fm", [64, Q], F32)
    r_fm = sbuf("r_fm", [64, Q], F32)
    hT = sbuf("hT", [128, NH * HP], BF16)
    dcol = sbuf("dcol", [128, NCH * NH], F32)
    dpb = sbuf("dpb", [128, DI], BF16)
    lndt_tm = sbuf("lndt_tm", [128, NCH, NH], F32)
    nrp_tm = sbuf("nrp_tm", [128, NCH, NH], F32)
    rl_cols = sbuf("rl_cols", [128, NCH * NH], F32)
    smalls = sbuf("smalls", [128, 256], F32)
    p_spacer_pending = True
    wq = sbuf("wq", [4, 64], F32)
    wq2 = sbuf("wq2", [3, 64], F32)
    wcols = sbuf("wcols", [128, 4 * NH], F32)
    cw_sb = sbuf("cw_sb", [128, 34, 4], F32)
    cb_sb = sbuf("cb_sb", [128, 34], F32)
    maskT_s = sbuf("maskT_s", [128, 128], BF16)
    h128_s = sbuf("h128_s", [128, 128], BF16)
    dtb_s = sbuf("dtb_s", [1, 128], F32)
    idb_s = sbuf("idb_s", [128, 128], BF16)
    idf_s = sbuf("idf_s", [128, 128], F32)
    onesr_s = sbuf("onesr_s", [1, 512], F32)
    rowb_s = sbuf("rowb_s", [1, 512], F32)

    R0 = sb_off[0]
    RSZ = SB_LIMIT - R0
    assert RSZ >= 150 * 1024, (R0, RSZ)

    def reg(name, shape, dt, at):
        nbytes = int(np.prod(shape[1:])) * mybir.dt.size(dt)
        assert at + nbytes <= RSZ, (name, at, nbytes, RSZ)
        return nc.alloc_sbuf_tensor_at(name, shape, dt, offset=R0 + at)

    w2a = reg("w2a", [128, 8192], F32, 0)
    w2b = reg("w2b", [128, 8192], F32, 32768)
    cvt = reg("cvt", [128, DI], F32, 65536)
    cvb = reg("cvb", [128, 2, 2048], BF16, 81920)
    nwb = reg("nwb", [128, DI], F32, 90112)
    srow = reg("srow", [1, 4096], F32, 106496)
    w2stg = reg("w2stg", [128, 4, 128], BF16, 122880)

    PB = []
    for i in range(6):
        PB.append(nc.alloc_psum_tensor(f"pb{i}", [128, 512], F32))
    PT = []
    for i in range(2):
        PT.append(nc.alloc_psum_tensor(f"pt{i}", [128, 1024], BF16))

    p = Prog(nc)
    p._sp_ap = smalls.ap()[:, 254:255]
    SQ = [p.sem(f"sq{i}") for i in range(16)]
    GQ = [p.sem(f"gq{i}") for i in range(6)]
    T = p.sem("T")
    V = p.sem("V")
    A = p.sem("A")
    CC = p.sem("CC")

    bank_free = {i: None for i in range(8)}

    def bank_wait(i):
        return [bank_free[i]] if bank_free[i] is not None else []

    def cast_weight(src, dst, rows, qsem_ld, qsem_st):
        done = None
        r0 = 0
        i = 0
        while r0 < rows:
            n = min(128, rows - r0)
            buf = i % 2
            w = [cvb_free[buf]] if cvb_free[buf] else []
            c1 = p.dma("gpsimd", cvb.ap()[0:n, buf, :], src[r0:r0 + n, :],
                       waits=w, sig=qsem_ld)
            done = p.dma("gpsimd", dst[r0:r0 + n, :], cvb.ap()[0:n, buf, :],
                         waits=[(qsem_ld, c1)], sig=qsem_st)
            cvb_free[buf] = (qsem_st, done)
            r0 += n
            i += 1
        return done

    zv0 = p.emit("vector", lambda: nc.vector.memset(
        cvb.ap()[:, :, :].rearrange("p a c -> p (a c)"), 0.0), sig=V)
    czp = None
    cvb_free = [(V, zv0), (V, zv0)]

    def cast_weight(src_ap, dst, rows, cols, qsem_ld, qsem_st, extra=None):
        done = None
        i = 0
        for r0 in range(0, rows, 128):
            n = min(128, rows - r0)
            for c0 in range(0, cols, 2048):
                w = min(2048, cols - c0)
                buf = i % 2
                ww = [cvb_free[buf]] if cvb_free[buf] else []
                c1 = p.dma("gpsimd", cvb.ap()[0:n, buf, 0:w],
                           src_ap[r0:r0 + n, c0:c0 + w], waits=ww, sig=qsem_ld)
                done = p.dma("gpsimd", dst[r0:r0 + n, c0:c0 + w],
                             cvb.ap()[0:n, buf, 0:w],
                             waits=[(qsem_ld, c1)], sig=qsem_st)
                cvb_free[buf] = (qsem_st, done)
                i += 1
        return done

    cwin = cast_weight(win_sl, ccin_win, 256, OPAD, GQ[0], GQ[1])
    agwin = p.emit("gpsimd", lambda: nc.gpsimd.collective_compute(
        "AllGather", ALU.bypass, replica_groups=[list(range(8))],
        ins=[ccin_win.opt()], outs=[win_bf[:, :]]),
        waits=[(GQ[1], cwin)], sig=CC)

    lnw = p.dma("sync", smalls.ap()[:, 16:20], nw_t[:, :], sig=SQ[1])
    cwout = None
    cvt_free = None
    for i in range(4):
        w = [cvt_free] if cvt_free else []
        c1 = p.dma("sync", cvt.ap()[:, 0:2048], wout_sl[i * 128:(i + 1) * 128, :],
                   waits=w + [(SQ[1], lnw)], sig=SQ[1])
        buf = i % 2
        ww = [cvb_free[buf]] if cvb_free[buf] else []
        mv = p.emit("vector", (lambda i=i, buf=buf: nc.vector.tensor_single_scalar(
            cvb.ap()[:, buf, :], cvt.ap()[:, 0:2048],
            smalls.ap()[:, 16 + i:17 + i], ALU.mult)),
            waits=[(SQ[1], c1)] + ww, sig=V)
        cvt_free = (V, mv)
        cwout = p.dma("gpsimd", ccin_wout[i * 128:(i + 1) * 128, :],
                      cvb.ap()[:, buf, :], waits=[(V, mv)], sig=GQ[1])
        cvb_free[buf] = (GQ[1], cwout)
    agwout = p.emit("gpsimd", lambda: nc.gpsimd.collective_compute(
        "AllGather", ALU.bypass, replica_groups=[list(range(8))],
        ins=[ccin_wout.opt()], outs=[wout_bf[:, :]]),
        waits=[(GQ[1], cwout)], sig=CC)

    cw1 = cast_weight(w1_sl, ccin_w1, 256, 8192, GQ[0], GQ[1])
    agw1 = p.emit("gpsimd", lambda: nc.gpsimd.collective_compute(
        "AllGather", ALU.bypass, replica_groups=[list(range(8))],
        ins=[ccin_w1.opt()], outs=[w1_bf[:, :]]),
        waits=[(GQ[1], cw1)], sig=CC)
    cw3 = cast_weight(w3_sl, ccin_w3, 256, 8192, GQ[0], GQ[1])
    agw3 = p.emit("gpsimd", lambda: nc.gpsimd.collective_compute(
        "AllGather", ALU.bypass, replica_groups=[list(range(8))],
        ins=[ccin_w3.opt()], outs=[w3_bf[:, :]]),
        waits=[(GQ[1], cw3)], sig=CC)

    idld = p.dma("sync", idb_s.ap()[:, :], idb[:, :], sig=SQ[0])
    vv = None
    cw2 = None
    PT0_F = None
    for d in range(2):
        cw2l = p.dma("sync", w2a.ap()[:, :], w2_sl[d * 128:(d + 1) * 128, :],
                     waits=[(V, vv)] if vv else [], sig=SQ[1])
        pa = w2a.ap()[:, :]
        pb_ = w2b.ap()[:, :]
        for b_ in range(6):
            s = 1 << b_
            blk = 2 * s * 128
            g = 8192 // blk
            cur_ap = pa if (b_ % 2 == 0) else pb_
            oth_ap = pb_ if (b_ % 2 == 0) else pa
            lo_in = cur_ap.rearrange("p (g r) -> p g r", g=g)[:, :, 0:s * 128]
            hi_in = cur_ap.rearrange("p (g r) -> p g r", g=g)[:, :, s * 128:blk]
            lo_out = oth_ap.rearrange("p (g r) -> p g r", g=g)[:, :, 0:s * 128]
            hi_out = oth_ap.rearrange("p (g r) -> p g r", g=g)[:, :, s * 128:blk]
            w = [(SQ[1], cw2l)] if b_ == 0 else []
            vv = p.emit("vector", (lambda a=lo_out, x=lo_in, y=hi_in:
                                   nc.vector.tensor_add(a, x, y)), waits=w, sig=V)
            vv = p.emit("vector", (lambda a=hi_out, x=lo_in, y=hi_in:
                                   nc.vector.tensor_sub(a, x, y)), sig=V)
        W2SEM = [SQ[3], SQ[5], SQ[11], SQ[12]]
        w2stg_free = [None] * 4
        pt_free2 = [None, None]
        cw2_marks = []
        for hhalf in range(2):
            vv2 = p.emit("vector", (lambda h=hhalf: nc.vector.tensor_scalar_mul(
                cvb.ap()[:, :, :].rearrange("p a c -> p (a c)")[:, 0:4096],
                w2a.ap()[:, h * 4096:(h + 1) * 4096], FWHT_SCALE)),
                waits=([(V, vv)] if vv else []) + [(T, p.cnt[T])] +
                      [x for x in cvb_free if x], sig=V)
            vv = vv2
            for ft in range(32):
                f = hhalf * 32 + ft
                ptb = ft % 2
                sb4 = ft % 4
                wpt0 = [pt_free2[ptb]] if pt_free2[ptb] else []
                tm = p.emit("tensor", (lambda ft=ft, ptb=ptb: nc.tensor.transpose(
                    PT[ptb].ap()[:, 0:128],
                    cvb.ap()[:, :, :].rearrange("p a c -> p (a c)")[:, ft * 128:(ft + 1) * 128],
                    idb_s.ap()[:, :])),
                    waits=[(V, vv2), (SQ[0], idld)] + wpt0, sig=T)
                wst4 = [w2stg_free[sb4]] if w2stg_free[sb4] else []
                ev = p.emit("vector", (lambda ptb=ptb, sb4=sb4: nc.vector.tensor_copy(
                    w2stg.ap()[:, sb4, :], PT[ptb].ap()[:, 0:128])),
                    waits=[(T, tm)] + wst4, sig=V)
                pt_free2[ptb] = (V, ev)
                cw2 = p.dma("sync", ccin_w2[f * 128:(f + 1) * 128, d * 128:(d + 1) * 128],
                            w2stg.ap()[:, sb4, :], waits=[(V, ev)], sig=W2SEM[sb4])
                w2stg_free[sb4] = (W2SEM[sb4], cw2)
                cw2_marks.append((W2SEM[sb4], cw2))
            cvb_free = [cw2_marks[-1], cw2_marks[-2]]
    w2done = {}
    for sm, vval in cw2_marks:
        w2done[sm] = max(w2done.get(sm, 0), vval)
    agw2 = p.emit("gpsimd", lambda: nc.gpsimd.collective_compute(
        "AllGather", ALU.bypass, replica_groups=[list(range(8))],
        ins=[ccin_w2.opt()], outs=[w2pp_bf[:, :]]),
        waits=list(w2done.items()), sig=CC)

    AG_WIN, AG_WOUT, AG_W1, AG_W3, AG_W2 = agwin, agwout, agw1, agw3, agw2


    rowhbm = internal("rowhbm", [1, 512], F32)
    ln1_dbg = internal("ln1_dbg", [2, 128, 16, 515], BF16)
    u_dbg = internal("u_dbg", [128, 2048], F32)
    sm_dbg = internal("sm_dbg", [128, 256], F32)
    lt_dbg = internal("lt_dbg", [128, 2048], BF16)

    wcache = reg("wc", [128, 2, 16, 512], BF16, 0)
    ln1fm = reg("ln1fm", [128, 16, 1027], BF16, 32768)
    xbc = reg("xbc", [128, 4, 1027], BF16, 65632)
    ustage = reg("ustage", [128, 2, 2048], F32, 73856)
    ln1tm = reg("ln1tm", [128, 2, 2048], BF16, 90240)
    lnscr = reg("lnscr", [128, 2048], BF16, 98432)
    zstage = reg("zstage", [128, 2, 512], BF16, 102528)
    convacc = reg("convacc", [128, 2, 1024], F32, 104576)
    convstage = reg("convstage", [128, 2, 1024], BF16, 112768)
    dtscr = reg("dtscr", [128, 64], F32, 116864)
    xstage = reg("xstage", [128, 2, 4096], BF16, 117120)
    xw_sb = reg("xw_sb", [128, 512], BF16, 133504)
    stst = reg("stst", [128, 4096], BF16, 134528)
    wtm = reg("wtm", [128, 64], F32, 142720)
    wtm2 = reg("wtm2", [128, 64], F32, 142976)
    dascr = reg("dascr", [64, 1024], F32, 143232)
    nrpscr = reg("nrpscr", [64, 1024], F32, 147328)
    btm = reg("btm", [128, 128], BF16, 151424)
    stripsb = reg("stripsb", [3, 512], BF16, 151680)

    PH0_V = p.cnt[V]
    PH0_A = p.cnt[A]
    PH0_T = p.cnt[T]
    PH0_GQ1 = p.cnt[GQ[1]]
    PH0_SQ1 = p.cnt[SQ[1]]
    PH0_BAR = [(V, PH0_V), (A, PH0_A), (T, PH0_T), (GQ[1], PH0_GQ1), (SQ[1], PH0_SQ1)]
    for _e in ("sync", "gpsimd", "tensor", "vector", "scalar"):
        p.wait(_e, PH0_BAR)

    lsm = p.dma("sync", cw_sb.ap()[:, :, :], convw_t[:, :, :], sig=SQ[0])
    lsm = p.dma("sync", cb_sb.ap()[:, :], convb_t[:, :], sig=SQ[0])
    lsm = p.dma("sync", maskT_s.ap()[:, :], maskT_c[:, :], sig=SQ[0])
    lsm = p.dma("sync", h128_s.ap()[:, :], h128c[:, :], sig=SQ[0])
    lsm = p.dma("sync", dtb_s.ap()[:, :], dtb_row[:, :], sig=SQ[0])
    lsm = p.dma("sync", onesr_s.ap()[:, :], ones_row[:, :], sig=SQ[0])
    lsm = p.dma("sync", idb_s.ap()[:, :], idb[:, :], sig=SQ[0])
    lsm = p.dma("sync", idf_s.ap()[:, :], idf[:, :], sig=SQ[0])
    lsm = p.dma("sync", smalls.ap()[0:64, 48:49], alog_col[:, :], sig=SQ[0])
    SMALL_LD = lsm
    ldp = p.dma("gpsimd", dpb.ap()[0:1, 0:4096], dp_rep[:, :], sig=GQ[2])
    aa = p.emit("scalar", lambda: nc.scalar.activation(
        smalls.ap()[0:64, 49:50], smalls.ap()[0:64, 48:49], AF.Exp),
        waits=[(SQ[0], SMALL_LD)], sig=A)
    vna = p.emit("vector", lambda: nc.vector.tensor_scalar_mul(
        smalls.ap()[0:64, 50:51], smalls.ap()[0:64, 49:50], -1.0),
        waits=[(A, aa)], sig=V)
    NEGA = vna
    dpw = None
    for j in range(8):
        mm = p.emit("tensor", (lambda j=j: nc.tensor.matmul(
            PB[4].ap()[:, :], h128_s.ap()[0:1, 0:128],
            dpb.ap()[0:1, j * 512:(j + 1) * 512], start=True, stop=True)),
            waits=[(SQ[0], SMALL_LD), (GQ[2], ldp)] + bank_wait(4), sig=T)
        dpw = p.emit("vector", (lambda j=j: nc.vector.tensor_copy(
            dpb.ap()[:, j * 512:(j + 1) * 512], PB[4].ap()[:, :])),
            waits=[(T, mm)], sig=V)
        bank_free[4] = (V, dpw)
    DPB_OK = dpw

    hz = p.emit("vector", lambda: nc.vector.memset(hT.ap()[:, :], 0.0), sig=V)
    ve1 = p.emit("vector", lambda: nc.vector.memset(smalls.ap()[:, 54:55], EPS), sig=V)
    ve2 = p.emit("vector", lambda: nc.vector.memset(smalls.ap()[:, 55:56], 1.0 / DI), sig=V)
    EPSC = ve2

    UST_FREE = [None, None]
    LN1TM_FREE = [None, None]
    PT_FREE = [None, None]
    SSQ_FREE = [None, None]
    tiles = [(0, 128, 0), (128, 128, 128), (256, 128, 256), (384, 128, 384),
             (512, 128, 512), (640, 128, 640), (768, 128, 768), (896, 128, 896),
             (1024, 3, 1024)]
    for ti, (e0, n, co) in enumerate(tiles):
        ub = ti % 2
        w = list(UST_FREE[ub]) if UST_FREE[ub] else []
        usem = SQ[1 + 5 * ub]
        cu = p.dma("sync", ustage.ap()[0:n, ub, :], u_ext[e0:e0 + n, :],
                   waits=w, sig=usem)
        base = 16 * (ti % 2)
        vsum = p.emit("vector", (lambda n=n, ub=ub, b=base: nc.vector.reduce_sum(
            smalls.ap()[0:n, b + 0:b + 1], ustage.ap()[0:n, ub, :],
            axis=mybir.AxisListType.X)), waits=[(usem, cu)], sig=V)
        p.emit("scalar", (lambda n=n, ub=ub, b=base: nc.scalar.activation(
            lnscr.ap()[0:n, :], ustage.ap()[0:n, ub, :], AF.Square,
            accum_out=smalls.ap()[0:n, b + 1:b + 2])),
            waits=[(usem, cu)] + ([SSQ_FREE[ti % 2]] if SSQ_FREE[ti % 2] else []))
        asq = p.emit("scalar", (lambda n=n, b=base: nc.scalar.copy(
            smalls.ap()[0:n, b + 7:b + 8], smalls.ap()[0:n, b + 1:b + 2])),
            sig=A)
        p.vspace(nc)
        v1 = p.emit("vector", (lambda n=n, b=base: nc.vector.tensor_scalar_mul(
            smalls.ap()[0:n, b + 2:b + 3], smalls.ap()[0:n, b + 0:b + 1],
            1.0 / 2048.0)), waits=[(V, vsum)], sig=V)
        v2 = p.emit("vector", (lambda n=n, b=base: nc.vector.tensor_scalar_mul(
            smalls.ap()[0:n, b + 3:b + 4], smalls.ap()[0:n, b + 1:b + 2],
            1.0 / 2048.0)), waits=[(A, asq)], sig=V)
        SSQ_FREE[ti % 2] = (V, v2)
        v3 = p.emit("vector", (lambda n=n, b=base: nc.vector.tensor_single_scalar(
            smalls.ap()[0:n, b + 4:b + 5], smalls.ap()[0:n, b + 2:b + 3],
            smalls.ap()[0:n, b + 2:b + 3], ALU.mult)), sig=V)
        p.vspace(nc)
        v4 = p.emit("vector", (lambda n=n, b=base: nc.vector.tensor_single_scalar(
            smalls.ap()[0:n, b + 3:b + 4], smalls.ap()[0:n, b + 3:b + 4],
            smalls.ap()[0:n, b + 4:b + 5], ALU.subtract)), sig=V)
        a2 = p.emit("scalar", (lambda n=n, b=base: nc.scalar.activation(
            smalls.ap()[0:n, b + 5:b + 6], smalls.ap()[0:n, b + 3:b + 4],
            AF.Sqrt, bias=smalls.ap()[0:n, 54:55])),
            waits=[(V, v4), (V, EPSC)], sig=A)
        v5 = p.emit("vector", (lambda n=n, b=base: nc.vector.reciprocal(
            smalls.ap()[0:n, b + 6:b + 7], smalls.ap()[0:n, b + 5:b + 6])),
            waits=[(A, a2)], sig=V)
        wlt = [LN1TM_FREE[ub]] if LN1TM_FREE[ub] else []
        p.vspace(nc)
        v6 = p.emit("vector", (lambda n=n, ub=ub, b=base: nc.vector.tensor_scalar(
            ln1tm.ap()[0:n, ub, :], ustage.ap()[0:n, ub, :],
            smalls.ap()[0:n, b + 2:b + 3], smalls.ap()[0:n, b + 6:b + 7],
            ALU.subtract, ALU.mult)), waits=[(V, v5)] + wlt, sig=V)
        UST_FREE[ub] = [(V, v6), (A, asq)]
        for k in range(16):
            ptb = k % 2
            wpt = [PT_FREE[ptb]] if PT_FREE[ptb] else []
            tm = p.emit("tensor", (lambda n=n, ub=ub, k=k, ptb=ptb:
                        nc.tensor.transpose(
                            PT[ptb].ap()[:, 0:n],
                            ln1tm.ap()[0:n, ub, k * 128:(k + 1) * 128],
                            idb_s.ap()[0:n, 0:n])),
                        waits=[(V, v6), (SQ[0], SMALL_LD)] + wpt, sig=T)
            if k % 2 == 0:
                ev = p.emit("vector", (lambda n=n, k=k, co=co, ptb=ptb:
                            nc.vector.tensor_copy(
                                ln1fm.ap()[:, k, co:co + n], PT[ptb].ap()[:, 0:n])),
                            waits=[(T, tm)], sig=V)
                PT_FREE[ptb] = (V, ev)
            else:
                ev = p.emit("scalar", (lambda n=n, k=k, co=co, ptb=ptb:
                            nc.scalar.copy(
                                ln1fm.ap()[:, k, co:co + n], PT[ptb].ap()[:, 0:n])),
                            waits=[(T, tm)], sig=A)
                PT_FREE[ptb] = (A, ev)
        LN1TM_FREE[ub] = (T, p.cnt[T])
    LN1FM_OK = [(V, p.cnt[V]), (A, p.cnt[A])]

    OCW = [512] * 16 + [384]
    WC_FREE = [None, None]
    ZST_FREE = [None, None]
    CONVST_FREE = [None, None]
    CONVACC_FREE = [None, None]
    STRIP_FREE = None
    slab_free = {}
    slab_ready = {}
    xspill = {}
    dt_mm_done = {}
    z_done = 0

    for oc in range(17):
        wb = oc % 2
        width = OCW[oc]
        wwc = [WC_FREE[wb]] if WC_FREE[wb] else []
        lw = []
        for k in range(16):
            wsem = SQ[[2, 7, 11, 12][k % 4]]
            lwv = p.dma("sync", wcache.ap()[:, wb, k, 0:width],
                        win_bf[k * 128:(k + 1) * 128, oc * 512:oc * 512 + width],
                        waits=wwc + [(CC, AG_WIN)], sig=wsem)
            lw.append((wsem, lwv))
            wwc = []
        lw = [(sm, max(vv for sm2, vv in lw if sm2 == sm)) for sm in set(s0 for s0, _ in lw)]
        nots = 4 if oc < 16 else 2
        if oc >= 8:
            mms = None
            for k in range(16):
                mms = p.emit("tensor", (lambda k=k, wb=wb, width=width:
                            nc.tensor.matmul(
                                PB[5].ap()[0:3, 0:width],
                                ln1fm.ap()[:, k, 0:3],
                                wcache.ap()[:, wb, k, 0:width],
                                start=(k == 0), stop=(k == 15))),
                            waits=(lw + LN1FM_OK + bank_wait(5)) if k == 0 else [],
                            sig=(T if k == 15 else None))
            wstf = [STRIP_FREE] if STRIP_FREE else []
            evs = p.emit("vector", (lambda width=width: nc.vector.tensor_copy(
                stripsb.ap()[0:3, 0:width], PB[5].ap()[0:3, 0:width])),
                waits=[(T, mms)] + wstf, sig=V)
            bank_free[5] = (V, evs)
            STRIP_V = evs
        for oti in range(nots):
            g = oc * 4 + oti
            rot = g % 4
            pp = g % 2
            for h in range(2):
                mm = None
                for k in range(16):
                    mm = p.emit("tensor", (lambda oti=oti, k=k, pp=pp, wb=wb, h=h:
                                nc.tensor.matmul(
                                    PB[pp].ap()[:, :],
                                    wcache.ap()[:, wb, k, oti * 128:(oti + 1) * 128],
                                    ln1fm.ap()[:, k, 3 + h * 512:3 + (h + 1) * 512],
                                    start=(k == 0), stop=(k == 15))),
                                waits=(lw + LN1FM_OK + bank_wait(pp)) if k == 0 else [],
                                sig=(T if k == 15 else None))
                if g < 32:
                    zb = (g * 2 + h) % 2
                    wz = [ZST_FREE[zb]] if ZST_FREE[zb] else []
                    ve = p.emit("vector", (lambda zb=zb, pp=pp: nc.vector.tensor_copy(
                        zstage.ap()[:, zb, :], PB[pp].ap()[:, :])),
                        waits=[(T, mm)] + wz, sig=V)
                    bank_free[pp] = (V, ve)
                    zsem = SQ[3 + 5 * zb]
                    zd = p.dma("sync", z_sp[g * 128:(g + 1) * 128,
                                            h * 512:(h + 1) * 512],
                               zstage.ap()[:, zb, :], waits=[(V, ve)], sig=zsem)
                    ZST_FREE[zb] = (zsem, zd)
                    z_done = zd
                else:
                    wsl = [slab_free[rot]] if rot in slab_free else []
                    ve = p.emit("vector", (lambda rot=rot, h=h, pp=pp:
                                nc.vector.tensor_copy(
                                    xbc.ap()[:, rot, 3 + h * 512:3 + (h + 1) * 512],
                                    PB[pp].ap()[:, :])),
                                waits=[(T, mm)] + (wsl if h == 0 else []), sig=V)
                    bank_free[pp] = (V, ve)
            if g >= 32:
                wpt = [PT_FREE[0]] if PT_FREE[0] else []
                tms = p.emit("tensor", (lambda oti=oti: nc.tensor.transpose(
                    PT[0].ap()[:, 0:3], stripsb.ap()[0:3, oti * 128:(oti + 1) * 128],
                    idb_s.ap()[0:3, 0:3])),
                    waits=[(V, STRIP_V)] + wpt, sig=T)
                evh = p.emit("vector", (lambda rot=rot: nc.vector.tensor_copy(
                    xbc.ap()[:, rot, 0:3], PT[0].ap()[:, 0:3])),
                    waits=[(T, tms)], sig=V)
                PT_FREE[0] = (V, evh)
                slab_ready[g] = (V, p.cnt[V])
                ot = g - 32
                ca = ot % 2
                wc0 = ([(SQ[0], SMALL_LD)] if ot == 0 else [])
                wca = [CONVACC_FREE[ca]] if CONVACC_FREE[ca] else []
                v1 = p.emit("vector", (lambda ot=ot, ca=ca, rot=rot:
                            nc.vector.tensor_single_scalar(
                                convacc.ap()[:, ca, :], xbc.ap()[:, rot, 0:1024],
                                cw_sb.ap()[:, ot, 0:1], ALU.mult)),
                            waits=wc0 + wca + [slab_ready[g]], sig=V)
                for k in range(1, 4):
                    v1 = p.emit("vector", (lambda ot=ot, k=k, ca=ca, rot=rot:
                                nc.vector.scalar_tensor_tensor(
                                    convacc.ap()[:, ca, :], xbc.ap()[:, rot, k:1024 + k],
                                    cw_sb.ap()[:, ot, k:k + 1], convacc.ap()[:, ca, :],
                                    ALU.mult, ALU.add)), sig=V)
                slab_free[rot] = (V, v1)
                if ot < 32:
                    cb2 = ot % 2
                    wcs = [CONVST_FREE[cb2]] if CONVST_FREE[cb2] else []
                    a1 = p.emit("scalar", (lambda ot=ot, cb2=cb2, ca=ca:
                                nc.scalar.activation(
                                    convstage.ap()[:, cb2, :], convacc.ap()[:, ca, :],
                                    AF.Silu, bias=cb_sb.ap()[:, ot:ot + 1])),
                                waits=[(V, v1)] + wcs, sig=A)
                    CONVACC_FREE[ca] = (A, a1)
                    xsem = SQ[13 + cb2]
                    xd = p.dma("sync", xfm_sp[ot * 128:(ot + 1) * 128, :],
                               convstage.ap()[:, cb2, :], waits=[(A, a1)], sig=xsem)
                    CONVST_FREE[cb2] = (xsem, xd)
                    xspill[ot] = (xsem, xd)
                else:
                    dstap = b_fm if ot == 32 else c_fm
                    a1 = p.emit("scalar", (lambda dstap=dstap, ca=ca, ot=ot:
                                nc.scalar.activation(
                                    dstap.ap()[:, :], convacc.ap()[:, ca, :],
                                    AF.Silu, bias=cb_sb.ap()[:, ot:ot + 1])),
                                waits=[(V, v1)], sig=A)
                    CONVACC_FREE[ca] = (A, a1)
        WC_FREE[wb] = (T, p.cnt[T])
        if oc >= 8:
            STRIP_FREE = (T, p.cnt[T])
        if oc == 16:
            for tt in range(8):
                c = tt
                pp = tt % 2
                for k in range(16):
                    p.emit("tensor", (lambda tt=tt, k=k, pp=pp, wb=wb:
                           nc.tensor.matmul(
                               PB[pp].ap()[:, 0:128],
                               ln1fm.ap()[:, k, 3 + tt * 128:3 + (tt + 1) * 128],
                               wcache.ap()[:, wb, k, 256:384],
                               start=(k == 0), stop=False)),
                           waits=(bank_wait(pp) if k == 0 else []), sig=None)
                mm = p.emit("tensor", (lambda pp=pp: nc.tensor.matmul(
                    PB[pp].ap()[:, 0:128], onesr_s.ap()[0:1, 0:128],
                    dtb_s.ap()[0:1, 0:128], start=False, stop=True)),
                    waits=[(SQ[0], SMALL_LD)], sig=T)
                a1 = p.emit("scalar", (lambda pp=pp: nc.scalar.activation(
                    dtscr.ap()[:, :], PB[pp].ap()[:, 0:64], AF.Exp)),
                    waits=[(T, mm)], sig=A)
                a2 = p.emit("scalar", (lambda c=c: nc.scalar.activation(
                    dt_tm.ap()[:, c, :], dtscr.ap()[:, :], AF.Ln, bias=1.0)),
                    sig=A)
                a3 = p.emit("scalar", (lambda c=c: nc.scalar.activation(
                    lndt_tm.ap()[:, c, :], dt_tm.ap()[:, c, :], AF.Ln)),
                    sig=A)
                bank_free[pp] = (A, a1)
                dt_mm_done[c] = a3
            WC_FREE[wb] = (T, p.cnt[T])
    XSPILL_HALF = [(SQ[13], p.cnt[SQ[13]]), (SQ[14], p.cnt[SQ[14]])]
    Z_OK = [(SQ[3], p.cnt[SQ[3]]), (SQ[8], p.cnt[SQ[8]])]
    BC_OK = (A, p.cnt[A])

    for tt in range(8):
        c = tt
        tm = p.emit("tensor", (lambda c=c: nc.tensor.transpose(
            PB[4].ap()[0:64, 0:128], dt_tm.ap()[:, c, :], idf_s.ap()[:, :])),
            waits=[(A, dt_mm_done[c])] + bank_wait(4), sig=T)
        ev = p.emit("vector", (lambda c=c: nc.vector.tensor_copy(
            dt_fm.ap()[0:64, c * 128:(c + 1) * 128], PB[4].ap()[0:64, 0:128])),
            waits=[(T, tm)], sig=V)
        bank_free[4] = (V, ev)
    a1 = p.emit("scalar", (lambda: nc.scalar.activation(
        lndt_fm.ap()[0:64, :], dt_fm.ap()[0:64, :], AF.Ln)),
        waits=[(V, p.cnt[V])], sig=A)
    v1 = p.emit("vector", (lambda: nc.vector.tensor_single_scalar(
        dascr.ap()[0:64, 0:1024], dt_fm.ap()[0:64, :],
        smalls.ap()[0:64, 50:51], ALU.mult)),
        waits=[(V, NEGA)], sig=V)
    p.vspace(nc)
    for c in range(8):
        p.emit("vector", (lambda c=c: nc.vector.tensor_tensor_scan(
            r_fm.ap()[0:64, c * 128:(c + 1) * 128],
            dascr.ap()[0:64, c * 128:(c + 1) * 128],
            dascr.ap()[0:64, c * 128:(c + 1) * 128],
            0.0, ALU.add, ALU.bypass)), sig=V)
    p.vspace(nc)
    v2 = p.emit("vector", (lambda: nc.vector.tensor_sub(
        nrpscr.ap()[0:64, 0:1024], lndt_fm.ap()[0:64, :], r_fm.ap()[0:64, :])),
        waits=[(A, a1)], sig=V)
    fd = p.dma("gpsimd",
               fl_negrp[0:1, :].rearrange("o (c h j) -> (o h) c j", c=8, h=64, j=128),
               nrpscr.ap()[0:64, 0:1024].rearrange("h (c j) -> h c j", c=8),
               waits=[(V, v2), (CC, AG_W2)], sig=GQ[2])
    fd = p.dma("gpsimd",
               fl_r[0:1, :].rearrange("o (c h j) -> (o h) c j", c=8, h=64, j=128),
               r_fm.ap()[0:64, :].rearrange("h (c j) -> h c j", c=8),
               waits=[(V, v2)], sig=GQ[2])
    FLATS_OK = fd
    for c in range(8):
        tm = p.emit("tensor", (lambda c=c: nc.tensor.transpose(
            PB[4].ap()[:, 0:64], r_fm.ap()[0:64, c * 128:(c + 1) * 128],
            idf_s.ap()[0:64, 0:64])),
            waits=[(V, p.cnt[V])] + bank_wait(4), sig=T)
        ev = p.emit("vector", (lambda c=c: nc.vector.tensor_copy(
            r_tm.ap()[:, c, :], PB[4].ap()[:, 0:64])),
            waits=[(T, tm)], sig=V)
        bank_free[4] = (V, ev)
    aex = p.emit("scalar", (lambda: nc.scalar.activation(
        expf_tm.ap()[:, :, :], r_tm.ap()[:, :, :], AF.Exp)),
        waits=[(V, p.cnt[V])], sig=A)
    rb1 = p.dma("gpsimd", rowhbm[0:1, 0:512].rearrange("o (c h) -> o c h", c=8),
                r_tm.ap()[127:128, :, :], waits=[(V, p.cnt[V])], sig=GQ[2])
    rb2 = p.dma("gpsimd", rowb_s.ap()[0:1, 0:512], rowhbm[0:1, 0:512],
                waits=[(GQ[2], rb1)], sig=GQ[2])
    mm = p.emit("tensor", (lambda: nc.tensor.matmul(
        PB[4].ap()[:, 0:512], onesr_s.ap()[0:1, 0:128],
        rowb_s.ap()[0:1, 0:512], start=True, stop=True)),
        waits=[(GQ[2], rb2), (SQ[0], SMALL_LD)] + bank_wait(4), sig=T)
    ev = p.emit("vector", (lambda: nc.vector.tensor_copy(
        rl_cols.ap()[:, :], PB[4].ap()[:, 0:512])),
        waits=[(T, mm)], sig=V)
    bank_free[4] = (V, ev)
    adc = p.emit("scalar", (lambda: nc.scalar.activation(
        dcol.ap()[:, :], rl_cols.ap()[:, :], AF.Exp)), waits=[(V, ev)], sig=A)
    vnr = p.emit("vector", (lambda: nc.vector.tensor_sub(
        nrp_tm.ap()[:, :, :], lndt_tm.ap()[:, :, :], r_tm.ap()[:, :, :])),
        waits=[(A, p.cnt[A])], sig=V)
    DTM_A = p.cnt[A]
    DTM_V = p.cnt[V]

    XST_FREE = [None, None]
    STST_FREE = None
    hT_tail = hz
    xw_free = None
    stspill = {}
    for c in range(8):
        xb2 = c % 2
        wxs = list(XST_FREE[xb2]) if XST_FREE[xb2] else []
        xrsem = SQ[4 + 5 * xb2]
        cx = p.dma("sync", xstage.ap()[:, xb2, :],
                   xfm_sp[0:4096, c * 128:(c + 1) * 128],
                   waits=wxs + XSPILL_HALF, sig=xrsem, transpose=True)
        tb = p.emit("tensor", (lambda c=c: nc.tensor.transpose(
            PT[0].ap()[:, 0:128], b_fm.ap()[:, c * 128:(c + 1) * 128],
            idb_s.ap()[:, :])),
            waits=[BC_OK] + ([PT_FREE[0]] if PT_FREE[0] else []), sig=T)
        evb = p.emit("vector", lambda: nc.vector.tensor_copy(
            btm.ap()[:, :], PT[0].ap()[:, 0:128]), waits=[(T, tb)], sig=V)
        PT_FREE[0] = (V, evb)
        p.vspace(nc)
        v1 = p.emit("vector", (lambda c=c: nc.vector.tensor_add(
            wtm2.ap()[:, :], nrp_tm.ap()[:, c, :],
            rl_cols.ap()[:, c * 64:(c + 1) * 64])),
            waits=[(V, DTM_V), (A, DTM_A)], sig=V)
        a1 = p.emit("scalar", lambda: nc.scalar.activation(
            wtm.ap()[:, :], wtm2.ap()[:, :], AF.Exp), waits=[(V, v1)], sig=A)
        for g in range(8):
            vxw = None
            for i in range(8):
                h = g * 8 + i
                vxw = p.emit("vector", (lambda h=h, i=i, xb2=xb2:
                            nc.vector.tensor_single_scalar(
                                xw_sb.ap()[:, i * 64:(i + 1) * 64],
                                xstage.ap()[:, xb2, h * 64:(h + 1) * 64],
                                wtm.ap()[:, h:h + 1], ALU.mult)),
                            waits=(([(xrsem, cx), (A, a1)] +
                                    ([xw_free] if xw_free else [])) if i == 0 else []),
                            sig=V)
            pp = 2 + g % 2
            mm = p.emit("tensor", (lambda pp=pp: nc.tensor.matmul(
                PB[pp].ap()[:, :], btm.ap()[:, :], xw_sb.ap()[:, :],
                start=True, stop=True)),
                waits=[(V, vxw), (V, evb)] + bank_wait(pp), sig=T)
            xw_free = (T, p.cnt[T])
            for i in range(8):
                h = g * 8 + i
                p.emit("vector", (lambda h=h, i=i, c=c, pp=pp:
                       nc.vector.scalar_tensor_tensor(
                           hT.ap()[:, h * 64:(h + 1) * 64],
                           hT.ap()[:, h * 64:(h + 1) * 64],
                           dcol.ap()[:, c * 64 + h:c * 64 + h + 1],
                           PB[pp].ap()[:, i * 64:(i + 1) * 64],
                           ALU.mult, ALU.add)),
                       waits=([(T, mm), (A, adc)] if i == 0 else []), sig=V)
            wst = [STST_FREE] if STST_FREE else []
            vcp = p.emit("vector", (lambda g=g, pp=pp: nc.vector.tensor_copy(
                stst.ap()[:, g * 512:(g + 1) * 512], PB[pp].ap()[:, :])),
                waits=wst, sig=V)
            bank_free[pp] = (V, vcp)
        sd = p.dma("sync", st_sp[c, :, :], stst.ap()[:, :],
                   waits=[(V, p.cnt[V])], sig=SQ[10])
        STST_FREE = (SQ[10], sd)
        stspill[c] = sd
        XST_FREE[xb2] = [(V, p.cnt[V])]
        hT_tail = p.cnt[V]
    STSP_OK = (SQ[10], p.cnt[SQ[10]])

    cs1 = p.dma("gpsimd", cc_st_in[:, :], hT.ap()[:, :],
                waits=[(V, hT_tail)], sig=GQ[2])
    vld = p.emit("vector", lambda: nc.vector.reduce_sum(
        smalls.ap()[0:64, 52:53],
        r_fm.ap()[0:64, :].rearrange("h (c j) -> h c j", j=128)[:, :, 127],
        axis=mybir.AxisListType.X), waits=[(V, p.cnt[V])], sig=V)
    cs2 = p.dma("gpsimd", cc_ld_in[:, :], smalls.ap()[0:64, 52:53],
                waits=[(V, vld)], sig=GQ[2])
    agst = p.emit("gpsimd", lambda: nc.gpsimd.collective_compute(
        "AllGather", ALU.bypass, replica_groups=[[0, 1, 2, 3], [4, 5, 6, 7]],
        ins=[cc_st_in.opt()], outs=[cc_st_out.opt()]),
        waits=[(GQ[2], cs1)], sig=CC)
    agld = p.emit("gpsimd", lambda: nc.gpsimd.collective_compute(
        "AllGather", ALU.bypass, replica_groups=[[0, 1, 2, 3], [4, 5, 6, 7]],
        ins=[cc_ld_in.opt()], outs=[cc_ld_out.opt()]),
        waits=[(GQ[2], cs2)], sig=CC)
    lsel = p.dma("sync", wq.ap()[:, :],
                 cc_ld_out[:, :].rearrange("(p h) o -> p (h o)", p=4),
                 waits=[(CC, agld)], sig=SQ[0])
    lsel = p.dma("sync", smalls.ap()[0:4, 32:35], selm[:, :], sig=SQ[0])
    lsel = p.dma("sync", smalls.ap()[0:3, 35:36], selv[:, :], sig=SQ[0])
    mm = p.emit("tensor", lambda: nc.tensor.matmul(
        PB[4].ap()[0:3, 0:64], smalls.ap()[0:4, 32:35], wq.ap()[:, :],
        start=True, stop=True),
        waits=[(SQ[0], lsel)] + bank_wait(4), sig=T)
    ae = p.emit("scalar", lambda: nc.scalar.activation(
        wq2.ap()[:, :], PB[4].ap()[0:3, 0:64], AF.Exp), waits=[(T, mm)], sig=A)
    bank_free[4] = (A, ae)
    vmu = p.emit("vector", lambda: nc.vector.tensor_single_scalar(
        wq2.ap()[:, :], wq2.ap()[:, :], smalls.ap()[0:3, 35:36], ALU.mult),
        waits=[(A, ae)], sig=V)
    bw1 = p.dma("gpsimd", wrow_b[0:1, 0:192].rearrange("o (q h) -> (o q) h", q=3),
                wq2.ap()[:, :], waits=[(V, vmu)], sig=GQ[2])
    bw2 = p.dma("gpsimd", rowb_s.ap()[0:1, 0:192], wrow_b[0:1, 0:192],
                waits=[(GQ[2], bw1)], sig=GQ[2])
    mm = p.emit("tensor", lambda: nc.tensor.matmul(
        PB[4].ap()[:, 0:192], onesr_s.ap()[0:1, 0:128], rowb_s.ap()[0:1, 0:192],
        start=True, stop=True), waits=[(GQ[2], bw2)] + bank_wait(4), sig=T)
    vwc = p.emit("vector", lambda: nc.vector.tensor_copy(
        wcols.ap()[:, 0:192], PB[4].ap()[:, 0:192]), waits=[(T, mm)], sig=V)
    bank_free[4] = (V, vwc)
    hz2 = p.emit("vector", lambda: nc.vector.memset(hT.ap()[:, :], 0.0),
                 waits=[(GQ[2], cs1)], sig=V)
    vh = hz2
    for pidx in range(3):
        gsem = SQ[4 + 5 * (pidx % 2)]
        cg = p.dma("sync", xstage.ap()[:, pidx % 2, :],
                   cc_st_out[pidx * 128:(pidx + 1) * 128, :],
                   waits=[(CC, agst), (V, vh)], sig=gsem)
        for h in range(NH):
            vh = p.emit("vector", (lambda pidx=pidx, h=h: nc.vector.scalar_tensor_tensor(
                hT.ap()[:, h * 64:(h + 1) * 64],
                xstage.ap()[:, pidx % 2, h * 64:(h + 1) * 64],
                wcols.ap()[:, pidx * 64 + h:pidx * 64 + h + 1],
                hT.ap()[:, h * 64:(h + 1) * 64],
                ALU.mult, ALU.add)),
                waits=([(gsem, cg), (V, vwc)] if h == 0 else []), sig=V)
    HIN_OK = vh
    hin_sp = internal("hin_sp", [128, NH * HP], BF16)
    p.dma("gpsimd", hin_sp[:, :], hT.ap()[:, :], waits=[(V, HIN_OK)], sig=GQ[3])

    attn_sp = internal("attn_sp", [Q, DM], BF16)
    y_dbg = internal("y_dbg", [128, DI], BF16)
    sc_dbg = internal("sc_dbg", [128, 4096], BF16)
    cb_dbg = internal("cb_dbg", [128, 128], BF16)

    ygfm = reg("ygfm", [128, 32, Q], BF16, 0)
    segL = reg("segL", [2, 2, 1024], F32, 65536)
    segR = reg("segR", [2, 2, 1024], F32, 73728)
    xst3 = reg("xst3", [128, 2, 4096], BF16, 81920)
    zst3 = reg("zst3", [128, 4096], BF16, 98304)
    stst3 = reg("stst3", [128, 4096], BF16, 106496)
    ysb = reg("ysb", [128, 4096], BF16, 114688)
    scor = reg("scor", [128, 8, 128], BF16, 122880)
    segexp = reg("segexp", [128, 2, 512], BF16, 124928)
    cbm = reg("cbm", [128, 128], BF16, 126976)
    ytmp = reg("ytmp", [128, 512], BF16, 127232)
    ytmp2 = reg("ytmp2", [128, 512], BF16, 128256)

    BAR12 = [(V, p.cnt[V]), (A, p.cnt[A]), (T, p.cnt[T]), (CC, p.cnt[CC])]
    for _e in ENGS:
        p.wait(_e, BAR12)

    oi = p.dma("gpsimd", segL.ap()[1:2, 0, :], ones_big[0:1, 0:1024], sig=GQ[0])
    oi = p.dma("gpsimd", segL.ap()[1:2, 1, :], ones_big[0:1, 0:1024], sig=GQ[0])
    oi = p.dma("gpsimd", segR.ap()[0:1, 0, :], ones_big[0:1, 0:1024], sig=GQ[0])
    oi = p.dma("gpsimd", segR.ap()[0:1, 1, :], ones_big[0:1, 0:1024], sig=GQ[0])
    SEG_ONES = oi

    XST3_FREE = [None, None]
    SEG_FREE = [None, None]
    SEGEXP_FREE = [None, None]
    ZST3_FREE = None
    STST3_FREE = None
    YSB_FREE = None
    hT_read = None
    hT_upd = (V, HIN_OK)
    SCOR_FREE = None
    YTMP_FREE = None
    YTMP2_FREE = None
    ygfm_marks = []
    yMM_SEM_Q = [SQ[9], SQ[15]]

    for c in range(NCH if PH_LEVEL >= 3 else 0):
        xb = c % 2
        if c >= PH3_CHUNKS:
            break
        wx = list(XST3_FREE[xb]) if XST3_FREE[xb] else []
        cx3 = p.dma("sync", xst3.ap()[:, xb, :], xfm_sp[0:4096, c * 128:(c + 1) * 128],
                    waits=wx, sig=SQ[4 + 5 * xb], transpose=True)
        xsem = SQ[4 + 5 * xb]
        wz = [ZST3_FREE] if ZST3_FREE else []
        cz = p.dma("sync", zst3.ap()[:, :], z_sp[0:4096, c * 128:(c + 1) * 128],
                   waits=wz + (Z_OK if c == 0 else []), sig=SQ[1],
                   transpose=True)
        wst = [STST3_FREE] if STST3_FREE else []
        cst = p.dma("sync", stst3.ap()[:, :], st_sp[c, :, :],
                    waits=wst + ([STSP_OK] if c == 0 else []), sig=SQ[6])
        mmcb = p.emit("tensor", (lambda c=c: nc.tensor.matmul(
            PB[4].ap()[:, 0:128], b_fm.ap()[:, c * 128:(c + 1) * 128],
            c_fm.ap()[:, c * 128:(c + 1) * 128], start=True, stop=True)),
            waits=bank_wait(4), sig=T)
        vcb = p.emit("vector", lambda: nc.vector.tensor_mul(
            cbm.ap()[:, :], PB[4].ap()[:, 0:128], maskT_s.ap()[:, :]),
            waits=[(T, mmcb)], sig=V)
        bank_free[4] = (V, vcb)
        seg_ld = {}
        for hg in range(8):
            e = hg
            sb = e % 2
            wsg = list(SEG_FREE[sb]) if SEG_FREE[sb] else []
            if c == 0 and e < 2:
                wsg.append((GQ[0], SEG_ONES))
            g1 = p.dma("gpsimd", segL.ap()[0:1, sb, :],
                       fl_negrp[0:1, (c * 64 + e * 8) * 128:(c * 64 + e * 8 + 8) * 128],
                       waits=wsg + [(GQ[2], FLATS_OK)], sig=GQ[0])
            g2 = p.dma("gpsimd", segR.ap()[1:2, sb, :],
                       fl_r[0:1, (c * 64 + e * 8) * 128:(c * 64 + e * 8 + 8) * 128],
                       waits=list(SEG_FREE[sb]) if SEG_FREE[sb] else [], sig=GQ[1])
            seg_ld[e] = [(GQ[0], g1), (GQ[1], g2)]
            segmm = {}
            for q4 in range(2):
                pp = (hg * 2 + q4) % 2
                mm = None
                for i in range(4):
                    hh = q4 * 4 + i
                    mm = p.emit("tensor", (lambda sb=sb, hh=hh, i=i, pp=pp:
                                nc.tensor.matmul(
                                    PB[pp].ap()[:, i * 128:(i + 1) * 128],
                                    segL.ap()[0:2, sb, hh * 128:(hh + 1) * 128],
                                    segR.ap()[0:2, sb, hh * 128:(hh + 1) * 128],
                                    start=(i == 0), stop=(i == 3))),
                                waits=(seg_ld[e] + bank_wait(pp)) if i == 0 else [],
                                sig=(T if i == 3 else None))
                eb = (hg * 2 + q4) % 2
                wse = [SEGEXP_FREE[eb]] if SEGEXP_FREE[eb] else []
                vmn = p.emit("vector", (lambda eb=eb, pp=pp: nc.vector.tensor_scalar_min(
                    segexp.ap()[:, eb, :], PB[pp].ap()[:, :], 30.0)),
                    waits=[(T, mm)] + wse, sig=V)
                bank_free[pp] = (V, vmn)
                aex = p.emit("scalar", (lambda eb=eb: nc.scalar.activation(
                    segexp.ap()[:, eb, :], segexp.ap()[:, eb, :], AF.Exp)),
                    waits=[(V, vmn)], sig=A)
                vsc = None
                for i in range(4):
                    hh = q4 * 4 + i
                    slot = q4 * 4 + i
                    vsc = p.emit("vector", (lambda eb=eb, i=i, slot=slot: nc.vector.tensor_mul(
                        scor.ap()[:, slot, :], segexp.ap()[:, eb, i * 128:(i + 1) * 128],
                        cbm.ap()[:, :])),
                        waits=([(A, aex), (V, vcb)] +
                               ([SCOR_FREE] if SCOR_FREE else [])) if i == 0 else [],
                        sig=V)
                SEGEXP_FREE[eb] = (V, vsc)
                segmm[q4] = vsc
            mmd = None
            for i in range(8):
                h = hg * 8 + i
                slot = i
                mmd = p.emit("tensor", (lambda h=h, i=i, slot=slot, xb=xb:
                            nc.tensor.matmul(
                                PB[2].ap()[:, i * 64:(i + 1) * 64],
                                scor.ap()[:, slot, :],
                                xst3.ap()[:, xb, h * 64:(h + 1) * 64],
                                start=(i == 0), stop=(i == 7))),
                            waits=([(V, segmm[1]), (xsem, cx3)] + bank_wait(2))
                                  if i == 0 else [],
                            sig=(T if i == 7 else None))
            mmo = None
            for i in range(8):
                h = hg * 8 + i
                mmo = p.emit("tensor", (lambda h=h, i=i, c=c: nc.tensor.matmul(
                            PB[3].ap()[:, i * 64:(i + 1) * 64],
                            c_fm.ap()[:, c * 128:(c + 1) * 128],
                            hT.ap()[:, h * 64:(h + 1) * 64],
                            start=(i == 0), stop=(i == 7))),
                            waits=(list(hT_upd if isinstance(hT_upd, list) else [hT_upd])
                                   + bank_wait(3)) if i == 0 else [],
                            sig=(T if i == 7 else None))
            hT_read = (T, mmo)
            aev = None
            for i in range(8):
                h = hg * 8 + i
                aev = p.emit("scalar", (lambda h=h, i=i, c=c: nc.scalar.activation(
                    ytmp.ap()[:, i * 64:(i + 1) * 64], PB[3].ap()[:, i * 64:(i + 1) * 64],
                    AF.Copy, scale=expf_tm.ap()[:, c, h:h + 1])),
                    waits=([(T, mmo)] + ([YTMP_FREE] if YTMP_FREE else []))
                          if i == 0 else [], sig=A)
            wyb = [YSB_FREE] if (YSB_FREE and hg == 0) else []
            v2 = p.emit("vector", (lambda hg=hg, xb=xb: nc.vector.tensor_mul(
                ytmp2.ap()[:, :], xst3.ap()[:, xb, hg * 512:(hg + 1) * 512],
                dpb.ap()[:, hg * 512:(hg + 1) * 512])),
                waits=[(V, DPB_OK)] + ([YTMP2_FREE] if YTMP2_FREE else []), sig=V)
            v1 = p.emit("vector", (lambda hg=hg: nc.vector.tensor_add(
                ysb.ap()[:, hg * 512:(hg + 1) * 512], PB[2].ap()[:, :], ytmp.ap()[:, :])),
                waits=[(A, aev), (T, mmd)] + wyb, sig=V)
            bank_free[2] = (V, v1)
            bank_free[3] = (A, aev)
            YTMP_FREE = (V, v1)
            v3 = p.emit("vector", (lambda hg=hg: nc.vector.tensor_add(
                ysb.ap()[:, hg * 512:(hg + 1) * 512],
                ysb.ap()[:, hg * 512:(hg + 1) * 512], ytmp2.ap()[:, :])), sig=V)
            YTMP2_FREE = (V, v3)
            SEG_FREE[sb] = [(T, mmd)]
            SCOR_FREE = (T, mmd)
        if DEBUG_PH3 and c == 7:
            tb9 = p.dma("gpsimd", y_dbg[:, :], hT.ap()[:, :],
                        waits=list(hT_upd if isinstance(hT_upd, list) else [hT_upd]),
                        sig=GQ[3])
            for _e in ENGS:
                p.wait(_e, [(GQ[3], tb9)])
        vrec = None
        for h in range(NH):
            vrec = p.emit("vector", (lambda h=h, c=c: nc.vector.scalar_tensor_tensor(
                hT.ap()[:, h * 64:(h + 1) * 64], hT.ap()[:, h * 64:(h + 1) * 64],
                dcol.ap()[:, c * 64 + h:c * 64 + h + 1],
                stst3.ap()[:, h * 64:(h + 1) * 64], ALU.mult, ALU.add)),
                waits=[hT_read, (SQ[6], cst)] if h == 0 else [], sig=V)
        hT_upd = (V, vrec)
        STST3_FREE = (V, vrec)
        asz = p.emit("scalar", (lambda: nc.scalar.activation(
            zst3.ap()[:, :], zst3.ap()[:, :], AF.Silu)),
            waits=[(SQ[1], cz)], sig=A)
        if DEBUG_PH3 and c == 0:
            tbz = p.dma("gpsimd", cb_dbg[0:128, 0:128], zst3.ap()[:, 0:128],
                        waits=[(A, asz)], sig=GQ[3])
            for _e in ENGS:
                p.wait(_e, [(GQ[3], tbz)])
        vg = p.emit("vector", lambda: nc.vector.tensor_mul(
            ysb.ap()[:, :], ysb.ap()[:, :], zst3.ap()[:, :]),
            waits=[(A, asz), (V, p.cnt[V])], sig=V)
        p.emit("scalar", (lambda: nc.scalar.activation(
            zst3.ap()[:, :], ysb.ap()[:, :], AF.Square,
            accum_out=smalls.ap()[:, 60:61])), waits=[(V, vg)])
        asq2 = p.emit("scalar", lambda: nc.scalar.copy(
            smalls.ap()[:, 61:62], smalls.ap()[:, 60:61]), sig=A)
        ZST3_FREE = (A, asq2)
        ast2 = p.emit("scalar", lambda: nc.scalar.activation(
            smalls.ap()[:, 62:63], smalls.ap()[:, 60:61], AF.Sqrt,
            bias=smalls.ap()[:, 54:55], scale=smalls.ap()[:, 55:56]),
            waits=[(A, asq2)], sig=A)
        vrc = p.emit("vector", lambda: nc.vector.reciprocal(
            smalls.ap()[:, 63:64], smalls.ap()[:, 62:63]), waits=[(A, ast2)], sig=V)
        if DEBUG_PH3 and c == 0:
            tbm = p.dma("gpsimd", sm_dbg[:, :], smalls.ap()[:, :],
                        waits=[(V, vrc)], sig=GQ[3])
            tbm = p.dma("gpsimd", u_dbg[0:128, 0:2048],
                        ysb.ap()[:, 0:2048], waits=[(V, vg)], sig=GQ[3])
            for _e in ENGS:
                p.wait(_e, [(GQ[3], tbm)])
        p.vspace(nc)
        vsc2 = p.emit("vector", lambda: nc.vector.tensor_single_scalar(
            ysb.ap()[:, :], ysb.ap()[:, :], smalls.ap()[:, 63:64], ALU.mult),
            waits=[(V, vrc)], sig=V)
        if DEBUG_PH3 and c == 0:
            tb8 = p.dma("gpsimd", sc_dbg[0:128, 0:4096], ysb.ap()[:, :],
                        waits=[(V, vsc2)], sig=GQ[3])
            for _e in ENGS:
                p.wait(_e, [(GQ[3], tb8)])
        for kt in range(32):
            ptb = kt % 2
            wpt = [PT_FREE[ptb]] if PT_FREE[ptb] else []
            tm = p.emit("tensor", (lambda kt=kt, ptb=ptb: nc.tensor.transpose(
                PT[ptb].ap()[:, 0:128], ysb.ap()[:, kt * 128:(kt + 1) * 128],
                idb_s.ap()[:, :])), waits=[(V, vsc2)] + wpt, sig=T)
            if kt % 2 == 0:
                ev = p.emit("vector", (lambda kt=kt, c=c, ptb=ptb: nc.vector.tensor_copy(
                    ygfm.ap()[:, kt, c * 128:(c + 1) * 128], PT[ptb].ap()[:, 0:128])),
                    waits=[(T, tm)], sig=V)
                PT_FREE[ptb] = (V, ev)
            else:
                ev = p.emit("scalar", (lambda kt=kt, c=c, ptb=ptb: nc.scalar.copy(
                    ygfm.ap()[:, kt, c * 128:(c + 1) * 128], PT[ptb].ap()[:, 0:128])),
                    waits=[(T, tm)], sig=A)
                PT_FREE[ptb] = (A, ev)
        YSB_FREE = (T, p.cnt[T])
        XST3_FREE[xb] = [(T, p.cnt[T]), (V, p.cnt[V])]
    YGFM_OK = [(V, p.cnt[V]), (A, p.cnt[A])]
    ygfm_dbg = internal("ygfm_dbg", [128, 32 * Q], BF16)
    woc_dbg = internal("woc_dbg", [128, 32 * 512], BF16)
    if DEBUG_PH3:
        tbg = p.dma("gpsimd", ygfm_dbg[:, :],
                    ygfm.ap()[:, :, :].rearrange("p a b -> p (a b)"),
                    waits=YGFM_OK, sig=GQ[3])
        for _e in ENGS:
            p.wait(_e, [(GQ[3], tbg)])

    PH3B_ON = PH_LEVEL >= 4 and PH3_CHUNKS >= 8
    woc = reg("woc", [128, 32, 512], BF16, 98304)
    atst = reg("atst", [128, 2, 512], BF16, 131072)
    BAR3B = [(V, p.cnt[V]), (A, p.cnt[A]), (T, p.cnt[T])]
    for _e in ENGS:
        p.wait(_e, BAR3B)
    ATST_FREE = [None, None]
    WOSEMS = [SQ[2], SQ[7], SQ[11], SQ[12]]
    at_done = None
    for dc in range(4 if PH3B_ON else 0):
        lw = []
        for k in range(32):
            wsem = WOSEMS[k % 4]
            wwk = [(T, p.cnt[T])] if (dc > 0 and k < 4) else []
            lwv = p.dma("sync", woc.ap()[:, k, :],
                        wout_bf[k * 128:(k + 1) * 128, dc * 512:(dc + 1) * 512],
                        waits=wwk + [(CC, AG_WOUT)], sig=wsem)
            lw.append((wsem, lwv))
        lw = [(sm, max(vv for sm2, vv in lw if sm2 == sm)) for sm in set(s0 for s0, _ in lw)]
        if DEBUG_PH3 and dc == 0:
            tbw = p.dma("gpsimd", woc_dbg[:, :],
                        woc.ap()[:, :, :].rearrange("p a b -> p (a b)"),
                        waits=lw, sig=GQ[3])
            for _e in ENGS:
                p.wait(_e, [(GQ[3], tbw)])
        for tt in range(8):
            pp = tt % 2
            mm = None
            for k in range(32):
                mm = p.emit("tensor", (lambda tt=tt, k=k, pp=pp: nc.tensor.matmul(
                    PB[pp].ap()[:, :], ygfm.ap()[:, k, tt * 128:(tt + 1) * 128],
                    woc.ap()[:, k, :], start=(k == 0), stop=(k == 31))),
                    waits=(lw + YGFM_OK + bank_wait(pp)) if k == 0 else [],
                    sig=(T if k == 31 else None))
            ab = tt % 2
            wat = [ATST_FREE[ab]] if ATST_FREE[ab] else []
            ve = p.emit("vector", (lambda ab=ab, pp=pp: nc.vector.tensor_copy(
                atst.ap()[:, ab, :], PB[pp].ap()[:, :])),
                waits=[(T, mm)] + wat, sig=V)
            bank_free[pp] = (V, ve)
            ad = p.dma("sync", attn_sp[tt * 128:(tt + 1) * 128, dc * 512:(dc + 1) * 512],
                       atst.ap()[:, ab, :], waits=[(V, ve)], sig=SQ[3 + 5 * ab])
            ATST_FREE[ab] = (SQ[3 + 5 * ab], ad)
            at_done = (SQ[3 + 5 * ab], ad)
    AT_ALL = [(SQ[3], p.cnt[SQ[3]]), (SQ[8], p.cnt[SQ[8]])]

    ast3 = reg("ast3", [128, 2048], BF16, 98304)
    ust2 = reg("ust2", [128, 2048], F32, 102400)
    rsb = reg("rsb", [128, 2048], F32, 110592)
    x3tm = reg("x3tm", [128, 2048], BF16, 118784)
    x3fm = reg("x3fm", [128, 16, Q], BF16, 65536)
    BAR3C = [(V, p.cnt[V]), (A, p.cnt[A]), (T, p.cnt[T]),
             (SQ[3], p.cnt[SQ[3]]), (SQ[8], p.cnt[SQ[8]])]
    for _e in ENGS:
        p.wait(_e, BAR3C)
    RS_FREE = None
    for tt in range(8 if PH3B_ON else 0):
        ca = p.dma("sync", ast3.ap()[:, :], attn_sp[tt * 128:(tt + 1) * 128, :],
                   waits=AT_ALL, sig=SQ[1])
        cu2 = p.dma("sync", ust2.ap()[:, :], u_ext[3 + tt * 128:3 + (tt + 1) * 128, :],
                    sig=SQ[6])

        def lnstats(src_ap, wsrc, bb, width=2048):
            vs = p.emit("vector", (lambda: nc.vector.reduce_sum(
                smalls.ap()[:, bb + 0:bb + 1], src_ap, axis=mybir.AxisListType.X)),
                waits=wsrc, sig=V)
            p.emit("scalar", (lambda: nc.scalar.activation(
                x3tm.ap()[:, :], src_ap, AF.Square,
                accum_out=smalls.ap()[:, bb + 1:bb + 2])), waits=wsrc)
            af = p.emit("scalar", (lambda: nc.scalar.copy(
                smalls.ap()[:, bb + 7:bb + 8], smalls.ap()[:, bb + 1:bb + 2])), sig=A)
            p.vspace(nc)
            v1 = p.emit("vector", (lambda: nc.vector.tensor_scalar_mul(
                smalls.ap()[:, bb + 2:bb + 3], smalls.ap()[:, bb + 0:bb + 1],
                1.0 / width)), waits=[(V, vs)], sig=V)
            v2 = p.emit("vector", (lambda: nc.vector.tensor_scalar_mul(
                smalls.ap()[:, bb + 3:bb + 4], smalls.ap()[:, bb + 1:bb + 2],
                1.0 / width)), waits=[(A, af)], sig=V)
            v3 = p.emit("vector", (lambda: nc.vector.tensor_single_scalar(
                smalls.ap()[:, bb + 4:bb + 5], smalls.ap()[:, bb + 2:bb + 3],
                smalls.ap()[:, bb + 2:bb + 3], ALU.mult)), sig=V)
            p.vspace(nc)
            v4 = p.emit("vector", (lambda: nc.vector.tensor_single_scalar(
                smalls.ap()[:, bb + 3:bb + 4], smalls.ap()[:, bb + 3:bb + 4],
                smalls.ap()[:, bb + 4:bb + 5], ALU.subtract)), sig=V)
            a2 = p.emit("scalar", (lambda: nc.scalar.activation(
                smalls.ap()[:, bb + 5:bb + 6], smalls.ap()[:, bb + 3:bb + 4],
                AF.Sqrt, bias=smalls.ap()[:, 54:55])), waits=[(V, v4)], sig=A)
            v5 = p.emit("vector", (lambda: nc.vector.reciprocal(
                smalls.ap()[:, bb + 6:bb + 7], smalls.ap()[:, bb + 5:bb + 6])),
                waits=[(A, a2)], sig=V)
            p.vspace(nc)
            return v5

        v5 = lnstats(ast3.ap()[:, :], [(SQ[1], ca)], 0)
        vno = p.emit("vector", (lambda: nc.vector.tensor_scalar(
            rsb.ap()[:, :], ast3.ap()[:, :], smalls.ap()[:, 2:3],
            smalls.ap()[:, 6:7], ALU.subtract, ALU.mult)),
            waits=[(V, v5)] + ([RS_FREE] if RS_FREE else []), sig=V)
        vra = p.emit("vector", (lambda: nc.vector.tensor_add(
            rsb.ap()[:, :], rsb.ap()[:, :], ust2.ap()[:, :])),
            waits=[(SQ[6], cu2)], sig=V)
        rd = p.dma("sync", resid_sp[tt * 128:(tt + 1) * 128, :], rsb.ap()[:, :],
                   waits=[(V, vra)], sig=SQ[10])
        v5b = lnstats(rsb.ap()[:, :], [(V, vra)], 16)
        vn3 = p.emit("vector", (lambda: nc.vector.tensor_scalar(
            x3tm.ap()[:, :], rsb.ap()[:, :], smalls.ap()[:, 18:19],
            smalls.ap()[:, 22:23], ALU.subtract, ALU.mult)),
            waits=[(V, v5b)], sig=V)
        RS_FREE = (SQ[10], rd)
        for k in range(16):
            ptb = k % 2
            wpt = [PT_FREE[ptb]] if PT_FREE[ptb] else []
            tm = p.emit("tensor", (lambda k=k, ptb=ptb: nc.tensor.transpose(
                PT[ptb].ap()[:, 0:128], x3tm.ap()[:, k * 128:(k + 1) * 128],
                idb_s.ap()[:, :])), waits=[(V, vn3)] + wpt, sig=T)
            if k % 2 == 0:
                ev = p.emit("vector", (lambda k=k, tt=tt, ptb=ptb: nc.vector.tensor_copy(
                    x3fm.ap()[:, k, tt * 128:(tt + 1) * 128], PT[ptb].ap()[:, 0:128])),
                    waits=[(T, tm)], sig=V)
                PT_FREE[ptb] = (V, ev)
            else:
                ev = p.emit("scalar", (lambda k=k, tt=tt, ptb=ptb: nc.scalar.copy(
                    x3fm.ap()[:, k, tt * 128:(tt + 1) * 128], PT[ptb].ap()[:, 0:128])),
                    waits=[(T, tm)], sig=A)
                PT_FREE[ptb] = (A, ev)
    X3FM_OK = [(V, p.cnt[V]), (A, p.cnt[A])]
    RESID_OK = [(SQ[10], p.cnt[SQ[10]])]

    h1 = reg("h1", [128, 64, 512], BF16, 0)
    w1c = reg("w1c", [128, 16, 512], BF16, 98304)
    w3c = reg("w3c", [128, 16, 512], BF16, 114688)
    sg4 = reg("sg4", [128, 512], BF16, 131072)
    hid4 = reg("hid4", [128, 2, 512], BF16, 132096)
    w2c = reg("w2c", [128, 2, 512], BF16, 134144)
    ffnst = reg("ffnst", [128, 4, 4, 512], BF16, 136192)
    h1stg = reg("h1stg", [128, 2, 512], BF16, 152576)
    rsb2 = reg("rsb2", [128, 2048], F32, 98304)
    ostage = reg("ostage", [128, 2, 2048], F32, 106496)
    BAR4 = [(V, p.cnt[V]), (A, p.cnt[A]), (T, p.cnt[T])]
    for _e in ENGS:
        p.wait(_e, BAR4)
    W1SEMS = [SQ[2], SQ[7]]
    W3SEMS = [SQ[11], SQ[12]]
    W2SEMS = [SQ[4], SQ[9]]
    OUT_SEMS = [SQ[3], SQ[8]]
    H1SP_SEMS = [SQ[13], SQ[14]]
    h1sp = internal("h1sp", [8192, 512], BF16)
    out_last = []
    SG4_FREE = None
    HIDF = [None, None]
    h1stg_free = [None, None]
    for ug in range(16 if PH_LEVEL >= 5 and PH3B_ON else 0):
        lw1 = []
        lw3 = []
        for k in range(16):
            wwk = [(T, p.cnt[T])] if (k < 2 and ug > 0) else []
            l1 = p.dma("sync", w1c.ap()[:, k, :],
                       w1_bf[k * 128:(k + 1) * 128, ug * 512:(ug + 1) * 512],
                       waits=wwk + [(CC, AG_W1)], sig=W1SEMS[k % 2])
            lw1.append((W1SEMS[k % 2], l1))
            l3 = p.dma("sync", w3c.ap()[:, k, :],
                       w3_bf[k * 128:(k + 1) * 128, ug * 512:(ug + 1) * 512],
                       waits=wwk + [(CC, AG_W3)], sig=W3SEMS[k % 2])
            lw3.append((W3SEMS[k % 2], l3))
        lw1 = [(sm, max(v for s2, v in lw1 if s2 == sm)) for sm in set(s for s, _ in lw1)]
        lw3 = [(sm, max(v for s2, v in lw3 if s2 == sm)) for sm in set(s for s, _ in lw3)]
        for ut in range(4):
            u = ug * 4 + ut
            mms = {}
            for grp, (wc, pba, pbb, lww) in enumerate(
                    [(w1c, 0, 1, lw1), (w3c, 2, 3, lw3)]):
                for k in range(16):
                    m0 = p.emit("tensor", (lambda wc=wc, ut=ut, k=k, pba=pba:
                                nc.tensor.matmul(
                                    PB[pba].ap()[:, :],
                                    wc.ap()[:, k, ut * 128:(ut + 1) * 128],
                                    x3fm.ap()[:, k, 0:512],
                                    start=(k == 0), stop=(k == 15))),
                                waits=(lww + X3FM_OK + bank_wait(pba))
                                      if k == 0 else [],
                                sig=(T if k == 15 else None))
                    m1 = p.emit("tensor", (lambda wc=wc, ut=ut, k=k, pbb=pbb:
                                nc.tensor.matmul(
                                    PB[pbb].ap()[:, :],
                                    wc.ap()[:, k, ut * 128:(ut + 1) * 128],
                                    x3fm.ap()[:, k, 512:1024],
                                    start=(k == 0), stop=(k == 15))),
                                waits=bank_wait(pbb) if k == 0 else [],
                                sig=(T if k == 15 else None))
                    mms[pba] = m0
                    mms[pbb] = m1
            for hh in range(2):
                wsg = [SG4_FREE] if SG4_FREE else []
                ags = p.emit("scalar", (lambda hh=hh: nc.scalar.activation(
                    sg4.ap()[:, :], PB[hh].ap()[:, :], AF.Silu)),
                    waits=[(T, mms[hh])] + wsg, sig=A)
                bank_free[hh] = (A, ags)
                whd = [HIDF[hh]] if HIDF[hh] else []
                vhd = p.emit("vector", (lambda hh=hh: nc.vector.tensor_mul(
                    hid4.ap()[:, hh, :], sg4.ap()[:, :], PB[2 + hh].ap()[:, :])),
                    waits=[(A, ags), (T, mms[2 + hh])] + whd, sig=V)
                bank_free[2 + hh] = (V, vhd)
                SG4_FREE = (V, vhd)
                mmh = p.emit("tensor", (lambda hh=hh: nc.tensor.matmul(
                    PB[4 + hh].ap()[:, :], h128_s.ap()[:, :], hid4.ap()[:, hh, :],
                    start=True, stop=True)),
                    waits=[(V, vhd)] + bank_wait(4 + hh), sig=T)
                HIDF[hh] = (T, mmh)
                if hh == 0:
                    ev = p.emit("vector", (lambda u=u: nc.vector.tensor_copy(
                        h1.ap()[:, u, :], PB[4].ap()[:, :])),
                        waits=[(T, mmh)], sig=V)
                    bank_free[4] = (V, ev)
                else:
                    sb = u % 2
                    whs = [h1stg_free[sb]] if h1stg_free[sb] else []
                    ev = p.emit("vector", (lambda sb=sb: nc.vector.tensor_copy(
                        h1stg.ap()[:, sb, :], PB[5].ap()[:, :])),
                        waits=[(T, mmh)] + whs, sig=V)
                    bank_free[5] = (V, ev)
                    d1 = p.dma("sync", h1sp[u * 128:(u + 1) * 128, :],
                               h1stg.ap()[:, sb, :], waits=[(V, ev)],
                               sig=H1SP_SEMS[sb])
                    h1stg_free[sb] = (H1SP_SEMS[sb], d1)
    H1_OK = (V, p.cnt[V])
    H1SP_OK = [(H1SP_SEMS[0], p.cnt[H1SP_SEMS[0]]),
               (H1SP_SEMS[1], p.cnt[H1SP_SEMS[1]])]
    FF_FREE = None
    for th in range(2 if PH_LEVEL >= 5 and PH3B_ON else 0):
        rb_marks = []
        if th == 1:
            W2TH0 = (T, p.cnt[T])
            for ch in range(8):
                rsem = H1SP_SEMS[ch % 2]
                rb = p.dma("sync", h1.ap()[:, ch * 8:(ch + 1) * 8, :],
                           h1sp[ch * 1024:(ch + 1) * 1024, :].rearrange(
                               "(u j) t -> j u t", j=128),
                           waits=([W2TH0] + H1SP_OK) if ch < 2 else [],
                           sig=rsem)
                rb_marks.append((rsem, rb))
        w2free = [None, None]
        for dc in range(4):
            for u in range(64):
                wb2 = u % 2
                ww2 = ([w2free[wb2]] if w2free[wb2] else []) + [(CC, AG_W2)]
                p.dma("sync", w2c.ap()[:, wb2, 0:256],
                      w2pp_bf[(2 * dc) * 8192 + u * 128:(2 * dc) * 8192 + (u + 1) * 128, :],
                      waits=ww2, sig=W2SEMS[wb2])
                l2 = p.dma("sync", w2c.ap()[:, wb2, 256:512],
                           w2pp_bf[(2 * dc + 1) * 8192 + u * 128:(2 * dc + 1) * 8192 + (u + 1) * 128, :],
                           sig=W2SEMS[wb2])
                mmlast = None
                for tt4 in range(4):
                    wrb = ([rb_marks[u // 8]]
                           if (th == 1 and dc == 0 and tt4 == 0) else [])
                    mmlast = p.emit("tensor", (lambda u=u, tt4=tt4, wb2=wb2:
                                nc.tensor.matmul(
                                    PB[tt4].ap()[:, :],
                                    h1.ap()[:, u, tt4 * 128:(tt4 + 1) * 128],
                                    w2c.ap()[:, wb2, :],
                                    start=(u == 0), stop=(u == 63))),
                                waits=[(W2SEMS[wb2], l2), H1_OK] + wrb +
                                      (bank_wait(tt4) if u == 0 else []),
                                sig=(T if (tt4 == 3) else None))
                w2free[wb2] = (T, mmlast)
            for tt4 in range(4):
                wff = [FF_FREE] if (FF_FREE and dc == 0 and tt4 == 0) else []
                ev = p.emit("vector", (lambda tt4=tt4, dc=dc: nc.vector.tensor_copy(
                    ffnst.ap()[:, tt4, dc, :], PB[tt4].ap()[:, :])),
                    waits=[(T, p.cnt[T])] + wff, sig=V)
                bank_free[tt4] = (V, ev)
        FF_OK = (V, p.cnt[V])
        for tt4 in range(4):
            tt = th * 4 + tt4
            crr = p.dma("sync", rsb2.ap()[:, :], resid_sp[tt * 128:(tt + 1) * 128, :],
                        waits=RESID_OK + [(V, p.cnt[V])], sig=SQ[1])
            src_ap = ffnst.ap()[:, tt4, :, :].rearrange("p a b -> p (a b)")
            vs = p.emit("vector", (lambda src_ap=src_ap: nc.vector.reduce_sum(
                smalls.ap()[:, 0:1], src_ap, axis=mybir.AxisListType.X)),
                waits=[FF_OK], sig=V)
            p.emit("scalar", (lambda src_ap=src_ap: nc.scalar.activation(
                sg4.ap()[:, :], src_ap[:, 0:512], AF.Square,
                accum_out=smalls.ap()[:, 70:71])), waits=[FF_OK])
            for qq in range(1, 4):
                p.emit("scalar", (lambda src_ap=src_ap, qq=qq: nc.scalar.activation(
                    sg4.ap()[:, :], src_ap[:, qq * 512:(qq + 1) * 512], AF.Square,
                    accum_out=smalls.ap()[:, 70 + qq:71 + qq])))
            af = p.emit("scalar", (lambda: nc.scalar.copy(
                smalls.ap()[:, 74:75], smalls.ap()[:, 70:71])), sig=A)
            vq = p.emit("vector", (lambda: nc.vector.tensor_single_scalar(
                smalls.ap()[:, 1:2], smalls.ap()[:, 70:71], smalls.ap()[:, 71:72],
                ALU.add)), waits=[(A, af)], sig=V)
            p.vspace(nc)
            vq = p.emit("vector", (lambda: nc.vector.tensor_single_scalar(
                smalls.ap()[:, 1:2], smalls.ap()[:, 1:2], smalls.ap()[:, 72:73],
                ALU.add)), sig=V)
            p.vspace(nc)
            vq = p.emit("vector", (lambda: nc.vector.tensor_single_scalar(
                smalls.ap()[:, 1:2], smalls.ap()[:, 1:2], smalls.ap()[:, 73:74],
                ALU.add)), sig=V)
            v1 = p.emit("vector", (lambda: nc.vector.tensor_scalar_mul(
                smalls.ap()[:, 2:3], smalls.ap()[:, 0:1], 1.0 / 2048.0)), sig=V)
            v2 = p.emit("vector", (lambda: nc.vector.tensor_scalar_mul(
                smalls.ap()[:, 3:4], smalls.ap()[:, 1:2], 1.0 / 2048.0)), sig=V)
            v3 = p.emit("vector", (lambda: nc.vector.tensor_single_scalar(
                smalls.ap()[:, 4:5], smalls.ap()[:, 2:3], smalls.ap()[:, 2:3],
                ALU.mult)), sig=V)
            p.vspace(nc)
            v4 = p.emit("vector", (lambda: nc.vector.tensor_single_scalar(
                smalls.ap()[:, 3:4], smalls.ap()[:, 3:4], smalls.ap()[:, 4:5],
                ALU.subtract)), sig=V)
            a2 = p.emit("scalar", (lambda: nc.scalar.activation(
                smalls.ap()[:, 5:6], smalls.ap()[:, 3:4], AF.Sqrt,
                bias=smalls.ap()[:, 54:55])), waits=[(V, v4)], sig=A)
            v5 = p.emit("vector", (lambda: nc.vector.reciprocal(
                smalls.ap()[:, 6:7], smalls.ap()[:, 5:6])), waits=[(A, a2)], sig=V)
            p.vspace(nc)
            ob = tt4 % 2
            vno = p.emit("vector", (lambda src_ap=src_ap, ob=ob: nc.vector.tensor_scalar(
                ostage.ap()[:, ob, :], src_ap, smalls.ap()[:, 2:3],
                smalls.ap()[:, 6:7], ALU.subtract, ALU.mult)),
                waits=[(V, v5)] + ([(OUT_SEMS[ob], out_last[-2][1])]
                                   if len(out_last) >= 2 else []), sig=V)
            voa = p.emit("vector", (lambda ob=ob: nc.vector.tensor_add(
                ostage.ap()[:, ob, :], ostage.ap()[:, ob, :], rsb2.ap()[:, :])),
                waits=[(SQ[1], crr)], sig=V)
            od = p.dma("sync", out_h[tt * 128:(tt + 1) * 128, :], ostage.ap()[:, ob, :],
                       waits=[(V, voa)], sig=OUT_SEMS[ob])
            out_last.append((OUT_SEMS[ob], od))
        FF_FREE = (V, p.cnt[V])

    tapn = []
    internal_map = dict(win_bf=win_bf, wout_bf=wout_bf, w1_bf=w1_bf, w3_bf=w3_bf,
                        w2pp_bf=w2pp_bf, z_sp=z_sp, xfm_sp=xfm_sp, st_sp=st_sp,
                        resid_sp=resid_sp, fl_negrp=fl_negrp, fl_r=fl_r,
                        cc_st_out=cc_st_out, cc_ld_out=cc_ld_out, hin_sp=hin_sp,
                        ln1_dbg=ln1_dbg, u_dbg=u_dbg, sm_dbg=sm_dbg, lt_dbg=lt_dbg,
                        attn_sp=attn_sp, y_dbg=y_dbg, sc_dbg=sc_dbg, cb_dbg=cb_dbg,
                        ygfm_dbg=ygfm_dbg, woc_dbg=woc_dbg)
    p.wait("sync", [(nm, p.cnt[nm]) for nm in list(p.cnt)])
    final = None
    for name, shape, dt in debug_taps:
        src = internal_map[name]
        dst = nc.dram_tensor("dbg_" + name, list(shape), dt, kind="ExternalOutput").ap()
        final = p.dma("sync", dst[:, :] if len(shape) == 2 else dst[:, :, :],
                      src[:, :] if len(shape) == 2 else src[:, :, :], sig=SQ[5])
        tapn.append("dbg_" + name)


    with nc.Block() as block:
        @block.sync
        def _(e):
            p.replay("sync", e)

        @block.gpsimd
        def _(e):
            p.replay("gpsimd", e)

        @block.tensor
        def _(e):
            p.replay("tensor", e)

        @block.vector
        def _(e):
            p.replay("vector", e)

        @block.scalar
        def _(e):
            p.replay("scalar", e)

    return nc, tapn



def make_in_maps(inp):
    u = np.asarray(inp["L0_attn_u"], np.float32)
    W_in = np.ascontiguousarray(np.asarray(inp["L0_attn_W_in"], np.float32))
    conv_w = np.asarray(inp["L0_attn_conv_w"], np.float32)
    conv_b = np.asarray(inp["L0_attn_conv_b"], np.float32)
    dt_bias = np.asarray(inp["L0_attn_dt_bias"], np.float32)
    A_log = np.asarray(inp["L0_attn_A_log"], np.float32)
    Dp = np.asarray(inp["L0_attn_Dp"], np.float32)
    norm_w = np.asarray(inp["L0_attn_norm_w"], np.float32)
    W_out = np.ascontiguousarray(np.asarray(inp["L0_attn_W_out"], np.float32))
    W1 = np.ascontiguousarray(np.asarray(inp["L0_ffn_W1"], np.float32))
    W2 = np.ascontiguousarray(np.asarray(inp["L0_ffn_W2"], np.float32))
    W3 = np.ascontiguousarray(np.asarray(inp["L0_ffn_W3"], np.float32))

    convw_t = np.ascontiguousarray(conv_w.reshape(34, 128, 4).transpose(1, 0, 2))
    convb_t = np.ascontiguousarray(conv_b.reshape(34, 128).T)
    dtb_row = np.zeros((1, 128), np.float32)
    dtb_row[0, :64] = dt_bias
    alog_col = np.ascontiguousarray(A_log.reshape(64, 1))
    dp_rep = np.ascontiguousarray(np.repeat(Dp, HP).reshape(1, DI))
    nw_row = np.ascontiguousarray(norm_w.reshape(1, DI))

    WinT = np.zeros((DM, OPAD), np.float32)
    WinT[:, :OPROJ] = W_in.T
    WoutT = np.ascontiguousarray(W_out.T)
    W1T = np.ascontiguousarray(W1.T)
    W3T = np.ascontiguousarray(W3.T)
    maps = []
    for c in range(8):
        b, q = divmod(c, 4)
        t0 = q * Q
        ue = np.zeros((EXT, DM), np.float32)
        lo = max(0, t0 - 3)
        ue[3 - (t0 - lo):] = u[b, lo:t0 + Q]
        selm = np.zeros((4, 3), np.float32)
        selv = np.zeros((3, 1), np.float32)
        for s in range(3):
            if s < q:
                selv[s, 0] = 1.0
                for k in range(s + 1, q):
                    selm[k, s] = 1.0
        nwt = np.ascontiguousarray(
            norm_w[512 * c:512 * (c + 1)].reshape(4, 128).T)
        maps.append(dict(
            u_ext=np.ascontiguousarray(ue),
            win_sl=np.ascontiguousarray(WinT[256 * c:256 * (c + 1)]),
            wout_sl=np.ascontiguousarray(WoutT[512 * c:512 * (c + 1)]),
            w1_sl=np.ascontiguousarray(W1T[256 * c:256 * (c + 1)]),
            w3_sl=np.ascontiguousarray(W3T[256 * c:256 * (c + 1)]),
            w2_sl=np.ascontiguousarray(W2[256 * c:256 * (c + 1)]),
            convw_t=convw_t, convb_t=convb_t, dtb_row=dtb_row,
            alog_col=alog_col, dp_rep=dp_rep, nw_row=nw_row,
            nw_t=nwt, selm=selm, selv=selv,
        ))
    return maps


def kernel(**inp):
    nc, _ = build_program()
    maps = make_in_maps(inp)
    res = run_bass_kernel_spmd(nc, maps, core_ids=list(range(8)))
    out = np.stack([res.results[c]["out"] for c in range(8)], axis=0)
    return np.ascontiguousarray(out.reshape(2, 4, Q, DM).reshape(2, L, DM))
